# version 08
# speedup vs baseline: 1.2075x; 1.0488x over previous
import contextlib
import numpy as np
import ml_dtypes
import concourse.bass as bass
import concourse.mybir as mybir
from concourse.bass_utils import run_bass_kernel_spmd

F32 = mybir.dt.float32
BF16 = mybir.dt.bfloat16
U8 = mybir.dt.uint8
AF = mybir.ActivationFunctionType
ALU = mybir.AluOpType
AX = mybir.AxisListType

NCORES = 8
T = 2048
D = 2048


ENGINES = ("pe", "dve", "act", "pool", "sq")
NLANES = 4
_ENG_ATTR = {"pe": "tensor", "dve": "vector", "act": "scalar",
             "pool": "gpsimd", "sq": "sync"}


class Buf:
    __slots__ = ("name", "writers", "readers")

    def __init__(self, name=""):
        self.name = name
        self.writers = []
        self.readers = []


class _Op:
    __slots__ = ("fn", "deps", "inc", "observed")

    def __init__(self, fn, deps, inc):
        self.fn = fn
        self.deps = deps
        self.inc = inc
        self.observed = False


class Em:

    def __init__(self, nc):
        self.nc = nc
        self.ops = {e: [] for e in ENGINES}
        self._barrier = []
        self.phase_marks = []
        self.inst_marks = None

    def mark(self, label):
        self.phase_marks.append((label, {e: len(self.ops[e]) for e in ENGINES}))

    def barrier(self):
        bar = [(e, len(self.ops[e]) - 1) for e in ENGINES if self.ops[e]]
        for q in ("sq", "pool"):
            found = 0
            for i in range(len(self.ops[q]) - 1, -1, -1):
                if self.ops[q][i].inc == 16:
                    bar.append((q, i))
                    found += 1
                    if found >= NLANES:
                        break
        self._barrier = bar

    def op(self, eng, fn, reads=(), writes=(), accwrites=(), deps=(), dma=False):
        dep = set(deps)
        dep.update(self._barrier)
        for b in reads:
            dep.update(b.writers)
        for b in writes:
            dep.update(b.writers)
            dep.update(b.readers)
        for b in accwrites:
            dep.update(b.readers)
        tok = (eng, len(self.ops[eng]))
        dep.discard(tok)
        self.ops[eng].append(_Op(fn, sorted(dep), 16 if dma else 1))
        for b in reads:
            b.readers.append(tok)
        for b in writes:
            b.writers = [tok]
            b.readers = []
        for b in accwrites:
            b.writers.append(tok)
        return tok

    def dma(self, eng, out, in_, **bufs):
        eng = "pool" if type(out.tensor).__name__ == "DRamTensorHandle" else "sq"
        return self.op(eng, lambda e, o=out, i=in_: e.dma_start(out=o, in_=i),
                       dma=True, **bufs)

    def lower(self, es):
        sems = {e: es.enter_context(self.nc.semaphore(f"sem_{e}"))
                for e in ENGINES}
        lsem = {q: [es.enter_context(self.nc.semaphore(f"sem_{q}L{l}"))
                    for l in range(NLANES)] for q in ("sq", "pool")}
        for e in ENGINES:
            for i, op in enumerate(self.ops[e]):
                for (fe, fi) in op.deps:
                    if fe == "pe" and e == "pe":
                        continue
                    self.ops[fe][fi].observed = True
        for e in ("sq", "pool"):
            for op in self.ops[e]:
                if op.inc == 16:
                    op.observed = True
        cum = {}
        lane_final = {q: [0] * NLANES for q in ("sq", "pool")}
        lanes = {}
        for e in ENGINES:
            c = 0
            lcnt = [0] * NLANES
            ldma = 0
            lst = []
            lns = []
            for op in self.ops[e]:
                if e in ("sq", "pool") and op.inc == 16:
                    lane = ldma % NLANES
                    ldma += 1
                    lcnt[lane] += 16
                    lns.append(lane)
                    lst.append(((e, lane), lcnt[lane]))
                    lane_final[e][lane] = lcnt[lane]
                else:
                    lns.append(None)
                    if op.observed:
                        c += op.inc
                    lst.append((e, c))
            cum[e] = lst
            lanes[e] = lns
        block = es.enter_context(self.nc.Block())
        em = self

        def play(eng_name, eng_obj):
            hiwater = {}

            def wait(key, v):
                if v > hiwater.get(key, 0):
                    s = (lsem[key[0]][key[1]] if isinstance(key, tuple)
                         else sems[key])
                    eng_obj.wait_ge(s, v)
                    hiwater[key] = v

            for i, op in enumerate(em.ops[eng_name]):
                ln = lanes[eng_name][i]
                if ln is not None:
                    key, v = cum[eng_name][i]
                    if v > 16:
                        wait(key, v - 16)
                need = {}
                for (fe, fi) in op.deps:
                    if fe == "pe" and eng_name == "pe":
                        continue
                    key, v = cum[fe][fi]
                    if v > need.get(key, 0):
                        need[key] = v
                for key, v in sorted(need.items(), key=str):
                    wait(key, v)
                ins = op.fn(eng_obj)
                if ln is not None:
                    ins.then_inc(lsem[eng_name][ln], 16)
                elif op.observed:
                    ins.then_inc(sems[eng_name], op.inc)
            if eng_name in ("sq", "pool"):
                for l in range(NLANES):
                    if lane_final[eng_name][l]:
                        wait((eng_name, l), lane_final[eng_name][l])

        @block.tensor
        def _(t):
            play("pe", t)

        @block.vector
        def _(v):
            play("dve", v)

        @block.scalar
        def _(s):
            play("act", s)

        @block.gpsimd
        def _(g):
            play("pool", g)

        @block.sync
        def _(s):
            play("sq", s)



def pack_lhsT(M):
    K, N = M.shape
    A = np.asarray(M).reshape(K // 128, 128, N // 128, 128)
    return np.ascontiguousarray(A.transpose(2, 1, 0, 3))


def u8_of_i32(a):
    a = np.ascontiguousarray(a)
    assert a.dtype == np.int32
    return np.ascontiguousarray(a.view(np.uint8).reshape(*a.shape, 4)[..., 0])



class Ctx:

    def __init__(self, name):
        self.nc = bass.Bass("TRN2", target_bir_lowering=False)
        self.em = Em(self.nc)
        self.es = contextlib.ExitStack()
        self._scope = None
        self.prefix = ""
        self._consts = {}
        self._nbuf = 0

    def push_scope(self, prefix):
        assert self._scope is None
        self._scope = contextlib.ExitStack()
        self.prefix = prefix

    def pop_scope(self):
        self.em.barrier()
        self._scope.close()
        self._scope = None
        self.prefix = ""

    def sb(self, name, shape, dtype):
        es = self._scope if self._scope is not None else self.es
        t = es.enter_context(self.nc.sbuf_tensor(self.prefix + name,
                                                 list(shape), dtype))
        return t

    def ps(self, name, shape, dtype=F32):
        es = self._scope if self._scope is not None else self.es
        t = es.enter_context(self.nc.psum_tensor(self.prefix + name,
                                                 list(shape), dtype))
        return t

    def buf(self, name=""):
        self._nbuf += 1
        return Buf(name or f"b{self._nbuf}")

    def const(self, key, arr):
        if key in self._consts:
            return self._consts[key]
        arr = np.ascontiguousarray(arr)
        dt = {np.dtype(np.float32): F32,
              np.dtype(ml_dtypes.bfloat16): BF16,
              np.dtype(np.uint8): U8}[arr.dtype]
        dr = self.nc.inline_tensor(arr, name=f"c_{key}").ap()
        sb = self.es.enter_context(
            self.nc.sbuf_tensor(f"cs_{key}", list(arr.shape), dt))
        b = self.buf(f"const_{key}")
        self.em.dma("pool", sb[:], dr, accwrites=[b])
        self._consts[key] = (sb, b)
        return self._consts[key]

    def cvec(self, value, parts=128, dtype=None):
        key = ("cv", value, parts)
        if key in self._consts:
            return self._consts[key]
        sb = self.es.enter_context(
            self.nc.sbuf_tensor(f"cv_{len(self._consts)}", [parts, 1], F32))
        b = self.buf()
        self.em.op("pool", lambda e: e.memset(sb[:], value), accwrites=[b])
        self._consts[key] = (sb, b)
        return self._consts[key]

    def finish(self):
        self.em.lower(self.es)
        self.es.close()
        return self.nc


def np_triu_inc(n=128):
    return np.triu(np.ones((n, n))).astype(ml_dtypes.bfloat16)



NWT = 56
WT_Q0, WT_K0, WT_S0, WT_V0, WT_G0 = 0, 8, 16, 24, 40


def host_pack_l0_proj(wq, wk, ws, wv, wg):
    mats = [pack_lhsT(np.asarray(m, np.float32)) for m in (wq, wk, ws, wv, wg)]
    return np.concatenate(mats, axis=0)


def emit_p0(ctx, xT, w_l0p, proj):
    nc, em = ctx.nc, ctx.em
    xb = ctx.sb("p0_xb", [128, 16 * T], BF16)
    xf = [ctx.sb(f"p0_xf{i}", [128, T], F32) for i in range(2)]
    xf_b = [ctx.buf() for _ in range(2)]
    xb_b = ctx.buf("xb")
    for dc in range(16):
        i = dc % 2
        em.dma("sq", xf[i][:], xT[dc * 128:(dc + 1) * 128, :], writes=[xf_b[i]])
        em.op("dve",
              lambda e, dc=dc, i=i: e.tensor_copy(xb[:, dc * T:(dc + 1) * T],
                                                  xf[i][:]),
              reads=[xf_b[i]], accwrites=[xb_b])
    wf = [ctx.sb(f"p0_wf{i}", [128, 16 * 128], F32) for i in range(2)]
    wb = [ctx.sb(f"p0_wb{i}", [128, 16 * 128], BF16) for i in range(2)]
    wf_b = [ctx.buf() for _ in range(2)]
    wb_b = [ctx.buf() for _ in range(2)]
    pp = [ctx.ps(f"p0_ps{i}", [128, 512]) for i in range(4)]
    pp_b = [ctx.buf(f"psbank{i}") for i in range(4)]
    ob = [ctx.sb(f"p0_ob{i}", [128, 512], BF16) for i in range(4)]
    ob_b = [ctx.buf() for _ in range(4)]
    for wt in range(NWT):
        i = wt % 2
        em.dma("sq", wf[i][:].rearrange("p (c f) -> p c f", c=16),
               w_l0p[wt], writes=[wf_b[i]])
        em.op("dve", lambda e, i=i: e.tensor_copy(wb[i][:], wf[i][:]),
              reads=[wf_b[i]], writes=[wb_b[i]])
        for ts in range(4):
            j = (wt * 4 + ts) % 4
            def mmgroup(e, i=i, ts=ts, j=j):
                for dc in range(16):
                    ins = e.matmul(pp[j][:], wb[i][:, dc * 128:(dc + 1) * 128],
                                   xb[:, dc * T + ts * 512: dc * T + ts * 512 + 512],
                                   start=(dc == 0), stop=(dc == 15))
                return ins
            em.op("pe", mmgroup, reads=[wb_b[i], xb_b], writes=[pp_b[j]])
            scale = 0.125 if wt < 8 else 1.0
            em.op("act", lambda e, j=j, scale=scale: e.activation(
                ob[j][:], pp[j][:], AF.Copy, scale=scale),
                reads=[pp_b[j]], writes=[ob_b[j]])
            tg, cc = ts, None
            em.dma("sq", proj[ts, wt], ob[j][:], reads=[ob_b[j]])



def emit_p1(ctx, proj, gnw_dr, og):
    nc, em = ctx.nc, ctx.em
    H, DK, DV, M, C = 16, 64, 128, 64, 128

    triu, triu_b = ctx.const("triu", np_triu_inc(128))
    i128b, i128b_b = ctx.const("i128b", np.eye(128, dtype=ml_dtypes.bfloat16))
    i128f, i128f_b = ctx.const("i128f", np.eye(128, dtype=np.float32))
    irs_np = np.zeros((128, 64), dtype=ml_dtypes.bfloat16)
    for p in range(128):
        irs_np[p, p % 64] = 1
    irs, irs_b = ctx.const("irs", irs_np)
    onesr, onesr_b = ctx.const("onesr", np.ones((1, 128), dtype=ml_dtypes.bfloat16))
    onesrf, onesrf_b = ctx.const("onesrf", np.ones((1, 128), dtype=np.float32))
    onescol, onescol_b = ctx.const("onescol", np.ones((128, 1), dtype=ml_dtypes.bfloat16))
    o128, o128_b = ctx.const("o128", np.full((128, 1), 1.0 / 128, dtype=np.float32))

    eps5, eps5_b = ctx.cvec(1e-5)
    gnw = ctx.sb("gnw_sb", [128, 1], F32)
    gnw_b = ctx.buf("gnw")
    em.dma("pool", gnw[:], gnw_dr.rearrange("(p o) -> p o", o=1), writes=[gnw_b])

    slab = ctx.sb("slab", [128, NWT * 512], BF16)
    slab_b = ctx.buf("slab")
    ogs = ctx.sb("ogs", [128, H * 512], BF16)
    ogs_b = ctx.buf("ogs")
    cst = ctx.sb("cst", [128, 8 * 192], BF16)
    cst_b = ctx.buf("cst")
    asb = ctx.sb("asb", [1, H * 64], F32)
    asb_b = ctx.buf("asb")
    abf = ctx.sb("abf", [1, H * 64], BF16)
    abf_b = ctx.buf("abf")

    carry_ps = ctx.ps("carry_ps", [128, 192]); carry_pb = ctx.buf("carry_ps")
    sc_ps = ctx.ps("sc_ps", [128, 128]); sc_pb = ctx.buf("sc_ps")
    a_ps = ctx.ps("a_ps", [128, 64]); a_pb = ctx.buf("a_ps")
    ok_ps = ctx.ps("ok_ps", [128, 64]); ok_pb = ctx.buf("ok_ps")
    ct_ps = ctx.ps("ct_ps", [128, 128]); ct_pb = ctx.buf("ct_ps")
    ov_ps = ctx.ps("ov_ps", [128, 128]); ov_pb = ctx.buf("ov_ps")
    trp = ctx.ps("trp", [128, 384], BF16); trp_pb = ctx.buf("trp")
    msc = ctx.ps("msc", [128, 384]); msc_pb = ctx.buf("msc")

    e_tm = ctx.sb("e_tm", [128, 64], BF16); e_tm_b = ctx.buf()
    k_tm = ctx.sb("k_tm", [128, 64], BF16); k_tm_b = ctx.buf()
    v_tm = ctx.sb("v_tm", [128, 128], BF16); v_tm_b = ctx.buf()
    skq = ctx.sb("skq", [128, 128], BF16); skq_b = ctx.buf()
    recA = ctx.sb("recA", [128, 64], F32); recA_b = ctx.buf()
    okn = ctx.sb("okn", [128, 64], F32); okn_b = ctx.buf()
    nmx = ctx.sb("nmx", [128, 1], F32); nmx_b = ctx.buf()
    pve = ctx.sb("pve", [128, 64], F32); pve_b = ctx.buf()
    ell = ctx.sb("ell", [128, 1], F32); ell_b = ctx.buf()
    rcl = ctx.sb("rcl", [128, 1], F32); rcl_b = ctx.buf()
    pab = ctx.sb("pab", [128, 64], BF16); pab_b = ctx.buf()
    pam = ctx.sb("pam", [128, 128], BF16); pam_b = ctx.buf()
    ctb = ctx.sb("ctb", [128, 128], BF16); ctb_b = ctx.buf()
    sqv = ctx.sb("sqv", [128, 128], F32); sqv_b = ctx.buf()
    rsd = ctx.sb("rsd", [128, 1], F32); rsd_b = ctx.buf()
    rsd2 = ctx.sb("rsd2", [128, 1], F32); rsd2_b = ctx.buf()
    rsT = ctx.sb("rsT", [1, 128], F32); rsT_b = ctx.buf()
    rsB = ctx.sb("rsB", [128, 128], F32); rsB_b = ctx.buf()
    gt1 = ctx.sb("gt1", [128, 128], F32); gt1_b = ctx.buf()

    def sl(wt, part, nparts, c):
        return slab[part:part + nparts, wt * 512 + c * 128: wt * 512 + (c + 1) * 128]

    for tg in range(4):
        em.dma("sq", slab[:].rearrange("p (w f) -> p w f", w=NWT),
               proj[tg].rearrange("w p f -> p w f"), writes=[slab_b])
        for i in range(8):
            wt = WT_S0 + i
            em.op("dve", lambda e, wt=wt: e.tensor_scalar(
                slab[:, wt * 512:(wt + 1) * 512], slab[:, wt * 512:(wt + 1) * 512],
                32.0, -32.0, ALU.min, ALU.max),
                reads=[slab_b], accwrites=[slab_b])
            em.op("act", lambda e, wt=wt: e.activation(
                slab[:, wt * 512:(wt + 1) * 512], slab[:, wt * 512:(wt + 1) * 512],
                AF.Exp), reads=[slab_b], accwrites=[slab_b])
        for i in range(16):
            wt = WT_G0 + i
            em.op("act", lambda e, wt=wt: e.activation(
                slab[:, wt * 512:(wt + 1) * 512], slab[:, wt * 512:(wt + 1) * 512],
                AF.Silu), reads=[slab_b], accwrites=[slab_b])

        for h in range(H):
            qp = (h % 2) * 64
            qwt, kwt, swt = WT_Q0 + h // 2, WT_K0 + h // 2, WT_S0 + h // 2
            vwt, gwt = WT_V0 + h, WT_G0 + h
            cstr = cst[qp:qp + 64, (h // 2) * 192:(h // 2) * 192 + 192]
            u_bf = cst[qp:qp + 64, (h // 2) * 192:(h // 2) * 192 + 64]
            w_bf = cst[qp:qp + 64, (h // 2) * 192 + 64:(h // 2) * 192 + 192]
            a_f32 = asb[0:1, h * 64:(h + 1) * 64]
            a_bf = abf[0:1, h * 64:(h + 1) * 64]
            cps = carry_ps[qp:qp + 64, :]

            if tg > 0:
                em.op("pe", lambda e, qp=qp, cstr=cstr, cps=cps: e.matmul(
                    cps, irs[qp:qp + 64, :], cstr, start=True, stop=True,
                    tile_position=(qp, qp)),
                    reads=[irs_b, cst_b], writes=[carry_pb])

            for c in range(4):
                first = (tg == 0 and c == 0)
                em.op("pe", lambda e, swt=swt, qp=qp, c=c: e.transpose(
                    trp[:, 0:64], sl(swt, qp, 64, c), irs[qp:qp + 64, :]),
                    reads=[slab_b, irs_b], writes=[trp_pb])
                em.op("dve", lambda e: e.tensor_copy(e_tm[:], trp[:, 0:64]),
                      reads=[trp_pb], writes=[e_tm_b])
                em.op("pe", lambda e, kwt=kwt, qp=qp, c=c: e.transpose(
                    trp[:, 64:128], sl(kwt, qp, 64, c), irs[qp:qp + 64, :]),
                    reads=[slab_b, irs_b], writes=[trp_pb])
                em.op("dve", lambda e: e.tensor_copy(k_tm[:], trp[:, 64:128]),
                      reads=[trp_pb], writes=[k_tm_b])
                em.op("pe", lambda e, vwt=vwt, c=c: e.transpose(
                    trp[:, 128:256], sl(vwt, 0, 128, c), i128b[:]),
                    reads=[slab_b, i128b_b], writes=[trp_pb])
                em.op("dve", lambda e: e.tensor_copy(v_tm[:], trp[:, 128:256]),
                      reads=[trp_pb], writes=[v_tm_b])
                em.op("pe", lambda e, kwt=kwt, qwt=qwt, qp=qp, c=c: e.matmul(
                    sc_ps[:], sl(kwt, qp, 64, c), sl(qwt, qp, 64, c),
                    start=True, stop=True),
                    reads=[slab_b], writes=[sc_pb])
                em.op("dve", lambda e: e.tensor_tensor(skq[:], sc_ps[:], triu[:],
                                                       ALU.mult),
                      reads=[sc_pb, triu_b], writes=[skq_b])
                def a_mm(e, first=first, a_bf=a_bf):
                    ins = e.matmul(a_ps[:], triu[:], e_tm[:], start=True, stop=first)
                    if not first:
                        ins = e.matmul(a_ps[:], onesr[:], a_bf, start=False,
                                       stop=True)
                    return ins
                em.op("pe", a_mm, reads=[triu_b, e_tm_b, onesr_b, abf_b],
                      writes=[a_pb])
                em.op("dve", lambda e: e.reciprocal(recA[:], a_ps[:]),
                      reads=[a_pb], writes=[recA_b])
                def ok_mm(e, first=first, qwt=qwt, qp=qp, c=c, u_bf=u_bf):
                    ins = e.matmul(ok_ps[:], skq[:], e_tm[:], start=True, stop=first)
                    if not first:
                        ins = e.matmul(ok_ps[:], sl(qwt, qp, 64, c), u_bf,
                                       start=False, stop=True)
                    return ins
                em.op("pe", ok_mm, reads=[skq_b, e_tm_b, slab_b, cst_b],
                      writes=[ok_pb])
                em.op("dve", lambda e: e.tensor_tensor(okn[:], ok_ps[:], recA[:],
                                                       ALU.mult),
                      reads=[ok_pb, recA_b], writes=[okn_b])
                em.op("dve", lambda e: e.reduce_max(nmx[:], okn[:], axis=AX.X,
                                                    negate=True),
                      reads=[okn_b], writes=[nmx_b])
                em.op("act", lambda e: e.activation(pve[:], okn[:], AF.Exp,
                                                    bias=nmx[:]),
                      reads=[okn_b, nmx_b], writes=[pve_b])
                em.op("dve", lambda e: e.reduce_sum(ell[:], pve[:], axis=AX.X),
                      reads=[pve_b], writes=[ell_b])
                em.op("dve", lambda e: e.reciprocal(rcl[:], ell[:]),
                      reads=[ell_b], writes=[rcl_b])
                em.op("dve", lambda e: e.scalar_tensor_tensor(
                    pab[:], pve[:], rcl[:], recA[:], ALU.mult, ALU.mult),
                    reads=[pve_b, rcl_b, recA_b], writes=[pab_b])
                em.op("pe", lambda e, qp=qp: e.transpose(
                    trp[qp:qp + 64, 256:384], pab[:], i128b[:],
                    tile_position=(0, qp)),
                    reads=[pab_b, i128b_b], writes=[trp_pb])
                em.op("dve", lambda e, qp=qp: e.tensor_copy(
                    pam[qp:qp + 64, :], trp[qp:qp + 64, 256:384]),
                    reads=[trp_pb], accwrites=[pam_b])
                em.op("pe", lambda e, swt=swt, qp=qp, c=c: e.matmul(
                    ct_ps[:], sl(swt, qp, 64, c), pam[qp:qp + 64, :],
                    start=True, stop=True),
                    reads=[slab_b, pam_b], writes=[ct_pb])
                em.op("dve", lambda e: e.tensor_tensor(ctb[:], ct_ps[:], triu[:],
                                                       ALU.mult),
                      reads=[ct_pb, triu_b], writes=[ctb_b])
                def ov_mm(e, first=first, qp=qp, w_bf=w_bf):
                    ins = e.matmul(ov_ps[:], v_tm[:], ctb[:], start=True, stop=first)
                    if not first:
                        ins = e.matmul(ov_ps[:], w_bf, pam[qp:qp + 64, :],
                                       start=False, stop=True)
                    return ins
                em.op("pe", ov_mm, reads=[v_tm_b, ctb_b, cst_b, pam_b],
                      writes=[ov_pb])
                em.op("act", lambda e: e.activation(sqv[:], ov_ps[:], AF.Square),
                      reads=[ov_pb], writes=[sqv_b])
                em.op("pe", lambda e: e.matmul(msc[:, 0:1], sqv[:], o128[:],
                                               start=True, stop=True),
                      reads=[sqv_b, o128_b], writes=[msc_pb])
                em.op("act", lambda e: e.activation(rsd[:], msc[:, 0:1], AF.Sqrt,
                                                    bias=eps5[:]),
                      reads=[msc_pb, eps5_b], writes=[rsd_b])
                em.op("dve", lambda e: e.reciprocal(rsd2[:], rsd[:]),
                      reads=[rsd_b], writes=[rsd2_b])
                em.op("pe", lambda e: e.transpose(msc[0:1, 64:192], rsd2[:],
                                                  i128f[:]),
                      reads=[rsd2_b, i128f_b], writes=[msc_pb])
                em.op("dve", lambda e: e.tensor_copy(rsT[:], msc[0:1, 64:192]),
                      reads=[msc_pb], writes=[rsT_b])
                em.op("pe", lambda e: e.matmul(msc[:, 256:384], onesrf[:], rsT[:],
                                               start=True, stop=True),
                      reads=[onesrf_b, rsT_b], writes=[msc_pb])
                em.op("dve", lambda e: e.tensor_copy(rsB[:], msc[:, 256:384]),
                      reads=[msc_pb], writes=[rsB_b])
                em.op("dve", lambda e: e.tensor_tensor(gt1[:], ov_ps[:],
                                                       rsB[:], ALU.mult),
                      reads=[ov_pb, rsB_b], writes=[gt1_b])
                em.op("dve", lambda e, h=h, c=c, gwt=gwt: e.scalar_tensor_tensor(
                    ogs[:, h * 512 + c * 128: h * 512 + (c + 1) * 128],
                    gt1[:], gnw[:], sl(gwt, 0, 128, c), ALU.mult, ALU.mult),
                    reads=[gt1_b, gnw_b, slab_b], accwrites=[ogs_b])
                em.op("pe", lambda e, qp=qp, first=first: e.matmul(
                    carry_ps[qp:qp + 64, 0:64], k_tm[:], e_tm[:],
                    start=first, stop=True, tile_position=(0, qp)),
                    reads=[k_tm_b, e_tm_b], writes=[carry_pb])
                em.op("pe", lambda e, qp=qp, first=first: e.matmul(
                    carry_ps[qp:qp + 64, 64:192], e_tm[:], v_tm[:],
                    start=first, stop=True, tile_position=(0, qp)),
                    reads=[e_tm_b, v_tm_b], writes=[carry_pb])
                em.op("pe", lambda e: e.matmul(msc[0:1, 192:256], onescol[:],
                                               e_tm[:], start=True, stop=True),
                      reads=[onescol_b, e_tm_b], writes=[msc_pb])
                if first:
                    em.op("dve", lambda e, a_f32=a_f32: e.tensor_copy(
                        a_f32, msc[0:1, 192:256]),
                        reads=[msc_pb], accwrites=[asb_b])
                else:
                    em.op("dve", lambda e, a_f32=a_f32: e.tensor_tensor(
                        a_f32, a_f32, msc[0:1, 192:256], ALU.add),
                        reads=[msc_pb, asb_b], accwrites=[asb_b])
                em.op("dve", lambda e, a_f32=a_f32, a_bf=a_bf: e.tensor_copy(
                    a_bf, a_f32), reads=[asb_b], accwrites=[abf_b])
                em.op("dve", lambda e, cps=cps, cstr=cstr: e.tensor_copy(cstr, cps),
                      reads=[carry_pb], accwrites=[cst_b])
        for hh in range(H):
            em.dma("sq", og[hh, :, tg * 512:(tg + 1) * 512],
                   ogs[:, hh * 512:(hh + 1) * 512], reads=[ogs_b])


def build_p1():
    ctx = Ctx("p1")
    nc = ctx.nc
    proj = nc.dram_tensor("proj", [4, NWT, 128, 512], BF16,
                          kind="ExternalInput").ap()
    gnw = nc.dram_tensor("gnw", [128], F32, kind="ExternalInput").ap()
    og = nc.dram_tensor("og", [16, 128, T], BF16, kind="ExternalOutput").ap()
    emit_p1(ctx, proj, gnw, og)
    ctx.finish()
    return nc



class LNRes:
    def __init__(self, ctx):
        self.onescol, self.onescol_b = ctx.const(
            "onescolf", np.ones((128, 1), dtype=np.float32))
        self.onesrow, self.onesrow_b = ctx.const(
            "onesrowf", np.ones((1, 128), dtype=np.float32))
        self.eps1, self.eps1_b = ctx.cvec(1e-5, parts=1)
        self.st1 = ctx.ps("ln_st1", [1, 512]); self.st1_b = ctx.buf("ln_st1")
        self.st2 = ctx.ps("ln_st2", [1, 512]); self.st2_b = ctx.buf("ln_st2")
        self.bca = ctx.ps("ln_bca", [128, 512]); self.bca_b = ctx.buf("ln_bca")
        self.bcb = ctx.ps("ln_bcb", [128, 512]); self.bcb_b = ctx.buf("ln_bcb")
        self.sq = [ctx.sb(f"ln_sq{i}", [128, 512], F32) for i in range(2)]
        self.sq_b = [ctx.buf() for _ in range(2)]
        self.rows = ctx.sb("ln_rows", [1, 512 * 5], F32)
        self.rows_b = ctx.buf("ln_rows")
        self.absb = ctx.sb("ln_absb", [128, 512], F32); self.absb_b = ctx.buf()
        self.t1 = ctx.sb("ln_t1", [128, 512], F32); self.t1_b = ctx.buf()
        self.xo = [ctx.sb(f"ln_xo{i}", [128, 512], F32) for i in range(2)]
        self.xo_b = [ctx.buf() for _ in range(2)]
        self.xn = [ctx.sb(f"ln_xn{i}", [128, 512], BF16) for i in range(2)]
        self.xn_b = [ctx.buf() for _ in range(2)]


def emit_ln(ctx, ln, rsl, rsl_b, xo_dst, xn_dst):
    em = ctx.em
    R = lambda fc: rsl[:, fc * 512:(fc + 1) * 512]
    mrow = ln.rows[:, 0:512]
    qrow = ln.rows[:, 512:1024]
    var = ln.rows[:, 1024:1536]
    arow = ln.rows[:, 1536:2048]
    brow = ln.rows[:, 2048:2560]
    for fc in range(16):
        i = fc % 2
        em.op("act", lambda e, fc=fc, i=i: e.activation(
            ln.sq[i][:], R(fc), AF.Square), reads=[rsl_b], writes=[ln.sq_b[i]])
        em.op("pe", lambda e, fc=fc: e.matmul(
            ln.st1[:], ln.onescol[:], R(fc), start=(fc == 0), stop=(fc == 15)),
            reads=[ln.onescol_b, rsl_b],
            writes=[ln.st1_b] if fc == 0 else (), accwrites=() if fc == 0 else [ln.st1_b])
        em.op("pe", lambda e, i=i, fc=fc: e.matmul(
            ln.st2[:], ln.onescol[:], ln.sq[i][:], start=(fc == 0), stop=(fc == 15)),
            reads=[ln.onescol_b, ln.sq_b[i]],
            writes=[ln.st2_b] if fc == 0 else (), accwrites=() if fc == 0 else [ln.st2_b])
    em.op("act", lambda e: e.activation(mrow, ln.st1[:], AF.Copy, scale=1.0 / D),
          reads=[ln.st1_b], accwrites=[ln.rows_b])
    em.op("act", lambda e: e.activation(qrow, ln.st2[:], AF.Copy, scale=1.0 / D),
          reads=[ln.st2_b], accwrites=[ln.rows_b])
    em.op("dve", lambda e: e.tensor_tensor(var, mrow, mrow, ALU.mult),
          reads=[ln.rows_b], accwrites=[ln.rows_b])
    em.op("dve", lambda e: e.tensor_tensor(var, qrow, var, ALU.subtract),
          reads=[ln.rows_b], accwrites=[ln.rows_b])
    em.op("act", lambda e: e.activation(var, var, AF.Sqrt, bias=ln.eps1[:]),
          reads=[ln.rows_b, ln.eps1_b], accwrites=[ln.rows_b])
    em.op("dve", lambda e: e.reciprocal(arow, var),
          reads=[ln.rows_b], accwrites=[ln.rows_b])
    em.op("dve", lambda e: e.scalar_tensor_tensor(
        brow, mrow, -1.0, arow, ALU.mult, ALU.mult),
        reads=[ln.rows_b], accwrites=[ln.rows_b])
    em.op("pe", lambda e: e.matmul(ln.bca[:], ln.onesrow[:], arow,
                                   start=True, stop=True),
          reads=[ln.onesrow_b, ln.rows_b], writes=[ln.bca_b])
    em.op("pe", lambda e: e.matmul(ln.bcb[:], ln.onesrow[:], brow,
                                   start=True, stop=True),
          reads=[ln.onesrow_b, ln.rows_b], writes=[ln.bcb_b])
    em.op("dve", lambda e: e.tensor_copy(ln.absb[:], ln.bca[:]),
          reads=[ln.bca_b], writes=[ln.absb_b])
    for fc in range(16):
        i = fc % 2
        em.op("dve", lambda e, fc=fc: e.tensor_tensor(
            ln.t1[:], R(fc), ln.absb[:], ALU.mult),
            reads=[rsl_b, ln.absb_b], writes=[ln.t1_b])
        em.op("dve", lambda e, i=i: e.tensor_tensor(
            ln.xo[i][:], ln.t1[:], ln.bcb[:], ALU.add),
            reads=[ln.t1_b, ln.bcb_b], writes=[ln.xo_b[i]])
        em.dma("sq", xo_dst(fc), ln.xo[i][:], reads=[ln.xo_b[i]])
        if xn_dst is not None:
            em.op("act", lambda e, i=i: e.activation(ln.xn[i][:], ln.xo[i][:],
                                                     AF.Copy),
                  reads=[ln.xo_b[i]], writes=[ln.xn_b[i]])
            em.dma("pool", xn_dst(fc), ln.xn[i][:], reads=[ln.xn_b[i]])



def emit_proj_res_ln(ctx, src_dr, w_dr, xres_dr, xo_dr, xn_dr, res_scale=1.0):
    nc, em = ctx.nc, ctx.em
    ln = LNRes(ctx)
    wob = ctx.sb("prl_wob", [128, 16 * 2048], BF16)
    wob_b = ctx.buf("wob")
    wof = [ctx.sb(f"prl_wof{i}", [128, 2048], F32) for i in range(2)]
    wof_b = [ctx.buf() for _ in range(2)]
    for ot in range(16):
        i = ot % 2
        em.dma("sq", wof[i][:].rearrange("p (c f) -> p c f", c=16), w_dr[ot],
               writes=[wof_b[i]])
        em.op("dve", lambda e, ot=ot, i=i: e.tensor_copy(
            wob[:, ot * 2048:(ot + 1) * 2048], wof[i][:]),
            reads=[wof_b[i]], accwrites=[wob_b])
    srcs = [ctx.sb(f"prl_src{i}", [128, 16 * 512], BF16) for i in range(2)]
    srcs_b = [ctx.buf() for _ in range(2)]
    xc = [ctx.sb(f"prl_xc{i}", [128, 512], F32) for i in range(2)]
    xc_b = [ctx.buf() for _ in range(2)]
    rsl = ctx.sb("prl_rsl", [128, 16 * 512], F32)
    rsl_b = ctx.buf("prl_rsl")
    pp = [ctx.ps(f"prl_ps{i}", [128, 512]) for i in range(2)]
    pp_b = [ctx.buf() for _ in range(2)]
    for ts in range(4):
        i2 = ts % 2
        em.dma("sq", srcs[i2][:].rearrange("p (c f) -> p c f", c=16),
               src_dr[:, :, ts * 512:(ts + 1) * 512].rearrange("h p f -> p h f"),
               writes=[srcs_b[i2]])
        for d2t in range(16):
            j = d2t % 2
            em.dma("pool", xc[j][:],
                   xres_dr[d2t * 128:(d2t + 1) * 128, ts * 512:(ts + 1) * 512],
                   writes=[xc_b[j]])

            def mmg(e, d2t=d2t, i2=i2, j=j):
                for fc in range(16):
                    ins = e.matmul(pp[j][:],
                                   wob[:, d2t * 2048 + fc * 128: d2t * 2048 + fc * 128 + 128],
                                   srcs[i2][:, fc * 512:(fc + 1) * 512],
                                   start=(fc == 0), stop=(fc == 15))
                return ins
            em.op("pe", mmg, reads=[wob_b, srcs_b[i2]], writes=[pp_b[j]])
            em.op("dve", lambda e, d2t=d2t, j=j, rs=res_scale: e.scalar_tensor_tensor(
                rsl[:, d2t * 512:(d2t + 1) * 512], xc[j][:], rs, pp[j][:],
                ALU.mult, ALU.add),
                reads=[xc_b[j], pp_b[j]], accwrites=[rsl_b])
        emit_ln(ctx, ln, rsl, rsl_b,
                lambda fc, ts=ts: xo_dr[fc * 128:(fc + 1) * 128,
                                        ts * 512:(ts + 1) * 512],
                None if xn_dr is None else
                (lambda fc, ts=ts: xn_dr[fc * 128:(fc + 1) * 128,
                                         ts * 512:(ts + 1) * 512]))


def build_p2():
    ctx = Ctx("p2")
    nc = ctx.nc
    og = nc.dram_tensor("og", [16, 128, T], BF16, kind="ExternalInput").ap()
    w_l0o = nc.dram_tensor("w_l0o", [16, 128, 16, 128], F32,
                           kind="ExternalInput").ap()
    xT = nc.dram_tensor("xT", [D, T], F32, kind="ExternalInput").ap()
    x1 = nc.dram_tensor("x1", [D, T], F32, kind="ExternalOutput").ap()
    x1n = nc.dram_tensor("x1n", [D, T], BF16, kind="ExternalOutput").ap()
    emit_proj_res_ln(ctx, og, w_l0o, xT, x1, x1n, res_scale=1.0)
    ctx.finish()
    return nc



def emit_p3(ctx, x1n_dr, wg_dr, wu_dr, gsc_dr, usc_dr, hsc_dr):
    nc, em = ctx.nc, ctx.em
    xb = ctx.sb("p3_xb", [128, 16 * T], BF16)
    xb_b = ctx.buf("p3_xb")
    em.dma("sq", xb[:].rearrange("p (c t) -> p c t", c=16),
           x1n_dr.rearrange("(c p) t -> p c t", p=128), writes=[xb_b])
    gscs = ctx.sb("p3_gsc", [128, 64], F32); gscs_b = ctx.buf()
    uscs = ctx.sb("p3_usc", [128, 64], F32); uscs_b = ctx.buf()
    em.dma("pool", gscs[:], gsc_dr, writes=[gscs_b])
    em.dma("pool", uscs[:], usc_dr, writes=[uscs_b])
    wg8 = [ctx.sb(f"p3_wg8{i}", [128, 2048], U8) for i in range(2)]
    wg8_b = [ctx.buf() for _ in range(2)]
    wu8 = [ctx.sb(f"p3_wu8{i}", [128, 2048], U8) for i in range(2)]
    wu8_b = [ctx.buf() for _ in range(2)]
    wgb = [ctx.sb(f"p3_wgb{i}", [128, 2048], BF16) for i in range(2)]
    wgb_b = [ctx.buf() for _ in range(2)]
    wub = [ctx.sb(f"p3_wub{i}", [128, 2048], BF16) for i in range(2)]
    wub_b = [ctx.buf() for _ in range(2)]
    gs = ctx.sb("p3_gs", [128, 512], F32); gs_b = ctx.buf()
    hst = [ctx.sb(f"p3_hst{i}", [128, 2048], BF16) for i in range(2)]
    hst_b = [ctx.buf() for _ in range(2)]
    pg = [ctx.ps(f"p3_pg{i}", [128, 512]) for i in range(2)]
    pg_b = [ctx.buf() for _ in range(2)]
    pu = [ctx.ps(f"p3_pu{i}", [128, 512]) for i in range(2)]
    pu_b = [ctx.buf() for _ in range(2)]
    for it in range(64):
        i = it % 2
        em.dma("sq", wg8[i][:].rearrange("p (c f) -> p c f", c=16), wg_dr[it],
               writes=[wg8_b[i]])
        em.dma("sq", wu8[i][:].rearrange("p (c f) -> p c f", c=16), wu_dr[it],
               writes=[wu8_b[i]])
        em.op("act", lambda e, i=i: e.activation(wgb[i][:], wg8[i][:], AF.Copy,
                                                 bias=-128.0),
              reads=[wg8_b[i]], writes=[wgb_b[i]])
        em.op("act", lambda e, i=i: e.activation(wub[i][:], wu8[i][:], AF.Copy,
                                                 bias=-128.0),
              reads=[wu8_b[i]], writes=[wub_b[i]])
        for ts in range(4):
            j = ts % 2

            def mmg(e, i=i, ts=ts, j=j, w=None):
                for fc in range(16):
                    ins = e.matmul(pg[j][:], wgb[i][:, fc * 128:(fc + 1) * 128],
                                   xb[:, fc * T + ts * 512: fc * T + ts * 512 + 512],
                                   start=(fc == 0), stop=(fc == 15))
                for fc in range(16):
                    ins = e.matmul(pu[j][:], wub[i][:, fc * 128:(fc + 1) * 128],
                                   xb[:, fc * T + ts * 512: fc * T + ts * 512 + 512],
                                   start=(fc == 0), stop=(fc == 15))
                return ins
            em.op("pe", mmg, reads=[wgb_b[i], wub_b[i], xb_b],
                  writes=[pg_b[j], pu_b[j]])
            em.op("act", lambda e, it=it, j=j: e.activation(
                gs[:], pg[j][:], AF.Silu, scale=gscs[:, it:it + 1]),
                reads=[pg_b[j], gscs_b], writes=[gs_b])
            em.op("dve", lambda e, it=it, i=i, ts=ts, j=j: e.scalar_tensor_tensor(
                hst[i][:, ts * 512:(ts + 1) * 512], pu[j][:], uscs[:, it:it + 1],
                gs[:], ALU.mult, ALU.mult),
                reads=[pu_b[j], uscs_b, gs_b], accwrites=[hst_b[i]])
        em.dma("sq", hsc_dr[it], hst[i][:], reads=[hst_b[i]], writes=())


def build_p3():
    ctx = Ctx("p3")
    nc = ctx.nc
    x1n = nc.dram_tensor("x1n", [D, T], BF16, kind="ExternalInput").ap()
    wg = nc.dram_tensor("w_l0g", [64, 128, 16, 128], U8, kind="ExternalInput").ap()
    wu = nc.dram_tensor("w_l0u", [64, 128, 16, 128], U8, kind="ExternalInput").ap()
    gsc = nc.dram_tensor("gsc", [128, 64], F32, kind="ExternalInput").ap()
    usc = nc.dram_tensor("usc", [128, 64], F32, kind="ExternalInput").ap()
    hsc = nc.dram_tensor("hsc", [64, 128, T], BF16, kind="ExternalOutput").ap()
    emit_p3(ctx, x1n, wg, wu, gsc, usc, hsc)
    ctx.finish()
    return nc



def emit_p4(ctx, hsc_dr, wd_dr, dsc_dr, x1_dr, x2_dr, x2n_dr):
    nc, em = ctx.nc, ctx.em
    ln = LNRes(ctx)
    dscs = ctx.sb("p4_dsc", [128, 16], F32); dscs_b = ctx.buf()
    em.dma("pool", dscs[:], dsc_dr, writes=[dscs_b])
    hsl = ctx.sb("p4_hsl", [128, 64 * 512], BF16)
    hsl_b = ctx.buf("p4_hsl")
    wd8 = [ctx.sb(f"p4_wd8{i}", [128, 8192], U8) for i in range(2)]
    wd8_b = [ctx.buf() for _ in range(2)]
    wdb = [ctx.sb(f"p4_wdb{i}", [128, 8192], BF16) for i in range(2)]
    wdb_b = [ctx.buf() for _ in range(2)]
    xc = [ctx.sb(f"p4_xc{i}", [128, 512], F32) for i in range(2)]
    xc_b = [ctx.buf() for _ in range(2)]
    rsl = ctx.sb("p4_rsl", [128, 16 * 512], F32)
    rsl_b = ctx.buf("p4_rsl")
    pp = [ctx.ps(f"p4_ps{i}", [128, 512]) for i in range(2)]
    pp_b = [ctx.buf() for _ in range(2)]
    for tc in range(4):
        em.dma("sq", hsl[:].rearrange("p (c f) -> p c f", c=64),
               hsc_dr[:, :, tc * 512:(tc + 1) * 512].rearrange("i p f -> p i f"),
               writes=[hsl_b])
        for d2t in range(16):
            i = d2t % 2
            j = d2t % 2
            em.dma("sq", wd8[i][:].rearrange("p (c f) -> p c f", c=64),
                   wd_dr[d2t], writes=[wd8_b[i]])
            em.op("act", lambda e, i=i: e.activation(wdb[i][:], wd8[i][:],
                                                     AF.Copy, bias=-128.0),
                  reads=[wd8_b[i]], writes=[wdb_b[i]])
            em.dma("pool", xc[j][:],
                   x1_dr[d2t * 128:(d2t + 1) * 128, tc * 512:(tc + 1) * 512],
                   writes=[xc_b[j]])

            def mmg(e, i=i, j=j):
                for kc in range(64):
                    ins = e.matmul(pp[j][:], wdb[i][:, kc * 128:(kc + 1) * 128],
                                   hsl[:, kc * 512:(kc + 1) * 512],
                                   start=(kc == 0), stop=(kc == 63))
                return ins
            em.op("pe", mmg, reads=[wdb_b[i], hsl_b], writes=[pp_b[j]])
            em.op("dve", lambda e, d2t=d2t, j=j: e.scalar_tensor_tensor(
                rsl[:, d2t * 512:(d2t + 1) * 512], pp[j][:], dscs[:, d2t:d2t + 1],
                xc[j][:], ALU.mult, ALU.add),
                reads=[pp_b[j], dscs_b, xc_b[j]], accwrites=[rsl_b])
        emit_ln(ctx, ln, rsl, rsl_b,
                lambda fc, tc=tc: x2_dr[fc * 128:(fc + 1) * 128,
                                        tc * 512:(tc + 1) * 512],
                None if x2n_dr is None else
                (lambda fc, tc=tc: x2n_dr[fc * 128:(fc + 1) * 128,
                                          tc * 512:(tc + 1) * 512]))


def build_p4():
    ctx = Ctx("p4")
    nc = ctx.nc
    hsc = nc.dram_tensor("hsc", [64, 128, T], BF16, kind="ExternalInput").ap()
    wd = nc.dram_tensor("w_l0d", [16, 128, 64, 128], U8, kind="ExternalInput").ap()
    dsc = nc.dram_tensor("dsc", [128, 16], F32, kind="ExternalInput").ap()
    x1 = nc.dram_tensor("x1", [D, T], F32, kind="ExternalInput").ap()
    x2 = nc.dram_tensor("x2", [D, T], F32, kind="ExternalOutput").ap()
    x2n = nc.dram_tensor("x2n", [D, T], BF16, kind="ExternalOutput").ap()
    emit_p4(ctx, hsc, wd, dsc, x1, x2, x2n)
    ctx.finish()
    return nc



def emit_p5(ctx, x2_dr, ns_dr, ss_dr, wq_dr, wk_dr, wv_dr, qf_dr, kf_dr, vf_dr):
    nc, em = ctx.nc, ctx.em
    onescol, onescol_b = ctx.const("onescolf", np.ones((128, 1), dtype=np.float32))
    onesrow, onesrow_b = ctx.const("onesrowf", np.ones((1, 128), dtype=np.float32))
    eps1, eps1_b = ctx.cvec(1e-8, parts=1)
    nss = ctx.sb("p5_nss", [128, 16], F32); nss_b = ctx.buf()
    for name, dr in (("ns", ns_dr), ("ss", ss_dr)):
        t = ctx.sb(f"p5_{name}", [128, 16], F32)
        b = ctx.buf()
        em.dma("pool", t[:], dr, writes=[b])
        if name == "ns":
            tns, tns_b = t, b
        else:
            tss, tss_b = t, b
    em.op("dve", lambda e: e.tensor_tensor(nss[:], tns[:], tss[:], ALU.mult),
          reads=[tns_b, tss_b], writes=[nss_b])

    xsl = ctx.sb("p5_xsl", [128, 16 * 512], F32); xsl_b = ctx.buf()
    sq = [ctx.sb(f"p5_sq{i}", [128, 512], F32) for i in range(2)]
    sq_b = [ctx.buf() for _ in range(2)]
    rows = ctx.sb("p5_rows", [1, 512], F32); rows_b = ctx.buf()
    cbs = ctx.sb("p5_cbs", [128, 512], F32); cbs_b = ctx.buf()
    hsl = ctx.sb("p5_hsl", [128, 16 * T], BF16); hsl_b = ctx.buf("p5_hsl")
    st = ctx.ps("p5_st", [1, 512]); st_b = ctx.buf()
    bc = ctx.ps("p5_bc", [128, 512]); bc_b = ctx.buf()
    for ts in range(4):
        for fc in range(16):
            i = fc % 2
            em.dma("sq", xsl[:, fc * 512:(fc + 1) * 512],
                   x2_dr[fc * 128:(fc + 1) * 128, ts * 512:(ts + 1) * 512],
                   accwrites=[xsl_b])
            em.op("act", lambda e, fc=fc, i=i: e.activation(
                sq[i][:], xsl[:, fc * 512:(fc + 1) * 512], AF.Square),
                reads=[xsl_b], writes=[sq_b[i]])
            em.op("pe", lambda e, fc=fc, i=i: e.matmul(
                st[:], onescol[:], sq[i][:], start=(fc == 0), stop=(fc == 15)),
                reads=[onescol_b, sq_b[i]],
                writes=[st_b] if fc == 0 else (), accwrites=() if fc == 0 else [st_b])
        em.op("act", lambda e: e.activation(rows[:], st[:], AF.Sqrt,
                                            scale=1.0 / D),
              reads=[st_b], writes=[rows_b])
        em.op("dve", lambda e: e.tensor_scalar(rows[:], rows[:], eps1[:], None,
                                               ALU.add),
              reads=[rows_b, eps1_b], accwrites=[rows_b])
        em.op("dve", lambda e: e.reciprocal(rows[:], rows[:]),
              reads=[rows_b], accwrites=[rows_b])
        em.op("pe", lambda e: e.matmul(bc[:], onesrow[:], rows[:],
                                       start=True, stop=True),
              reads=[onesrow_b, rows_b], writes=[bc_b])
        em.op("dve", lambda e: e.tensor_copy(cbs[:], bc[:]),
              reads=[bc_b], writes=[cbs_b])
        for fc in range(16):
            em.op("dve", lambda e, fc=fc, ts=ts: e.scalar_tensor_tensor(
                hsl[:, fc * T + ts * 512: fc * T + ts * 512 + 512],
                xsl[:, fc * 512:(fc + 1) * 512], nss[:, fc:fc + 1], cbs[:],
                ALU.mult, ALU.mult),
                reads=[xsl_b, nss_b, cbs_b], accwrites=[hsl_b])
    wf = [ctx.sb(f"p5_wf{i}", [128, 2048], F32) for i in range(2)]
    wf_b = [ctx.buf() for _ in range(2)]
    wb = [ctx.sb(f"p5_wb{i}", [128, 2048], BF16) for i in range(2)]
    wb_b = [ctx.buf() for _ in range(2)]
    ob = [ctx.sb(f"p5_ob{i}", [128, 512], BF16) for i in range(4)]
    ob_b = [ctx.buf() for _ in range(4)]
    pp = [ctx.ps(f"p5_pp{i}", [128, 512]) for i in range(2)]
    pp_b = [ctx.buf() for _ in range(2)]
    jobs = [(wq_dr, ot, qf_dr, ot, 0.125) for ot in range(16)]
    jobs += [(wk_dr, 0, kf_dr, 0, 1.0), (wv_dr, 0, vf_dr, 0, 1.0)]
    for n, (wdr, ot, odr, oot, qsc) in enumerate(jobs):
        i = n % 2
        em.dma("sq", wf[i][:].rearrange("p (c f) -> p c f", c=16), wdr[ot],
               writes=[wf_b[i]])
        em.op("dve", lambda e, i=i: e.tensor_copy(wb[i][:], wf[i][:]),
              reads=[wf_b[i]], writes=[wb_b[i]])
        for ts in range(4):
            j = (n * 4 + ts) % 2
            jo = (n * 4 + ts) % 4

            def mmg(e, i=i, ts=ts, j=j):
                for fc in range(16):
                    ins = e.matmul(pp[j][:], wb[i][:, fc * 128:(fc + 1) * 128],
                                   hsl[:, fc * T + ts * 512: fc * T + ts * 512 + 512],
                                   start=(fc == 0), stop=(fc == 15))
                return ins
            em.op("pe", mmg, reads=[wb_b[i], hsl_b], writes=[pp_b[j]])
            em.op("act", lambda e, j=j, jo=jo, qsc=qsc: e.activation(
                ob[jo][:], pp[j][:], AF.Copy, scale=qsc),
                reads=[pp_b[j]], writes=[ob_b[jo]])
            em.dma("sq", odr[oot * 128:(oot + 1) * 128, ts * 512:(ts + 1) * 512],
                   ob[jo][:], reads=[ob_b[jo]])


def build_p5():
    ctx = Ctx("p5")
    nc = ctx.nc
    x2 = nc.dram_tensor("x2", [D, T], F32, kind="ExternalInput").ap()
    ns = nc.dram_tensor("ns", [128, 16], F32, kind="ExternalInput").ap()
    ss = nc.dram_tensor("ss", [128, 16], F32, kind="ExternalInput").ap()
    wq = nc.dram_tensor("w_l1q", [16, 128, 16, 128], F32, kind="ExternalInput").ap()
    wk = nc.dram_tensor("w_l1k", [1, 128, 16, 128], F32, kind="ExternalInput").ap()
    wv = nc.dram_tensor("w_l1v", [1, 128, 16, 128], F32, kind="ExternalInput").ap()
    qf = nc.dram_tensor("qf", [D, T], BF16, kind="ExternalOutput").ap()
    kf = nc.dram_tensor("kf", [128, T], BF16, kind="ExternalOutput").ap()
    vf = nc.dram_tensor("vf", [128, T], BF16, kind="ExternalOutput").ap()
    emit_p5(ctx, x2, ns, ss, wq, wk, wv, qf, kf, vf)
    ctx.finish()
    return nc



P6_DEBUG = False


def emit_p6(ctx, qf_dr, kf_dr, vf_dr, oatt_dr):
    nc, em = ctx.nc, ctx.em
    i128f, i128f_b = ctx.const("i128f", np.eye(128, dtype=np.float32))
    irs_np = np.zeros((128, 64), dtype=ml_dtypes.bfloat16)
    for p in range(128):
        irs_np[p, p % 64] = 1
    irs, irs_b = ctx.const("irs", irs_np)
    onesall, onesall_b = ctx.const("onesall",
                                   np.ones((128, 128), dtype=ml_dtypes.bfloat16))
    ones64f, ones64f_b = ctx.const("ones64f", np.ones((1, 64), dtype=np.float32))

    qsb = ctx.sb("qsb", [128, 16 * T], BF16); qsb_b = ctx.buf("qsb")
    ksb = ctx.sb("ksb", [128, T], BF16); ksb_b = ctx.buf()
    vsb = ctx.sb("vsb", [128, T], BF16); vsb_b = ctx.buf()
    vtm = ctx.sb("vtm", [128, 32 * 65], BF16); vtm_b = ctx.buf()
    pta = ctx.sb("pta", [128, 16 * 1024], BF16); pta_b = ctx.buf("pta")
    ptb2 = ctx.sb("ptb2", [128, 16 * 1024], BF16); ptb2_b = ctx.buf("ptb2")
    mall = [ctx.sb(f"mall{i}", [128, 32], F32) for i in range(2)]
    mall_b = [ctx.buf() for _ in range(2)]
    mbf = ctx.sb("mbf", [128, 32], BF16); mbf_b = ctx.buf()
    mrow = ctx.sb("mrow", [16, 256], BF16); mrow_b = ctx.buf()
    mneg2 = [ctx.sb(f"mneg2_{i}", [128, T], BF16) for i in range(2)]
    mneg2_b = [ctx.buf() for _ in range(2)]
    mcs = [ctx.sb(f"mcs{i}", [128, 8], F32) for i in range(2)]
    mcs_b = [ctx.buf() for _ in range(2)]
    rcl = ctx.sb("rcl", [1, 1024], F32); rcl_b = ctx.buf()
    bcs = ctx.sb("bcs", [64, 1024], F32); bcs_b = ctx.buf()
    obf = [ctx.sb(f"obf{i}", [64, 512], BF16) for i in range(4)]
    obf_b = [ctx.buf() for _ in range(4)]
    mdr = nc.dram_tensor("p6_mscr", [2, 2, T], BF16, kind="Internal").ap()
    mdr_b = [ctx.buf("mdr0"), ctx.buf("mdr1")]
    if P6_DEBUG:
        dbg_mall = nc.dram_tensor("dbg_mall", [128, 32], F32,
                                  kind="ExternalOutput").ap()
        dbg_mneg = nc.dram_tensor("dbg_mneg", [2, T], BF16,
                                  kind="ExternalOutput").ap()
        dbg_pta = nc.dram_tensor("dbg_pta", [128, 1024], BF16,
                                 kind="ExternalOutput").ap()
        dbg_vtm = nc.dram_tensor("dbg_vtm", [128, 32 * 65], BF16,
                                 kind="ExternalOutput").ap()

    G = [ctx.ps(f"G{i}", [128, 512]) for i in range(8)]
    G_b = [ctx.buf(f"G{i}") for i in range(8)]


    for g in range(2):
        em.dma("sq", qsb[g * 64:(g + 1) * 64, :].rearrange("p (j t) -> p j t", j=16),
               qf_dr.rearrange("(g j p) t -> g p j t", g=2, p=64)[g],
               accwrites=[qsb_b])
    em.dma("sq", ksb[:], kf_dr, writes=[ksb_b])
    em.dma("sq", vsb[:], vf_dr, writes=[vsb_b])
    em.op("pool", lambda e: e.memset(
        vtm[:].rearrange("p (k c) -> p k c", c=65)[:, :, 64:65], 1.0),
        accwrites=[vtm_b])
    g0bf = G[0][:].bitcast(BF16)
    for g in range(2):
        gp = 64 * g
        for kt in range(16):
            em.op("pe", lambda e, gp=gp, kt=kt: e.transpose(
                g0bf[:, 0:64], vsb[gp:gp + 64, kt * 128:(kt + 1) * 128],
                irs[gp:gp + 64, :]),
                reads=[vsb_b, irs_b], writes=[G_b[0]])
            em.op("dve", lambda e, g=g, kt=kt: e.tensor_copy(
                vtm[:, (g * 16 + kt) * 65:(g * 16 + kt) * 65 + 64],
                g0bf[:, 0:64]),
                reads=[G_b[0]], accwrites=[vtm_b])

    def emit_stats_step(j, step):
        pj = j % 2
        qt, qq = divmod(step, 4)
        c = step % 2
        qof = j * T
        ga, gb = 4 + c, 6 + c

        def smm(e, qt=qt, qq=qq, qof=qof, ga=ga, gb=gb):
            e.matmul(G[ga][:], qsb[0:64, qof + qt * 128: qof + qt * 128 + 128],
                     ksb[0:64, qq * 512:(qq + 1) * 512],
                     start=True, stop=True, tile_position=(0, 0))
            return e.matmul(G[gb][:],
                            qsb[64:128, qof + qt * 128: qof + qt * 128 + 128],
                            ksb[64:128, qq * 512:(qq + 1) * 512],
                            start=True, stop=True, tile_position=(64, 0))
        em.op("pe", smm, reads=[qsb_b, ksb_b], writes=[G_b[ga], G_b[gb]])
        gav = G[ga][:].rearrange("p (f two) -> p f two", two=2)[:, :, 0]
        gbv = G[gb][:].rearrange("p (f two) -> p f two", two=2)[:, :, 0]
        em.op("dve", lambda e, qq=qq, gav=gav, ga=ga, pj=pj: e.reduce_max(
            mcs[pj][:, qq:qq + 1], gav, axis=AX.X),
            reads=[G_b[ga]], accwrites=[mcs_b[pj]])
        em.op("dve", lambda e, qq=qq, gbv=gbv, gb=gb, pj=pj: e.reduce_max(
            mcs[pj][:, 4 + qq:5 + qq], gbv, axis=AX.X),
            reads=[G_b[gb]], accwrites=[mcs_b[pj]])
        if qq == 3:
            def fin1(e, pj=pj):
                e.tensor_tensor(mcs[pj][:, 0:1], mcs[pj][:, 0:1],
                                mcs[pj][:, 1:2], ALU.max)
                return e.tensor_tensor(mcs[pj][:, 4:5], mcs[pj][:, 4:5],
                                       mcs[pj][:, 5:6], ALU.max)
            def fin2(e, pj=pj):
                e.tensor_tensor(mcs[pj][:, 2:3], mcs[pj][:, 2:3],
                                mcs[pj][:, 3:4], ALU.max)
                return e.tensor_tensor(mcs[pj][:, 6:7], mcs[pj][:, 6:7],
                                       mcs[pj][:, 7:8], ALU.max)
            def fin3(e, qt=qt, pj=pj):
                e.tensor_tensor(mall[pj][:, qt:qt + 1], mcs[pj][:, 0:1],
                                mcs[pj][:, 2:3], ALU.max)
                return e.tensor_tensor(mall[pj][:, 16 + qt:17 + qt],
                                       mcs[pj][:, 4:5], mcs[pj][:, 6:7], ALU.max)
            em.op("dve", fin1, reads=[mcs_b[pj]], accwrites=[mcs_b[pj]])
            em.op("dve", fin2, reads=[mcs_b[pj]], accwrites=[mcs_b[pj]])
            em.op("dve", fin3, reads=[mcs_b[pj]], accwrites=[mall_b[pj]])

    def emit_mrow(j):
        pj = j % 2
        em.op("dve", lambda e, pj=pj: e.tensor_copy(mbf[:], mall[pj][:]),
              reads=[mall_b[pj]], writes=[mbf_b])
        for s, base in ((0, 0), (1, 64)):
            em.op("pe", lambda e, s=s, pj=pj: e.transpose(
                G[2][0:16, 0:128], mall[pj][:, s * 16:(s + 1) * 16], i128f[:]),
                reads=[mall_b[pj], i128f_b], writes=[G_b[2]])
            em.op("act", lambda e, s=s: e.activation(
                mrow[:, s * 128:(s + 1) * 128], G[2][0:16, 0:128], AF.Copy,
                scale=-1.0),
                reads=[G_b[2]], accwrites=[mrow_b])
            em.dma("pool",
                   mdr[pj, s:s + 1, :].rearrange("o (j f) -> (o j) f", j=16),
                   mrow[:, s * 128:(s + 1) * 128], reads=[mrow_b],
                   accwrites=[mdr_b[pj]])
            em.dma("pool", mneg2[pj][base:base + 1, :], mdr[pj, s:s + 1, :],
                   reads=[mdr_b[pj]], accwrites=[mneg2_b[pj]])

    def emit_work_steps(j):
        pj = j % 2
        qof = j * T
        hA, hB = j, 16 + j
        for qh in range(2):
            for kt in range(16):
                for qs in range(2):
                    c = (kt * 2 + qs) % 2
                    def stmm(e, kt=kt, qh=qh, qs=qs, qof=qof, pj=pj, c=c):
                        qsl = slice(qof + qh * 1024 + qs * 512,
                                    qof + qh * 1024 + qs * 512 + 512)
                        msl = slice(qh * 1024 + qs * 512,
                                    qh * 1024 + qs * 512 + 512)
                        e.matmul(G[c][:], ksb[0:64, kt * 128:(kt + 1) * 128],
                                 qsb[0:64, qsl], start=True, stop=False,
                                 tile_position=(0, 0))
                        e.matmul(G[2 + c][:], onesall[64:65, :],
                                 mneg2[pj][64:65, msl],
                                 start=True, stop=False, tile_position=(64, 0))
                        e.matmul(G[2 + c][:], ksb[64:128, kt * 128:(kt + 1) * 128],
                                 qsb[64:128, qsl], start=False, stop=True,
                                 tile_position=(64, 0))
                        ins = e.matmul(G[c][:], onesall[0:1, :],
                                       mneg2[pj][0:1, msl],
                                       start=False, stop=True,
                                       tile_position=(0, 0))
                        return ins
                    def stw(stmm=stmm, pj=pj, c=c):
                        em.op("pe", stmm,
                              reads=[ksb_b, qsb_b, onesall_b, mneg2_b[pj]],
                              writes=[G_b[c], G_b[2 + c]])
                    yield stw
                    def expw(kt=kt, qs=qs, c=c):
                        off = kt * 1024 + qs * 512
                        em.op("act", lambda e, off=off, c=c: e.activation(
                            pta[:, off:off + 512], G[c][:], AF.Exp),
                            reads=[G_b[c]], accwrites=[pta_b])
                        em.op("act", lambda e, off=off, c=c: e.activation(
                            ptb2[:, off:off + 512], G[2 + c][:], AF.Exp),
                            reads=[G_b[2 + c]], accwrites=[ptb2_b])
                    yield expw
            for t2 in range(2):
                ts = qh * 2 + t2
                for s in range(2):
                    pt_sl, pt_bb = (pta, pta_b) if s == 0 else (ptb2, ptb2_b)
                    h = hA if s == 0 else hB

                    def avw(s=s, t2=t2, pt_sl=pt_sl, pt_bb=pt_bb, h=h, ts=ts):
                        def avmm(e, g=s, t2=t2, pt_sl=pt_sl):
                            for kt in range(16):
                                ins = e.matmul(
                                    G[0][0:65, :],
                                    vtm[:, (g * 16 + kt) * 65:
                                        (g * 16 + kt) * 65 + 65],
                                    pt_sl[:, kt * 1024 + t2 * 512:
                                          kt * 1024 + t2 * 512 + 512],
                                    start=(kt == 0), stop=(kt == 15))
                            return ins
                        em.op("pe", avmm, reads=[vtm_b, pt_bb],
                              writes=[G_b[0]])
                        em.op("dve", lambda e, s=s: e.reciprocal(
                            rcl[:, s * 512:(s + 1) * 512], G[0][64:65, :]),
                            reads=[G_b[0]], accwrites=[rcl_b])
                        em.op("pe", lambda e, s=s: e.matmul(
                            G[1][0:64, :], ones64f[:],
                            rcl[:, s * 512:(s + 1) * 512],
                            start=True, stop=True),
                            reads=[ones64f_b, rcl_b], writes=[G_b[1]])
                        em.op("dve", lambda e, s=s: e.tensor_copy(
                            bcs[:, s * 512:(s + 1) * 512], G[1][0:64, :]),
                            reads=[G_b[1]], accwrites=[bcs_b])
                        ii = s * 2 + t2
                        em.op("dve", lambda e, s=s, ii=ii: e.tensor_tensor(
                            obf[ii][:], G[0][0:64, :],
                            bcs[:, s * 512:(s + 1) * 512], ALU.mult),
                            reads=[G_b[0], bcs_b], writes=[obf_b[ii]])
                        em.dma("sq", oatt_dr[h * 64:(h + 1) * 64,
                                             ts * 512:(ts + 1) * 512],
                               obf[ii][:], reads=[obf_b[ii]])
                    yield avw

    for step in range(64):
        emit_stats_step(0, step)
    emit_mrow(0)
    if P6_DEBUG:
        em.dma("sq", dbg_vtm, vtm[:], reads=[vtm_b])
        em.dma("sq", dbg_mall, mall[0][:], reads=[mall_b[0]])
        em.dma("sq", dbg_mneg[0:1, :], mneg2[0][0:1, :], reads=[mneg2_b[0]])
        em.dma("sq", dbg_mneg[1:2, :], mneg2[0][64:65, :], reads=[mneg2_b[0]])
    for j in range(16):
        nxt = iter(range(64)) if j + 1 < 16 else iter(())
        wl = list(emit_work_steps(j))
        for i, wfn in enumerate(wl):
            wfn()
            if P6_DEBUG and j == 0 and i == 64:
                em.dma("sq", dbg_pta, pta[:, 0:1024], reads=[pta_b])
            if i % 2 == 1:
                s = next(nxt, None)
                if s is not None:
                    emit_stats_step(j + 1, s)
                    if s == 63:
                        emit_mrow(j + 1)
        if j + 1 < 16:
            for s in nxt:
                emit_stats_step(j + 1, s)
                if s == 63:
                    emit_mrow(j + 1)


def build_p6():
    ctx = Ctx("p6")
    nc = ctx.nc
    qf = nc.dram_tensor("qf", [D, T], BF16, kind="ExternalInput").ap()
    kf = nc.dram_tensor("kf", [128, T], BF16, kind="ExternalInput").ap()
    vf = nc.dram_tensor("vf", [128, T], BF16, kind="ExternalInput").ap()
    oatt = nc.dram_tensor("oatt", [D, T], BF16, kind="ExternalOutput").ap()
    emit_p6(ctx, qf, kf, vf, oatt)
    ctx.finish()
    return nc


def build_p7():
    ctx = Ctx("p7")
    nc = ctx.nc
    oatt = nc.dram_tensor("oatt", [16, 128, T], BF16, kind="ExternalInput").ap()
    w_l1o = nc.dram_tensor("w_l1o", [16, 128, 16, 128], F32,
                           kind="ExternalInput").ap()
    x2 = nc.dram_tensor("x2", [D, T], F32, kind="ExternalInput").ap()
    x3 = nc.dram_tensor("x3", [D, T], F32, kind="ExternalOutput").ap()
    x3n = nc.dram_tensor("x3n", [D, T], BF16, kind="ExternalOutput").ap()
    emit_proj_res_ln(ctx, oatt, w_l1o, x2, x3, x3n, res_scale=2.0)
    ctx.finish()
    return nc



def emit_p8(ctx, x3n_dr, wrt_dr, amp_dr, esc_dr, wtil_dr):
    nc, em = ctx.nc, ctx.em
    i128f, i128f_b = ctx.const("i128f", np.eye(128, dtype=np.float32))
    i128b, i128b_b = ctx.const("i128b", np.eye(128, dtype=ml_dtypes.bfloat16))
    xb = ctx.sb("p8_xb", [128, 16 * T], BF16); xb_b = ctx.buf()
    em.dma("sq", xb[:].rearrange("p (c t) -> p c t", c=16),
           x3n_dr.rearrange("(c p) t -> p c t", p=128), writes=[xb_b])
    wrf = ctx.sb("p8_wrf", [128, 2048], F32); wrf_b = ctx.buf()
    wrb = ctx.sb("p8_wrb", [128, 2048], BF16); wrb_b = ctx.buf()
    em.dma("sq", wrf[:].rearrange("p (c f) -> p c f", c=16), wrt_dr[0],
           writes=[wrf_b])
    em.op("dve", lambda e: e.tensor_copy(wrb[:], wrf[:]),
          reads=[wrf_b], writes=[wrb_b])
    lgs = ctx.sb("p8_lgs", [128, T], F32); lgs_b = ctx.buf()
    ltm = ctx.sb("p8_ltm", [128, T], F32); ltm_b = ctx.buf()
    afm = ctx.sb("p8_afm", [128, T], BF16); afm_b = ctx.buf()
    nm = ctx.sb("p8_nm", [128, 1], F32); nm_b = ctx.buf()
    pex = ctx.sb("p8_pex", [128, 128], F32); pex_b = ctx.buf()
    pc = [ctx.sb(f"p8_pc{i}", [128, 128], F32) for i in range(2)]
    pc_b = [ctx.buf() for _ in range(2)]
    zz = ctx.sb("p8_zz", [128, 4], F32); zz_b = ctx.buf()
    alp = ctx.sb("p8_alp", [128, 128], F32); alp_b = ctx.buf()
    atm = ctx.sb("p8_atm", [128, 128], BF16); atm_b = ctx.buf()
    amp_sb = ctx.sb("p8_amp", [128, 16], F32); amp_b = ctx.buf()
    esc_sb = ctx.sb("p8_esc", [128, 1], F32); esc_b = ctx.buf()
    sqa = ctx.sb("p8_sqa", [128, 16], F32); sqa_b = ctx.buf()
    pk8 = ctx.sb("p8_pk8", [128, 8], F32); pk8_b = ctx.buf()
    mixb = ctx.sb("p8_mixb", [128, 8], BF16); mixb_b = ctx.buf()
    wts = ctx.sb("p8_wts", [8, T], BF16); wts_b = ctx.buf()
    pp = ctx.ps("p8_pp", [128, 512]); pp_b = ctx.buf()
    ptt = ctx.ps("p8_ptt", [128, 128]); ptt_b = ctx.buf()
    ptb = ctx.ps("p8_ptb", [128, 128], BF16); ptb_b = ctx.buf()
    pw = ctx.ps("p8_pw", [8, 512]); pw_b = ctx.buf()
    em.dma("pool", amp_sb[:], amp_dr, writes=[amp_b])
    em.dma("pool", esc_sb[:], esc_dr.rearrange("(p o) -> p o", o=1), writes=[esc_b])
    for ts in range(4):
        def rmm(e, ts=ts):
            for fc in range(16):
                ins = e.matmul(pp[:], wrb[:, fc * 128:(fc + 1) * 128],
                               xb[:, fc * T + ts * 512: fc * T + ts * 512 + 512],
                               start=(fc == 0), stop=(fc == 15))
            return ins
        em.op("pe", rmm, reads=[wrb_b, xb_b], writes=[pp_b])
        em.op("dve", lambda e, ts=ts: e.tensor_copy(
            lgs[:, ts * 512:(ts + 1) * 512], pp[:]),
            reads=[pp_b], accwrites=[lgs_b])
    for tt in range(16):
        em.op("pe", lambda e, tt=tt: e.transpose(
            ptt[:], lgs[:, tt * 128:(tt + 1) * 128], i128f[:]),
            reads=[lgs_b, i128f_b], writes=[ptt_b])
        em.op("dve", lambda e, tt=tt: e.tensor_copy(
            ltm[:, tt * 128:(tt + 1) * 128], ptt[:]),
            reads=[ptt_b], accwrites=[ltm_b])
        L = ltm[:, tt * 128:(tt + 1) * 128]
        em.op("dve", lambda e, L=L: e.reduce_max(nm[:], L, axis=AX.X, negate=True),
              reads=[ltm_b], writes=[nm_b])
        em.op("act", lambda e, L=L: e.activation(pex[:], L, AF.Exp, bias=nm[:]),
              reads=[ltm_b, nm_b], writes=[pex_b])
        em.op("dve", lambda e: e.reduce_sum(zz[:, 0:1], pex[:], axis=AX.X),
              reads=[pex_b], writes=[zz_b])
        em.op("dve", lambda e: e.reduce_max(zz[:, 1:2], pex[:], axis=AX.X),
              reads=[pex_b], accwrites=[zz_b])
        src, src_b = pex, pex_b
        for j in range(3):
            dst, dst_b = pc[j % 2], pc_b[j % 2]
            em.op("dve", lambda e, src=src, dst=dst: e.scalar_tensor_tensor(
                dst[:], src[:], zz[:, 1:2], src[:], ALU.is_lt, ALU.mult),
                reads=[src_b, zz_b], writes=[dst_b])
            em.op("dve", lambda e, dst=dst: e.reduce_max(zz[:, 1:2], dst[:],
                                                         axis=AX.X),
                  reads=[dst_b], accwrites=[zz_b])
            src, src_b = dst, dst_b
        em.op("dve", lambda e: e.scalar_tensor_tensor(
            alp[:], pex[:], zz[:, 1:2], pex[:], ALU.is_ge, ALU.mult),
            reads=[pex_b, zz_b], writes=[alp_b])
        em.op("dve", lambda e: e.reduce_sum(zz[:, 2:3], alp[:], axis=AX.X),
              reads=[alp_b], accwrites=[zz_b])
        em.op("dve", lambda e: e.scalar_tensor_tensor(
            zz[:, 3:4], zz[:, 0:1], 1e-8, zz[:, 2:3], ALU.mult, ALU.add),
            reads=[zz_b], accwrites=[zz_b])
        em.op("dve", lambda e: e.reciprocal(zz[:, 3:4], zz[:, 3:4]),
              reads=[zz_b], accwrites=[zz_b])
        em.op("dve", lambda e: e.tensor_scalar(atm[:], alp[:], zz[:, 3:4], None,
                                               ALU.mult),
              reads=[alp_b, zz_b], writes=[atm_b])
        em.op("pe", lambda e: e.transpose(ptb[:], atm[:], i128b[:]),
              reads=[atm_b, i128b_b], writes=[ptb_b])
        em.op("dve", lambda e, tt=tt: e.tensor_copy(
            afm[:, tt * 128:(tt + 1) * 128], ptb[:]),
            reads=[ptb_b], accwrites=[afm_b])
    em.op("act", lambda e: e.activation(sqa[:], amp_sb[:], AF.Square),
          reads=[amp_b], writes=[sqa_b])
    sqv = sqa[:].rearrange("p (k j) -> p k j", j=2)
    em.op("dve", lambda e: e.tensor_tensor(pk8[:], sqv[:, :, 0], sqv[:, :, 1],
                                           ALU.add),
          reads=[sqa_b], writes=[pk8_b])
    em.op("dve", lambda e: e.reduce_sum(zz[:, 0:1], pk8[:], axis=AX.X),
          reads=[pk8_b], writes=[zz_b])
    em.op("dve", lambda e: e.tensor_scalar(zz[:, 0:1], zz[:, 0:1], 1e-8, None,
                                           ALU.add),
          reads=[zz_b], accwrites=[zz_b])
    em.op("dve", lambda e: e.reciprocal(zz[:, 0:1], zz[:, 0:1]),
          reads=[zz_b], accwrites=[zz_b])
    em.op("dve", lambda e: e.tensor_scalar(mixb[:], pk8[:], zz[:, 0:1],
                                           esc_sb[:], ALU.mult, ALU.mult),
          reads=[pk8_b, zz_b, esc_b], writes=[mixb_b])
    for ts in range(4):
        em.op("pe", lambda e, ts=ts: e.matmul(
            pw[:], mixb[:], afm[:, ts * 512:(ts + 1) * 512],
            start=True, stop=True),
            reads=[mixb_b, afm_b], writes=[pw_b])
        em.op("act", lambda e, ts=ts: e.activation(
            wts[:, ts * 512:(ts + 1) * 512], pw[:], AF.Copy),
            reads=[pw_b], accwrites=[wts_b])
    em.dma("sq", wtil_dr, wts[:], reads=[wts_b])


def build_p8():
    ctx = Ctx("p8")
    nc = ctx.nc
    x3n = nc.dram_tensor("x3n", [D, T], BF16, kind="ExternalInput").ap()
    wrt = nc.dram_tensor("w_rt", [1, 128, 16, 128], F32, kind="ExternalInput").ap()
    amp = nc.dram_tensor("amp", [128, 16], F32, kind="ExternalInput").ap()
    esc = nc.dram_tensor("esc", [128], F32, kind="ExternalInput").ap()
    wtil = nc.dram_tensor("wtil", [8, T], BF16, kind="ExternalOutput").ap()
    emit_p8(ctx, x3n, wrt, amp, esc, wtil)
    ctx.finish()
    return nc



def emit_p9(ctx, x3n_dr, wtil_dr, wg_dr, wu_dr, ysc_dr):
    nc, em = ctx.nc, ctx.em
    sel_np = np.zeros((8, 1024), dtype=ml_dtypes.bfloat16)
    for k in range(8):
        sel_np[k, k * 128:(k + 1) * 128] = 1
    sel, sel_b = ctx.const("sel8", sel_np)
    xb = ctx.sb("p9_xb", [128, 16 * T], BF16); xb_b = ctx.buf()
    em.dma("sq", xb[:].rearrange("p (c t) -> p c t", c=16),
           x3n_dr.rearrange("(c p) t -> p c t", p=128), writes=[xb_b])
    wtl = ctx.sb("p9_wtl", [8, T], BF16); wtl_b = ctx.buf()
    em.dma("pool", wtl[:], wtil_dr, writes=[wtl_b])
    wb_sl = ctx.sb("p9_wb", [128, T], BF16); wb_b = ctx.buf("p9_wb")
    ya = ctx.sb("p9_ya", [128, 8 * T], F32); ya_b = ctx.buf("p9_ya")
    for ic in range(8):
        em.op("pool", lambda e, ic=ic: e.memset(ya[:, ic * T:(ic + 1) * T], 0.0),
              accwrites=[ya_b])
    wgf = [ctx.sb(f"p9_wgf{i}", [128, 2048], F32) for i in range(2)]
    wgf_b = [ctx.buf() for _ in range(2)]
    wuf = [ctx.sb(f"p9_wuf{i}", [128, 2048], F32) for i in range(2)]
    wuf_b = [ctx.buf() for _ in range(2)]
    wgb = [ctx.sb(f"p9_wgb{i}", [128, 2048], BF16) for i in range(2)]
    wgb_b = [ctx.buf() for _ in range(2)]
    wub = [ctx.sb(f"p9_wub{i}", [128, 2048], BF16) for i in range(2)]
    wub_b = [ctx.buf() for _ in range(2)]
    gs = ctx.sb("p9_gs", [128, 512], F32); gs_b = ctx.buf()
    bas = ctx.sb("p9_bas", [128, 512], F32); bas_b = ctx.buf()
    prod = ctx.sb("p9_prod", [128, 512], F32); prod_b = ctx.buf()
    ysb = ctx.sb("p9_ysb", [128, T], BF16); ysb_b = ctx.buf()
    pg = [ctx.ps(f"p9_pg{i}", [128, 512]) for i in range(2)]
    pg_b = [ctx.buf() for _ in range(2)]
    pu = [ctx.ps(f"p9_pu{i}", [128, 512]) for i in range(2)]
    pu_b = [ctx.buf() for _ in range(2)]
    pbc = ctx.ps("p9_pbc", [128, 512]); pbc_b = ctx.buf()
    for k in range(8):
        for ts in range(4):
            em.op("pe", lambda e, k=k, ts=ts: e.matmul(
                pbc[:], sel[:, k * 128:(k + 1) * 128],
                wtl[:, ts * 512:(ts + 1) * 512], start=True, stop=True),
                reads=[sel_b, wtl_b], writes=[pbc_b])
            em.op("act", lambda e, ts=ts: e.activation(
                wb_sl[:, ts * 512:(ts + 1) * 512], pbc[:], AF.Copy),
                reads=[pbc_b], accwrites=[wb_b])
        for ic in range(8):
            ot = k * 8 + ic
            i = ot % 2
            em.dma("sq", wgf[i][:].rearrange("p (c f) -> p c f", c=16),
                   wg_dr[ot], writes=[wgf_b[i]])
            em.dma("sq", wuf[i][:].rearrange("p (c f) -> p c f", c=16),
                   wu_dr[ot], writes=[wuf_b[i]])
            em.op("act", lambda e, i=i: e.activation(wgb[i][:], wgf[i][:],
                                                     AF.Copy),
                  reads=[wgf_b[i]], writes=[wgb_b[i]])
            em.op("act", lambda e, i=i: e.activation(wub[i][:], wuf[i][:],
                                                     AF.Copy),
                  reads=[wuf_b[i]], writes=[wub_b[i]])
            for ts in range(4):
                j = ts % 2

                def mmg(e, i=i, ts=ts, j=j):
                    for fc in range(16):
                        e.matmul(pg[j][:], wgb[i][:, fc * 128:(fc + 1) * 128],
                                 xb[:, fc * T + ts * 512: fc * T + ts * 512 + 512],
                                 start=(fc == 0), stop=(fc == 15))
                    for fc in range(16):
                        ins = e.matmul(pu[j][:], wub[i][:, fc * 128:(fc + 1) * 128],
                                       xb[:, fc * T + ts * 512: fc * T + ts * 512 + 512],
                                       start=(fc == 0), stop=(fc == 15))
                    return ins
                em.op("pe", mmg, reads=[wgb_b[i], wub_b[i], xb_b],
                      writes=[pg_b[j], pu_b[j]])
                em.op("act", lambda e, j=j: e.activation(gs[:], pg[j][:], AF.Silu),
                      reads=[pg_b[j]], writes=[gs_b])
                em.op("dve", lambda e, j=j: e.tensor_tensor(
                    bas[:], pu[j][:], gs[:], ALU.mult),
                    reads=[pu_b[j], gs_b], writes=[bas_b])
                em.op("pool", lambda e, ts=ts: e.tensor_tensor(
                    prod[:], bas[:], wb_sl[:, ts * 512:(ts + 1) * 512], ALU.mult),
                    reads=[bas_b, wb_b], writes=[prod_b])
                em.op("dve", lambda e, ic=ic, ts=ts: e.tensor_tensor(
                    ya[:, ic * T + ts * 512: ic * T + ts * 512 + 512],
                    ya[:, ic * T + ts * 512: ic * T + ts * 512 + 512],
                    prod[:], ALU.add),
                    reads=[prod_b, ya_b], accwrites=[ya_b])
    for ic in range(8):
        em.op("dve", lambda e, ic=ic: e.tensor_copy(
            ysb[:], ya[:, ic * T:(ic + 1) * T]),
            reads=[ya_b], writes=[ysb_b])
        em.dma("sq", ysc_dr[ic], ysb[:], reads=[ysb_b])


def build_p9():
    ctx = Ctx("p9")
    nc = ctx.nc
    x3n = nc.dram_tensor("x3n", [D, T], BF16, kind="ExternalInput").ap()
    wtil = nc.dram_tensor("wtil", [8, T], BF16, kind="ExternalInput").ap()
    wg = nc.dram_tensor("w_g", [64, 128, 16, 128], F32, kind="ExternalInput").ap()
    wu = nc.dram_tensor("w_u", [64, 128, 16, 128], F32, kind="ExternalInput").ap()
    ysc = nc.dram_tensor("ysc", [8, 128, T], BF16, kind="ExternalOutput").ap()
    emit_p9(ctx, x3n, wtil, wg, wu, ysc)
    ctx.finish()
    return nc



def emit_p10(ctx, x3n_dr, wsg_dr, wsu_dr, shsc_dr):
    nc, em = ctx.nc, ctx.em
    xb = ctx.sb("pA_xb", [128, 16 * T], BF16); xb_b = ctx.buf()
    em.dma("sq", xb[:].rearrange("p (c t) -> p c t", c=16),
           x3n_dr.rearrange("(c p) t -> p c t", p=128), writes=[xb_b])
    wgf = [ctx.sb(f"pA_wgf{i}", [128, 2048], F32) for i in range(2)]
    wgf_b = [ctx.buf() for _ in range(2)]
    wuf = [ctx.sb(f"pA_wuf{i}", [128, 2048], F32) for i in range(2)]
    wuf_b = [ctx.buf() for _ in range(2)]
    wgb = [ctx.sb(f"pA_wgb{i}", [128, 2048], BF16) for i in range(2)]
    wgb_b = [ctx.buf() for _ in range(2)]
    wub = [ctx.sb(f"pA_wub{i}", [128, 2048], BF16) for i in range(2)]
    wub_b = [ctx.buf() for _ in range(2)]
    gs = ctx.sb("pA_gs", [128, 512], F32); gs_b = ctx.buf()
    shb = [ctx.sb(f"pA_shb{i}", [128, T], BF16) for i in range(2)]
    shb_b = [ctx.buf() for _ in range(2)]
    pg = [ctx.ps(f"pA_pg{i}", [128, 512]) for i in range(2)]
    pg_b = [ctx.buf() for _ in range(2)]
    pu = [ctx.ps(f"pA_pu{i}", [128, 512]) for i in range(2)]
    pu_b = [ctx.buf() for _ in range(2)]
    for ot in range(16):
        i = ot % 2
        em.dma("sq", wgf[i][:].rearrange("p (c f) -> p c f", c=16), wsg_dr[ot],
               writes=[wgf_b[i]])
        em.dma("sq", wuf[i][:].rearrange("p (c f) -> p c f", c=16), wsu_dr[ot],
               writes=[wuf_b[i]])
        em.op("dve", lambda e, i=i: e.tensor_copy(wgb[i][:], wgf[i][:]),
              reads=[wgf_b[i]], writes=[wgb_b[i]])
        em.op("dve", lambda e, i=i: e.tensor_copy(wub[i][:], wuf[i][:]),
              reads=[wuf_b[i]], writes=[wub_b[i]])
        for ts in range(4):
            j = ts % 2

            def mmg(e, i=i, ts=ts, j=j):
                for fc in range(16):
                    e.matmul(pg[j][:], wgb[i][:, fc * 128:(fc + 1) * 128],
                             xb[:, fc * T + ts * 512: fc * T + ts * 512 + 512],
                             start=(fc == 0), stop=(fc == 15))
                for fc in range(16):
                    ins = e.matmul(pu[j][:], wub[i][:, fc * 128:(fc + 1) * 128],
                                   xb[:, fc * T + ts * 512: fc * T + ts * 512 + 512],
                                   start=(fc == 0), stop=(fc == 15))
                return ins
            em.op("pe", mmg, reads=[wgb_b[i], wub_b[i], xb_b],
                  writes=[pg_b[j], pu_b[j]])
            em.op("act", lambda e, j=j: e.activation(gs[:], pg[j][:], AF.Silu),
                  reads=[pg_b[j]], writes=[gs_b])
            em.op("dve", lambda e, i=i, ts=ts, j=j: e.tensor_tensor(
                shb[i][:, ts * 512:(ts + 1) * 512], pu[j][:], gs[:], ALU.mult),
                reads=[pu_b[j], gs_b], accwrites=[shb_b[i]])
        em.dma("sq", shsc_dr[ot], shb[i][:], reads=[shb_b[i]])


def build_p10():
    ctx = Ctx("p10")
    nc = ctx.nc
    x3n = nc.dram_tensor("x3n", [D, T], BF16, kind="ExternalInput").ap()
    wsg = nc.dram_tensor("w_sg", [16, 128, 16, 128], F32, kind="ExternalInput").ap()
    wsu = nc.dram_tensor("w_su", [16, 128, 16, 128], F32, kind="ExternalInput").ap()
    shsc = nc.dram_tensor("shsc", [16, 128, T], BF16, kind="ExternalOutput").ap()
    emit_p10(ctx, x3n, wsg, wsu, shsc)
    ctx.finish()
    return nc



def emit_p11(ctx, ysc_dr, shsc_dr, wd_dr, wsd_dr, x3_dr, yT_dr):
    nc, em = ctx.nc, ctx.em
    ln = LNRes(ctx)
    wdb = ctx.sb("pB_wdb", [128, 16 * 1024], BF16); wdb_b = ctx.buf()
    wsdb = ctx.sb("pB_wsdb", [128, 16 * 2048], BF16); wsdb_b = ctx.buf()
    wf = [ctx.sb(f"pB_wf{i}", [128, 2048], F32) for i in range(2)]
    wf_b = [ctx.buf() for _ in range(2)]
    for ot in range(16):
        i = ot % 2
        em.dma("sq", wf[i][:, 0:1024].rearrange("p (c f) -> p c f", c=8),
               wd_dr[ot], writes=[wf_b[i]])
        em.op("dve", lambda e, ot=ot, i=i: e.tensor_copy(
            wdb[:, ot * 1024:(ot + 1) * 1024], wf[i][:, 0:1024]),
            reads=[wf_b[i]], accwrites=[wdb_b])
        em.dma("sq", wf[i][:].rearrange("p (c f) -> p c f", c=16),
               wsd_dr[ot], writes=[wf_b[i]])
        em.op("dve", lambda e, ot=ot, i=i: e.tensor_copy(
            wsdb[:, ot * 2048:(ot + 1) * 2048], wf[i][:]),
            reads=[wf_b[i]], accwrites=[wsdb_b])
    ysl = ctx.sb("pB_ysl", [128, 8 * 512], BF16); ysl_b = ctx.buf()
    shsl = ctx.sb("pB_shsl", [128, 16 * 512], BF16); shsl_b = ctx.buf()
    xc = [ctx.sb(f"pB_xc{i}", [128, 512], F32) for i in range(2)]
    xc_b = [ctx.buf() for _ in range(2)]
    rsl = ctx.sb("pB_rsl", [128, 16 * 512], F32); rsl_b = ctx.buf()
    pp = [ctx.ps(f"pB_ps{i}", [128, 512]) for i in range(2)]
    pp_b = [ctx.buf() for _ in range(2)]
    for ts in range(4):
        em.dma("sq", ysl[:].rearrange("p (c f) -> p c f", c=8),
               ysc_dr[:, :, ts * 512:(ts + 1) * 512].rearrange("i p f -> p i f"),
               writes=[ysl_b])
        em.dma("sq", shsl[:].rearrange("p (c f) -> p c f", c=16),
               shsc_dr[:, :, ts * 512:(ts + 1) * 512].rearrange("i p f -> p i f"),
               writes=[shsl_b])
        for d2t in range(16):
            j = d2t % 2
            em.dma("pool", xc[j][:],
                   x3_dr[d2t * 128:(d2t + 1) * 128, ts * 512:(ts + 1) * 512],
                   writes=[xc_b[j]])

            def mmg(e, d2t=d2t, j=j):
                for kc in range(8):
                    e.matmul(pp[j][:],
                             wdb[:, d2t * 1024 + kc * 128: d2t * 1024 + kc * 128 + 128],
                             ysl[:, kc * 512:(kc + 1) * 512],
                             start=(kc == 0), stop=False)
                for fc in range(16):
                    ins = e.matmul(
                        pp[j][:],
                        wsdb[:, d2t * 2048 + fc * 128: d2t * 2048 + fc * 128 + 128],
                        shsl[:, fc * 512:(fc + 1) * 512],
                        start=False, stop=(fc == 15))
                return ins
            em.op("pe", mmg, reads=[wdb_b, wsdb_b, ysl_b, shsl_b],
                  writes=[pp_b[j]])
            em.op("dve", lambda e, d2t=d2t, j=j: e.scalar_tensor_tensor(
                rsl[:, d2t * 512:(d2t + 1) * 512], xc[j][:], 1.0, pp[j][:],
                ALU.mult, ALU.add),
                reads=[xc_b[j], pp_b[j]], accwrites=[rsl_b])
        emit_ln(ctx, ln, rsl, rsl_b,
                lambda fc, ts=ts: yT_dr[fc * 128:(fc + 1) * 128,
                                        ts * 512:(ts + 1) * 512],
                None)


def build_p11():
    ctx = Ctx("p11")
    nc = ctx.nc
    ysc = nc.dram_tensor("ysc", [8, 128, T], BF16, kind="ExternalInput").ap()
    shsc = nc.dram_tensor("shsc", [16, 128, T], BF16, kind="ExternalInput").ap()
    wd = nc.dram_tensor("w_d", [16, 128, 8, 128], F32, kind="ExternalInput").ap()
    wsd = nc.dram_tensor("w_sd", [16, 128, 16, 128], F32,
                         kind="ExternalInput").ap()
    x3 = nc.dram_tensor("x3", [D, T], F32, kind="ExternalInput").ap()
    yT = nc.dram_tensor("yT", [D, T], F32, kind="ExternalOutput").ap()
    emit_p11(ctx, ysc, shsc, wd, wsd, x3, yT)
    ctx.finish()
    return nc




def build_p0():
    ctx = Ctx("p0")
    nc = ctx.nc
    xT = nc.dram_tensor("xT", [D, T], F32, kind="ExternalInput").ap()
    w_l0p = nc.dram_tensor("w_l0p", [NWT, 128, 16, 128], F32,
                           kind="ExternalInput").ap()
    proj = nc.dram_tensor("proj", [4, NWT, 128, 512], BF16,
                          kind="ExternalOutput").ap()
    emit_p0(ctx, xT, w_l0p, proj)
    ctx.finish()
    return nc



def prep_weights(ii):
    f32 = lambda a: np.asarray(a, np.float32)
    colT = lambda a, n: np.ascontiguousarray(f32(a).reshape(n, 128).T)
    w = {}
    w["w_l0p"] = host_pack_l0_proj(ii["L0_attn_Wq"], ii["L0_attn_Wk"],
                                   ii["L0_attn_Ws"], ii["L0_attn_Wv"],
                                   ii["L0_attn_Wg"])
    w["gnw"] = f32(ii["L0_attn_g_norm_weight"])
    w["w_l0o"] = pack_lhsT(f32(ii["L0_attn_Wo"]))
    w["w_l0g"] = pack_lhsT(u8_of_i32(np.asarray(ii["L0_ffn_gate_w"])).T)
    w["w_l0u"] = pack_lhsT(u8_of_i32(np.asarray(ii["L0_ffn_up_w"])).T)
    w["w_l0d"] = pack_lhsT(u8_of_i32(np.asarray(ii["L0_ffn_down_w"])).T)
    w["gsc"] = colT(ii["L0_ffn_gate_scale"], 64)
    w["usc"] = colT(ii["L0_ffn_up_scale"], 64)
    w["dsc"] = colT(ii["L0_ffn_down_scale"], 16)
    w["ns"] = colT(ii["L1_attn_norm_scale"], 16)
    w["ss"] = colT(ii["L1_attn_split_scale"], 16)
    w["w_l1q"] = pack_lhsT(f32(ii["L1_attn_Wq"]).T)
    w["w_l1k"] = pack_lhsT(f32(ii["L1_attn_Wk"]).T)
    w["w_l1v"] = pack_lhsT(f32(ii["L1_attn_Wv"]).T)
    w["w_l1o"] = pack_lhsT(f32(ii["L1_attn_Wo"]).T)
    w["w_rt"] = pack_lhsT(f32(ii["L1_ffn_w_router"]))
    w["amp"] = np.ascontiguousarray(f32(ii["L1_ffn_amplitudes"]).reshape(128, 16))
    w["esc"] = f32(ii["L1_ffn_expert_scale"])
    w["w_g"] = pack_lhsT(f32(ii["L1_ffn_w_gate"]))
    w["w_u"] = pack_lhsT(f32(ii["L1_ffn_w_up"]))
    w["w_d"] = pack_lhsT(f32(ii["L1_ffn_w_down"]))
    w["w_sg"] = pack_lhsT(f32(ii["L1_ffn_shared_w_gate"]))
    w["w_su"] = pack_lhsT(f32(ii["L1_ffn_shared_w_up"]))
    w["w_sd"] = pack_lhsT(f32(ii["L1_ffn_shared_w_down"]))
    return w


_BUILD_CACHE = {}


def _get(name, fn):
    if name not in _BUILD_CACHE:
        _BUILD_CACHE[name] = fn()
    return _BUILD_CACHE[name]



def build_full():
    ctx = Ctx("full")
    nc = ctx.nc
    EI = lambda n, s, d: nc.dram_tensor(n, s, d, kind="ExternalInput").ap()
    IN = lambda n, s, d: nc.dram_tensor(n, s, d, kind="Internal").ap()
    xT = EI("xT", [D, T], F32)
    w_l0p = EI("w_l0p", [NWT, 128, 16, 128], F32)
    gnw = EI("gnw", [128], F32)
    w_l0o = EI("w_l0o", [16, 128, 16, 128], F32)
    w_l0g = EI("w_l0g", [64, 128, 16, 128], U8)
    w_l0u = EI("w_l0u", [64, 128, 16, 128], U8)
    w_l0d = EI("w_l0d", [16, 128, 64, 128], U8)
    gsc = EI("gsc", [128, 64], F32)
    usc = EI("usc", [128, 64], F32)
    dsc = EI("dsc", [128, 16], F32)
    ns = EI("ns", [128, 16], F32)
    ss = EI("ss", [128, 16], F32)
    w_l1q = EI("w_l1q", [16, 128, 16, 128], F32)
    w_l1k = EI("w_l1k", [1, 128, 16, 128], F32)
    w_l1v = EI("w_l1v", [1, 128, 16, 128], F32)
    w_l1o = EI("w_l1o", [16, 128, 16, 128], F32)
    w_rt = EI("w_rt", [1, 128, 16, 128], F32)
    amp = EI("amp", [128, 16], F32)
    esc = EI("esc", [128], F32)
    w_g = EI("w_g", [64, 128, 16, 128], F32)
    w_u = EI("w_u", [64, 128, 16, 128], F32)
    w_d = EI("w_d", [16, 128, 8, 128], F32)
    w_sg = EI("w_sg", [16, 128, 16, 128], F32)
    w_su = EI("w_su", [16, 128, 16, 128], F32)
    w_sd = EI("w_sd", [16, 128, 16, 128], F32)
    yT = nc.dram_tensor("yT", [D, T], F32, kind="ExternalOutput").ap()

    proj = IN("s_proj", [4, NWT, 128, 512], BF16)
    og = IN("s_og", [16, 128, T], BF16)
    x1 = IN("s_x1", [D, T], F32)
    x1n = IN("s_x1n", [D, T], BF16)
    hsc = IN("s_hsc", [64, 128, T], BF16)
    x2 = IN("s_x2", [D, T], F32)
    qf = IN("s_qf", [D, T], BF16)
    kf = IN("s_kf", [128, T], BF16)
    vf = IN("s_vf", [128, T], BF16)
    oatt = IN("s_oatt", [16, 128, T], BF16)
    x3 = IN("s_x3", [D, T], F32)
    x3n = IN("s_x3n", [D, T], BF16)
    wtil = IN("s_wtil", [8, T], BF16)
    ysc = IN("s_ysc", [8, 128, T], BF16)
    shsc = IN("s_shsc", [16, 128, T], BF16)

    phases = [
        ("P0_", lambda: emit_p0(ctx, xT, w_l0p, proj)),
        ("P1_", lambda: emit_p1(ctx, proj, gnw, og)),
        ("P2_", lambda: emit_proj_res_ln(ctx, og, w_l0o, xT, x1, x1n, 1.0)),
        ("P3_", lambda: emit_p3(ctx, x1n, w_l0g, w_l0u, gsc, usc, hsc)),
        ("P4_", lambda: emit_p4(ctx, hsc, w_l0d, dsc, x1, x2, None)),
        ("P5_", lambda: emit_p5(ctx, x2, ns, ss, w_l1q, w_l1k, w_l1v,
                                qf, kf, vf)),
        ("P6_", lambda: emit_p6(ctx, qf, kf, vf,
                                oatt.rearrange("h p t -> (h p) t"))),
        ("P7_", lambda: emit_proj_res_ln(ctx, oatt, w_l1o, x2, x3, x3n, 2.0)),
        ("P8_", lambda: emit_p8(ctx, x3n, w_rt, amp, esc, wtil)),
        ("P9_", lambda: emit_p9(ctx, x3n, wtil, w_g, w_u, ysc)),
        ("PA_", lambda: emit_p10(ctx, x3n, w_sg, w_su, shsc)),
        ("PB_", lambda: emit_p11(ctx, ysc, shsc, w_d, w_sd, x3, yT)),
    ]
    for pref, fn in phases:
        ctx.push_scope(pref)
        fn()
        ctx.pop_scope()
    ctx.finish()
    return nc


def kernel(**inputs):
    import os
    if int(os.environ.get("MK_UNFUSED", "0")):
        return kernel_multilaunch(**inputs)
    w = prep_weights(inputs)
    x = np.asarray(inputs["L0_attn_hidden_states"], np.float32)
    B = x.shape[0]
    cores = list(range(B))
    nc = _get("full", build_full)
    maps = [dict(w, xT=np.ascontiguousarray(x[b].T)) for b in cores]
    res = run_bass_kernel_spmd(nc, maps, core_ids=cores)
    return np.stack([np.ascontiguousarray(res.results[b]["yT"].T)
                     for b in cores])


def kernel_multilaunch(**inputs):
    w = prep_weights(inputs)
    x = np.asarray(inputs["L0_attn_hidden_states"], np.float32)
    B = x.shape[0]
    cores = list(range(B))
    xT = [np.ascontiguousarray(x[b].T) for b in range(B)]

    def launch(name, build_fn, per_core):
        nc = _get(name, build_fn)
        res = run_bass_kernel_spmd(nc, per_core, core_ids=cores)
        return res.results

    r = launch("p0", build_p0, [dict(xT=xT[b], w_l0p=w["w_l0p"]) for b in cores])
    proj = [r[b]["proj"] for b in cores]
    r = launch("p1", build_p1, [dict(proj=proj[b], gnw=w["gnw"]) for b in cores])
    og = [r[b]["og"] for b in cores]
    r = launch("p2", build_p2, [dict(og=og[b], w_l0o=w["w_l0o"], xT=xT[b])
                                for b in cores])
    x1 = [r[b]["x1"] for b in cores]
    x1n = [r[b]["x1n"] for b in cores]
    r = launch("p3", build_p3, [dict(x1n=x1n[b], w_l0g=w["w_l0g"],
                                     w_l0u=w["w_l0u"], gsc=w["gsc"],
                                     usc=w["usc"]) for b in cores])
    hsc = [r[b]["hsc"] for b in cores]
    r = launch("p4", build_p4, [dict(hsc=hsc[b], w_l0d=w["w_l0d"],
                                     dsc=w["dsc"], x1=x1[b]) for b in cores])
    x2 = [r[b]["x2"] for b in cores]
    r = launch("p5", build_p5, [dict(x2=x2[b], ns=w["ns"], ss=w["ss"],
                                     w_l1q=w["w_l1q"], w_l1k=w["w_l1k"],
                                     w_l1v=w["w_l1v"]) for b in cores])
    qf = [r[b]["qf"] for b in cores]
    kf = [r[b]["kf"] for b in cores]
    vf = [r[b]["vf"] for b in cores]
    r = launch("p6", build_p6, [dict(qf=qf[b], kf=kf[b], vf=vf[b])
                                for b in cores])
    oatt = [r[b]["oatt"] for b in cores]
    r = launch("p7", build_p7, [dict(oatt=oatt[b].reshape(16, 128, T),
                                     w_l1o=w["w_l1o"], x2=x2[b])
                                for b in cores])
    x3 = [r[b]["x3"] for b in cores]
    x3n = [r[b]["x3n"] for b in cores]
    r = launch("p8", build_p8, [dict(x3n=x3n[b], w_rt=w["w_rt"], amp=w["amp"],
                                     esc=w["esc"]) for b in cores])
    wtil = [r[b]["wtil"] for b in cores]
    r = launch("p9", build_p9, [dict(x3n=x3n[b], wtil=wtil[b], w_g=w["w_g"],
                                     w_u=w["w_u"]) for b in cores])
    ysc = [r[b]["ysc"] for b in cores]
    r = launch("p10", build_p10, [dict(x3n=x3n[b], w_sg=w["w_sg"],
                                       w_su=w["w_su"]) for b in cores])
    shsc = [r[b]["shsc"] for b in cores]
    r = launch("p11", build_p11, [dict(ysc=ysc[b], shsc=shsc[b], w_d=w["w_d"],
                                       w_sd=w["w_sd"], x3=x3[b])
                                  for b in cores])
    out = np.stack([np.ascontiguousarray(r[b]["yT"].T) for b in cores])
    return out


# version 09
# speedup vs baseline: 1.2318x; 1.0201x over previous
import contextlib
import numpy as np
import ml_dtypes
import concourse.bass as bass
import concourse.mybir as mybir
from concourse.bass_utils import run_bass_kernel_spmd

F32 = mybir.dt.float32
BF16 = mybir.dt.bfloat16
U8 = mybir.dt.uint8
AF = mybir.ActivationFunctionType
ALU = mybir.AluOpType
AX = mybir.AxisListType

NCORES = 8
T = 2048
D = 2048


ENGINES = ("pe", "dve", "act", "pool", "sq")
NLANES = 4
_ENG_ATTR = {"pe": "tensor", "dve": "vector", "act": "scalar",
             "pool": "gpsimd", "sq": "sync"}


class Buf:
    __slots__ = ("name", "writers", "readers")

    def __init__(self, name=""):
        self.name = name
        self.writers = []
        self.readers = []


class _Op:
    __slots__ = ("fn", "deps", "inc", "observed")

    def __init__(self, fn, deps, inc):
        self.fn = fn
        self.deps = deps
        self.inc = inc
        self.observed = False


class Em:

    def __init__(self, nc):
        self.nc = nc
        self.ops = {e: [] for e in ENGINES}
        self._barrier = []
        self.phase_marks = []
        self.inst_marks = None

    def mark(self, label):
        self.phase_marks.append((label, {e: len(self.ops[e]) for e in ENGINES}))

    def barrier(self):
        bar = [(e, len(self.ops[e]) - 1) for e in ENGINES if self.ops[e]]
        for q in ("sq", "pool"):
            found = 0
            for i in range(len(self.ops[q]) - 1, -1, -1):
                if self.ops[q][i].inc == 16:
                    bar.append((q, i))
                    found += 1
                    if found >= NLANES:
                        break
        self._barrier = bar

    def op(self, eng, fn, reads=(), writes=(), accwrites=(), deps=(), dma=False):
        dep = set(deps)
        dep.update(self._barrier)
        for b in reads:
            dep.update(b.writers)
        for b in writes:
            dep.update(b.writers)
            dep.update(b.readers)
        for b in accwrites:
            dep.update(b.readers)
        tok = (eng, len(self.ops[eng]))
        dep.discard(tok)
        self.ops[eng].append(_Op(fn, sorted(dep), 16 if dma else 1))
        for b in reads:
            b.readers.append(tok)
        for b in writes:
            b.writers = [tok]
            b.readers = []
        for b in accwrites:
            b.writers.append(tok)
        return tok

    def dma(self, eng, out, in_, **bufs):
        eng = "pool" if type(out.tensor).__name__ == "DRamTensorHandle" else "sq"
        return self.op(eng, lambda e, o=out, i=in_: e.dma_start(out=o, in_=i),
                       dma=True, **bufs)

    def lower(self, es):
        sems = {e: es.enter_context(self.nc.semaphore(f"sem_{e}"))
                for e in ENGINES}
        lsem = {q: [es.enter_context(self.nc.semaphore(f"sem_{q}L{l}"))
                    for l in range(NLANES)] for q in ("sq", "pool")}
        for e in ENGINES:
            for i, op in enumerate(self.ops[e]):
                for (fe, fi) in op.deps:
                    if fe == "pe" and e == "pe":
                        continue
                    self.ops[fe][fi].observed = True
        for e in ("sq", "pool"):
            for op in self.ops[e]:
                if op.inc == 16:
                    op.observed = True
        cum = {}
        lane_final = {q: [0] * NLANES for q in ("sq", "pool")}
        lanes = {}
        for e in ENGINES:
            c = 0
            lcnt = [0] * NLANES
            ldma = 0
            lst = []
            lns = []
            for op in self.ops[e]:
                if e in ("sq", "pool") and op.inc == 16:
                    lane = ldma % NLANES
                    ldma += 1
                    lcnt[lane] += 16
                    lns.append(lane)
                    lst.append(((e, lane), lcnt[lane]))
                    lane_final[e][lane] = lcnt[lane]
                else:
                    lns.append(None)
                    if op.observed:
                        c += op.inc
                    lst.append((e, c))
            cum[e] = lst
            lanes[e] = lns
        block = es.enter_context(self.nc.Block())
        em = self

        def play(eng_name, eng_obj):
            hiwater = {}

            def wait(key, v):
                if v > hiwater.get(key, 0):
                    s = (lsem[key[0]][key[1]] if isinstance(key, tuple)
                         else sems[key])
                    eng_obj.wait_ge(s, v)
                    hiwater[key] = v

            for i, op in enumerate(em.ops[eng_name]):
                ln = lanes[eng_name][i]
                if ln is not None:
                    key, v = cum[eng_name][i]
                    if v > 16:
                        wait(key, v - 16)
                need = {}
                for (fe, fi) in op.deps:
                    if fe == "pe" and eng_name == "pe":
                        continue
                    key, v = cum[fe][fi]
                    if v > need.get(key, 0):
                        need[key] = v
                for key, v in sorted(need.items(), key=str):
                    wait(key, v)
                ins = op.fn(eng_obj)
                if ln is not None:
                    ins.then_inc(lsem[eng_name][ln], 16)
                elif op.observed:
                    ins.then_inc(sems[eng_name], op.inc)
            if eng_name in ("sq", "pool"):
                for l in range(NLANES):
                    if lane_final[eng_name][l]:
                        wait((eng_name, l), lane_final[eng_name][l])

        @block.tensor
        def _(t):
            play("pe", t)

        @block.vector
        def _(v):
            play("dve", v)

        @block.scalar
        def _(s):
            play("act", s)

        @block.gpsimd
        def _(g):
            play("pool", g)

        @block.sync
        def _(s):
            play("sq", s)



def pack_lhsT(M):
    K, N = M.shape
    A = np.asarray(M).reshape(K // 128, 128, N // 128, 128)
    return np.ascontiguousarray(A.transpose(2, 1, 0, 3))


def u8_of_i32(a):
    a = np.ascontiguousarray(a)
    assert a.dtype == np.int32
    return np.ascontiguousarray(a.view(np.uint8).reshape(*a.shape, 4)[..., 0])



class Ctx:

    def __init__(self, name):
        self.nc = bass.Bass("TRN2", target_bir_lowering=False)
        self.em = Em(self.nc)
        self.es = contextlib.ExitStack()
        self._scope = None
        self.prefix = ""
        self._consts = {}
        self._nbuf = 0

    def push_scope(self, prefix):
        assert self._scope is None
        self._scope = contextlib.ExitStack()
        self.prefix = prefix

    def pop_scope(self):
        self.em.barrier()
        self._scope.close()
        self._scope = None
        self.prefix = ""

    def sb(self, name, shape, dtype):
        es = self._scope if self._scope is not None else self.es
        t = es.enter_context(self.nc.sbuf_tensor(self.prefix + name,
                                                 list(shape), dtype))
        return t

    def ps(self, name, shape, dtype=F32):
        es = self._scope if self._scope is not None else self.es
        t = es.enter_context(self.nc.psum_tensor(self.prefix + name,
                                                 list(shape), dtype))
        return t

    def buf(self, name=""):
        self._nbuf += 1
        return Buf(name or f"b{self._nbuf}")

    def const(self, key, arr):
        if key in self._consts:
            return self._consts[key]
        arr = np.ascontiguousarray(arr)
        dt = {np.dtype(np.float32): F32,
              np.dtype(ml_dtypes.bfloat16): BF16,
              np.dtype(np.uint8): U8}[arr.dtype]
        dr = self.nc.inline_tensor(arr, name=f"c_{key}").ap()
        sb = self.es.enter_context(
            self.nc.sbuf_tensor(f"cs_{key}", list(arr.shape), dt))
        b = self.buf(f"const_{key}")
        self.em.dma("pool", sb[:], dr, accwrites=[b])
        self._consts[key] = (sb, b)
        return self._consts[key]

    def cvec(self, value, parts=128, dtype=None):
        key = ("cv", value, parts)
        if key in self._consts:
            return self._consts[key]
        sb = self.es.enter_context(
            self.nc.sbuf_tensor(f"cv_{len(self._consts)}", [parts, 1], F32))
        b = self.buf()
        self.em.op("pool", lambda e: e.memset(sb[:], value), accwrites=[b])
        self._consts[key] = (sb, b)
        return self._consts[key]

    def finish(self):
        self.em.lower(self.es)
        self.es.close()
        return self.nc


def np_triu_inc(n=128):
    return np.triu(np.ones((n, n))).astype(ml_dtypes.bfloat16)



NWT = 56
WT_Q0, WT_K0, WT_S0, WT_V0, WT_G0 = 0, 8, 16, 24, 40


def host_pack_l0_proj(wq, wk, ws, wv, wg):
    mats = [pack_lhsT(np.asarray(m, np.float32)) for m in (wq, wk, ws, wv, wg)]
    return np.concatenate(mats, axis=0)


def emit_p0(ctx, xT, w_l0p, proj):
    nc, em = ctx.nc, ctx.em
    xb = ctx.sb("p0_xb", [128, 16 * T], BF16)
    xf = [ctx.sb(f"p0_xf{i}", [128, T], F32) for i in range(2)]
    xf_b = [ctx.buf() for _ in range(2)]
    xb_b = ctx.buf("xb")
    for dc in range(16):
        i = dc % 2
        em.dma("sq", xf[i][:], xT[dc * 128:(dc + 1) * 128, :], writes=[xf_b[i]])
        em.op("dve",
              lambda e, dc=dc, i=i: e.tensor_copy(xb[:, dc * T:(dc + 1) * T],
                                                  xf[i][:]),
              reads=[xf_b[i]], accwrites=[xb_b])
    wf = [ctx.sb(f"p0_wf{i}", [128, 16 * 128], F32) for i in range(2)]
    wb = [ctx.sb(f"p0_wb{i}", [128, 16 * 128], BF16) for i in range(2)]
    wf_b = [ctx.buf() for _ in range(2)]
    wb_b = [ctx.buf() for _ in range(2)]
    pp = [ctx.ps(f"p0_ps{i}", [128, 512]) for i in range(4)]
    pp_b = [ctx.buf(f"psbank{i}") for i in range(4)]
    ob = [ctx.sb(f"p0_ob{i}", [128, 512], BF16) for i in range(4)]
    ob_b = [ctx.buf() for _ in range(4)]
    for wt in range(NWT):
        i = wt % 2
        em.dma("sq", wf[i][:].rearrange("p (c f) -> p c f", c=16),
               w_l0p[wt], writes=[wf_b[i]])
        em.op("dve", lambda e, i=i: e.tensor_copy(wb[i][:], wf[i][:]),
              reads=[wf_b[i]], writes=[wb_b[i]])
        for ts in range(4):
            j = (wt * 4 + ts) % 4
            def mmgroup(e, i=i, ts=ts, j=j):
                for dc in range(16):
                    ins = e.matmul(pp[j][:], wb[i][:, dc * 128:(dc + 1) * 128],
                                   xb[:, dc * T + ts * 512: dc * T + ts * 512 + 512],
                                   start=(dc == 0), stop=(dc == 15))
                return ins
            em.op("pe", mmgroup, reads=[wb_b[i], xb_b], writes=[pp_b[j]])
            scale = 0.125 if wt < 8 else 1.0
            em.op("act", lambda e, j=j, scale=scale: e.activation(
                ob[j][:], pp[j][:], AF.Copy, scale=scale),
                reads=[pp_b[j]], writes=[ob_b[j]])
            tg, cc = ts, None
            em.dma("sq", proj[ts, wt], ob[j][:], reads=[ob_b[j]])



def emit_p1(ctx, proj, gnw_dr, og):
    nc, em = ctx.nc, ctx.em
    H, C = 16, 128

    triu, triu_b = ctx.const("triu", np_triu_inc(128))
    i128b, i128b_b = ctx.const("i128b", np.eye(128, dtype=ml_dtypes.bfloat16))
    i128f, i128f_b = ctx.const("i128f", np.eye(128, dtype=np.float32))
    irs_np = np.zeros((128, 64), dtype=ml_dtypes.bfloat16)
    for p in range(128):
        irs_np[p, p % 64] = 1
    irs, irs_b = ctx.const("irs", irs_np)
    onesr, onesr_b = ctx.const("onesr", np.ones((1, 128), dtype=ml_dtypes.bfloat16))
    onesrf, onesrf_b = ctx.const("onesrf", np.ones((1, 128), dtype=np.float32))
    onescol, onescol_b = ctx.const("onescol", np.ones((128, 1), dtype=ml_dtypes.bfloat16))
    o128, o128_b = ctx.const("o128", np.full((128, 1), 1.0 / 128, dtype=np.float32))
    eps5, eps5_b = ctx.cvec(1e-5)

    gnw = ctx.sb("gnw_sb", [128, 1], F32)
    gnw_b = ctx.buf("gnw")
    em.dma("pool", gnw[:], gnw_dr.rearrange("(p o) -> p o", o=1), writes=[gnw_b])

    slab = [ctx.sb(f"slab{i}", [128, NWT * 512], BF16) for i in range(2)]
    slab_b = [ctx.buf(f"slab{i}") for i in range(2)]
    ogs = [ctx.sb(f"ogs{i}", [128, H * 512], BF16) for i in range(2)]
    ogs_b = [ctx.buf(f"ogs{i}") for i in range(2)]
    cUWf = ctx.sb("cUWf", [128, 8 * 192], F32)
    cUWf_b = [ctx.buf(f"cUWf{h}") for h in range(16)]
    cUWb = ctx.sb("cUWb", [128, 8 * 192], BF16)
    cUWb_b = [ctx.buf(f"cUWb{h}") for h in range(16)]
    asb = ctx.sb("asb", [1, H * 64], F32)
    asb_b = [ctx.buf(f"asb{h}") for h in range(16)]
    abf = ctx.sb("abf", [1, H * 64], BF16)
    abf_b = [ctx.buf(f"abf{h}") for h in range(16)]

    sc_ps = ctx.ps("sc_ps", [128, 128]); sc_pb = ctx.buf("sc")
    a_ps = ctx.ps("a_ps", [128, 512]); a_pb = ctx.buf("a")
    ok_ps = ctx.ps("ok_ps", [128, 64]); ok_pb = ctx.buf("ok")
    ct_ps = ctx.ps("ct_ps", [128, 512]); ct_pb = ctx.buf("ct")
    ov_ps = ctx.ps("ov_ps", [128, 128]); ov_pb = ctx.buf("ov")
    upd = [ctx.ps(f"upd{i}", [128, 192]) for i in range(2)]
    upd_b = [ctx.buf(f"upd{i}") for i in range(2)]
    trp = ctx.ps("trp", [128, 384], BF16); trp_pb = ctx.buf("trp")

    def sbuf2(name, shape, dtype):
        ts_ = [ctx.sb(f"{name}{i}", list(shape), dtype) for i in range(2)]
        bs = [ctx.buf(f"{name}{i}") for i in range(2)]
        return ts_, bs
    e_tm, e_tm_b = sbuf2("e_tm", [128, 64], BF16)
    k_tm, k_tm_b = sbuf2("k_tm", [128, 64], BF16)
    v_tm, v_tm_b = sbuf2("v_tm", [128, 128], BF16)
    skq, skq_b = sbuf2("skq", [128, 128], BF16)
    recA, recA_b = sbuf2("recA", [128, 64], F32)
    okn, okn_b = sbuf2("okn", [128, 64], F32)
    nmx, nmx_b = sbuf2("nmx", [128, 1], F32)
    pve, pve_b = sbuf2("pve", [128, 64], F32)
    ell, ell_b = sbuf2("ell", [128, 1], F32)
    rcl, rcl_b = sbuf2("rcl", [128, 1], F32)
    pab, pab_b = sbuf2("pab", [128, 64], BF16)
    pam, pam_b = sbuf2("pam", [128, 128], BF16)
    ctb, ctb_b = sbuf2("ctb", [128, 128], BF16)
    sqv, sqv_b = sbuf2("sqv", [128, 128], F32)
    ovs, ovs_b = sbuf2("ovs", [128, 128], F32)
    rsd, rsd_b = sbuf2("rsd", [128, 1], F32)
    rsT, rsT_b = sbuf2("rsT", [1, 128], F32)
    rsB, rsB_b = sbuf2("rsB", [128, 128], F32)
    gt1, gt1_b = sbuf2("gt1", [128, 128], F32)

    def sl(i, wt, part, nparts, c):
        return slab[i][part:part + nparts,
                       wt * 512 + c * 128: wt * 512 + (c + 1) * 128]

    def prep_group(tg, i):
        em.dma("sq", slab[i][:].rearrange("p (w f) -> p w f", w=NWT),
               proj[tg].rearrange("w p f -> p w f"), writes=[slab_b[i]])
        for q in range(8):
            wt = WT_S0 + q
            em.op("dve", lambda e, wt=wt, i=i: e.tensor_scalar(
                slab[i][:, wt * 512:(wt + 1) * 512],
                slab[i][:, wt * 512:(wt + 1) * 512],
                32.0, -32.0, ALU.min, ALU.max),
                reads=[slab_b[i]], accwrites=[slab_b[i]])
            em.op("act", lambda e, wt=wt, i=i: e.activation(
                slab[i][:, wt * 512:(wt + 1) * 512],
                slab[i][:, wt * 512:(wt + 1) * 512], AF.Exp),
                reads=[slab_b[i]], accwrites=[slab_b[i]])
        for q in range(16):
            wt = WT_G0 + q
            em.op("act", lambda e, wt=wt, i=i: e.activation(
                slab[i][:, wt * 512:(wt + 1) * 512],
                slab[i][:, wt * 512:(wt + 1) * 512], AF.Silu),
                reads=[slab_b[i]], accwrites=[slab_b[i]])

    prep_group(0, 0)
    for tg in range(4):
        i = tg % 2
        if tg + 1 < 4:
            prep_group(tg + 1, (tg + 1) % 2)
        oi = tg % 2
        if True:
            def head_steps(c, h, i, oi, first):
                steps = []
                _op = em.op
                def op_rec(*a, **k):
                    steps.append((a, k))
                em.op = op_rec
                par = h % 2
                qp = 64 * par
                qwt, kwt, swt = WT_Q0 + h // 2, WT_K0 + h // 2, WT_S0 + h // 2
                vwt, gwt = WT_V0 + h, WT_G0 + h
                pr = h // 2
                u_bf = cUWb[qp:qp + 64, pr * 192:pr * 192 + 64]
                w_bf = cUWb[qp:qp + 64, pr * 192 + 64:pr * 192 + 192]
                uw_f = cUWf[qp:qp + 64, pr * 192:pr * 192 + 192]
                uw_b16 = cUWb[qp:qp + 64, pr * 192:pr * 192 + 192]
                a_f32 = asb[0:1, h * 64:(h + 1) * 64]
                a_bf = abf[0:1, h * 64:(h + 1) * 64]
                trE = trp[:, 0:64]
                trK = trp[:, 64:128]
                trV = trp[:, 128:256]
                trP = trp[qp:qp + 64, 256:384]

                em.op("pe", lambda e, swt=swt, qp=qp, c=c, i=i: e.transpose(
                    trE, sl(i, swt, qp, 64, c), irs[qp:qp + 64, :]),
                    reads=[slab_b[i], irs_b], writes=[trp_pb])
                em.op("dve", lambda e, par=par: e.tensor_copy(
                    e_tm[par][:], trE),
                    reads=[trp_pb], writes=[e_tm_b[par]])
                em.op("pe", lambda e, kwt=kwt, qp=qp, c=c, i=i: e.transpose(
                    trK, sl(i, kwt, qp, 64, c), irs[qp:qp + 64, :]),
                    reads=[slab_b[i], irs_b], writes=[trp_pb])
                em.op("dve", lambda e, par=par: e.tensor_copy(
                    k_tm[par][:], trK),
                    reads=[trp_pb], writes=[k_tm_b[par]])
                em.op("pe", lambda e, vwt=vwt, c=c, i=i: e.transpose(
                    trV, sl(i, vwt, 0, 128, c), i128b[:]),
                    reads=[slab_b[i], i128b_b], writes=[trp_pb])
                em.op("dve", lambda e, par=par: e.tensor_copy(
                    v_tm[par][:], trV),
                    reads=[trp_pb], writes=[v_tm_b[par]])
                if not first:
                    em.op("dve", lambda e, uw_f=uw_f, uw_b16=uw_b16:
                          e.tensor_copy(uw_b16, uw_f),
                          reads=[cUWf_b[h]], writes=[cUWb_b[h]])
                    em.op("dve", lambda e, a_f32=a_f32, a_bf=a_bf:
                          e.tensor_copy(a_bf, a_f32),
                          reads=[asb_b[h]], writes=[abf_b[h]])
                em.op("pe", lambda e, kwt=kwt, qwt=qwt, qp=qp, c=c, i=i: e.matmul(
                    sc_ps[:], sl(i, kwt, qp, 64, c), sl(i, qwt, qp, 64, c),
                    start=True, stop=True),
                    reads=[slab_b[i]], writes=[sc_pb])
                em.op("dve", lambda e, par=par: e.tensor_tensor(
                    skq[par][:], sc_ps[:], triu[:], ALU.mult),
                    reads=[sc_pb, triu_b], writes=[skq_b[par]])
                def a_mm(e, first=first, a_bf=a_bf, par=par):
                    ins = e.matmul(a_ps[:, 0:64], triu[:], e_tm[par][:],
                                   start=True, stop=first)
                    if not first:
                        ins = e.matmul(a_ps[:, 0:64], onesr[:], a_bf,
                                       start=False, stop=True)
                    return ins
                em.op("pe", a_mm,
                      reads=[triu_b, e_tm_b[par], onesr_b, abf_b[h]],
                      writes=[a_pb])
                em.op("dve", lambda e, par=par: e.reciprocal(
                    recA[par][:], a_ps[:, 0:64]),
                    reads=[a_pb], writes=[recA_b[par]])
                def ok_mm(e, first=first, qwt=qwt, qp=qp, c=c, i=i,
                          u_bf=u_bf, par=par):
                    ins = e.matmul(ok_ps[:], skq[par][:], e_tm[par][:],
                                   start=True, stop=first)
                    if not first:
                        ins = e.matmul(ok_ps[:], sl(i, qwt, qp, 64, c), u_bf,
                                       start=False, stop=True)
                    return ins
                em.op("pe", ok_mm,
                      reads=[skq_b[par], e_tm_b[par], slab_b[i], cUWb_b[h]],
                      writes=[ok_pb])
                em.op("dve", lambda e, par=par: e.tensor_tensor(
                    okn[par][:], ok_ps[:], recA[par][:], ALU.mult),
                    reads=[ok_pb, recA_b[par]], writes=[okn_b[par]])
                em.op("dve", lambda e, par=par: e.reduce_max(
                    nmx[par][:], okn[par][:], axis=AX.X, negate=True),
                    reads=[okn_b[par]], writes=[nmx_b[par]])
                em.op("act", lambda e, par=par: e.activation(
                    pve[par][:], okn[par][:], AF.Exp, bias=nmx[par][:]),
                    reads=[okn_b[par], nmx_b[par]], writes=[pve_b[par]])
                em.op("dve", lambda e, par=par: e.reduce_sum(
                    ell[par][:], pve[par][:], axis=AX.X),
                    reads=[pve_b[par]], writes=[ell_b[par]])
                em.op("dve", lambda e, par=par: e.reciprocal(
                    rcl[par][:], ell[par][:]),
                    reads=[ell_b[par]], writes=[rcl_b[par]])
                em.op("dve", lambda e, par=par: e.scalar_tensor_tensor(
                    pab[par][:], pve[par][:], rcl[par][:], recA[par][:],
                    ALU.mult, ALU.mult),
                    reads=[pve_b[par], rcl_b[par], recA_b[par]],
                    writes=[pab_b[par]])
                em.op("pe", lambda e, qp=qp, par=par, trP=trP: e.transpose(
                    trP, pab[par][:], i128b[:], tile_position=(0, qp)),
                    reads=[pab_b[par], i128b_b], writes=[trp_pb])
                em.op("dve", lambda e, qp=qp, par=par: e.tensor_copy(
                    pam[par][qp:qp + 64, :], trp[qp:qp + 64, 256:384]),
                    reads=[trp_pb], accwrites=[pam_b[par]])
                em.op("pe", lambda e, swt=swt, qp=qp, c=c, i=i, par=par: e.matmul(
                    ct_ps[:, 0:128], sl(i, swt, qp, 64, c),
                    pam[par][qp:qp + 64, :], start=True, stop=True),
                    reads=[slab_b[i], pam_b[par]], writes=[ct_pb])
                em.op("dve", lambda e, par=par: e.tensor_tensor(
                    ctb[par][:], ct_ps[:, 0:128], triu[:], ALU.mult),
                    reads=[ct_pb, triu_b], writes=[ctb_b[par]])
                def ov_mm(e, first=first, qp=qp, w_bf=w_bf, par=par):
                    ins = e.matmul(ov_ps[:], v_tm[par][:], ctb[par][:],
                                   start=True, stop=first)
                    if not first:
                        ins = e.matmul(ov_ps[:], w_bf, pam[par][qp:qp + 64, :],
                                       start=False, stop=True)
                    return ins
                em.op("pe", ov_mm,
                      reads=[v_tm_b[par], ctb_b[par], cUWb_b[h], pam_b[par]],
                      writes=[ov_pb])
                em.op("dve", lambda e, par=par: e.tensor_copy(
                    ovs[par][:], ov_ps[:]),
                    reads=[ov_pb], writes=[ovs_b[par]])
                em.op("pool", lambda e, par=par: e.tensor_tensor(
                    sqv[par][:], ovs[par][:], ovs[par][:], ALU.mult),
                    reads=[ovs_b[par]], writes=[sqv_b[par]])
                em.op("pe", lambda e, par=par: e.matmul(
                    a_ps[:, 128:129], sqv[par][:], o128[:],
                    start=True, stop=True),
                    reads=[sqv_b[par], o128_b], writes=[a_pb])
                em.op("act", lambda e, par=par: e.activation(
                    rsd[par][:], a_ps[:, 128:129], AF.Ln, bias=eps5[:]),
                    reads=[a_pb, eps5_b], writes=[rsd_b[par]])
                em.op("act", lambda e, par=par: e.activation(
                    rsd[par][:], rsd[par][:], AF.Exp, scale=-0.5),
                    reads=[rsd_b[par]], accwrites=[rsd_b[par]])
                em.op("pe", lambda e, par=par: e.transpose(
                    ct_ps[0:1, 384:512], rsd[par][:], i128f[:]),
                    reads=[rsd_b[par], i128f_b], writes=[ct_pb])
                em.op("dve", lambda e, par=par: e.tensor_copy(
                    rsT[par][:], ct_ps[0:1, 384:512]),
                    reads=[ct_pb], writes=[rsT_b[par]])
                em.op("pe", lambda e, par=par: e.matmul(
                    ct_ps[:, 256:384], onesrf[:], rsT[par][:],
                    start=True, stop=True),
                    reads=[onesrf_b, rsT_b[par]], writes=[ct_pb])
                em.op("dve", lambda e, par=par: e.tensor_copy(
                    rsB[par][:], ct_ps[:, 256:384]),
                    reads=[ct_pb], writes=[rsB_b[par]])
                em.op("pool", lambda e, par=par: e.tensor_tensor(
                    gt1[par][:], ovs[par][:], rsB[par][:], ALU.mult),
                    reads=[ovs_b[par], rsB_b[par]], writes=[gt1_b[par]])
                em.op("dve", lambda e, h=h, c=c, gwt=gwt, i=i, oi=oi, par=par:
                      e.scalar_tensor_tensor(
                          ogs[oi][:, h * 512 + c * 128: h * 512 + (c + 1) * 128],
                          gt1[par][:], gnw[:], sl(i, gwt, 0, 128, c),
                          ALU.mult, ALU.mult),
                      reads=[gt1_b[par], gnw_b, slab_b[i]],
                      accwrites=[ogs_b[oi]])
                uj = h % 2

                def upd_mm(e, qp=qp, uj=uj, par=par):
                    e.matmul(upd[uj][qp:qp + 64, 0:64], k_tm[par][:],
                             e_tm[par][:], start=True, stop=True,
                             tile_position=(0, qp))
                    return e.matmul(upd[uj][qp:qp + 64, 64:192], e_tm[par][:],
                                    v_tm[par][:], start=True, stop=True,
                                    tile_position=(0, qp))
                em.op("pe", upd_mm,
                      reads=[k_tm_b[par], e_tm_b[par], v_tm_b[par]],
                      writes=[upd_b[uj]])
                if first:
                    em.op("dve", lambda e, uw_f=uw_f, qp=qp, uj=uj: e.tensor_copy(
                        uw_f, upd[uj][qp:qp + 64, :]),
                        reads=[upd_b[uj]], writes=[cUWf_b[h]])
                else:
                    em.op("dve", lambda e, uw_f=uw_f, qp=qp, uj=uj: e.tensor_tensor(
                        uw_f, uw_f, upd[uj][qp:qp + 64, :], ALU.add),
                        reads=[upd_b[uj], cUWf_b[h]], accwrites=[cUWf_b[h]])
                em.op("pe", lambda e, par=par: e.matmul(
                    a_ps[0:1, 192:256], onescol[:], e_tm[par][:],
                    start=True, stop=True),
                    reads=[onescol_b, e_tm_b[par]], writes=[a_pb])
                if first:
                    em.op("dve", lambda e, a_f32=a_f32: e.tensor_copy(
                        a_f32, a_ps[0:1, 192:256]),
                        reads=[a_pb], writes=[asb_b[h]])
                else:
                    em.op("dve", lambda e, a_f32=a_f32: e.tensor_tensor(
                        a_f32, a_f32, a_ps[0:1, 192:256], ALU.add),
                        reads=[a_pb, asb_b[h]], accwrites=[asb_b[h]])
                em.op = _op
                return steps

            for c in range(4):
                first = (tg == 0 and c == 0)
                GR = 1000
                for hp in range(0, H, 2):
                    sa = head_steps(c, hp, i, oi, first)
                    sb2 = head_steps(c, hp + 1, i, oi, first)
                    na, nb = len(sa), len(sb2)
                    ia = ib = 0
                    while ia < na or ib < nb:
                        for _ in range(GR):
                            if ia < na:
                                a, k = sa[ia]; em.op(*a, **k); ia += 1
                        for _ in range(GR):
                            if ib < nb:
                                a, k = sb2[ib]; em.op(*a, **k); ib += 1
        for hh in range(H):
            em.dma("sq", og[hh, :, tg * 512:(tg + 1) * 512],
                   ogs[oi][:, hh * 512:(hh + 1) * 512], reads=[ogs_b[oi]])


def build_p1():
    ctx = Ctx("p1")
    nc = ctx.nc
    proj = nc.dram_tensor("proj", [4, NWT, 128, 512], BF16,
                          kind="ExternalInput").ap()
    gnw = nc.dram_tensor("gnw", [128], F32, kind="ExternalInput").ap()
    og = nc.dram_tensor("og", [16, 128, T], BF16, kind="ExternalOutput").ap()
    emit_p1(ctx, proj, gnw, og)
    ctx.finish()
    return nc



class LNRes:
    def __init__(self, ctx):
        self.onescol, self.onescol_b = ctx.const(
            "onescolf", np.ones((128, 1), dtype=np.float32))
        self.onesrow, self.onesrow_b = ctx.const(
            "onesrowf", np.ones((1, 128), dtype=np.float32))
        self.eps1, self.eps1_b = ctx.cvec(1e-5, parts=1)
        self.st1 = ctx.ps("ln_st1", [1, 512]); self.st1_b = ctx.buf("ln_st1")
        self.st2 = ctx.ps("ln_st2", [1, 512]); self.st2_b = ctx.buf("ln_st2")
        self.bca = ctx.ps("ln_bca", [128, 512]); self.bca_b = ctx.buf("ln_bca")
        self.bcb = ctx.ps("ln_bcb", [128, 512]); self.bcb_b = ctx.buf("ln_bcb")
        self.sq = [ctx.sb(f"ln_sq{i}", [128, 512], F32) for i in range(2)]
        self.sq_b = [ctx.buf() for _ in range(2)]
        self.rows = ctx.sb("ln_rows", [1, 512 * 5], F32)
        self.rows_b = ctx.buf("ln_rows")
        self.absb = ctx.sb("ln_absb", [128, 512], F32); self.absb_b = ctx.buf()
        self.t1 = ctx.sb("ln_t1", [128, 512], F32); self.t1_b = ctx.buf()
        self.xo = [ctx.sb(f"ln_xo{i}", [128, 512], F32) for i in range(2)]
        self.xo_b = [ctx.buf() for _ in range(2)]
        self.xn = [ctx.sb(f"ln_xn{i}", [128, 512], BF16) for i in range(2)]
        self.xn_b = [ctx.buf() for _ in range(2)]


def emit_ln(ctx, ln, rsl, rsl_b, xo_dst, xn_dst):
    em = ctx.em
    R = lambda fc: rsl[:, fc * 512:(fc + 1) * 512]
    mrow = ln.rows[:, 0:512]
    qrow = ln.rows[:, 512:1024]
    var = ln.rows[:, 1024:1536]
    arow = ln.rows[:, 1536:2048]
    brow = ln.rows[:, 2048:2560]
    for fc in range(16):
        i = fc % 2
        em.op("act", lambda e, fc=fc, i=i: e.activation(
            ln.sq[i][:], R(fc), AF.Square), reads=[rsl_b], writes=[ln.sq_b[i]])
        em.op("pe", lambda e, fc=fc: e.matmul(
            ln.st1[:], ln.onescol[:], R(fc), start=(fc == 0), stop=(fc == 15)),
            reads=[ln.onescol_b, rsl_b],
            writes=[ln.st1_b] if fc == 0 else (), accwrites=() if fc == 0 else [ln.st1_b])
        em.op("pe", lambda e, i=i, fc=fc: e.matmul(
            ln.st2[:], ln.onescol[:], ln.sq[i][:], start=(fc == 0), stop=(fc == 15)),
            reads=[ln.onescol_b, ln.sq_b[i]],
            writes=[ln.st2_b] if fc == 0 else (), accwrites=() if fc == 0 else [ln.st2_b])
    em.op("act", lambda e: e.activation(mrow, ln.st1[:], AF.Copy, scale=1.0 / D),
          reads=[ln.st1_b], accwrites=[ln.rows_b])
    em.op("act", lambda e: e.activation(qrow, ln.st2[:], AF.Copy, scale=1.0 / D),
          reads=[ln.st2_b], accwrites=[ln.rows_b])
    em.op("dve", lambda e: e.tensor_tensor(var, mrow, mrow, ALU.mult),
          reads=[ln.rows_b], accwrites=[ln.rows_b])
    em.op("dve", lambda e: e.tensor_tensor(var, qrow, var, ALU.subtract),
          reads=[ln.rows_b], accwrites=[ln.rows_b])
    em.op("act", lambda e: e.activation(var, var, AF.Sqrt, bias=ln.eps1[:]),
          reads=[ln.rows_b, ln.eps1_b], accwrites=[ln.rows_b])
    em.op("dve", lambda e: e.reciprocal(arow, var),
          reads=[ln.rows_b], accwrites=[ln.rows_b])
    em.op("dve", lambda e: e.scalar_tensor_tensor(
        brow, mrow, -1.0, arow, ALU.mult, ALU.mult),
        reads=[ln.rows_b], accwrites=[ln.rows_b])
    em.op("pe", lambda e: e.matmul(ln.bca[:], ln.onesrow[:], arow,
                                   start=True, stop=True),
          reads=[ln.onesrow_b, ln.rows_b], writes=[ln.bca_b])
    em.op("pe", lambda e: e.matmul(ln.bcb[:], ln.onesrow[:], brow,
                                   start=True, stop=True),
          reads=[ln.onesrow_b, ln.rows_b], writes=[ln.bcb_b])
    em.op("dve", lambda e: e.tensor_copy(ln.absb[:], ln.bca[:]),
          reads=[ln.bca_b], writes=[ln.absb_b])
    for fc in range(16):
        i = fc % 2
        em.op("dve", lambda e, fc=fc: e.tensor_tensor(
            ln.t1[:], R(fc), ln.absb[:], ALU.mult),
            reads=[rsl_b, ln.absb_b], writes=[ln.t1_b])
        em.op("dve", lambda e, i=i: e.tensor_tensor(
            ln.xo[i][:], ln.t1[:], ln.bcb[:], ALU.add),
            reads=[ln.t1_b, ln.bcb_b], writes=[ln.xo_b[i]])
        em.dma("sq", xo_dst(fc), ln.xo[i][:], reads=[ln.xo_b[i]])
        if xn_dst is not None:
            em.op("act", lambda e, i=i: e.activation(ln.xn[i][:], ln.xo[i][:],
                                                     AF.Copy),
                  reads=[ln.xo_b[i]], writes=[ln.xn_b[i]])
            em.dma("pool", xn_dst(fc), ln.xn[i][:], reads=[ln.xn_b[i]])



def emit_proj_res_ln(ctx, src_dr, w_dr, xres_dr, xo_dr, xn_dr, res_scale=1.0):
    nc, em = ctx.nc, ctx.em
    ln = LNRes(ctx)
    wob = ctx.sb("prl_wob", [128, 16 * 2048], BF16)
    wob_b = ctx.buf("wob")
    wof = [ctx.sb(f"prl_wof{i}", [128, 2048], F32) for i in range(2)]
    wof_b = [ctx.buf() for _ in range(2)]
    for ot in range(16):
        i = ot % 2
        em.dma("sq", wof[i][:].rearrange("p (c f) -> p c f", c=16), w_dr[ot],
               writes=[wof_b[i]])
        em.op("dve", lambda e, ot=ot, i=i: e.tensor_copy(
            wob[:, ot * 2048:(ot + 1) * 2048], wof[i][:]),
            reads=[wof_b[i]], accwrites=[wob_b])
    srcs = [ctx.sb(f"prl_src{i}", [128, 16 * 512], BF16) for i in range(2)]
    srcs_b = [ctx.buf() for _ in range(2)]
    xc = [ctx.sb(f"prl_xc{i}", [128, 512], F32) for i in range(2)]
    xc_b = [ctx.buf() for _ in range(2)]
    rsl = ctx.sb("prl_rsl", [128, 16 * 512], F32)
    rsl_b = ctx.buf("prl_rsl")
    pp = [ctx.ps(f"prl_ps{i}", [128, 512]) for i in range(2)]
    pp_b = [ctx.buf() for _ in range(2)]
    for ts in range(4):
        i2 = ts % 2
        em.dma("sq", srcs[i2][:].rearrange("p (c f) -> p c f", c=16),
               src_dr[:, :, ts * 512:(ts + 1) * 512].rearrange("h p f -> p h f"),
               writes=[srcs_b[i2]])
        for d2t in range(16):
            j = d2t % 2
            em.dma("pool", xc[j][:],
                   xres_dr[d2t * 128:(d2t + 1) * 128, ts * 512:(ts + 1) * 512],
                   writes=[xc_b[j]])

            def mmg(e, d2t=d2t, i2=i2, j=j):
                for fc in range(16):
                    ins = e.matmul(pp[j][:],
                                   wob[:, d2t * 2048 + fc * 128: d2t * 2048 + fc * 128 + 128],
                                   srcs[i2][:, fc * 512:(fc + 1) * 512],
                                   start=(fc == 0), stop=(fc == 15))
                return ins
            em.op("pe", mmg, reads=[wob_b, srcs_b[i2]], writes=[pp_b[j]])
            em.op("dve", lambda e, d2t=d2t, j=j, rs=res_scale: e.scalar_tensor_tensor(
                rsl[:, d2t * 512:(d2t + 1) * 512], xc[j][:], rs, pp[j][:],
                ALU.mult, ALU.add),
                reads=[xc_b[j], pp_b[j]], accwrites=[rsl_b])
        emit_ln(ctx, ln, rsl, rsl_b,
                lambda fc, ts=ts: xo_dr[fc * 128:(fc + 1) * 128,
                                        ts * 512:(ts + 1) * 512],
                None if xn_dr is None else
                (lambda fc, ts=ts: xn_dr[fc * 128:(fc + 1) * 128,
                                         ts * 512:(ts + 1) * 512]))


def build_p2():
    ctx = Ctx("p2")
    nc = ctx.nc
    og = nc.dram_tensor("og", [16, 128, T], BF16, kind="ExternalInput").ap()
    w_l0o = nc.dram_tensor("w_l0o", [16, 128, 16, 128], F32,
                           kind="ExternalInput").ap()
    xT = nc.dram_tensor("xT", [D, T], F32, kind="ExternalInput").ap()
    x1 = nc.dram_tensor("x1", [D, T], F32, kind="ExternalOutput").ap()
    x1n = nc.dram_tensor("x1n", [D, T], BF16, kind="ExternalOutput").ap()
    emit_proj_res_ln(ctx, og, w_l0o, xT, x1, x1n, res_scale=1.0)
    ctx.finish()
    return nc



def emit_p3(ctx, x1n_dr, wg_dr, wu_dr, gsc_dr, usc_dr, hsc_dr):
    nc, em = ctx.nc, ctx.em
    xb = ctx.sb("p3_xb", [128, 16 * T], BF16)
    xb_b = ctx.buf("p3_xb")
    em.dma("sq", xb[:].rearrange("p (c t) -> p c t", c=16),
           x1n_dr.rearrange("(c p) t -> p c t", p=128), writes=[xb_b])
    gscs = ctx.sb("p3_gsc", [128, 64], F32); gscs_b = ctx.buf()
    uscs = ctx.sb("p3_usc", [128, 64], F32); uscs_b = ctx.buf()
    em.dma("pool", gscs[:], gsc_dr, writes=[gscs_b])
    em.dma("pool", uscs[:], usc_dr, writes=[uscs_b])
    wg8 = [ctx.sb(f"p3_wg8{i}", [128, 2048], U8) for i in range(2)]
    wg8_b = [ctx.buf() for _ in range(2)]
    wu8 = [ctx.sb(f"p3_wu8{i}", [128, 2048], U8) for i in range(2)]
    wu8_b = [ctx.buf() for _ in range(2)]
    wgb = [ctx.sb(f"p3_wgb{i}", [128, 2048], BF16) for i in range(2)]
    wgb_b = [ctx.buf() for _ in range(2)]
    wub = [ctx.sb(f"p3_wub{i}", [128, 2048], BF16) for i in range(2)]
    wub_b = [ctx.buf() for _ in range(2)]
    gs = ctx.sb("p3_gs", [128, 512], F32); gs_b = ctx.buf()
    hst = [ctx.sb(f"p3_hst{i}", [128, 2048], BF16) for i in range(2)]
    hst_b = [ctx.buf() for _ in range(2)]
    pg = [ctx.ps(f"p3_pg{i}", [128, 512]) for i in range(2)]
    pg_b = [ctx.buf() for _ in range(2)]
    pu = [ctx.ps(f"p3_pu{i}", [128, 512]) for i in range(2)]
    pu_b = [ctx.buf() for _ in range(2)]
    for it in range(64):
        i = it % 2
        em.dma("sq", wg8[i][:].rearrange("p (c f) -> p c f", c=16), wg_dr[it],
               writes=[wg8_b[i]])
        em.dma("sq", wu8[i][:].rearrange("p (c f) -> p c f", c=16), wu_dr[it],
               writes=[wu8_b[i]])
        em.op("act", lambda e, i=i: e.activation(wgb[i][:], wg8[i][:], AF.Copy,
                                                 bias=-128.0),
              reads=[wg8_b[i]], writes=[wgb_b[i]])
        em.op("act", lambda e, i=i: e.activation(wub[i][:], wu8[i][:], AF.Copy,
                                                 bias=-128.0),
              reads=[wu8_b[i]], writes=[wub_b[i]])
        for ts in range(4):
            j = ts % 2

            def mmg(e, i=i, ts=ts, j=j, w=None):
                for fc in range(16):
                    ins = e.matmul(pg[j][:], wgb[i][:, fc * 128:(fc + 1) * 128],
                                   xb[:, fc * T + ts * 512: fc * T + ts * 512 + 512],
                                   start=(fc == 0), stop=(fc == 15))
                for fc in range(16):
                    ins = e.matmul(pu[j][:], wub[i][:, fc * 128:(fc + 1) * 128],
                                   xb[:, fc * T + ts * 512: fc * T + ts * 512 + 512],
                                   start=(fc == 0), stop=(fc == 15))
                return ins
            em.op("pe", mmg, reads=[wgb_b[i], wub_b[i], xb_b],
                  writes=[pg_b[j], pu_b[j]])
            em.op("act", lambda e, it=it, j=j: e.activation(
                gs[:], pg[j][:], AF.Silu, scale=gscs[:, it:it + 1]),
                reads=[pg_b[j], gscs_b], writes=[gs_b])
            em.op("dve", lambda e, it=it, i=i, ts=ts, j=j: e.scalar_tensor_tensor(
                hst[i][:, ts * 512:(ts + 1) * 512], pu[j][:], uscs[:, it:it + 1],
                gs[:], ALU.mult, ALU.mult),
                reads=[pu_b[j], uscs_b, gs_b], accwrites=[hst_b[i]])
        em.dma("sq", hsc_dr[it], hst[i][:], reads=[hst_b[i]], writes=())


def build_p3():
    ctx = Ctx("p3")
    nc = ctx.nc
    x1n = nc.dram_tensor("x1n", [D, T], BF16, kind="ExternalInput").ap()
    wg = nc.dram_tensor("w_l0g", [64, 128, 16, 128], U8, kind="ExternalInput").ap()
    wu = nc.dram_tensor("w_l0u", [64, 128, 16, 128], U8, kind="ExternalInput").ap()
    gsc = nc.dram_tensor("gsc", [128, 64], F32, kind="ExternalInput").ap()
    usc = nc.dram_tensor("usc", [128, 64], F32, kind="ExternalInput").ap()
    hsc = nc.dram_tensor("hsc", [64, 128, T], BF16, kind="ExternalOutput").ap()
    emit_p3(ctx, x1n, wg, wu, gsc, usc, hsc)
    ctx.finish()
    return nc



def emit_p4(ctx, hsc_dr, wd_dr, dsc_dr, x1_dr, x2_dr, x2n_dr):
    nc, em = ctx.nc, ctx.em
    ln = LNRes(ctx)
    dscs = ctx.sb("p4_dsc", [128, 16], F32); dscs_b = ctx.buf()
    em.dma("pool", dscs[:], dsc_dr, writes=[dscs_b])
    hsl = ctx.sb("p4_hsl", [128, 64 * 512], BF16)
    hsl_b = ctx.buf("p4_hsl")
    wd8 = [ctx.sb(f"p4_wd8{i}", [128, 8192], U8) for i in range(2)]
    wd8_b = [ctx.buf() for _ in range(2)]
    wdb = [ctx.sb(f"p4_wdb{i}", [128, 8192], BF16) for i in range(2)]
    wdb_b = [ctx.buf() for _ in range(2)]
    xc = [ctx.sb(f"p4_xc{i}", [128, 512], F32) for i in range(2)]
    xc_b = [ctx.buf() for _ in range(2)]
    rsl = ctx.sb("p4_rsl", [128, 16 * 512], F32)
    rsl_b = ctx.buf("p4_rsl")
    pp = [ctx.ps(f"p4_ps{i}", [128, 512]) for i in range(2)]
    pp_b = [ctx.buf() for _ in range(2)]
    for tc in range(4):
        em.dma("sq", hsl[:].rearrange("p (c f) -> p c f", c=64),
               hsc_dr[:, :, tc * 512:(tc + 1) * 512].rearrange("i p f -> p i f"),
               writes=[hsl_b])
        for d2t in range(16):
            i = d2t % 2
            j = d2t % 2
            em.dma("sq", wd8[i][:].rearrange("p (c f) -> p c f", c=64),
                   wd_dr[d2t], writes=[wd8_b[i]])
            em.op("act", lambda e, i=i: e.activation(wdb[i][:], wd8[i][:],
                                                     AF.Copy, bias=-128.0),
                  reads=[wd8_b[i]], writes=[wdb_b[i]])
            em.dma("pool", xc[j][:],
                   x1_dr[d2t * 128:(d2t + 1) * 128, tc * 512:(tc + 1) * 512],
                   writes=[xc_b[j]])

            def mmg(e, i=i, j=j):
                for kc in range(64):
                    ins = e.matmul(pp[j][:], wdb[i][:, kc * 128:(kc + 1) * 128],
                                   hsl[:, kc * 512:(kc + 1) * 512],
                                   start=(kc == 0), stop=(kc == 63))
                return ins
            em.op("pe", mmg, reads=[wdb_b[i], hsl_b], writes=[pp_b[j]])
            em.op("dve", lambda e, d2t=d2t, j=j: e.scalar_tensor_tensor(
                rsl[:, d2t * 512:(d2t + 1) * 512], pp[j][:], dscs[:, d2t:d2t + 1],
                xc[j][:], ALU.mult, ALU.add),
                reads=[pp_b[j], dscs_b, xc_b[j]], accwrites=[rsl_b])
        emit_ln(ctx, ln, rsl, rsl_b,
                lambda fc, tc=tc: x2_dr[fc * 128:(fc + 1) * 128,
                                        tc * 512:(tc + 1) * 512],
                None if x2n_dr is None else
                (lambda fc, tc=tc: x2n_dr[fc * 128:(fc + 1) * 128,
                                          tc * 512:(tc + 1) * 512]))


def build_p4():
    ctx = Ctx("p4")
    nc = ctx.nc
    hsc = nc.dram_tensor("hsc", [64, 128, T], BF16, kind="ExternalInput").ap()
    wd = nc.dram_tensor("w_l0d", [16, 128, 64, 128], U8, kind="ExternalInput").ap()
    dsc = nc.dram_tensor("dsc", [128, 16], F32, kind="ExternalInput").ap()
    x1 = nc.dram_tensor("x1", [D, T], F32, kind="ExternalInput").ap()
    x2 = nc.dram_tensor("x2", [D, T], F32, kind="ExternalOutput").ap()
    x2n = nc.dram_tensor("x2n", [D, T], BF16, kind="ExternalOutput").ap()
    emit_p4(ctx, hsc, wd, dsc, x1, x2, x2n)
    ctx.finish()
    return nc



def emit_p5(ctx, x2_dr, ns_dr, ss_dr, wq_dr, wk_dr, wv_dr, qf_dr, kf_dr, vf_dr):
    nc, em = ctx.nc, ctx.em
    onescol, onescol_b = ctx.const("onescolf", np.ones((128, 1), dtype=np.float32))
    onesrow, onesrow_b = ctx.const("onesrowf", np.ones((1, 128), dtype=np.float32))
    eps1, eps1_b = ctx.cvec(1e-8, parts=1)
    nss = ctx.sb("p5_nss", [128, 16], F32); nss_b = ctx.buf()
    for name, dr in (("ns", ns_dr), ("ss", ss_dr)):
        t = ctx.sb(f"p5_{name}", [128, 16], F32)
        b = ctx.buf()
        em.dma("pool", t[:], dr, writes=[b])
        if name == "ns":
            tns, tns_b = t, b
        else:
            tss, tss_b = t, b
    em.op("dve", lambda e: e.tensor_tensor(nss[:], tns[:], tss[:], ALU.mult),
          reads=[tns_b, tss_b], writes=[nss_b])

    xsl = ctx.sb("p5_xsl", [128, 16 * 512], F32); xsl_b = ctx.buf()
    sq = [ctx.sb(f"p5_sq{i}", [128, 512], F32) for i in range(2)]
    sq_b = [ctx.buf() for _ in range(2)]
    rows = ctx.sb("p5_rows", [1, 512], F32); rows_b = ctx.buf()
    cbs = ctx.sb("p5_cbs", [128, 512], F32); cbs_b = ctx.buf()
    hsl = ctx.sb("p5_hsl", [128, 16 * T], BF16); hsl_b = ctx.buf("p5_hsl")
    st = ctx.ps("p5_st", [1, 512]); st_b = ctx.buf()
    bc = ctx.ps("p5_bc", [128, 512]); bc_b = ctx.buf()
    for ts in range(4):
        for fc in range(16):
            i = fc % 2
            em.dma("sq", xsl[:, fc * 512:(fc + 1) * 512],
                   x2_dr[fc * 128:(fc + 1) * 128, ts * 512:(ts + 1) * 512],
                   accwrites=[xsl_b])
            em.op("act", lambda e, fc=fc, i=i: e.activation(
                sq[i][:], xsl[:, fc * 512:(fc + 1) * 512], AF.Square),
                reads=[xsl_b], writes=[sq_b[i]])
            em.op("pe", lambda e, fc=fc, i=i: e.matmul(
                st[:], onescol[:], sq[i][:], start=(fc == 0), stop=(fc == 15)),
                reads=[onescol_b, sq_b[i]],
                writes=[st_b] if fc == 0 else (), accwrites=() if fc == 0 else [st_b])
        em.op("act", lambda e: e.activation(rows[:], st[:], AF.Sqrt,
                                            scale=1.0 / D),
              reads=[st_b], writes=[rows_b])
        em.op("dve", lambda e: e.tensor_scalar(rows[:], rows[:], eps1[:], None,
                                               ALU.add),
              reads=[rows_b, eps1_b], accwrites=[rows_b])
        em.op("dve", lambda e: e.reciprocal(rows[:], rows[:]),
              reads=[rows_b], accwrites=[rows_b])
        em.op("pe", lambda e: e.matmul(bc[:], onesrow[:], rows[:],
                                       start=True, stop=True),
              reads=[onesrow_b, rows_b], writes=[bc_b])
        em.op("dve", lambda e: e.tensor_copy(cbs[:], bc[:]),
              reads=[bc_b], writes=[cbs_b])
        for fc in range(16):
            em.op("dve", lambda e, fc=fc, ts=ts: e.scalar_tensor_tensor(
                hsl[:, fc * T + ts * 512: fc * T + ts * 512 + 512],
                xsl[:, fc * 512:(fc + 1) * 512], nss[:, fc:fc + 1], cbs[:],
                ALU.mult, ALU.mult),
                reads=[xsl_b, nss_b, cbs_b], accwrites=[hsl_b])
    wf = [ctx.sb(f"p5_wf{i}", [128, 2048], F32) for i in range(2)]
    wf_b = [ctx.buf() for _ in range(2)]
    wb = [ctx.sb(f"p5_wb{i}", [128, 2048], BF16) for i in range(2)]
    wb_b = [ctx.buf() for _ in range(2)]
    ob = [ctx.sb(f"p5_ob{i}", [128, 512], BF16) for i in range(4)]
    ob_b = [ctx.buf() for _ in range(4)]
    pp = [ctx.ps(f"p5_pp{i}", [128, 512]) for i in range(2)]
    pp_b = [ctx.buf() for _ in range(2)]
    jobs = [(wq_dr, ot, qf_dr, ot, 0.125) for ot in range(16)]
    jobs += [(wk_dr, 0, kf_dr, 0, 1.0), (wv_dr, 0, vf_dr, 0, 1.0)]
    for n, (wdr, ot, odr, oot, qsc) in enumerate(jobs):
        i = n % 2
        em.dma("sq", wf[i][:].rearrange("p (c f) -> p c f", c=16), wdr[ot],
               writes=[wf_b[i]])
        em.op("dve", lambda e, i=i: e.tensor_copy(wb[i][:], wf[i][:]),
              reads=[wf_b[i]], writes=[wb_b[i]])
        for ts in range(4):
            j = (n * 4 + ts) % 2
            jo = (n * 4 + ts) % 4

            def mmg(e, i=i, ts=ts, j=j):
                for fc in range(16):
                    ins = e.matmul(pp[j][:], wb[i][:, fc * 128:(fc + 1) * 128],
                                   hsl[:, fc * T + ts * 512: fc * T + ts * 512 + 512],
                                   start=(fc == 0), stop=(fc == 15))
                return ins
            em.op("pe", mmg, reads=[wb_b[i], hsl_b], writes=[pp_b[j]])
            em.op("act", lambda e, j=j, jo=jo, qsc=qsc: e.activation(
                ob[jo][:], pp[j][:], AF.Copy, scale=qsc),
                reads=[pp_b[j]], writes=[ob_b[jo]])
            em.dma("sq", odr[oot * 128:(oot + 1) * 128, ts * 512:(ts + 1) * 512],
                   ob[jo][:], reads=[ob_b[jo]])


def build_p5():
    ctx = Ctx("p5")
    nc = ctx.nc
    x2 = nc.dram_tensor("x2", [D, T], F32, kind="ExternalInput").ap()
    ns = nc.dram_tensor("ns", [128, 16], F32, kind="ExternalInput").ap()
    ss = nc.dram_tensor("ss", [128, 16], F32, kind="ExternalInput").ap()
    wq = nc.dram_tensor("w_l1q", [16, 128, 16, 128], F32, kind="ExternalInput").ap()
    wk = nc.dram_tensor("w_l1k", [1, 128, 16, 128], F32, kind="ExternalInput").ap()
    wv = nc.dram_tensor("w_l1v", [1, 128, 16, 128], F32, kind="ExternalInput").ap()
    qf = nc.dram_tensor("qf", [D, T], BF16, kind="ExternalOutput").ap()
    kf = nc.dram_tensor("kf", [128, T], BF16, kind="ExternalOutput").ap()
    vf = nc.dram_tensor("vf", [128, T], BF16, kind="ExternalOutput").ap()
    emit_p5(ctx, x2, ns, ss, wq, wk, wv, qf, kf, vf)
    ctx.finish()
    return nc



P6_DEBUG = False


def emit_p6(ctx, qf_dr, kf_dr, vf_dr, oatt_dr):
    nc, em = ctx.nc, ctx.em
    i128f, i128f_b = ctx.const("i128f", np.eye(128, dtype=np.float32))
    irs_np = np.zeros((128, 64), dtype=ml_dtypes.bfloat16)
    for p in range(128):
        irs_np[p, p % 64] = 1
    irs, irs_b = ctx.const("irs", irs_np)
    onesall, onesall_b = ctx.const("onesall",
                                   np.ones((128, 128), dtype=ml_dtypes.bfloat16))
    ones64f, ones64f_b = ctx.const("ones64f", np.ones((1, 64), dtype=np.float32))

    qsb = ctx.sb("qsb", [128, 16 * T], BF16); qsb_b = ctx.buf("qsb")
    ksb = ctx.sb("ksb", [128, T], BF16); ksb_b = ctx.buf()
    vsb = ctx.sb("vsb", [128, T], BF16); vsb_b = ctx.buf()
    vtm = ctx.sb("vtm", [128, 32 * 65], BF16); vtm_b = ctx.buf()
    pta = ctx.sb("pta", [128, 16 * 1024], BF16); pta_b = ctx.buf("pta")
    ptb2 = ctx.sb("ptb2", [128, 16 * 1024], BF16); ptb2_b = ctx.buf("ptb2")
    mall = [ctx.sb(f"mall{i}", [128, 32], F32) for i in range(2)]
    mall_b = [ctx.buf() for _ in range(2)]
    mbf = ctx.sb("mbf", [128, 32], BF16); mbf_b = ctx.buf()
    mrow = ctx.sb("mrow", [16, 256], BF16); mrow_b = ctx.buf()
    mneg2 = [ctx.sb(f"mneg2_{i}", [128, T], BF16) for i in range(2)]
    mneg2_b = [ctx.buf() for _ in range(2)]
    mcs = [ctx.sb(f"mcs{i}", [128, 8], F32) for i in range(2)]
    mcs_b = [ctx.buf() for _ in range(2)]
    rcl = ctx.sb("rcl", [1, 1024], F32); rcl_b = ctx.buf()
    bcs = ctx.sb("bcs", [64, 1024], F32); bcs_b = ctx.buf()
    obf = [ctx.sb(f"obf{i}", [64, 512], BF16) for i in range(4)]
    obf_b = [ctx.buf() for _ in range(4)]
    mdr = nc.dram_tensor("p6_mscr", [2, 2, T], BF16, kind="Internal").ap()
    mdr_b = [ctx.buf("mdr0"), ctx.buf("mdr1")]
    if P6_DEBUG:
        dbg_mall = nc.dram_tensor("dbg_mall", [128, 32], F32,
                                  kind="ExternalOutput").ap()
        dbg_mneg = nc.dram_tensor("dbg_mneg", [2, T], BF16,
                                  kind="ExternalOutput").ap()
        dbg_pta = nc.dram_tensor("dbg_pta", [128, 1024], BF16,
                                 kind="ExternalOutput").ap()
        dbg_vtm = nc.dram_tensor("dbg_vtm", [128, 32 * 65], BF16,
                                 kind="ExternalOutput").ap()

    G = [ctx.ps(f"G{i}", [128, 512]) for i in range(8)]
    G_b = [ctx.buf(f"G{i}") for i in range(8)]


    for g in range(2):
        em.dma("sq", qsb[g * 64:(g + 1) * 64, :].rearrange("p (j t) -> p j t", j=16),
               qf_dr.rearrange("(g j p) t -> g p j t", g=2, p=64)[g],
               accwrites=[qsb_b])
    em.dma("sq", ksb[:], kf_dr, writes=[ksb_b])
    em.dma("sq", vsb[:], vf_dr, writes=[vsb_b])
    em.op("pool", lambda e: e.memset(
        vtm[:].rearrange("p (k c) -> p k c", c=65)[:, :, 64:65], 1.0),
        accwrites=[vtm_b])
    g0bf = G[0][:].bitcast(BF16)
    for g in range(2):
        gp = 64 * g
        for kt in range(16):
            em.op("pe", lambda e, gp=gp, kt=kt: e.transpose(
                g0bf[:, 0:64], vsb[gp:gp + 64, kt * 128:(kt + 1) * 128],
                irs[gp:gp + 64, :]),
                reads=[vsb_b, irs_b], writes=[G_b[0]])
            em.op("dve", lambda e, g=g, kt=kt: e.tensor_copy(
                vtm[:, (g * 16 + kt) * 65:(g * 16 + kt) * 65 + 64],
                g0bf[:, 0:64]),
                reads=[G_b[0]], accwrites=[vtm_b])

    def emit_stats_step(j, step):
        pj = j % 2
        qt, qq = divmod(step, 4)
        c = step % 2
        qof = j * T
        ga, gb = 4 + c, 6 + c

        def smm(e, qt=qt, qq=qq, qof=qof, ga=ga, gb=gb):
            e.matmul(G[ga][:], qsb[0:64, qof + qt * 128: qof + qt * 128 + 128],
                     ksb[0:64, qq * 512:(qq + 1) * 512],
                     start=True, stop=True, tile_position=(0, 0))
            return e.matmul(G[gb][:],
                            qsb[64:128, qof + qt * 128: qof + qt * 128 + 128],
                            ksb[64:128, qq * 512:(qq + 1) * 512],
                            start=True, stop=True, tile_position=(64, 0))
        em.op("pe", smm, reads=[qsb_b, ksb_b], writes=[G_b[ga], G_b[gb]])
        gav = G[ga][:].rearrange("p (f two) -> p f two", two=2)[:, :, 0]
        gbv = G[gb][:].rearrange("p (f two) -> p f two", two=2)[:, :, 0]
        em.op("dve", lambda e, qq=qq, gav=gav, ga=ga, pj=pj: e.reduce_max(
            mcs[pj][:, qq:qq + 1], gav, axis=AX.X),
            reads=[G_b[ga]], accwrites=[mcs_b[pj]])
        em.op("dve", lambda e, qq=qq, gbv=gbv, gb=gb, pj=pj: e.reduce_max(
            mcs[pj][:, 4 + qq:5 + qq], gbv, axis=AX.X),
            reads=[G_b[gb]], accwrites=[mcs_b[pj]])
        if qq == 3:
            def fin1(e, pj=pj):
                e.tensor_tensor(mcs[pj][:, 0:1], mcs[pj][:, 0:1],
                                mcs[pj][:, 1:2], ALU.max)
                return e.tensor_tensor(mcs[pj][:, 4:5], mcs[pj][:, 4:5],
                                       mcs[pj][:, 5:6], ALU.max)
            def fin2(e, pj=pj):
                e.tensor_tensor(mcs[pj][:, 2:3], mcs[pj][:, 2:3],
                                mcs[pj][:, 3:4], ALU.max)
                return e.tensor_tensor(mcs[pj][:, 6:7], mcs[pj][:, 6:7],
                                       mcs[pj][:, 7:8], ALU.max)
            def fin3(e, qt=qt, pj=pj):
                e.tensor_tensor(mall[pj][:, qt:qt + 1], mcs[pj][:, 0:1],
                                mcs[pj][:, 2:3], ALU.max)
                return e.tensor_tensor(mall[pj][:, 16 + qt:17 + qt],
                                       mcs[pj][:, 4:5], mcs[pj][:, 6:7], ALU.max)
            em.op("dve", fin1, reads=[mcs_b[pj]], accwrites=[mcs_b[pj]])
            em.op("dve", fin2, reads=[mcs_b[pj]], accwrites=[mcs_b[pj]])
            em.op("dve", fin3, reads=[mcs_b[pj]], accwrites=[mall_b[pj]])

    def emit_mrow(j):
        pj = j % 2
        em.op("dve", lambda e, pj=pj: e.tensor_copy(mbf[:], mall[pj][:]),
              reads=[mall_b[pj]], writes=[mbf_b])
        for s, base in ((0, 0), (1, 64)):
            em.op("pe", lambda e, s=s, pj=pj: e.transpose(
                G[2][0:16, 0:128], mall[pj][:, s * 16:(s + 1) * 16], i128f[:]),
                reads=[mall_b[pj], i128f_b], writes=[G_b[2]])
            em.op("act", lambda e, s=s: e.activation(
                mrow[:, s * 128:(s + 1) * 128], G[2][0:16, 0:128], AF.Copy,
                scale=-1.0),
                reads=[G_b[2]], accwrites=[mrow_b])
            em.dma("pool",
                   mdr[pj, s:s + 1, :].rearrange("o (j f) -> (o j) f", j=16),
                   mrow[:, s * 128:(s + 1) * 128], reads=[mrow_b],
                   accwrites=[mdr_b[pj]])
            em.dma("pool", mneg2[pj][base:base + 1, :], mdr[pj, s:s + 1, :],
                   reads=[mdr_b[pj]], accwrites=[mneg2_b[pj]])

    def emit_work_steps(j):
        pj = j % 2
        qof = j * T
        hA, hB = j, 16 + j
        for qh in range(2):
            for kt in range(16):
                for qs in range(2):
                    c = (kt * 2 + qs) % 2
                    def stmm(e, kt=kt, qh=qh, qs=qs, qof=qof, pj=pj, c=c):
                        qsl = slice(qof + qh * 1024 + qs * 512,
                                    qof + qh * 1024 + qs * 512 + 512)
                        msl = slice(qh * 1024 + qs * 512,
                                    qh * 1024 + qs * 512 + 512)
                        e.matmul(G[c][:], ksb[0:64, kt * 128:(kt + 1) * 128],
                                 qsb[0:64, qsl], start=True, stop=False,
                                 tile_position=(0, 0))
                        e.matmul(G[2 + c][:], onesall[64:65, :],
                                 mneg2[pj][64:65, msl],
                                 start=True, stop=False, tile_position=(64, 0))
                        e.matmul(G[2 + c][:], ksb[64:128, kt * 128:(kt + 1) * 128],
                                 qsb[64:128, qsl], start=False, stop=True,
                                 tile_position=(64, 0))
                        ins = e.matmul(G[c][:], onesall[0:1, :],
                                       mneg2[pj][0:1, msl],
                                       start=False, stop=True,
                                       tile_position=(0, 0))
                        return ins
                    def stw(stmm=stmm, pj=pj, c=c):
                        em.op("pe", stmm,
                              reads=[ksb_b, qsb_b, onesall_b, mneg2_b[pj]],
                              writes=[G_b[c], G_b[2 + c]])
                    yield stw
                    def expw(kt=kt, qs=qs, c=c):
                        off = kt * 1024 + qs * 512
                        em.op("act", lambda e, off=off, c=c: e.activation(
                            pta[:, off:off + 512], G[c][:], AF.Exp),
                            reads=[G_b[c]], accwrites=[pta_b])
                        em.op("act", lambda e, off=off, c=c: e.activation(
                            ptb2[:, off:off + 512], G[2 + c][:], AF.Exp),
                            reads=[G_b[2 + c]], accwrites=[ptb2_b])
                    yield expw
            for t2 in range(2):
                ts = qh * 2 + t2
                for s in range(2):
                    pt_sl, pt_bb = (pta, pta_b) if s == 0 else (ptb2, ptb2_b)
                    h = hA if s == 0 else hB

                    def avw(s=s, t2=t2, pt_sl=pt_sl, pt_bb=pt_bb, h=h, ts=ts):
                        def avmm(e, g=s, t2=t2, pt_sl=pt_sl):
                            for kt in range(16):
                                ins = e.matmul(
                                    G[0][0:65, :],
                                    vtm[:, (g * 16 + kt) * 65:
                                        (g * 16 + kt) * 65 + 65],
                                    pt_sl[:, kt * 1024 + t2 * 512:
                                          kt * 1024 + t2 * 512 + 512],
                                    start=(kt == 0), stop=(kt == 15))
                            return ins
                        em.op("pe", avmm, reads=[vtm_b, pt_bb],
                              writes=[G_b[0]])
                        em.op("dve", lambda e, s=s: e.reciprocal(
                            rcl[:, s * 512:(s + 1) * 512], G[0][64:65, :]),
                            reads=[G_b[0]], accwrites=[rcl_b])
                        em.op("pe", lambda e, s=s: e.matmul(
                            G[1][0:64, :], ones64f[:],
                            rcl[:, s * 512:(s + 1) * 512],
                            start=True, stop=True),
                            reads=[ones64f_b, rcl_b], writes=[G_b[1]])
                        em.op("dve", lambda e, s=s: e.tensor_copy(
                            bcs[:, s * 512:(s + 1) * 512], G[1][0:64, :]),
                            reads=[G_b[1]], accwrites=[bcs_b])
                        ii = s * 2 + t2
                        em.op("dve", lambda e, s=s, ii=ii: e.tensor_tensor(
                            obf[ii][:], G[0][0:64, :],
                            bcs[:, s * 512:(s + 1) * 512], ALU.mult),
                            reads=[G_b[0], bcs_b], writes=[obf_b[ii]])
                        em.dma("sq", oatt_dr[h * 64:(h + 1) * 64,
                                             ts * 512:(ts + 1) * 512],
                               obf[ii][:], reads=[obf_b[ii]])
                    yield avw

    for step in range(64):
        emit_stats_step(0, step)
    emit_mrow(0)
    if P6_DEBUG:
        em.dma("sq", dbg_vtm, vtm[:], reads=[vtm_b])
        em.dma("sq", dbg_mall, mall[0][:], reads=[mall_b[0]])
        em.dma("sq", dbg_mneg[0:1, :], mneg2[0][0:1, :], reads=[mneg2_b[0]])
        em.dma("sq", dbg_mneg[1:2, :], mneg2[0][64:65, :], reads=[mneg2_b[0]])
    for j in range(16):
        nxt = iter(range(64)) if j + 1 < 16 else iter(())
        wl = list(emit_work_steps(j))
        for i, wfn in enumerate(wl):
            wfn()
            if P6_DEBUG and j == 0 and i == 64:
                em.dma("sq", dbg_pta, pta[:, 0:1024], reads=[pta_b])
            if i % 2 == 1:
                s = next(nxt, None)
                if s is not None:
                    emit_stats_step(j + 1, s)
                    if s == 63:
                        emit_mrow(j + 1)
        if j + 1 < 16:
            for s in nxt:
                emit_stats_step(j + 1, s)
                if s == 63:
                    emit_mrow(j + 1)


def build_p6():
    ctx = Ctx("p6")
    nc = ctx.nc
    qf = nc.dram_tensor("qf", [D, T], BF16, kind="ExternalInput").ap()
    kf = nc.dram_tensor("kf", [128, T], BF16, kind="ExternalInput").ap()
    vf = nc.dram_tensor("vf", [128, T], BF16, kind="ExternalInput").ap()
    oatt = nc.dram_tensor("oatt", [D, T], BF16, kind="ExternalOutput").ap()
    emit_p6(ctx, qf, kf, vf, oatt)
    ctx.finish()
    return nc


def build_p7():
    ctx = Ctx("p7")
    nc = ctx.nc
    oatt = nc.dram_tensor("oatt", [16, 128, T], BF16, kind="ExternalInput").ap()
    w_l1o = nc.dram_tensor("w_l1o", [16, 128, 16, 128], F32,
                           kind="ExternalInput").ap()
    x2 = nc.dram_tensor("x2", [D, T], F32, kind="ExternalInput").ap()
    x3 = nc.dram_tensor("x3", [D, T], F32, kind="ExternalOutput").ap()
    x3n = nc.dram_tensor("x3n", [D, T], BF16, kind="ExternalOutput").ap()
    emit_proj_res_ln(ctx, oatt, w_l1o, x2, x3, x3n, res_scale=2.0)
    ctx.finish()
    return nc



def emit_p8(ctx, x3n_dr, wrt_dr, amp_dr, esc_dr, wtil_dr):
    nc, em = ctx.nc, ctx.em
    i128f, i128f_b = ctx.const("i128f", np.eye(128, dtype=np.float32))
    i128b, i128b_b = ctx.const("i128b", np.eye(128, dtype=ml_dtypes.bfloat16))
    xb = ctx.sb("p8_xb", [128, 16 * T], BF16); xb_b = ctx.buf()
    em.dma("sq", xb[:].rearrange("p (c t) -> p c t", c=16),
           x3n_dr.rearrange("(c p) t -> p c t", p=128), writes=[xb_b])
    wrf = ctx.sb("p8_wrf", [128, 2048], F32); wrf_b = ctx.buf()
    wrb = ctx.sb("p8_wrb", [128, 2048], BF16); wrb_b = ctx.buf()
    em.dma("sq", wrf[:].rearrange("p (c f) -> p c f", c=16), wrt_dr[0],
           writes=[wrf_b])
    em.op("dve", lambda e: e.tensor_copy(wrb[:], wrf[:]),
          reads=[wrf_b], writes=[wrb_b])
    lgs = ctx.sb("p8_lgs", [128, T], F32); lgs_b = ctx.buf()
    ltm = ctx.sb("p8_ltm", [128, T], F32); ltm_b = ctx.buf()
    afm = ctx.sb("p8_afm", [128, T], BF16); afm_b = ctx.buf()
    nm = ctx.sb("p8_nm", [128, 1], F32); nm_b = ctx.buf()
    pex = ctx.sb("p8_pex", [128, 128], F32); pex_b = ctx.buf()
    pc = [ctx.sb(f"p8_pc{i}", [128, 128], F32) for i in range(2)]
    pc_b = [ctx.buf() for _ in range(2)]
    zz = ctx.sb("p8_zz", [128, 4], F32); zz_b = ctx.buf()
    alp = ctx.sb("p8_alp", [128, 128], F32); alp_b = ctx.buf()
    atm = ctx.sb("p8_atm", [128, 128], BF16); atm_b = ctx.buf()
    amp_sb = ctx.sb("p8_amp", [128, 16], F32); amp_b = ctx.buf()
    esc_sb = ctx.sb("p8_esc", [128, 1], F32); esc_b = ctx.buf()
    sqa = ctx.sb("p8_sqa", [128, 16], F32); sqa_b = ctx.buf()
    pk8 = ctx.sb("p8_pk8", [128, 8], F32); pk8_b = ctx.buf()
    mixb = ctx.sb("p8_mixb", [128, 8], BF16); mixb_b = ctx.buf()
    wts = ctx.sb("p8_wts", [8, T], BF16); wts_b = ctx.buf()
    pp = ctx.ps("p8_pp", [128, 512]); pp_b = ctx.buf()
    ptt = ctx.ps("p8_ptt", [128, 128]); ptt_b = ctx.buf()
    ptb = ctx.ps("p8_ptb", [128, 128], BF16); ptb_b = ctx.buf()
    pw = ctx.ps("p8_pw", [8, 512]); pw_b = ctx.buf()
    em.dma("pool", amp_sb[:], amp_dr, writes=[amp_b])
    em.dma("pool", esc_sb[:], esc_dr.rearrange("(p o) -> p o", o=1), writes=[esc_b])
    for ts in range(4):
        def rmm(e, ts=ts):
            for fc in range(16):
                ins = e.matmul(pp[:], wrb[:, fc * 128:(fc + 1) * 128],
                               xb[:, fc * T + ts * 512: fc * T + ts * 512 + 512],
                               start=(fc == 0), stop=(fc == 15))
            return ins
        em.op("pe", rmm, reads=[wrb_b, xb_b], writes=[pp_b])
        em.op("dve", lambda e, ts=ts: e.tensor_copy(
            lgs[:, ts * 512:(ts + 1) * 512], pp[:]),
            reads=[pp_b], accwrites=[lgs_b])
    for tt in range(16):
        em.op("pe", lambda e, tt=tt: e.transpose(
            ptt[:], lgs[:, tt * 128:(tt + 1) * 128], i128f[:]),
            reads=[lgs_b, i128f_b], writes=[ptt_b])
        em.op("dve", lambda e, tt=tt: e.tensor_copy(
            ltm[:, tt * 128:(tt + 1) * 128], ptt[:]),
            reads=[ptt_b], accwrites=[ltm_b])
        L = ltm[:, tt * 128:(tt + 1) * 128]
        em.op("dve", lambda e, L=L: e.reduce_max(nm[:], L, axis=AX.X, negate=True),
              reads=[ltm_b], writes=[nm_b])
        em.op("act", lambda e, L=L: e.activation(pex[:], L, AF.Exp, bias=nm[:]),
              reads=[ltm_b, nm_b], writes=[pex_b])
        em.op("dve", lambda e: e.reduce_sum(zz[:, 0:1], pex[:], axis=AX.X),
              reads=[pex_b], writes=[zz_b])
        em.op("dve", lambda e: e.reduce_max(zz[:, 1:2], pex[:], axis=AX.X),
              reads=[pex_b], accwrites=[zz_b])
        src, src_b = pex, pex_b
        for j in range(3):
            dst, dst_b = pc[j % 2], pc_b[j % 2]
            em.op("dve", lambda e, src=src, dst=dst: e.scalar_tensor_tensor(
                dst[:], src[:], zz[:, 1:2], src[:], ALU.is_lt, ALU.mult),
                reads=[src_b, zz_b], writes=[dst_b])
            em.op("dve", lambda e, dst=dst: e.reduce_max(zz[:, 1:2], dst[:],
                                                         axis=AX.X),
                  reads=[dst_b], accwrites=[zz_b])
            src, src_b = dst, dst_b
        em.op("dve", lambda e: e.scalar_tensor_tensor(
            alp[:], pex[:], zz[:, 1:2], pex[:], ALU.is_ge, ALU.mult),
            reads=[pex_b, zz_b], writes=[alp_b])
        em.op("dve", lambda e: e.reduce_sum(zz[:, 2:3], alp[:], axis=AX.X),
              reads=[alp_b], accwrites=[zz_b])
        em.op("dve", lambda e: e.scalar_tensor_tensor(
            zz[:, 3:4], zz[:, 0:1], 1e-8, zz[:, 2:3], ALU.mult, ALU.add),
            reads=[zz_b], accwrites=[zz_b])
        em.op("dve", lambda e: e.reciprocal(zz[:, 3:4], zz[:, 3:4]),
              reads=[zz_b], accwrites=[zz_b])
        em.op("dve", lambda e: e.tensor_scalar(atm[:], alp[:], zz[:, 3:4], None,
                                               ALU.mult),
              reads=[alp_b, zz_b], writes=[atm_b])
        em.op("pe", lambda e: e.transpose(ptb[:], atm[:], i128b[:]),
              reads=[atm_b, i128b_b], writes=[ptb_b])
        em.op("dve", lambda e, tt=tt: e.tensor_copy(
            afm[:, tt * 128:(tt + 1) * 128], ptb[:]),
            reads=[ptb_b], accwrites=[afm_b])
    em.op("act", lambda e: e.activation(sqa[:], amp_sb[:], AF.Square),
          reads=[amp_b], writes=[sqa_b])
    sqv = sqa[:].rearrange("p (k j) -> p k j", j=2)
    em.op("dve", lambda e: e.tensor_tensor(pk8[:], sqv[:, :, 0], sqv[:, :, 1],
                                           ALU.add),
          reads=[sqa_b], writes=[pk8_b])
    em.op("dve", lambda e: e.reduce_sum(zz[:, 0:1], pk8[:], axis=AX.X),
          reads=[pk8_b], writes=[zz_b])
    em.op("dve", lambda e: e.tensor_scalar(zz[:, 0:1], zz[:, 0:1], 1e-8, None,
                                           ALU.add),
          reads=[zz_b], accwrites=[zz_b])
    em.op("dve", lambda e: e.reciprocal(zz[:, 0:1], zz[:, 0:1]),
          reads=[zz_b], accwrites=[zz_b])
    em.op("dve", lambda e: e.tensor_scalar(mixb[:], pk8[:], zz[:, 0:1],
                                           esc_sb[:], ALU.mult, ALU.mult),
          reads=[pk8_b, zz_b, esc_b], writes=[mixb_b])
    for ts in range(4):
        em.op("pe", lambda e, ts=ts: e.matmul(
            pw[:], mixb[:], afm[:, ts * 512:(ts + 1) * 512],
            start=True, stop=True),
            reads=[mixb_b, afm_b], writes=[pw_b])
        em.op("act", lambda e, ts=ts: e.activation(
            wts[:, ts * 512:(ts + 1) * 512], pw[:], AF.Copy),
            reads=[pw_b], accwrites=[wts_b])
    em.dma("sq", wtil_dr, wts[:], reads=[wts_b])


def build_p8():
    ctx = Ctx("p8")
    nc = ctx.nc
    x3n = nc.dram_tensor("x3n", [D, T], BF16, kind="ExternalInput").ap()
    wrt = nc.dram_tensor("w_rt", [1, 128, 16, 128], F32, kind="ExternalInput").ap()
    amp = nc.dram_tensor("amp", [128, 16], F32, kind="ExternalInput").ap()
    esc = nc.dram_tensor("esc", [128], F32, kind="ExternalInput").ap()
    wtil = nc.dram_tensor("wtil", [8, T], BF16, kind="ExternalOutput").ap()
    emit_p8(ctx, x3n, wrt, amp, esc, wtil)
    ctx.finish()
    return nc



def emit_p9(ctx, x3n_dr, wtil_dr, wg_dr, wu_dr, ysc_dr):
    nc, em = ctx.nc, ctx.em
    sel_np = np.zeros((8, 1024), dtype=ml_dtypes.bfloat16)
    for k in range(8):
        sel_np[k, k * 128:(k + 1) * 128] = 1
    sel, sel_b = ctx.const("sel8", sel_np)
    xb = ctx.sb("p9_xb", [128, 16 * T], BF16); xb_b = ctx.buf()
    em.dma("sq", xb[:].rearrange("p (c t) -> p c t", c=16),
           x3n_dr.rearrange("(c p) t -> p c t", p=128), writes=[xb_b])
    wtl = ctx.sb("p9_wtl", [8, T], BF16); wtl_b = ctx.buf()
    em.dma("pool", wtl[:], wtil_dr, writes=[wtl_b])
    wb_sl = ctx.sb("p9_wb", [128, T], BF16); wb_b = ctx.buf("p9_wb")
    ya = ctx.sb("p9_ya", [128, 8 * T], F32); ya_b = ctx.buf("p9_ya")
    for ic in range(8):
        em.op("pool", lambda e, ic=ic: e.memset(ya[:, ic * T:(ic + 1) * T], 0.0),
              accwrites=[ya_b])
    wgf = [ctx.sb(f"p9_wgf{i}", [128, 2048], F32) for i in range(2)]
    wgf_b = [ctx.buf() for _ in range(2)]
    wuf = [ctx.sb(f"p9_wuf{i}", [128, 2048], F32) for i in range(2)]
    wuf_b = [ctx.buf() for _ in range(2)]
    wgb = [ctx.sb(f"p9_wgb{i}", [128, 2048], BF16) for i in range(2)]
    wgb_b = [ctx.buf() for _ in range(2)]
    wub = [ctx.sb(f"p9_wub{i}", [128, 2048], BF16) for i in range(2)]
    wub_b = [ctx.buf() for _ in range(2)]
    gs = ctx.sb("p9_gs", [128, 512], F32); gs_b = ctx.buf()
    bas = ctx.sb("p9_bas", [128, 512], F32); bas_b = ctx.buf()
    prod = ctx.sb("p9_prod", [128, 512], F32); prod_b = ctx.buf()
    ysb = ctx.sb("p9_ysb", [128, T], BF16); ysb_b = ctx.buf()
    pg = [ctx.ps(f"p9_pg{i}", [128, 512]) for i in range(2)]
    pg_b = [ctx.buf() for _ in range(2)]
    pu = [ctx.ps(f"p9_pu{i}", [128, 512]) for i in range(2)]
    pu_b = [ctx.buf() for _ in range(2)]
    pbc = ctx.ps("p9_pbc", [128, 512]); pbc_b = ctx.buf()
    for k in range(8):
        for ts in range(4):
            em.op("pe", lambda e, k=k, ts=ts: e.matmul(
                pbc[:], sel[:, k * 128:(k + 1) * 128],
                wtl[:, ts * 512:(ts + 1) * 512], start=True, stop=True),
                reads=[sel_b, wtl_b], writes=[pbc_b])
            em.op("act", lambda e, ts=ts: e.activation(
                wb_sl[:, ts * 512:(ts + 1) * 512], pbc[:], AF.Copy),
                reads=[pbc_b], accwrites=[wb_b])
        for ic in range(8):
            ot = k * 8 + ic
            i = ot % 2
            em.dma("sq", wgf[i][:].rearrange("p (c f) -> p c f", c=16),
                   wg_dr[ot], writes=[wgf_b[i]])
            em.dma("sq", wuf[i][:].rearrange("p (c f) -> p c f", c=16),
                   wu_dr[ot], writes=[wuf_b[i]])
            em.op("act", lambda e, i=i: e.activation(wgb[i][:], wgf[i][:],
                                                     AF.Copy),
                  reads=[wgf_b[i]], writes=[wgb_b[i]])
            em.op("act", lambda e, i=i: e.activation(wub[i][:], wuf[i][:],
                                                     AF.Copy),
                  reads=[wuf_b[i]], writes=[wub_b[i]])
            for ts in range(4):
                j = ts % 2

                def mmg(e, i=i, ts=ts, j=j):
                    for fc in range(16):
                        e.matmul(pg[j][:], wgb[i][:, fc * 128:(fc + 1) * 128],
                                 xb[:, fc * T + ts * 512: fc * T + ts * 512 + 512],
                                 start=(fc == 0), stop=(fc == 15))
                    for fc in range(16):
                        ins = e.matmul(pu[j][:], wub[i][:, fc * 128:(fc + 1) * 128],
                                       xb[:, fc * T + ts * 512: fc * T + ts * 512 + 512],
                                       start=(fc == 0), stop=(fc == 15))
                    return ins
                em.op("pe", mmg, reads=[wgb_b[i], wub_b[i], xb_b],
                      writes=[pg_b[j], pu_b[j]])
                em.op("act", lambda e, j=j: e.activation(gs[:], pg[j][:], AF.Silu),
                      reads=[pg_b[j]], writes=[gs_b])
                em.op("dve", lambda e, j=j: e.tensor_tensor(
                    bas[:], pu[j][:], gs[:], ALU.mult),
                    reads=[pu_b[j], gs_b], writes=[bas_b])
                em.op("pool", lambda e, ts=ts: e.tensor_tensor(
                    prod[:], bas[:], wb_sl[:, ts * 512:(ts + 1) * 512], ALU.mult),
                    reads=[bas_b, wb_b], writes=[prod_b])
                em.op("dve", lambda e, ic=ic, ts=ts: e.tensor_tensor(
                    ya[:, ic * T + ts * 512: ic * T + ts * 512 + 512],
                    ya[:, ic * T + ts * 512: ic * T + ts * 512 + 512],
                    prod[:], ALU.add),
                    reads=[prod_b, ya_b], accwrites=[ya_b])
    for ic in range(8):
        em.op("dve", lambda e, ic=ic: e.tensor_copy(
            ysb[:], ya[:, ic * T:(ic + 1) * T]),
            reads=[ya_b], writes=[ysb_b])
        em.dma("sq", ysc_dr[ic], ysb[:], reads=[ysb_b])


def build_p9():
    ctx = Ctx("p9")
    nc = ctx.nc
    x3n = nc.dram_tensor("x3n", [D, T], BF16, kind="ExternalInput").ap()
    wtil = nc.dram_tensor("wtil", [8, T], BF16, kind="ExternalInput").ap()
    wg = nc.dram_tensor("w_g", [64, 128, 16, 128], F32, kind="ExternalInput").ap()
    wu = nc.dram_tensor("w_u", [64, 128, 16, 128], F32, kind="ExternalInput").ap()
    ysc = nc.dram_tensor("ysc", [8, 128, T], BF16, kind="ExternalOutput").ap()
    emit_p9(ctx, x3n, wtil, wg, wu, ysc)
    ctx.finish()
    return nc



def emit_p10(ctx, x3n_dr, wsg_dr, wsu_dr, shsc_dr):
    nc, em = ctx.nc, ctx.em
    xb = ctx.sb("pA_xb", [128, 16 * T], BF16); xb_b = ctx.buf()
    em.dma("sq", xb[:].rearrange("p (c t) -> p c t", c=16),
           x3n_dr.rearrange("(c p) t -> p c t", p=128), writes=[xb_b])
    wgf = [ctx.sb(f"pA_wgf{i}", [128, 2048], F32) for i in range(2)]
    wgf_b = [ctx.buf() for _ in range(2)]
    wuf = [ctx.sb(f"pA_wuf{i}", [128, 2048], F32) for i in range(2)]
    wuf_b = [ctx.buf() for _ in range(2)]
    wgb = [ctx.sb(f"pA_wgb{i}", [128, 2048], BF16) for i in range(2)]
    wgb_b = [ctx.buf() for _ in range(2)]
    wub = [ctx.sb(f"pA_wub{i}", [128, 2048], BF16) for i in range(2)]
    wub_b = [ctx.buf() for _ in range(2)]
    gs = ctx.sb("pA_gs", [128, 512], F32); gs_b = ctx.buf()
    shb = [ctx.sb(f"pA_shb{i}", [128, T], BF16) for i in range(2)]
    shb_b = [ctx.buf() for _ in range(2)]
    pg = [ctx.ps(f"pA_pg{i}", [128, 512]) for i in range(2)]
    pg_b = [ctx.buf() for _ in range(2)]
    pu = [ctx.ps(f"pA_pu{i}", [128, 512]) for i in range(2)]
    pu_b = [ctx.buf() for _ in range(2)]
    for ot in range(16):
        i = ot % 2
        em.dma("sq", wgf[i][:].rearrange("p (c f) -> p c f", c=16), wsg_dr[ot],
               writes=[wgf_b[i]])
        em.dma("sq", wuf[i][:].rearrange("p (c f) -> p c f", c=16), wsu_dr[ot],
               writes=[wuf_b[i]])
        em.op("dve", lambda e, i=i: e.tensor_copy(wgb[i][:], wgf[i][:]),
              reads=[wgf_b[i]], writes=[wgb_b[i]])
        em.op("dve", lambda e, i=i: e.tensor_copy(wub[i][:], wuf[i][:]),
              reads=[wuf_b[i]], writes=[wub_b[i]])
        for ts in range(4):
            j = ts % 2

            def mmg(e, i=i, ts=ts, j=j):
                for fc in range(16):
                    e.matmul(pg[j][:], wgb[i][:, fc * 128:(fc + 1) * 128],
                             xb[:, fc * T + ts * 512: fc * T + ts * 512 + 512],
                             start=(fc == 0), stop=(fc == 15))
                for fc in range(16):
                    ins = e.matmul(pu[j][:], wub[i][:, fc * 128:(fc + 1) * 128],
                                   xb[:, fc * T + ts * 512: fc * T + ts * 512 + 512],
                                   start=(fc == 0), stop=(fc == 15))
                return ins
            em.op("pe", mmg, reads=[wgb_b[i], wub_b[i], xb_b],
                  writes=[pg_b[j], pu_b[j]])
            em.op("act", lambda e, j=j: e.activation(gs[:], pg[j][:], AF.Silu),
                  reads=[pg_b[j]], writes=[gs_b])
            em.op("dve", lambda e, i=i, ts=ts, j=j: e.tensor_tensor(
                shb[i][:, ts * 512:(ts + 1) * 512], pu[j][:], gs[:], ALU.mult),
                reads=[pu_b[j], gs_b], accwrites=[shb_b[i]])
        em.dma("sq", shsc_dr[ot], shb[i][:], reads=[shb_b[i]])


def build_p10():
    ctx = Ctx("p10")
    nc = ctx.nc
    x3n = nc.dram_tensor("x3n", [D, T], BF16, kind="ExternalInput").ap()
    wsg = nc.dram_tensor("w_sg", [16, 128, 16, 128], F32, kind="ExternalInput").ap()
    wsu = nc.dram_tensor("w_su", [16, 128, 16, 128], F32, kind="ExternalInput").ap()
    shsc = nc.dram_tensor("shsc", [16, 128, T], BF16, kind="ExternalOutput").ap()
    emit_p10(ctx, x3n, wsg, wsu, shsc)
    ctx.finish()
    return nc



def emit_p11(ctx, ysc_dr, shsc_dr, wd_dr, wsd_dr, x3_dr, yT_dr):
    nc, em = ctx.nc, ctx.em
    ln = LNRes(ctx)
    wdb = ctx.sb("pB_wdb", [128, 16 * 1024], BF16); wdb_b = ctx.buf()
    wsdb = ctx.sb("pB_wsdb", [128, 16 * 2048], BF16); wsdb_b = ctx.buf()
    wf = [ctx.sb(f"pB_wf{i}", [128, 2048], F32) for i in range(2)]
    wf_b = [ctx.buf() for _ in range(2)]
    for ot in range(16):
        i = ot % 2
        em.dma("sq", wf[i][:, 0:1024].rearrange("p (c f) -> p c f", c=8),
               wd_dr[ot], writes=[wf_b[i]])
        em.op("dve", lambda e, ot=ot, i=i: e.tensor_copy(
            wdb[:, ot * 1024:(ot + 1) * 1024], wf[i][:, 0:1024]),
            reads=[wf_b[i]], accwrites=[wdb_b])
        em.dma("sq", wf[i][:].rearrange("p (c f) -> p c f", c=16),
               wsd_dr[ot], writes=[wf_b[i]])
        em.op("dve", lambda e, ot=ot, i=i: e.tensor_copy(
            wsdb[:, ot * 2048:(ot + 1) * 2048], wf[i][:]),
            reads=[wf_b[i]], accwrites=[wsdb_b])
    ysl = ctx.sb("pB_ysl", [128, 8 * 512], BF16); ysl_b = ctx.buf()
    shsl = ctx.sb("pB_shsl", [128, 16 * 512], BF16); shsl_b = ctx.buf()
    xc = [ctx.sb(f"pB_xc{i}", [128, 512], F32) for i in range(2)]
    xc_b = [ctx.buf() for _ in range(2)]
    rsl = ctx.sb("pB_rsl", [128, 16 * 512], F32); rsl_b = ctx.buf()
    pp = [ctx.ps(f"pB_ps{i}", [128, 512]) for i in range(2)]
    pp_b = [ctx.buf() for _ in range(2)]
    for ts in range(4):
        em.dma("sq", ysl[:].rearrange("p (c f) -> p c f", c=8),
               ysc_dr[:, :, ts * 512:(ts + 1) * 512].rearrange("i p f -> p i f"),
               writes=[ysl_b])
        em.dma("sq", shsl[:].rearrange("p (c f) -> p c f", c=16),
               shsc_dr[:, :, ts * 512:(ts + 1) * 512].rearrange("i p f -> p i f"),
               writes=[shsl_b])
        for d2t in range(16):
            j = d2t % 2
            em.dma("pool", xc[j][:],
                   x3_dr[d2t * 128:(d2t + 1) * 128, ts * 512:(ts + 1) * 512],
                   writes=[xc_b[j]])

            def mmg(e, d2t=d2t, j=j):
                for kc in range(8):
                    e.matmul(pp[j][:],
                             wdb[:, d2t * 1024 + kc * 128: d2t * 1024 + kc * 128 + 128],
                             ysl[:, kc * 512:(kc + 1) * 512],
                             start=(kc == 0), stop=False)
                for fc in range(16):
                    ins = e.matmul(
                        pp[j][:],
                        wsdb[:, d2t * 2048 + fc * 128: d2t * 2048 + fc * 128 + 128],
                        shsl[:, fc * 512:(fc + 1) * 512],
                        start=False, stop=(fc == 15))
                return ins
            em.op("pe", mmg, reads=[wdb_b, wsdb_b, ysl_b, shsl_b],
                  writes=[pp_b[j]])
            em.op("dve", lambda e, d2t=d2t, j=j: e.scalar_tensor_tensor(
                rsl[:, d2t * 512:(d2t + 1) * 512], xc[j][:], 1.0, pp[j][:],
                ALU.mult, ALU.add),
                reads=[xc_b[j], pp_b[j]], accwrites=[rsl_b])
        emit_ln(ctx, ln, rsl, rsl_b,
                lambda fc, ts=ts: yT_dr[fc * 128:(fc + 1) * 128,
                                        ts * 512:(ts + 1) * 512],
                None)


def build_p11():
    ctx = Ctx("p11")
    nc = ctx.nc
    ysc = nc.dram_tensor("ysc", [8, 128, T], BF16, kind="ExternalInput").ap()
    shsc = nc.dram_tensor("shsc", [16, 128, T], BF16, kind="ExternalInput").ap()
    wd = nc.dram_tensor("w_d", [16, 128, 8, 128], F32, kind="ExternalInput").ap()
    wsd = nc.dram_tensor("w_sd", [16, 128, 16, 128], F32,
                         kind="ExternalInput").ap()
    x3 = nc.dram_tensor("x3", [D, T], F32, kind="ExternalInput").ap()
    yT = nc.dram_tensor("yT", [D, T], F32, kind="ExternalOutput").ap()
    emit_p11(ctx, ysc, shsc, wd, wsd, x3, yT)
    ctx.finish()
    return nc




def build_p0():
    ctx = Ctx("p0")
    nc = ctx.nc
    xT = nc.dram_tensor("xT", [D, T], F32, kind="ExternalInput").ap()
    w_l0p = nc.dram_tensor("w_l0p", [NWT, 128, 16, 128], F32,
                           kind="ExternalInput").ap()
    proj = nc.dram_tensor("proj", [4, NWT, 128, 512], BF16,
                          kind="ExternalOutput").ap()
    emit_p0(ctx, xT, w_l0p, proj)
    ctx.finish()
    return nc



def prep_weights(ii):
    f32 = lambda a: np.asarray(a, np.float32)
    colT = lambda a, n: np.ascontiguousarray(f32(a).reshape(n, 128).T)
    w = {}
    w["w_l0p"] = host_pack_l0_proj(ii["L0_attn_Wq"], ii["L0_attn_Wk"],
                                   ii["L0_attn_Ws"], ii["L0_attn_Wv"],
                                   ii["L0_attn_Wg"])
    w["gnw"] = f32(ii["L0_attn_g_norm_weight"])
    w["w_l0o"] = pack_lhsT(f32(ii["L0_attn_Wo"]))
    w["w_l0g"] = pack_lhsT(u8_of_i32(np.asarray(ii["L0_ffn_gate_w"])).T)
    w["w_l0u"] = pack_lhsT(u8_of_i32(np.asarray(ii["L0_ffn_up_w"])).T)
    w["w_l0d"] = pack_lhsT(u8_of_i32(np.asarray(ii["L0_ffn_down_w"])).T)
    w["gsc"] = colT(ii["L0_ffn_gate_scale"], 64)
    w["usc"] = colT(ii["L0_ffn_up_scale"], 64)
    w["dsc"] = colT(ii["L0_ffn_down_scale"], 16)
    w["ns"] = colT(ii["L1_attn_norm_scale"], 16)
    w["ss"] = colT(ii["L1_attn_split_scale"], 16)
    w["w_l1q"] = pack_lhsT(f32(ii["L1_attn_Wq"]).T)
    w["w_l1k"] = pack_lhsT(f32(ii["L1_attn_Wk"]).T)
    w["w_l1v"] = pack_lhsT(f32(ii["L1_attn_Wv"]).T)
    w["w_l1o"] = pack_lhsT(f32(ii["L1_attn_Wo"]).T)
    w["w_rt"] = pack_lhsT(f32(ii["L1_ffn_w_router"]))
    w["amp"] = np.ascontiguousarray(f32(ii["L1_ffn_amplitudes"]).reshape(128, 16))
    w["esc"] = f32(ii["L1_ffn_expert_scale"])
    w["w_g"] = pack_lhsT(f32(ii["L1_ffn_w_gate"]))
    w["w_u"] = pack_lhsT(f32(ii["L1_ffn_w_up"]))
    w["w_d"] = pack_lhsT(f32(ii["L1_ffn_w_down"]))
    w["w_sg"] = pack_lhsT(f32(ii["L1_ffn_shared_w_gate"]))
    w["w_su"] = pack_lhsT(f32(ii["L1_ffn_shared_w_up"]))
    w["w_sd"] = pack_lhsT(f32(ii["L1_ffn_shared_w_down"]))
    return w


_BUILD_CACHE = {}


def _get(name, fn):
    if name not in _BUILD_CACHE:
        _BUILD_CACHE[name] = fn()
    return _BUILD_CACHE[name]



def build_full():
    ctx = Ctx("full")
    nc = ctx.nc
    EI = lambda n, s, d: nc.dram_tensor(n, s, d, kind="ExternalInput").ap()
    IN = lambda n, s, d: nc.dram_tensor(n, s, d, kind="Internal").ap()
    xT = EI("xT", [D, T], F32)
    w_l0p = EI("w_l0p", [NWT, 128, 16, 128], F32)
    gnw = EI("gnw", [128], F32)
    w_l0o = EI("w_l0o", [16, 128, 16, 128], F32)
    w_l0g = EI("w_l0g", [64, 128, 16, 128], U8)
    w_l0u = EI("w_l0u", [64, 128, 16, 128], U8)
    w_l0d = EI("w_l0d", [16, 128, 64, 128], U8)
    gsc = EI("gsc", [128, 64], F32)
    usc = EI("usc", [128, 64], F32)
    dsc = EI("dsc", [128, 16], F32)
    ns = EI("ns", [128, 16], F32)
    ss = EI("ss", [128, 16], F32)
    w_l1q = EI("w_l1q", [16, 128, 16, 128], F32)
    w_l1k = EI("w_l1k", [1, 128, 16, 128], F32)
    w_l1v = EI("w_l1v", [1, 128, 16, 128], F32)
    w_l1o = EI("w_l1o", [16, 128, 16, 128], F32)
    w_rt = EI("w_rt", [1, 128, 16, 128], F32)
    amp = EI("amp", [128, 16], F32)
    esc = EI("esc", [128], F32)
    w_g = EI("w_g", [64, 128, 16, 128], F32)
    w_u = EI("w_u", [64, 128, 16, 128], F32)
    w_d = EI("w_d", [16, 128, 8, 128], F32)
    w_sg = EI("w_sg", [16, 128, 16, 128], F32)
    w_su = EI("w_su", [16, 128, 16, 128], F32)
    w_sd = EI("w_sd", [16, 128, 16, 128], F32)
    yT = nc.dram_tensor("yT", [D, T], F32, kind="ExternalOutput").ap()

    proj = IN("s_proj", [4, NWT, 128, 512], BF16)
    og = IN("s_og", [16, 128, T], BF16)
    x1 = IN("s_x1", [D, T], F32)
    x1n = IN("s_x1n", [D, T], BF16)
    hsc = IN("s_hsc", [64, 128, T], BF16)
    x2 = IN("s_x2", [D, T], F32)
    qf = IN("s_qf", [D, T], BF16)
    kf = IN("s_kf", [128, T], BF16)
    vf = IN("s_vf", [128, T], BF16)
    oatt = IN("s_oatt", [16, 128, T], BF16)
    x3 = IN("s_x3", [D, T], F32)
    x3n = IN("s_x3n", [D, T], BF16)
    wtil = IN("s_wtil", [8, T], BF16)
    ysc = IN("s_ysc", [8, 128, T], BF16)
    shsc = IN("s_shsc", [16, 128, T], BF16)

    phases = [
        ("P0_", lambda: emit_p0(ctx, xT, w_l0p, proj)),
        ("P1_", lambda: emit_p1(ctx, proj, gnw, og)),
        ("P2_", lambda: emit_proj_res_ln(ctx, og, w_l0o, xT, x1, x1n, 1.0)),
        ("P3_", lambda: emit_p3(ctx, x1n, w_l0g, w_l0u, gsc, usc, hsc)),
        ("P4_", lambda: emit_p4(ctx, hsc, w_l0d, dsc, x1, x2, None)),
        ("P5_", lambda: emit_p5(ctx, x2, ns, ss, w_l1q, w_l1k, w_l1v,
                                qf, kf, vf)),
        ("P6_", lambda: emit_p6(ctx, qf, kf, vf,
                                oatt.rearrange("h p t -> (h p) t"))),
        ("P7_", lambda: emit_proj_res_ln(ctx, oatt, w_l1o, x2, x3, x3n, 2.0)),
        ("P8_", lambda: emit_p8(ctx, x3n, w_rt, amp, esc, wtil)),
        ("P9_", lambda: emit_p9(ctx, x3n, wtil, w_g, w_u, ysc)),
        ("PA_", lambda: emit_p10(ctx, x3n, w_sg, w_su, shsc)),
        ("PB_", lambda: emit_p11(ctx, ysc, shsc, w_d, w_sd, x3, yT)),
    ]
    for pref, fn in phases:
        ctx.push_scope(pref)
        fn()
        ctx.pop_scope()
    ctx.finish()
    return nc


def kernel(**inputs):
    import os
    if int(os.environ.get("MK_UNFUSED", "0")):
        return kernel_multilaunch(**inputs)
    w = prep_weights(inputs)
    x = np.asarray(inputs["L0_attn_hidden_states"], np.float32)
    B = x.shape[0]
    cores = list(range(B))
    nc = _get("full", build_full)
    maps = [dict(w, xT=np.ascontiguousarray(x[b].T)) for b in cores]
    res = run_bass_kernel_spmd(nc, maps, core_ids=cores)
    return np.stack([np.ascontiguousarray(res.results[b]["yT"].T)
                     for b in cores])


def kernel_multilaunch(**inputs):
    w = prep_weights(inputs)
    x = np.asarray(inputs["L0_attn_hidden_states"], np.float32)
    B = x.shape[0]
    cores = list(range(B))
    xT = [np.ascontiguousarray(x[b].T) for b in range(B)]

    def launch(name, build_fn, per_core):
        nc = _get(name, build_fn)
        res = run_bass_kernel_spmd(nc, per_core, core_ids=cores)
        return res.results

    r = launch("p0", build_p0, [dict(xT=xT[b], w_l0p=w["w_l0p"]) for b in cores])
    proj = [r[b]["proj"] for b in cores]
    r = launch("p1", build_p1, [dict(proj=proj[b], gnw=w["gnw"]) for b in cores])
    og = [r[b]["og"] for b in cores]
    r = launch("p2", build_p2, [dict(og=og[b], w_l0o=w["w_l0o"], xT=xT[b])
                                for b in cores])
    x1 = [r[b]["x1"] for b in cores]
    x1n = [r[b]["x1n"] for b in cores]
    r = launch("p3", build_p3, [dict(x1n=x1n[b], w_l0g=w["w_l0g"],
                                     w_l0u=w["w_l0u"], gsc=w["gsc"],
                                     usc=w["usc"]) for b in cores])
    hsc = [r[b]["hsc"] for b in cores]
    r = launch("p4", build_p4, [dict(hsc=hsc[b], w_l0d=w["w_l0d"],
                                     dsc=w["dsc"], x1=x1[b]) for b in cores])
    x2 = [r[b]["x2"] for b in cores]
    r = launch("p5", build_p5, [dict(x2=x2[b], ns=w["ns"], ss=w["ss"],
                                     w_l1q=w["w_l1q"], w_l1k=w["w_l1k"],
                                     w_l1v=w["w_l1v"]) for b in cores])
    qf = [r[b]["qf"] for b in cores]
    kf = [r[b]["kf"] for b in cores]
    vf = [r[b]["vf"] for b in cores]
    r = launch("p6", build_p6, [dict(qf=qf[b], kf=kf[b], vf=vf[b])
                                for b in cores])
    oatt = [r[b]["oatt"] for b in cores]
    r = launch("p7", build_p7, [dict(oatt=oatt[b].reshape(16, 128, T),
                                     w_l1o=w["w_l1o"], x2=x2[b])
                                for b in cores])
    x3 = [r[b]["x3"] for b in cores]
    x3n = [r[b]["x3n"] for b in cores]
    r = launch("p8", build_p8, [dict(x3n=x3n[b], w_rt=w["w_rt"], amp=w["amp"],
                                     esc=w["esc"]) for b in cores])
    wtil = [r[b]["wtil"] for b in cores]
    r = launch("p9", build_p9, [dict(x3n=x3n[b], wtil=wtil[b], w_g=w["w_g"],
                                     w_u=w["w_u"]) for b in cores])
    ysc = [r[b]["ysc"] for b in cores]
    r = launch("p10", build_p10, [dict(x3n=x3n[b], w_sg=w["w_sg"],
                                       w_su=w["w_su"]) for b in cores])
    shsc = [r[b]["shsc"] for b in cores]
    r = launch("p11", build_p11, [dict(ysc=ysc[b], shsc=shsc[b], w_d=w["w_d"],
                                       w_sd=w["w_sd"], x3=x3[b])
                                  for b in cores])
    out = np.stack([np.ascontiguousarray(r[b]["yT"].T) for b in cores])
    return out


# version 10
# speedup vs baseline: 1.2988x; 1.0544x over previous
import contextlib
import numpy as np
import ml_dtypes
import concourse.bass as bass
import concourse.mybir as mybir
from concourse.bass_utils import run_bass_kernel_spmd

F32 = mybir.dt.float32
BF16 = mybir.dt.bfloat16
U8 = mybir.dt.uint8
AF = mybir.ActivationFunctionType
ALU = mybir.AluOpType
AX = mybir.AxisListType

NCORES = 8
T = 2048
D = 2048


ENGINES = ("pe", "dve", "act", "pool", "sq")
NLANES = 4
_ENG_ATTR = {"pe": "tensor", "dve": "vector", "act": "scalar",
             "pool": "gpsimd", "sq": "sync"}


class Buf:
    __slots__ = ("name", "writers", "readers")

    def __init__(self, name=""):
        self.name = name
        self.writers = []
        self.readers = []


class _Op:
    __slots__ = ("fn", "deps", "inc", "observed")

    def __init__(self, fn, deps, inc):
        self.fn = fn
        self.deps = deps
        self.inc = inc
        self.observed = False


class Em:

    def __init__(self, nc):
        self.nc = nc
        self.ops = {e: [] for e in ENGINES}
        self._barrier = []
        self.phase_marks = []
        self.inst_marks = None

    def mark(self, label):
        self.phase_marks.append((label, {e: len(self.ops[e]) for e in ENGINES}))

    def barrier(self):
        bar = [(e, len(self.ops[e]) - 1) for e in ENGINES if self.ops[e]]
        for q in ("sq", "pool"):
            found = 0
            for i in range(len(self.ops[q]) - 1, -1, -1):
                if self.ops[q][i].inc == 16:
                    bar.append((q, i))
                    found += 1
                    if found >= NLANES:
                        break
        self._barrier = bar

    def op(self, eng, fn, reads=(), writes=(), accwrites=(), deps=(), dma=False):
        dep = set(deps)
        dep.update(self._barrier)
        for b in reads:
            dep.update(b.writers)
        for b in writes:
            dep.update(b.writers)
            dep.update(b.readers)
        for b in accwrites:
            dep.update(b.readers)
        tok = (eng, len(self.ops[eng]))
        dep.discard(tok)
        self.ops[eng].append(_Op(fn, sorted(dep), 16 if dma else 1))
        for b in reads:
            b.readers.append(tok)
        for b in writes:
            b.writers = [tok]
            b.readers = []
        for b in accwrites:
            b.writers.append(tok)
        return tok

    def dma(self, eng, out, in_, **bufs):
        eng = "pool" if type(out.tensor).__name__ == "DRamTensorHandle" else "sq"
        return self.op(eng, lambda e, o=out, i=in_: e.dma_start(out=o, in_=i),
                       dma=True, **bufs)

    def lower(self, es):
        sems = {e: es.enter_context(self.nc.semaphore(f"sem_{e}"))
                for e in ENGINES}
        lsem = {q: [es.enter_context(self.nc.semaphore(f"sem_{q}L{l}"))
                    for l in range(NLANES)] for q in ("sq", "pool")}
        for e in ENGINES:
            for i, op in enumerate(self.ops[e]):
                for (fe, fi) in op.deps:
                    if fe == "pe" and e == "pe":
                        continue
                    self.ops[fe][fi].observed = True
        for e in ("sq", "pool"):
            for op in self.ops[e]:
                if op.inc == 16:
                    op.observed = True
        cum = {}
        lane_final = {q: [0] * NLANES for q in ("sq", "pool")}
        lanes = {}
        for e in ENGINES:
            c = 0
            lcnt = [0] * NLANES
            ldma = 0
            lst = []
            lns = []
            for op in self.ops[e]:
                if e in ("sq", "pool") and op.inc == 16:
                    lane = ldma % NLANES
                    ldma += 1
                    lcnt[lane] += 16
                    lns.append(lane)
                    lst.append(((e, lane), lcnt[lane]))
                    lane_final[e][lane] = lcnt[lane]
                else:
                    lns.append(None)
                    if op.observed:
                        c += op.inc
                    lst.append((e, c))
            cum[e] = lst
            lanes[e] = lns
        block = es.enter_context(self.nc.Block())
        em = self

        def play(eng_name, eng_obj):
            hiwater = {}

            def wait(key, v):
                if v > hiwater.get(key, 0):
                    s = (lsem[key[0]][key[1]] if isinstance(key, tuple)
                         else sems[key])
                    eng_obj.wait_ge(s, v)
                    hiwater[key] = v

            for i, op in enumerate(em.ops[eng_name]):
                ln = lanes[eng_name][i]
                if ln is not None:
                    key, v = cum[eng_name][i]
                    if v > 16:
                        wait(key, v - 16)
                need = {}
                for (fe, fi) in op.deps:
                    if fe == "pe" and eng_name == "pe":
                        continue
                    key, v = cum[fe][fi]
                    if v > need.get(key, 0):
                        need[key] = v
                for key, v in sorted(need.items(), key=str):
                    wait(key, v)
                ins = op.fn(eng_obj)
                if ln is not None:
                    ins.then_inc(lsem[eng_name][ln], 16)
                elif op.observed:
                    ins.then_inc(sems[eng_name], op.inc)
            if eng_name in ("sq", "pool"):
                for l in range(NLANES):
                    if lane_final[eng_name][l]:
                        wait((eng_name, l), lane_final[eng_name][l])

        @block.tensor
        def _(t):
            play("pe", t)

        @block.vector
        def _(v):
            play("dve", v)

        @block.scalar
        def _(s):
            play("act", s)

        @block.gpsimd
        def _(g):
            play("pool", g)

        @block.sync
        def _(s):
            play("sq", s)



def pack_lhsT(M):
    K, N = M.shape
    A = np.asarray(M).reshape(K // 128, 128, N // 128, 128)
    return np.ascontiguousarray(A.transpose(2, 1, 0, 3))


def u8_of_i32(a):
    a = np.ascontiguousarray(a)
    assert a.dtype == np.int32
    return np.ascontiguousarray(a.view(np.uint8).reshape(*a.shape, 4)[..., 0])



class Ctx:

    def __init__(self, name):
        self.nc = bass.Bass("TRN2", target_bir_lowering=False)
        self.em = Em(self.nc)
        self.es = contextlib.ExitStack()
        self._scope = None
        self.prefix = ""
        self._consts = {}
        self._nbuf = 0

    def push_scope(self, prefix):
        assert self._scope is None
        self._scope = contextlib.ExitStack()
        self.prefix = prefix

    def pop_scope(self):
        self.em.barrier()
        self._scope.close()
        self._scope = None
        self.prefix = ""

    def sb(self, name, shape, dtype):
        es = self._scope if self._scope is not None else self.es
        t = es.enter_context(self.nc.sbuf_tensor(self.prefix + name,
                                                 list(shape), dtype))
        return t

    def ps(self, name, shape, dtype=F32):
        es = self._scope if self._scope is not None else self.es
        t = es.enter_context(self.nc.psum_tensor(self.prefix + name,
                                                 list(shape), dtype))
        return t

    def buf(self, name=""):
        self._nbuf += 1
        return Buf(name or f"b{self._nbuf}")

    def const(self, key, arr):
        if key in self._consts:
            return self._consts[key]
        arr = np.ascontiguousarray(arr)
        dt = {np.dtype(np.float32): F32,
              np.dtype(ml_dtypes.bfloat16): BF16,
              np.dtype(np.uint8): U8}[arr.dtype]
        dr = self.nc.inline_tensor(arr, name=f"c_{key}").ap()
        sb = self.es.enter_context(
            self.nc.sbuf_tensor(f"cs_{key}", list(arr.shape), dt))
        b = self.buf(f"const_{key}")
        self.em.dma("pool", sb[:], dr, accwrites=[b])
        self._consts[key] = (sb, b)
        return self._consts[key]

    def cvec(self, value, parts=128, dtype=None):
        key = ("cv", value, parts)
        if key in self._consts:
            return self._consts[key]
        sb = self.es.enter_context(
            self.nc.sbuf_tensor(f"cv_{len(self._consts)}", [parts, 1], F32))
        b = self.buf()
        self.em.op("pool", lambda e: e.memset(sb[:], value), accwrites=[b])
        self._consts[key] = (sb, b)
        return self._consts[key]

    def finish(self):
        self.em.lower(self.es)
        self.es.close()
        return self.nc


def np_triu_inc(n=128):
    return np.triu(np.ones((n, n))).astype(ml_dtypes.bfloat16)



NWT = 56
WT_Q0, WT_K0, WT_S0, WT_V0, WT_G0 = 0, 8, 16, 24, 40


def host_pack_l0_proj(wq, wk, ws, wv, wg):
    mats = [pack_lhsT(np.asarray(m, np.float32)) for m in (wq, wk, ws, wv, wg)]
    return np.concatenate(mats, axis=0)


def emit_p0(ctx, xT, w_l0p, proj):
    nc, em = ctx.nc, ctx.em
    xb = ctx.sb("p0_xb", [128, 16 * T], BF16)
    xf = [ctx.sb(f"p0_xf{i}", [128, T], F32) for i in range(2)]
    xf_b = [ctx.buf() for _ in range(2)]
    xb_b = ctx.buf("xb")
    for dc in range(16):
        i = dc % 2
        em.dma("sq", xf[i][:], xT[dc * 128:(dc + 1) * 128, :], writes=[xf_b[i]])
        em.op("dve",
              lambda e, dc=dc, i=i: e.tensor_copy(xb[:, dc * T:(dc + 1) * T],
                                                  xf[i][:]),
              reads=[xf_b[i]], accwrites=[xb_b])
    wf = [ctx.sb(f"p0_wf{i}", [128, 16 * 128], F32) for i in range(2)]
    wb = [ctx.sb(f"p0_wb{i}", [128, 16 * 128], BF16) for i in range(2)]
    wf_b = [ctx.buf() for _ in range(2)]
    wb_b = [ctx.buf() for _ in range(2)]
    pp = [ctx.ps(f"p0_ps{i}", [128, 512]) for i in range(4)]
    pp_b = [ctx.buf(f"psbank{i}") for i in range(4)]
    ob = [ctx.sb(f"p0_ob{i}", [128, 512], BF16) for i in range(4)]
    ob_b = [ctx.buf() for _ in range(4)]
    for wt in range(NWT):
        i = wt % 2
        em.dma("sq", wf[i][:].rearrange("p (c f) -> p c f", c=16),
               w_l0p[wt], writes=[wf_b[i]])
        em.op("dve", lambda e, i=i: e.tensor_copy(wb[i][:], wf[i][:]),
              reads=[wf_b[i]], writes=[wb_b[i]])
        for ts in range(4):
            j = (wt * 4 + ts) % 4
            def mmgroup(e, i=i, ts=ts, j=j):
                for dc in range(16):
                    ins = e.matmul(pp[j][:], wb[i][:, dc * 128:(dc + 1) * 128],
                                   xb[:, dc * T + ts * 512: dc * T + ts * 512 + 512],
                                   start=(dc == 0), stop=(dc == 15))
                return ins
            em.op("pe", mmgroup, reads=[wb_b[i], xb_b], writes=[pp_b[j]])
            scale = 0.125 if wt < 8 else 1.0
            em.op("act", lambda e, j=j, scale=scale: e.activation(
                ob[j][:], pp[j][:], AF.Copy, scale=scale),
                reads=[pp_b[j]], writes=[ob_b[j]])
            tg, cc = ts, None
            em.dma("sq", proj[ts, wt], ob[j][:], reads=[ob_b[j]])



def emit_p1(ctx, proj, gnw_dr, og):
    nc, em = ctx.nc, ctx.em
    H, C = 16, 128

    triu, triu_b = ctx.const("triu", np_triu_inc(128))
    i128b, i128b_b = ctx.const("i128b", np.eye(128, dtype=ml_dtypes.bfloat16))
    i128f, i128f_b = ctx.const("i128f", np.eye(128, dtype=np.float32))
    irs_np = np.zeros((128, 64), dtype=ml_dtypes.bfloat16)
    for p in range(128):
        irs_np[p, p % 64] = 1
    irs, irs_b = ctx.const("irs", irs_np)
    onesr, onesr_b = ctx.const("onesr", np.ones((1, 128), dtype=ml_dtypes.bfloat16))
    onesrf, onesrf_b = ctx.const("onesrf", np.ones((1, 128), dtype=np.float32))
    onescol, onescol_b = ctx.const("onescol", np.ones((128, 1), dtype=ml_dtypes.bfloat16))
    o128, o128_b = ctx.const("o128", np.full((128, 1), 1.0 / 128, dtype=np.float32))
    eps5, eps5_b = ctx.cvec(1e-5)

    gnw = ctx.sb("gnw_sb", [128, 1], F32)
    gnw_b = ctx.buf("gnw")
    em.dma("pool", gnw[:], gnw_dr.rearrange("(p o) -> p o", o=1), writes=[gnw_b])

    slab = [ctx.sb(f"slab{i}", [128, NWT * 512], BF16) for i in range(2)]
    slab_b = [ctx.buf(f"slab{i}") for i in range(2)]
    ogs = [ctx.sb(f"ogs{i}", [128, H * 512], BF16) for i in range(2)]
    ogs_b = [ctx.buf(f"ogs{i}") for i in range(2)]
    cUWf = ctx.sb("cUWf", [128, 8 * 192], F32)
    cUWf_b = [ctx.buf(f"cUWf{h}") for h in range(16)]
    cUWb = ctx.sb("cUWb", [128, 8 * 192], BF16)
    cUWb_b = [ctx.buf(f"cUWb{h}") for h in range(16)]
    asb = ctx.sb("asb", [1, H * 64], F32)
    asb_b = [ctx.buf(f"asb{h}") for h in range(16)]
    abf = ctx.sb("abf", [1, H * 64], BF16)
    abf_b = [ctx.buf(f"abf{h}") for h in range(16)]

    sc_ps = ctx.ps("sc_ps", [128, 128]); sc_pb = ctx.buf("sc")
    a_ps = ctx.ps("a_ps", [128, 512]); a_pb = ctx.buf("a")
    ok_ps = ctx.ps("ok_ps", [128, 64]); ok_pb = ctx.buf("ok")
    ct_ps = ctx.ps("ct_ps", [128, 512]); ct_pb = ctx.buf("ct")
    ov_ps = ctx.ps("ov_ps", [128, 128]); ov_pb = ctx.buf("ov")
    upd = [ctx.ps(f"upd{i}", [128, 192]) for i in range(2)]
    upd_b = [ctx.buf(f"upd{i}") for i in range(2)]
    trp = ctx.ps("trp", [128, 384], BF16); trp_pb = ctx.buf("trp")

    def sbuf2(name, shape, dtype):
        ts_ = [ctx.sb(f"{name}{i}", list(shape), dtype) for i in range(2)]
        bs = [ctx.buf(f"{name}{i}") for i in range(2)]
        return ts_, bs
    e_tm, e_tm_b = sbuf2("e_tm", [128, 64], BF16)
    k_tm, k_tm_b = sbuf2("k_tm", [128, 64], BF16)
    v_tm, v_tm_b = sbuf2("v_tm", [128, 128], BF16)
    skq, skq_b = sbuf2("skq", [128, 128], BF16)
    recA, recA_b = sbuf2("recA", [128, 64], F32)
    okn, okn_b = sbuf2("okn", [128, 64], F32)
    nmx, nmx_b = sbuf2("nmx", [128, 1], F32)
    pve, pve_b = sbuf2("pve", [128, 64], F32)
    ell, ell_b = sbuf2("ell", [128, 1], F32)
    rcl, rcl_b = sbuf2("rcl", [128, 1], F32)
    pab, pab_b = sbuf2("pab", [128, 64], BF16)
    pam, pam_b = sbuf2("pam", [128, 128], BF16)
    ctb, ctb_b = sbuf2("ctb", [128, 128], BF16)
    sqv, sqv_b = sbuf2("sqv", [128, 128], F32)
    ovs, ovs_b = sbuf2("ovs", [128, 128], F32)
    rsd, rsd_b = sbuf2("rsd", [128, 1], F32)
    rsT, rsT_b = sbuf2("rsT", [1, 128], F32)
    rsB, rsB_b = sbuf2("rsB", [128, 128], F32)
    gt1, gt1_b = sbuf2("gt1", [128, 128], F32)

    def sl(i, wt, part, nparts, c):
        return slab[i][part:part + nparts,
                       wt * 512 + c * 128: wt * 512 + (c + 1) * 128]

    def prep_group(tg, i):
        em.dma("sq", slab[i][:].rearrange("p (w f) -> p w f", w=NWT),
               proj[tg].rearrange("w p f -> p w f"), writes=[slab_b[i]])
        for q in range(8):
            wt = WT_S0 + q
            em.op("dve", lambda e, wt=wt, i=i: e.tensor_scalar(
                slab[i][:, wt * 512:(wt + 1) * 512],
                slab[i][:, wt * 512:(wt + 1) * 512],
                32.0, -32.0, ALU.min, ALU.max),
                reads=[slab_b[i]], accwrites=[slab_b[i]])
            em.op("act", lambda e, wt=wt, i=i: e.activation(
                slab[i][:, wt * 512:(wt + 1) * 512],
                slab[i][:, wt * 512:(wt + 1) * 512], AF.Exp),
                reads=[slab_b[i]], accwrites=[slab_b[i]])
        for q in range(16):
            wt = WT_G0 + q
            em.op("act", lambda e, wt=wt, i=i: e.activation(
                slab[i][:, wt * 512:(wt + 1) * 512],
                slab[i][:, wt * 512:(wt + 1) * 512], AF.Silu),
                reads=[slab_b[i]], accwrites=[slab_b[i]])

    prep_group(0, 0)
    for tg in range(4):
        i = tg % 2
        if tg + 1 < 4:
            prep_group(tg + 1, (tg + 1) % 2)
        oi = tg % 2
        if True:
            def head_steps(c, h, i, oi, first):
                steps = []
                _op = em.op
                def op_rec(*a, **k):
                    steps.append((a, k))
                em.op = op_rec
                par = h % 2
                qp = 64 * par
                qwt, kwt, swt = WT_Q0 + h // 2, WT_K0 + h // 2, WT_S0 + h // 2
                vwt, gwt = WT_V0 + h, WT_G0 + h
                pr = h // 2
                u_bf = cUWb[qp:qp + 64, pr * 192:pr * 192 + 64]
                w_bf = cUWb[qp:qp + 64, pr * 192 + 64:pr * 192 + 192]
                uw_f = cUWf[qp:qp + 64, pr * 192:pr * 192 + 192]
                uw_b16 = cUWb[qp:qp + 64, pr * 192:pr * 192 + 192]
                a_f32 = asb[0:1, h * 64:(h + 1) * 64]
                a_bf = abf[0:1, h * 64:(h + 1) * 64]
                trE = trp[:, 0:64]
                trK = trp[:, 64:128]
                trV = trp[:, 128:256]
                trP = trp[qp:qp + 64, 256:384]

                em.op("pe", lambda e, swt=swt, qp=qp, c=c, i=i: e.transpose(
                    trE, sl(i, swt, qp, 64, c), irs[qp:qp + 64, :]),
                    reads=[slab_b[i], irs_b], writes=[trp_pb])
                em.op("dve", lambda e, par=par: e.tensor_copy(
                    e_tm[par][:], trE),
                    reads=[trp_pb], writes=[e_tm_b[par]])
                em.op("pe", lambda e, kwt=kwt, qp=qp, c=c, i=i: e.transpose(
                    trK, sl(i, kwt, qp, 64, c), irs[qp:qp + 64, :]),
                    reads=[slab_b[i], irs_b], writes=[trp_pb])
                em.op("dve", lambda e, par=par: e.tensor_copy(
                    k_tm[par][:], trK),
                    reads=[trp_pb], writes=[k_tm_b[par]])
                em.op("pe", lambda e, vwt=vwt, c=c, i=i: e.transpose(
                    trV, sl(i, vwt, 0, 128, c), i128b[:]),
                    reads=[slab_b[i], i128b_b], writes=[trp_pb])
                em.op("dve", lambda e, par=par: e.tensor_copy(
                    v_tm[par][:], trV),
                    reads=[trp_pb], writes=[v_tm_b[par]])
                if not first:
                    em.op("dve", lambda e, uw_f=uw_f, uw_b16=uw_b16:
                          e.tensor_copy(uw_b16, uw_f),
                          reads=[cUWf_b[h]], writes=[cUWb_b[h]])
                    em.op("dve", lambda e, a_f32=a_f32, a_bf=a_bf:
                          e.tensor_copy(a_bf, a_f32),
                          reads=[asb_b[h]], writes=[abf_b[h]])
                em.op("pe", lambda e, kwt=kwt, qwt=qwt, qp=qp, c=c, i=i: e.matmul(
                    sc_ps[:], sl(i, kwt, qp, 64, c), sl(i, qwt, qp, 64, c),
                    start=True, stop=True),
                    reads=[slab_b[i]], writes=[sc_pb])
                em.op("dve", lambda e, par=par: e.tensor_tensor(
                    skq[par][:], sc_ps[:], triu[:], ALU.mult),
                    reads=[sc_pb, triu_b], writes=[skq_b[par]])
                def a_mm(e, first=first, a_bf=a_bf, par=par):
                    ins = e.matmul(a_ps[:, 0:64], triu[:], e_tm[par][:],
                                   start=True, stop=first)
                    if not first:
                        ins = e.matmul(a_ps[:, 0:64], onesr[:], a_bf,
                                       start=False, stop=True)
                    return ins
                em.op("pe", a_mm,
                      reads=[triu_b, e_tm_b[par], onesr_b, abf_b[h]],
                      writes=[a_pb])
                em.op("dve", lambda e, par=par: e.reciprocal(
                    recA[par][:], a_ps[:, 0:64]),
                    reads=[a_pb], writes=[recA_b[par]])
                def ok_mm(e, first=first, qwt=qwt, qp=qp, c=c, i=i,
                          u_bf=u_bf, par=par):
                    ins = e.matmul(ok_ps[:], skq[par][:], e_tm[par][:],
                                   start=True, stop=first)
                    if not first:
                        ins = e.matmul(ok_ps[:], sl(i, qwt, qp, 64, c), u_bf,
                                       start=False, stop=True)
                    return ins
                em.op("pe", ok_mm,
                      reads=[skq_b[par], e_tm_b[par], slab_b[i], cUWb_b[h]],
                      writes=[ok_pb])
                em.op("dve", lambda e, par=par: e.tensor_tensor(
                    okn[par][:], ok_ps[:], recA[par][:], ALU.mult),
                    reads=[ok_pb, recA_b[par]], writes=[okn_b[par]])
                em.op("dve", lambda e, par=par: e.reduce_max(
                    nmx[par][:], okn[par][:], axis=AX.X, negate=True),
                    reads=[okn_b[par]], writes=[nmx_b[par]])
                em.op("act", lambda e, par=par: e.activation(
                    pve[par][:], okn[par][:], AF.Exp, bias=nmx[par][:]),
                    reads=[okn_b[par], nmx_b[par]], writes=[pve_b[par]])
                em.op("dve", lambda e, par=par: e.reduce_sum(
                    ell[par][:], pve[par][:], axis=AX.X),
                    reads=[pve_b[par]], writes=[ell_b[par]])
                em.op("dve", lambda e, par=par: e.reciprocal(
                    rcl[par][:], ell[par][:]),
                    reads=[ell_b[par]], writes=[rcl_b[par]])
                em.op("dve", lambda e, par=par: e.scalar_tensor_tensor(
                    pab[par][:], pve[par][:], rcl[par][:], recA[par][:],
                    ALU.mult, ALU.mult),
                    reads=[pve_b[par], rcl_b[par], recA_b[par]],
                    writes=[pab_b[par]])
                em.op("pe", lambda e, qp=qp, par=par, trP=trP: e.transpose(
                    trP, pab[par][:], i128b[:], tile_position=(0, qp)),
                    reads=[pab_b[par], i128b_b], writes=[trp_pb])
                em.op("dve", lambda e, qp=qp, par=par: e.tensor_copy(
                    pam[par][qp:qp + 64, :], trp[qp:qp + 64, 256:384]),
                    reads=[trp_pb], accwrites=[pam_b[par]])
                em.op("pe", lambda e, swt=swt, qp=qp, c=c, i=i, par=par: e.matmul(
                    ct_ps[:, 0:128], sl(i, swt, qp, 64, c),
                    pam[par][qp:qp + 64, :], start=True, stop=True),
                    reads=[slab_b[i], pam_b[par]], writes=[ct_pb])
                em.op("dve", lambda e, par=par: e.tensor_tensor(
                    ctb[par][:], ct_ps[:, 0:128], triu[:], ALU.mult),
                    reads=[ct_pb, triu_b], writes=[ctb_b[par]])
                def ov_mm(e, first=first, qp=qp, w_bf=w_bf, par=par):
                    ins = e.matmul(ov_ps[:], v_tm[par][:], ctb[par][:],
                                   start=True, stop=first)
                    if not first:
                        ins = e.matmul(ov_ps[:], w_bf, pam[par][qp:qp + 64, :],
                                       start=False, stop=True)
                    return ins
                em.op("pe", ov_mm,
                      reads=[v_tm_b[par], ctb_b[par], cUWb_b[h], pam_b[par]],
                      writes=[ov_pb])
                em.op("dve", lambda e, par=par: e.tensor_copy(
                    ovs[par][:], ov_ps[:]),
                    reads=[ov_pb], writes=[ovs_b[par]])
                em.op("pool", lambda e, par=par: e.tensor_tensor(
                    sqv[par][:], ovs[par][:], ovs[par][:], ALU.mult),
                    reads=[ovs_b[par]], writes=[sqv_b[par]])
                em.op("pe", lambda e, par=par: e.matmul(
                    a_ps[:, 128:129], sqv[par][:], o128[:],
                    start=True, stop=True),
                    reads=[sqv_b[par], o128_b], writes=[a_pb])
                em.op("act", lambda e, par=par: e.activation(
                    rsd[par][:], a_ps[:, 128:129], AF.Ln, bias=eps5[:]),
                    reads=[a_pb, eps5_b], writes=[rsd_b[par]])
                em.op("act", lambda e, par=par: e.activation(
                    rsd[par][:], rsd[par][:], AF.Exp, scale=-0.5),
                    reads=[rsd_b[par]], accwrites=[rsd_b[par]])
                em.op("pe", lambda e, par=par: e.transpose(
                    ct_ps[0:1, 384:512], rsd[par][:], i128f[:]),
                    reads=[rsd_b[par], i128f_b], writes=[ct_pb])
                em.op("dve", lambda e, par=par: e.tensor_copy(
                    rsT[par][:], ct_ps[0:1, 384:512]),
                    reads=[ct_pb], writes=[rsT_b[par]])
                em.op("pe", lambda e, par=par: e.matmul(
                    ct_ps[:, 256:384], onesrf[:], rsT[par][:],
                    start=True, stop=True),
                    reads=[onesrf_b, rsT_b[par]], writes=[ct_pb])
                em.op("dve", lambda e, par=par: e.tensor_copy(
                    rsB[par][:], ct_ps[:, 256:384]),
                    reads=[ct_pb], writes=[rsB_b[par]])
                em.op("pool", lambda e, par=par: e.tensor_tensor(
                    gt1[par][:], ovs[par][:], rsB[par][:], ALU.mult),
                    reads=[ovs_b[par], rsB_b[par]], writes=[gt1_b[par]])
                em.op("dve", lambda e, h=h, c=c, gwt=gwt, i=i, oi=oi, par=par:
                      e.scalar_tensor_tensor(
                          ogs[oi][:, h * 512 + c * 128: h * 512 + (c + 1) * 128],
                          gt1[par][:], gnw[:], sl(i, gwt, 0, 128, c),
                          ALU.mult, ALU.mult),
                      reads=[gt1_b[par], gnw_b, slab_b[i]],
                      accwrites=[ogs_b[oi]])
                uj = h % 2

                def upd_mm(e, qp=qp, uj=uj, par=par):
                    e.matmul(upd[uj][qp:qp + 64, 0:64], k_tm[par][:],
                             e_tm[par][:], start=True, stop=True,
                             tile_position=(0, qp))
                    return e.matmul(upd[uj][qp:qp + 64, 64:192], e_tm[par][:],
                                    v_tm[par][:], start=True, stop=True,
                                    tile_position=(0, qp))
                em.op("pe", upd_mm,
                      reads=[k_tm_b[par], e_tm_b[par], v_tm_b[par]],
                      writes=[upd_b[uj]])
                if first:
                    em.op("dve", lambda e, uw_f=uw_f, qp=qp, uj=uj: e.tensor_copy(
                        uw_f, upd[uj][qp:qp + 64, :]),
                        reads=[upd_b[uj]], writes=[cUWf_b[h]])
                else:
                    em.op("dve", lambda e, uw_f=uw_f, qp=qp, uj=uj: e.tensor_tensor(
                        uw_f, uw_f, upd[uj][qp:qp + 64, :], ALU.add),
                        reads=[upd_b[uj], cUWf_b[h]], accwrites=[cUWf_b[h]])
                em.op("pe", lambda e, par=par: e.matmul(
                    a_ps[0:1, 192:256], onescol[:], e_tm[par][:],
                    start=True, stop=True),
                    reads=[onescol_b, e_tm_b[par]], writes=[a_pb])
                if first:
                    em.op("dve", lambda e, a_f32=a_f32: e.tensor_copy(
                        a_f32, a_ps[0:1, 192:256]),
                        reads=[a_pb], writes=[asb_b[h]])
                else:
                    em.op("dve", lambda e, a_f32=a_f32: e.tensor_tensor(
                        a_f32, a_f32, a_ps[0:1, 192:256], ALU.add),
                        reads=[a_pb, asb_b[h]], accwrites=[asb_b[h]])
                em.op = _op
                return steps

            for c in range(4):
                first = (tg == 0 and c == 0)
                GR = 1000
                for hp in range(0, H, 2):
                    sa = head_steps(c, hp, i, oi, first)
                    sb2 = head_steps(c, hp + 1, i, oi, first)
                    na, nb = len(sa), len(sb2)
                    ia = ib = 0
                    while ia < na or ib < nb:
                        for _ in range(GR):
                            if ia < na:
                                a, k = sa[ia]; em.op(*a, **k); ia += 1
                        for _ in range(GR):
                            if ib < nb:
                                a, k = sb2[ib]; em.op(*a, **k); ib += 1
        for hh in range(H):
            em.dma("sq", og[hh, :, tg * 512:(tg + 1) * 512],
                   ogs[oi][:, hh * 512:(hh + 1) * 512], reads=[ogs_b[oi]])


def build_p1():
    ctx = Ctx("p1")
    nc = ctx.nc
    proj = nc.dram_tensor("proj", [4, NWT, 128, 512], BF16,
                          kind="ExternalInput").ap()
    gnw = nc.dram_tensor("gnw", [128], F32, kind="ExternalInput").ap()
    og = nc.dram_tensor("og", [16, 128, T], BF16, kind="ExternalOutput").ap()
    emit_p1(ctx, proj, gnw, og)
    ctx.finish()
    return nc



class LNRes:
    def __init__(self, ctx):
        self.onescol, self.onescol_b = ctx.const(
            "onescolf", np.ones((128, 1), dtype=np.float32))
        self.onesrow, self.onesrow_b = ctx.const(
            "onesrowf", np.ones((1, 128), dtype=np.float32))
        self.eps1, self.eps1_b = ctx.cvec(1e-5, parts=1)
        self.st1 = ctx.ps("ln_st1", [1, 512]); self.st1_b = ctx.buf("ln_st1")
        self.st2 = ctx.ps("ln_st2", [1, 512]); self.st2_b = ctx.buf("ln_st2")
        self.bca = ctx.ps("ln_bca", [128, 512]); self.bca_b = ctx.buf("ln_bca")
        self.bcb = ctx.ps("ln_bcb", [128, 512]); self.bcb_b = ctx.buf("ln_bcb")
        self.sq = [ctx.sb(f"ln_sq{i}", [128, 512], F32) for i in range(2)]
        self.sq_b = [ctx.buf() for _ in range(2)]
        self.rows = ctx.sb("ln_rows", [1, 512 * 5], F32)
        self.rows_b = ctx.buf("ln_rows")
        self.absb = ctx.sb("ln_absb", [128, 512], F32); self.absb_b = ctx.buf()
        self.t1 = ctx.sb("ln_t1", [128, 512], F32); self.t1_b = ctx.buf()
        self.xo = [ctx.sb(f"ln_xo{i}", [128, 512], F32) for i in range(2)]
        self.xo_b = [ctx.buf() for _ in range(2)]
        self.xn = [ctx.sb(f"ln_xn{i}", [128, 512], BF16) for i in range(2)]
        self.xn_b = [ctx.buf() for _ in range(2)]


def emit_ln(ctx, ln, rsl, rsl_b, xo_dst, xn_dst):
    em = ctx.em
    R = lambda fc: rsl[:, fc * 512:(fc + 1) * 512]
    mrow = ln.rows[:, 0:512]
    qrow = ln.rows[:, 512:1024]
    var = ln.rows[:, 1024:1536]
    arow = ln.rows[:, 1536:2048]
    brow = ln.rows[:, 2048:2560]
    for fc in range(16):
        i = fc % 2
        em.op("act", lambda e, fc=fc, i=i: e.activation(
            ln.sq[i][:], R(fc), AF.Square), reads=[rsl_b], writes=[ln.sq_b[i]])
        em.op("pe", lambda e, fc=fc: e.matmul(
            ln.st1[:], ln.onescol[:], R(fc), start=(fc == 0), stop=(fc == 15)),
            reads=[ln.onescol_b, rsl_b],
            writes=[ln.st1_b] if fc == 0 else (), accwrites=() if fc == 0 else [ln.st1_b])
        em.op("pe", lambda e, i=i, fc=fc: e.matmul(
            ln.st2[:], ln.onescol[:], ln.sq[i][:], start=(fc == 0), stop=(fc == 15)),
            reads=[ln.onescol_b, ln.sq_b[i]],
            writes=[ln.st2_b] if fc == 0 else (), accwrites=() if fc == 0 else [ln.st2_b])
    em.op("act", lambda e: e.activation(mrow, ln.st1[:], AF.Copy, scale=1.0 / D),
          reads=[ln.st1_b], accwrites=[ln.rows_b])
    em.op("act", lambda e: e.activation(qrow, ln.st2[:], AF.Copy, scale=1.0 / D),
          reads=[ln.st2_b], accwrites=[ln.rows_b])
    em.op("dve", lambda e: e.tensor_tensor(var, mrow, mrow, ALU.mult),
          reads=[ln.rows_b], accwrites=[ln.rows_b])
    em.op("dve", lambda e: e.tensor_tensor(var, qrow, var, ALU.subtract),
          reads=[ln.rows_b], accwrites=[ln.rows_b])
    em.op("act", lambda e: e.activation(var, var, AF.Sqrt, bias=ln.eps1[:]),
          reads=[ln.rows_b, ln.eps1_b], accwrites=[ln.rows_b])
    em.op("dve", lambda e: e.reciprocal(arow, var),
          reads=[ln.rows_b], accwrites=[ln.rows_b])
    em.op("dve", lambda e: e.scalar_tensor_tensor(
        brow, mrow, -1.0, arow, ALU.mult, ALU.mult),
        reads=[ln.rows_b], accwrites=[ln.rows_b])
    em.op("pe", lambda e: e.matmul(ln.bca[:], ln.onesrow[:], arow,
                                   start=True, stop=True),
          reads=[ln.onesrow_b, ln.rows_b], writes=[ln.bca_b])
    em.op("pe", lambda e: e.matmul(ln.bcb[:], ln.onesrow[:], brow,
                                   start=True, stop=True),
          reads=[ln.onesrow_b, ln.rows_b], writes=[ln.bcb_b])
    em.op("dve", lambda e: e.tensor_copy(ln.absb[:], ln.bca[:]),
          reads=[ln.bca_b], writes=[ln.absb_b])
    for fc in range(16):
        i = fc % 2
        em.op("dve", lambda e, fc=fc: e.tensor_tensor(
            ln.t1[:], R(fc), ln.absb[:], ALU.mult),
            reads=[rsl_b, ln.absb_b], writes=[ln.t1_b])
        em.op("dve", lambda e, i=i: e.tensor_tensor(
            ln.xo[i][:], ln.t1[:], ln.bcb[:], ALU.add),
            reads=[ln.t1_b, ln.bcb_b], writes=[ln.xo_b[i]])
        em.dma("sq", xo_dst(fc), ln.xo[i][:], reads=[ln.xo_b[i]])
        if xn_dst is not None:
            em.op("act", lambda e, i=i: e.activation(ln.xn[i][:], ln.xo[i][:],
                                                     AF.Copy),
                  reads=[ln.xo_b[i]], writes=[ln.xn_b[i]])
            em.dma("pool", xn_dst(fc), ln.xn[i][:], reads=[ln.xn_b[i]])



def emit_proj_res_ln(ctx, src_dr, w_dr, xres_dr, xo_dr, xn_dr, res_scale=1.0):
    nc, em = ctx.nc, ctx.em
    ln = LNRes(ctx)
    wob = ctx.sb("prl_wob", [128, 16 * 2048], BF16)
    wob_b = ctx.buf("wob")
    wof = [ctx.sb(f"prl_wof{i}", [128, 2048], F32) for i in range(2)]
    wof_b = [ctx.buf() for _ in range(2)]
    for ot in range(16):
        i = ot % 2
        em.dma("sq", wof[i][:].rearrange("p (c f) -> p c f", c=16), w_dr[ot],
               writes=[wof_b[i]])
        em.op("dve", lambda e, ot=ot, i=i: e.tensor_copy(
            wob[:, ot * 2048:(ot + 1) * 2048], wof[i][:]),
            reads=[wof_b[i]], accwrites=[wob_b])
    srcs = [ctx.sb(f"prl_src{i}", [128, 16 * 512], BF16) for i in range(2)]
    srcs_b = [ctx.buf() for _ in range(2)]
    xc = [ctx.sb(f"prl_xc{i}", [128, 512], F32) for i in range(2)]
    xc_b = [ctx.buf() for _ in range(2)]
    rsl = ctx.sb("prl_rsl", [128, 16 * 512], F32)
    rsl_b = ctx.buf("prl_rsl")
    pp = [ctx.ps(f"prl_ps{i}", [128, 512]) for i in range(2)]
    pp_b = [ctx.buf() for _ in range(2)]
    for ts in range(4):
        i2 = ts % 2
        em.dma("sq", srcs[i2][:].rearrange("p (c f) -> p c f", c=16),
               src_dr[:, :, ts * 512:(ts + 1) * 512].rearrange("h p f -> p h f"),
               writes=[srcs_b[i2]])
        for d2t in range(16):
            j = d2t % 2
            em.dma("pool", xc[j][:],
                   xres_dr[d2t * 128:(d2t + 1) * 128, ts * 512:(ts + 1) * 512],
                   writes=[xc_b[j]])

            def mmg(e, d2t=d2t, i2=i2, j=j):
                for fc in range(16):
                    ins = e.matmul(pp[j][:],
                                   wob[:, d2t * 2048 + fc * 128: d2t * 2048 + fc * 128 + 128],
                                   srcs[i2][:, fc * 512:(fc + 1) * 512],
                                   start=(fc == 0), stop=(fc == 15))
                return ins
            em.op("pe", mmg, reads=[wob_b, srcs_b[i2]], writes=[pp_b[j]])
            em.op("dve", lambda e, d2t=d2t, j=j, rs=res_scale: e.scalar_tensor_tensor(
                rsl[:, d2t * 512:(d2t + 1) * 512], xc[j][:], rs, pp[j][:],
                ALU.mult, ALU.add),
                reads=[xc_b[j], pp_b[j]], accwrites=[rsl_b])
        emit_ln(ctx, ln, rsl, rsl_b,
                lambda fc, ts=ts: xo_dr[fc * 128:(fc + 1) * 128,
                                        ts * 512:(ts + 1) * 512],
                None if xn_dr is None else
                (lambda fc, ts=ts: xn_dr[fc * 128:(fc + 1) * 128,
                                         ts * 512:(ts + 1) * 512]))


def build_p2():
    ctx = Ctx("p2")
    nc = ctx.nc
    og = nc.dram_tensor("og", [16, 128, T], BF16, kind="ExternalInput").ap()
    w_l0o = nc.dram_tensor("w_l0o", [16, 128, 16, 128], F32,
                           kind="ExternalInput").ap()
    xT = nc.dram_tensor("xT", [D, T], F32, kind="ExternalInput").ap()
    x1 = nc.dram_tensor("x1", [D, T], F32, kind="ExternalOutput").ap()
    x1n = nc.dram_tensor("x1n", [D, T], BF16, kind="ExternalOutput").ap()
    emit_proj_res_ln(ctx, og, w_l0o, xT, x1, x1n, res_scale=1.0)
    ctx.finish()
    return nc



def emit_p3(ctx, x1n_dr, wg_dr, wu_dr, gsc_dr, usc_dr, hsc_dr):
    nc, em = ctx.nc, ctx.em
    xb = ctx.sb("p3_xb", [128, 16 * T], BF16)
    xb_b = ctx.buf("p3_xb")
    em.dma("sq", xb[:].rearrange("p (c t) -> p c t", c=16),
           x1n_dr.rearrange("(c p) t -> p c t", p=128), writes=[xb_b])
    gscs = ctx.sb("p3_gsc", [128, 64], F32); gscs_b = ctx.buf()
    uscs = ctx.sb("p3_usc", [128, 64], F32); uscs_b = ctx.buf()
    em.dma("pool", gscs[:], gsc_dr, writes=[gscs_b])
    em.dma("pool", uscs[:], usc_dr, writes=[uscs_b])
    wg8 = [ctx.sb(f"p3_wg8{i}", [128, 2048], U8) for i in range(2)]
    wg8_b = [ctx.buf() for _ in range(2)]
    wu8 = [ctx.sb(f"p3_wu8{i}", [128, 2048], U8) for i in range(2)]
    wu8_b = [ctx.buf() for _ in range(2)]
    wgb = [ctx.sb(f"p3_wgb{i}", [128, 2048], BF16) for i in range(2)]
    wgb_b = [ctx.buf() for _ in range(2)]
    wub = [ctx.sb(f"p3_wub{i}", [128, 2048], BF16) for i in range(2)]
    wub_b = [ctx.buf() for _ in range(2)]
    gs = ctx.sb("p3_gs", [128, 512], F32); gs_b = ctx.buf()
    hst = [ctx.sb(f"p3_hst{i}", [128, 2048], BF16) for i in range(2)]
    hst_b = [ctx.buf() for _ in range(2)]
    pg = [ctx.ps(f"p3_pg{i}", [128, 512]) for i in range(2)]
    pg_b = [ctx.buf() for _ in range(2)]
    pu = [ctx.ps(f"p3_pu{i}", [128, 512]) for i in range(2)]
    pu_b = [ctx.buf() for _ in range(2)]
    for it in range(64):
        i = it % 2
        em.dma("sq", wg8[i][:].rearrange("p (c f) -> p c f", c=16), wg_dr[it],
               writes=[wg8_b[i]])
        em.dma("sq", wu8[i][:].rearrange("p (c f) -> p c f", c=16), wu_dr[it],
               writes=[wu8_b[i]])
        em.op("act", lambda e, i=i: e.activation(wgb[i][:], wg8[i][:], AF.Copy,
                                                 bias=-128.0),
              reads=[wg8_b[i]], writes=[wgb_b[i]])
        em.op("act", lambda e, i=i: e.activation(wub[i][:], wu8[i][:], AF.Copy,
                                                 bias=-128.0),
              reads=[wu8_b[i]], writes=[wub_b[i]])
        for ts in range(4):
            j = ts % 2

            def mmg(e, i=i, ts=ts, j=j, w=None):
                for fc in range(16):
                    ins = e.matmul(pg[j][:], wgb[i][:, fc * 128:(fc + 1) * 128],
                                   xb[:, fc * T + ts * 512: fc * T + ts * 512 + 512],
                                   start=(fc == 0), stop=(fc == 15))
                for fc in range(16):
                    ins = e.matmul(pu[j][:], wub[i][:, fc * 128:(fc + 1) * 128],
                                   xb[:, fc * T + ts * 512: fc * T + ts * 512 + 512],
                                   start=(fc == 0), stop=(fc == 15))
                return ins
            em.op("pe", mmg, reads=[wgb_b[i], wub_b[i], xb_b],
                  writes=[pg_b[j], pu_b[j]])
            em.op("act", lambda e, it=it, j=j: e.activation(
                gs[:], pg[j][:], AF.Silu, scale=gscs[:, it:it + 1]),
                reads=[pg_b[j], gscs_b], writes=[gs_b])
            em.op("dve", lambda e, it=it, i=i, ts=ts, j=j: e.scalar_tensor_tensor(
                hst[i][:, ts * 512:(ts + 1) * 512], pu[j][:], uscs[:, it:it + 1],
                gs[:], ALU.mult, ALU.mult),
                reads=[pu_b[j], uscs_b, gs_b], accwrites=[hst_b[i]])
        em.dma("sq", hsc_dr[it], hst[i][:], reads=[hst_b[i]], writes=())


def build_p3():
    ctx = Ctx("p3")
    nc = ctx.nc
    x1n = nc.dram_tensor("x1n", [D, T], BF16, kind="ExternalInput").ap()
    wg = nc.dram_tensor("w_l0g", [64, 128, 16, 128], U8, kind="ExternalInput").ap()
    wu = nc.dram_tensor("w_l0u", [64, 128, 16, 128], U8, kind="ExternalInput").ap()
    gsc = nc.dram_tensor("gsc", [128, 64], F32, kind="ExternalInput").ap()
    usc = nc.dram_tensor("usc", [128, 64], F32, kind="ExternalInput").ap()
    hsc = nc.dram_tensor("hsc", [64, 128, T], BF16, kind="ExternalOutput").ap()
    emit_p3(ctx, x1n, wg, wu, gsc, usc, hsc)
    ctx.finish()
    return nc



def emit_p4(ctx, hsc_dr, wd_dr, dsc_dr, x1_dr, x2_dr, x2n_dr):
    nc, em = ctx.nc, ctx.em
    ln = LNRes(ctx)
    dscs = ctx.sb("p4_dsc", [128, 16], F32); dscs_b = ctx.buf()
    em.dma("pool", dscs[:], dsc_dr, writes=[dscs_b])
    hsl = ctx.sb("p4_hsl", [128, 64 * 512], BF16)
    hsl_b = ctx.buf("p4_hsl")
    wd8 = [ctx.sb(f"p4_wd8{i}", [128, 8192], U8) for i in range(2)]
    wd8_b = [ctx.buf() for _ in range(2)]
    wdb = [ctx.sb(f"p4_wdb{i}", [128, 8192], BF16) for i in range(2)]
    wdb_b = [ctx.buf() for _ in range(2)]
    xc = [ctx.sb(f"p4_xc{i}", [128, 512], F32) for i in range(2)]
    xc_b = [ctx.buf() for _ in range(2)]
    rsl = ctx.sb("p4_rsl", [128, 16 * 512], F32)
    rsl_b = ctx.buf("p4_rsl")
    pp = [ctx.ps(f"p4_ps{i}", [128, 512]) for i in range(2)]
    pp_b = [ctx.buf() for _ in range(2)]
    for tc in range(4):
        em.dma("sq", hsl[:].rearrange("p (c f) -> p c f", c=64),
               hsc_dr[:, :, tc * 512:(tc + 1) * 512].rearrange("i p f -> p i f"),
               writes=[hsl_b])
        for d2t in range(16):
            i = d2t % 2
            j = d2t % 2
            em.dma("sq", wd8[i][:].rearrange("p (c f) -> p c f", c=64),
                   wd_dr[d2t], writes=[wd8_b[i]])
            em.op("act", lambda e, i=i: e.activation(wdb[i][:], wd8[i][:],
                                                     AF.Copy, bias=-128.0),
                  reads=[wd8_b[i]], writes=[wdb_b[i]])
            em.dma("pool", xc[j][:],
                   x1_dr[d2t * 128:(d2t + 1) * 128, tc * 512:(tc + 1) * 512],
                   writes=[xc_b[j]])

            def mmg(e, i=i, j=j):
                for kc in range(64):
                    ins = e.matmul(pp[j][:], wdb[i][:, kc * 128:(kc + 1) * 128],
                                   hsl[:, kc * 512:(kc + 1) * 512],
                                   start=(kc == 0), stop=(kc == 63))
                return ins
            em.op("pe", mmg, reads=[wdb_b[i], hsl_b], writes=[pp_b[j]])
            em.op("dve", lambda e, d2t=d2t, j=j: e.scalar_tensor_tensor(
                rsl[:, d2t * 512:(d2t + 1) * 512], pp[j][:], dscs[:, d2t:d2t + 1],
                xc[j][:], ALU.mult, ALU.add),
                reads=[pp_b[j], dscs_b, xc_b[j]], accwrites=[rsl_b])
        emit_ln(ctx, ln, rsl, rsl_b,
                lambda fc, tc=tc: x2_dr[fc * 128:(fc + 1) * 128,
                                        tc * 512:(tc + 1) * 512],
                None if x2n_dr is None else
                (lambda fc, tc=tc: x2n_dr[fc * 128:(fc + 1) * 128,
                                          tc * 512:(tc + 1) * 512]))


def build_p4():
    ctx = Ctx("p4")
    nc = ctx.nc
    hsc = nc.dram_tensor("hsc", [64, 128, T], BF16, kind="ExternalInput").ap()
    wd = nc.dram_tensor("w_l0d", [16, 128, 64, 128], U8, kind="ExternalInput").ap()
    dsc = nc.dram_tensor("dsc", [128, 16], F32, kind="ExternalInput").ap()
    x1 = nc.dram_tensor("x1", [D, T], F32, kind="ExternalInput").ap()
    x2 = nc.dram_tensor("x2", [D, T], F32, kind="ExternalOutput").ap()
    x2n = nc.dram_tensor("x2n", [D, T], BF16, kind="ExternalOutput").ap()
    emit_p4(ctx, hsc, wd, dsc, x1, x2, x2n)
    ctx.finish()
    return nc



def emit_p5(ctx, x2_dr, ns_dr, ss_dr, wq_dr, wk_dr, wv_dr, qf_dr, kf_dr, vf_dr):
    nc, em = ctx.nc, ctx.em
    onescol, onescol_b = ctx.const("onescolf", np.ones((128, 1), dtype=np.float32))
    onesrow, onesrow_b = ctx.const("onesrowf", np.ones((1, 128), dtype=np.float32))
    eps1, eps1_b = ctx.cvec(1e-8, parts=1)
    nss = ctx.sb("p5_nss", [128, 16], F32); nss_b = ctx.buf()
    for name, dr in (("ns", ns_dr), ("ss", ss_dr)):
        t = ctx.sb(f"p5_{name}", [128, 16], F32)
        b = ctx.buf()
        em.dma("pool", t[:], dr, writes=[b])
        if name == "ns":
            tns, tns_b = t, b
        else:
            tss, tss_b = t, b
    em.op("dve", lambda e: e.tensor_tensor(nss[:], tns[:], tss[:], ALU.mult),
          reads=[tns_b, tss_b], writes=[nss_b])

    xsl = ctx.sb("p5_xsl", [128, 16 * 512], F32); xsl_b = ctx.buf()
    sq = [ctx.sb(f"p5_sq{i}", [128, 512], F32) for i in range(2)]
    sq_b = [ctx.buf() for _ in range(2)]
    rows = ctx.sb("p5_rows", [1, 512], F32); rows_b = ctx.buf()
    cbs = ctx.sb("p5_cbs", [128, 512], F32); cbs_b = ctx.buf()
    hsl = ctx.sb("p5_hsl", [128, 16 * T], BF16); hsl_b = ctx.buf("p5_hsl")
    st = ctx.ps("p5_st", [1, 512]); st_b = ctx.buf()
    bc = ctx.ps("p5_bc", [128, 512]); bc_b = ctx.buf()
    for ts in range(4):
        for fc in range(16):
            i = fc % 2
            em.dma("sq", xsl[:, fc * 512:(fc + 1) * 512],
                   x2_dr[fc * 128:(fc + 1) * 128, ts * 512:(ts + 1) * 512],
                   accwrites=[xsl_b])
            em.op("act", lambda e, fc=fc, i=i: e.activation(
                sq[i][:], xsl[:, fc * 512:(fc + 1) * 512], AF.Square),
                reads=[xsl_b], writes=[sq_b[i]])
            em.op("pe", lambda e, fc=fc, i=i: e.matmul(
                st[:], onescol[:], sq[i][:], start=(fc == 0), stop=(fc == 15)),
                reads=[onescol_b, sq_b[i]],
                writes=[st_b] if fc == 0 else (), accwrites=() if fc == 0 else [st_b])
        em.op("act", lambda e: e.activation(rows[:], st[:], AF.Sqrt,
                                            scale=1.0 / D),
              reads=[st_b], writes=[rows_b])
        em.op("dve", lambda e: e.tensor_scalar(rows[:], rows[:], eps1[:], None,
                                               ALU.add),
              reads=[rows_b, eps1_b], accwrites=[rows_b])
        em.op("dve", lambda e: e.reciprocal(rows[:], rows[:]),
              reads=[rows_b], accwrites=[rows_b])
        em.op("pe", lambda e: e.matmul(bc[:], onesrow[:], rows[:],
                                       start=True, stop=True),
              reads=[onesrow_b, rows_b], writes=[bc_b])
        em.op("dve", lambda e: e.tensor_copy(cbs[:], bc[:]),
              reads=[bc_b], writes=[cbs_b])
        for fc in range(16):
            em.op("dve", lambda e, fc=fc, ts=ts: e.scalar_tensor_tensor(
                hsl[:, fc * T + ts * 512: fc * T + ts * 512 + 512],
                xsl[:, fc * 512:(fc + 1) * 512], nss[:, fc:fc + 1], cbs[:],
                ALU.mult, ALU.mult),
                reads=[xsl_b, nss_b, cbs_b], accwrites=[hsl_b])
    wf = [ctx.sb(f"p5_wf{i}", [128, 2048], F32) for i in range(2)]
    wf_b = [ctx.buf() for _ in range(2)]
    wb = [ctx.sb(f"p5_wb{i}", [128, 2048], BF16) for i in range(2)]
    wb_b = [ctx.buf() for _ in range(2)]
    ob = [ctx.sb(f"p5_ob{i}", [128, 512], BF16) for i in range(4)]
    ob_b = [ctx.buf() for _ in range(4)]
    pp = [ctx.ps(f"p5_pp{i}", [128, 512]) for i in range(2)]
    pp_b = [ctx.buf() for _ in range(2)]
    jobs = [(wq_dr, ot, qf_dr, ot, 0.125) for ot in range(16)]
    jobs += [(wk_dr, 0, kf_dr, 0, 1.0), (wv_dr, 0, vf_dr, 0, 1.0)]
    for n, (wdr, ot, odr, oot, qsc) in enumerate(jobs):
        i = n % 2
        em.dma("sq", wf[i][:].rearrange("p (c f) -> p c f", c=16), wdr[ot],
               writes=[wf_b[i]])
        em.op("dve", lambda e, i=i: e.tensor_copy(wb[i][:], wf[i][:]),
              reads=[wf_b[i]], writes=[wb_b[i]])
        for ts in range(4):
            j = (n * 4 + ts) % 2
            jo = (n * 4 + ts) % 4

            def mmg(e, i=i, ts=ts, j=j):
                for fc in range(16):
                    ins = e.matmul(pp[j][:], wb[i][:, fc * 128:(fc + 1) * 128],
                                   hsl[:, fc * T + ts * 512: fc * T + ts * 512 + 512],
                                   start=(fc == 0), stop=(fc == 15))
                return ins
            em.op("pe", mmg, reads=[wb_b[i], hsl_b], writes=[pp_b[j]])
            em.op("act", lambda e, j=j, jo=jo, qsc=qsc: e.activation(
                ob[jo][:], pp[j][:], AF.Copy, scale=qsc),
                reads=[pp_b[j]], writes=[ob_b[jo]])
            em.dma("sq", odr[oot * 128:(oot + 1) * 128, ts * 512:(ts + 1) * 512],
                   ob[jo][:], reads=[ob_b[jo]])


def build_p5():
    ctx = Ctx("p5")
    nc = ctx.nc
    x2 = nc.dram_tensor("x2", [D, T], F32, kind="ExternalInput").ap()
    ns = nc.dram_tensor("ns", [128, 16], F32, kind="ExternalInput").ap()
    ss = nc.dram_tensor("ss", [128, 16], F32, kind="ExternalInput").ap()
    wq = nc.dram_tensor("w_l1q", [16, 128, 16, 128], F32, kind="ExternalInput").ap()
    wk = nc.dram_tensor("w_l1k", [1, 128, 16, 128], F32, kind="ExternalInput").ap()
    wv = nc.dram_tensor("w_l1v", [1, 128, 16, 128], F32, kind="ExternalInput").ap()
    qf = nc.dram_tensor("qf", [D, T], BF16, kind="ExternalOutput").ap()
    kf = nc.dram_tensor("kf", [128, T], BF16, kind="ExternalOutput").ap()
    vf = nc.dram_tensor("vf", [128, T], BF16, kind="ExternalOutput").ap()
    emit_p5(ctx, x2, ns, ss, wq, wk, wv, qf, kf, vf)
    ctx.finish()
    return nc



P6_DEBUG = False


def emit_p6(ctx, qf_dr, kf_dr, vf_dr, oatt_dr):
    nc, em = ctx.nc, ctx.em
    i128f, i128f_b = ctx.const("i128f", np.eye(128, dtype=np.float32))
    irs_np = np.zeros((128, 64), dtype=ml_dtypes.bfloat16)
    for p in range(128):
        irs_np[p, p % 64] = 1
    irs, irs_b = ctx.const("irs", irs_np)
    onesall, onesall_b = ctx.const("onesall",
                                   np.ones((128, 128), dtype=ml_dtypes.bfloat16))
    ones64f, ones64f_b = ctx.const("ones64f", np.ones((1, 64), dtype=np.float32))

    qsb = ctx.sb("qsb", [128, 16 * T], BF16); qsb_b = ctx.buf("qsb")
    ksb = ctx.sb("ksb", [128, T], BF16); ksb_b = ctx.buf()
    vsb = ctx.sb("vsb", [128, T], BF16); vsb_b = ctx.buf()
    vtm = ctx.sb("vtm", [128, 32 * 65], BF16); vtm_b = ctx.buf()
    pta = ctx.sb("pta", [128, 16 * 1024], BF16); pta_b = ctx.buf("pta")
    ptb2 = ctx.sb("ptb2", [128, 16 * 1024], BF16); ptb2_b = ctx.buf("ptb2")
    mall = [ctx.sb(f"mall{i}", [128, 32], F32) for i in range(2)]
    mall_b = [ctx.buf() for _ in range(2)]
    mbf = ctx.sb("mbf", [128, 32], BF16); mbf_b = ctx.buf()
    mrow = ctx.sb("mrow", [16, 256], BF16); mrow_b = ctx.buf()
    mneg2 = [ctx.sb(f"mneg2_{i}", [128, T], BF16) for i in range(2)]
    mneg2_b = [ctx.buf() for _ in range(2)]
    mcs = [ctx.sb(f"mcs{i}", [128, 8], F32) for i in range(2)]
    mcs_b = [ctx.buf() for _ in range(2)]
    rcl = ctx.sb("rcl", [1, 1024], F32); rcl_b = ctx.buf()
    bcs = ctx.sb("bcs", [64, 1024], F32); bcs_b = ctx.buf()
    obf = [ctx.sb(f"obf{i}", [64, 512], BF16) for i in range(4)]
    obf_b = [ctx.buf() for _ in range(4)]
    mdr = nc.dram_tensor("p6_mscr", [2, 2, T], BF16, kind="Internal").ap()
    mdr_b = [ctx.buf("mdr0"), ctx.buf("mdr1")]
    if P6_DEBUG:
        dbg_mall = nc.dram_tensor("dbg_mall", [128, 32], F32,
                                  kind="ExternalOutput").ap()
        dbg_mneg = nc.dram_tensor("dbg_mneg", [2, T], BF16,
                                  kind="ExternalOutput").ap()
        dbg_pta = nc.dram_tensor("dbg_pta", [128, 1024], BF16,
                                 kind="ExternalOutput").ap()
        dbg_vtm = nc.dram_tensor("dbg_vtm", [128, 32 * 65], BF16,
                                 kind="ExternalOutput").ap()

    G = [ctx.ps(f"G{i}", [128, 512]) for i in range(8)]
    G_b = [ctx.buf(f"G{i}") for i in range(8)]


    for g in range(2):
        em.dma("sq", qsb[g * 64:(g + 1) * 64, :].rearrange("p (j t) -> p j t", j=16),
               qf_dr.rearrange("(g j p) t -> g p j t", g=2, p=64)[g],
               accwrites=[qsb_b])
    em.dma("sq", ksb[:], kf_dr, writes=[ksb_b])
    em.dma("sq", vsb[:], vf_dr, writes=[vsb_b])
    em.op("pool", lambda e: e.memset(
        vtm[:].rearrange("p (k c) -> p k c", c=65)[:, :, 64:65], 1.0),
        accwrites=[vtm_b])
    g0bf = G[0][:].bitcast(BF16)
    for g in range(2):
        gp = 64 * g
        for kt in range(16):
            em.op("pe", lambda e, gp=gp, kt=kt: e.transpose(
                g0bf[:, 0:64], vsb[gp:gp + 64, kt * 128:(kt + 1) * 128],
                irs[gp:gp + 64, :]),
                reads=[vsb_b, irs_b], writes=[G_b[0]])
            em.op("dve", lambda e, g=g, kt=kt: e.tensor_copy(
                vtm[:, (g * 16 + kt) * 65:(g * 16 + kt) * 65 + 64],
                g0bf[:, 0:64]),
                reads=[G_b[0]], accwrites=[vtm_b])

    N_STEPS = 16

    def emit_stats_step(j, step):
        pj = j % 2
        qt = step
        qq = 0
        c = step % 2
        qof = j * T
        ga, gb = 4 + c, 6 + c

        def smm(e, qt=qt, qq=qq, qof=qof, ga=ga, gb=gb):
            e.matmul(G[ga][:], qsb[0:64, qof + qt * 128: qof + qt * 128 + 128],
                     ksb[0:64, qq * 512:(qq + 1) * 512],
                     start=True, stop=True, tile_position=(0, 0))
            return e.matmul(G[gb][:],
                            qsb[64:128, qof + qt * 128: qof + qt * 128 + 128],
                            ksb[64:128, qq * 512:(qq + 1) * 512],
                            start=True, stop=True, tile_position=(64, 0))
        em.op("pe", smm, reads=[qsb_b, ksb_b], writes=[G_b[ga], G_b[gb]])
        gav = G[ga][:].rearrange("p (f two) -> p f two", two=2)[:, :, 0]
        gbv = G[gb][:].rearrange("p (f two) -> p f two", two=2)[:, :, 0]
        em.op("dve", lambda e, qq=qq, gav=gav, ga=ga, pj=pj: e.reduce_max(
            mcs[pj][:, qq:qq + 1], gav, axis=AX.X),
            reads=[G_b[ga]], accwrites=[mcs_b[pj]])
        em.op("dve", lambda e, qq=qq, gbv=gbv, gb=gb, pj=pj: e.reduce_max(
            mcs[pj][:, 4 + qq:5 + qq], gbv, axis=AX.X),
            reads=[G_b[gb]], accwrites=[mcs_b[pj]])
        def fin3(e, qt=qt, pj=pj, ga=ga, gb=gb):
            gav2 = G[ga][:].rearrange("p (f two) -> p f two", two=2)[:, :, 0]
            return None
        em.op("dve", lambda e, qt=qt, pj=pj: e.tensor_copy(
            mall[pj][:, qt:qt + 1], mcs[pj][:, 0:1]),
            reads=[mcs_b[pj]], accwrites=[mall_b[pj]])
        em.op("dve", lambda e, qt=qt, pj=pj: e.tensor_copy(
            mall[pj][:, 16 + qt:17 + qt], mcs[pj][:, 4:5]),
            reads=[mcs_b[pj]], accwrites=[mall_b[pj]])

    def emit_mrow(j):
        pj = j % 2
        em.op("dve", lambda e, pj=pj: e.tensor_copy(mbf[:], mall[pj][:]),
              reads=[mall_b[pj]], writes=[mbf_b])
        for s, base in ((0, 0), (1, 64)):
            em.op("pe", lambda e, s=s, pj=pj: e.transpose(
                G[2][0:16, 0:128], mall[pj][:, s * 16:(s + 1) * 16], i128f[:]),
                reads=[mall_b[pj], i128f_b], writes=[G_b[2]])
            em.op("act", lambda e, s=s: e.activation(
                mrow[:, s * 128:(s + 1) * 128], G[2][0:16, 0:128], AF.Copy,
                scale=-1.0),
                reads=[G_b[2]], accwrites=[mrow_b])
            em.dma("pool",
                   mdr[pj, s:s + 1, :].rearrange("o (j f) -> (o j) f", j=16),
                   mrow[:, s * 128:(s + 1) * 128], reads=[mrow_b],
                   accwrites=[mdr_b[pj]])
            em.dma("pool", mneg2[pj][base:base + 1, :], mdr[pj, s:s + 1, :],
                   reads=[mdr_b[pj]], accwrites=[mneg2_b[pj]])

    def emit_work_steps(j):
        pj = j % 2
        qof = j * T
        hA, hB = j, 16 + j
        for qh in range(2):
            for kt in range(16):
                for qs in range(2):
                    c = (kt * 2 + qs) % 2
                    def stmm(e, kt=kt, qh=qh, qs=qs, qof=qof, pj=pj, c=c):
                        qsl = slice(qof + qh * 1024 + qs * 512,
                                    qof + qh * 1024 + qs * 512 + 512)
                        msl = slice(qh * 1024 + qs * 512,
                                    qh * 1024 + qs * 512 + 512)
                        e.matmul(G[c][:], ksb[0:64, kt * 128:(kt + 1) * 128],
                                 qsb[0:64, qsl], start=True, stop=False,
                                 tile_position=(0, 0))
                        e.matmul(G[c][:], onesall[0:1, :],
                                 mneg2[pj][0:1, msl],
                                 start=False, stop=True,
                                 tile_position=(0, 0))
                        e.matmul(G[2 + c][:], ksb[64:128, kt * 128:(kt + 1) * 128],
                                 qsb[64:128, qsl], start=True, stop=False,
                                 tile_position=(64, 0))
                        ins = e.matmul(G[2 + c][:], onesall[64:65, :],
                                       mneg2[pj][64:65, msl],
                                       start=False, stop=True,
                                       tile_position=(64, 0))
                        return ins
                    def stw(stmm=stmm, pj=pj, c=c):
                        em.op("pe", stmm,
                              reads=[ksb_b, qsb_b, onesall_b, mneg2_b[pj]],
                              writes=[G_b[c], G_b[2 + c]])
                    yield stw
                    def expw(kt=kt, qs=qs, c=c):
                        off = kt * 1024 + qs * 512
                        em.op("act", lambda e, off=off, c=c: e.activation(
                            pta[:, off:off + 512], G[c][:], AF.Exp),
                            reads=[G_b[c]], accwrites=[pta_b])
                        em.op("act", lambda e, off=off, c=c: e.activation(
                            ptb2[:, off:off + 512], G[2 + c][:], AF.Exp),
                            reads=[G_b[2 + c]], accwrites=[ptb2_b])
                    yield expw
            for t2 in range(2):
                ts = qh * 2 + t2
                for s in range(2):
                    pt_sl, pt_bb = (pta, pta_b) if s == 0 else (ptb2, ptb2_b)
                    h = hA if s == 0 else hB

                    def avw(s=s, t2=t2, pt_sl=pt_sl, pt_bb=pt_bb, h=h, ts=ts):
                        def avmm(e, g=s, t2=t2, pt_sl=pt_sl):
                            for kt in range(16):
                                ins = e.matmul(
                                    G[0][0:65, :],
                                    vtm[:, (g * 16 + kt) * 65:
                                        (g * 16 + kt) * 65 + 65],
                                    pt_sl[:, kt * 1024 + t2 * 512:
                                          kt * 1024 + t2 * 512 + 512],
                                    start=(kt == 0), stop=(kt == 15))
                            return ins
                        em.op("pe", avmm, reads=[vtm_b, pt_bb],
                              writes=[G_b[0]])
                        em.op("dve", lambda e, s=s: e.reciprocal(
                            rcl[:, s * 512:(s + 1) * 512], G[0][64:65, :]),
                            reads=[G_b[0]], accwrites=[rcl_b])
                        em.op("pe", lambda e, s=s: e.matmul(
                            G[1][0:64, :], ones64f[:],
                            rcl[:, s * 512:(s + 1) * 512],
                            start=True, stop=True),
                            reads=[ones64f_b, rcl_b], writes=[G_b[1]])
                        em.op("dve", lambda e, s=s: e.tensor_copy(
                            bcs[:, s * 512:(s + 1) * 512], G[1][0:64, :]),
                            reads=[G_b[1]], accwrites=[bcs_b])
                        ii = s * 2 + t2
                        em.op("dve", lambda e, s=s, ii=ii: e.tensor_tensor(
                            obf[ii][:], G[0][0:64, :],
                            bcs[:, s * 512:(s + 1) * 512], ALU.mult),
                            reads=[G_b[0], bcs_b], writes=[obf_b[ii]])
                        em.dma("sq", oatt_dr[h * 64:(h + 1) * 64,
                                             ts * 512:(ts + 1) * 512],
                               obf[ii][:], reads=[obf_b[ii]])
                    yield avw

    for step in range(N_STEPS):
        emit_stats_step(0, step)
    emit_mrow(0)
    if P6_DEBUG:
        em.dma("sq", dbg_vtm, vtm[:], reads=[vtm_b])
        em.dma("sq", dbg_mall, mall[0][:], reads=[mall_b[0]])
        em.dma("sq", dbg_mneg[0:1, :], mneg2[0][0:1, :], reads=[mneg2_b[0]])
        em.dma("sq", dbg_mneg[1:2, :], mneg2[0][64:65, :], reads=[mneg2_b[0]])
    for j in range(16):
        nxt = iter(range(N_STEPS)) if j + 1 < 16 else iter(())
        for s in range(10):
            s2 = next(nxt, None)
            if s2 is not None:
                emit_stats_step(j + 1, s2)
        wl = list(emit_work_steps(j))
        for i, wfn in enumerate(wl):
            wfn()
            if P6_DEBUG and j == 0 and i == 64:
                em.dma("sq", dbg_pta, pta[:, 0:1024], reads=[pta_b])
            if i % 2 == 1:
                s = next(nxt, None)
                if s is not None:
                    emit_stats_step(j + 1, s)
                    if s == N_STEPS - 1:
                        emit_mrow(j + 1)
        if j + 1 < 16:
            for s in nxt:
                emit_stats_step(j + 1, s)
                if s == N_STEPS - 1:
                    emit_mrow(j + 1)


def build_p6():
    ctx = Ctx("p6")
    nc = ctx.nc
    qf = nc.dram_tensor("qf", [D, T], BF16, kind="ExternalInput").ap()
    kf = nc.dram_tensor("kf", [128, T], BF16, kind="ExternalInput").ap()
    vf = nc.dram_tensor("vf", [128, T], BF16, kind="ExternalInput").ap()
    oatt = nc.dram_tensor("oatt", [D, T], BF16, kind="ExternalOutput").ap()
    emit_p6(ctx, qf, kf, vf, oatt)
    ctx.finish()
    return nc


def build_p7():
    ctx = Ctx("p7")
    nc = ctx.nc
    oatt = nc.dram_tensor("oatt", [16, 128, T], BF16, kind="ExternalInput").ap()
    w_l1o = nc.dram_tensor("w_l1o", [16, 128, 16, 128], F32,
                           kind="ExternalInput").ap()
    x2 = nc.dram_tensor("x2", [D, T], F32, kind="ExternalInput").ap()
    x3 = nc.dram_tensor("x3", [D, T], F32, kind="ExternalOutput").ap()
    x3n = nc.dram_tensor("x3n", [D, T], BF16, kind="ExternalOutput").ap()
    emit_proj_res_ln(ctx, oatt, w_l1o, x2, x3, x3n, res_scale=2.0)
    ctx.finish()
    return nc



def emit_p8(ctx, x3n_dr, wrt_dr, amp_dr, esc_dr, wtil_dr):
    nc, em = ctx.nc, ctx.em
    i128f, i128f_b = ctx.const("i128f", np.eye(128, dtype=np.float32))
    i128b, i128b_b = ctx.const("i128b", np.eye(128, dtype=ml_dtypes.bfloat16))
    xb = ctx.sb("p8_xb", [128, 16 * T], BF16); xb_b = ctx.buf()
    em.dma("sq", xb[:].rearrange("p (c t) -> p c t", c=16),
           x3n_dr.rearrange("(c p) t -> p c t", p=128), writes=[xb_b])
    wrf = ctx.sb("p8_wrf", [128, 2048], F32); wrf_b = ctx.buf()
    wrb = ctx.sb("p8_wrb", [128, 2048], BF16); wrb_b = ctx.buf()
    em.dma("sq", wrf[:].rearrange("p (c f) -> p c f", c=16), wrt_dr[0],
           writes=[wrf_b])
    em.op("dve", lambda e: e.tensor_copy(wrb[:], wrf[:]),
          reads=[wrf_b], writes=[wrb_b])
    lgs = ctx.sb("p8_lgs", [128, T], F32); lgs_b = ctx.buf()
    ltm = ctx.sb("p8_ltm", [128, T], F32); ltm_b = ctx.buf()
    afm = ctx.sb("p8_afm", [128, T], BF16); afm_b = ctx.buf()
    nm = ctx.sb("p8_nm", [128, 1], F32); nm_b = ctx.buf()
    pex = ctx.sb("p8_pex", [128, 128], F32); pex_b = ctx.buf()
    pc = [ctx.sb(f"p8_pc{i}", [128, 128], F32) for i in range(2)]
    pc_b = [ctx.buf() for _ in range(2)]
    zz = ctx.sb("p8_zz", [128, 4], F32); zz_b = ctx.buf()
    alp = ctx.sb("p8_alp", [128, 128], F32); alp_b = ctx.buf()
    atm = ctx.sb("p8_atm", [128, 128], BF16); atm_b = ctx.buf()
    amp_sb = ctx.sb("p8_amp", [128, 16], F32); amp_b = ctx.buf()
    esc_sb = ctx.sb("p8_esc", [128, 1], F32); esc_b = ctx.buf()
    sqa = ctx.sb("p8_sqa", [128, 16], F32); sqa_b = ctx.buf()
    pk8 = ctx.sb("p8_pk8", [128, 8], F32); pk8_b = ctx.buf()
    mixb = ctx.sb("p8_mixb", [128, 8], BF16); mixb_b = ctx.buf()
    wts = ctx.sb("p8_wts", [8, T], BF16); wts_b = ctx.buf()
    pp = ctx.ps("p8_pp", [128, 512]); pp_b = ctx.buf()
    ptt = ctx.ps("p8_ptt", [128, 128]); ptt_b = ctx.buf()
    ptb = ctx.ps("p8_ptb", [128, 128], BF16); ptb_b = ctx.buf()
    pw = ctx.ps("p8_pw", [8, 512]); pw_b = ctx.buf()
    em.dma("pool", amp_sb[:], amp_dr, writes=[amp_b])
    em.dma("pool", esc_sb[:], esc_dr.rearrange("(p o) -> p o", o=1), writes=[esc_b])
    for ts in range(4):
        def rmm(e, ts=ts):
            for fc in range(16):
                ins = e.matmul(pp[:], wrb[:, fc * 128:(fc + 1) * 128],
                               xb[:, fc * T + ts * 512: fc * T + ts * 512 + 512],
                               start=(fc == 0), stop=(fc == 15))
            return ins
        em.op("pe", rmm, reads=[wrb_b, xb_b], writes=[pp_b])
        em.op("dve", lambda e, ts=ts: e.tensor_copy(
            lgs[:, ts * 512:(ts + 1) * 512], pp[:]),
            reads=[pp_b], accwrites=[lgs_b])
    for tt in range(16):
        em.op("pe", lambda e, tt=tt: e.transpose(
            ptt[:], lgs[:, tt * 128:(tt + 1) * 128], i128f[:]),
            reads=[lgs_b, i128f_b], writes=[ptt_b])
        em.op("dve", lambda e, tt=tt: e.tensor_copy(
            ltm[:, tt * 128:(tt + 1) * 128], ptt[:]),
            reads=[ptt_b], accwrites=[ltm_b])
        L = ltm[:, tt * 128:(tt + 1) * 128]
        em.op("dve", lambda e, L=L: e.reduce_max(nm[:], L, axis=AX.X, negate=True),
              reads=[ltm_b], writes=[nm_b])
        em.op("act", lambda e, L=L: e.activation(pex[:], L, AF.Exp, bias=nm[:]),
              reads=[ltm_b, nm_b], writes=[pex_b])
        em.op("dve", lambda e: e.reduce_sum(zz[:, 0:1], pex[:], axis=AX.X),
              reads=[pex_b], writes=[zz_b])
        em.op("dve", lambda e: e.reduce_max(zz[:, 1:2], pex[:], axis=AX.X),
              reads=[pex_b], accwrites=[zz_b])
        src, src_b = pex, pex_b
        for j in range(3):
            dst, dst_b = pc[j % 2], pc_b[j % 2]
            em.op("dve", lambda e, src=src, dst=dst: e.scalar_tensor_tensor(
                dst[:], src[:], zz[:, 1:2], src[:], ALU.is_lt, ALU.mult),
                reads=[src_b, zz_b], writes=[dst_b])
            em.op("dve", lambda e, dst=dst: e.reduce_max(zz[:, 1:2], dst[:],
                                                         axis=AX.X),
                  reads=[dst_b], accwrites=[zz_b])
            src, src_b = dst, dst_b
        em.op("dve", lambda e: e.scalar_tensor_tensor(
            alp[:], pex[:], zz[:, 1:2], pex[:], ALU.is_ge, ALU.mult),
            reads=[pex_b, zz_b], writes=[alp_b])
        em.op("dve", lambda e: e.reduce_sum(zz[:, 2:3], alp[:], axis=AX.X),
              reads=[alp_b], accwrites=[zz_b])
        em.op("dve", lambda e: e.scalar_tensor_tensor(
            zz[:, 3:4], zz[:, 0:1], 1e-8, zz[:, 2:3], ALU.mult, ALU.add),
            reads=[zz_b], accwrites=[zz_b])
        em.op("dve", lambda e: e.reciprocal(zz[:, 3:4], zz[:, 3:4]),
              reads=[zz_b], accwrites=[zz_b])
        em.op("dve", lambda e: e.tensor_scalar(atm[:], alp[:], zz[:, 3:4], None,
                                               ALU.mult),
              reads=[alp_b, zz_b], writes=[atm_b])
        em.op("pe", lambda e: e.transpose(ptb[:], atm[:], i128b[:]),
              reads=[atm_b, i128b_b], writes=[ptb_b])
        em.op("dve", lambda e, tt=tt: e.tensor_copy(
            afm[:, tt * 128:(tt + 1) * 128], ptb[:]),
            reads=[ptb_b], accwrites=[afm_b])
    em.op("act", lambda e: e.activation(sqa[:], amp_sb[:], AF.Square),
          reads=[amp_b], writes=[sqa_b])
    sqv = sqa[:].rearrange("p (k j) -> p k j", j=2)
    em.op("dve", lambda e: e.tensor_tensor(pk8[:], sqv[:, :, 0], sqv[:, :, 1],
                                           ALU.add),
          reads=[sqa_b], writes=[pk8_b])
    em.op("dve", lambda e: e.reduce_sum(zz[:, 0:1], pk8[:], axis=AX.X),
          reads=[pk8_b], writes=[zz_b])
    em.op("dve", lambda e: e.tensor_scalar(zz[:, 0:1], zz[:, 0:1], 1e-8, None,
                                           ALU.add),
          reads=[zz_b], accwrites=[zz_b])
    em.op("dve", lambda e: e.reciprocal(zz[:, 0:1], zz[:, 0:1]),
          reads=[zz_b], accwrites=[zz_b])
    em.op("dve", lambda e: e.tensor_scalar(mixb[:], pk8[:], zz[:, 0:1],
                                           esc_sb[:], ALU.mult, ALU.mult),
          reads=[pk8_b, zz_b, esc_b], writes=[mixb_b])
    for ts in range(4):
        em.op("pe", lambda e, ts=ts: e.matmul(
            pw[:], mixb[:], afm[:, ts * 512:(ts + 1) * 512],
            start=True, stop=True),
            reads=[mixb_b, afm_b], writes=[pw_b])
        em.op("act", lambda e, ts=ts: e.activation(
            wts[:, ts * 512:(ts + 1) * 512], pw[:], AF.Copy),
            reads=[pw_b], accwrites=[wts_b])
    em.dma("sq", wtil_dr, wts[:], reads=[wts_b])


def build_p8():
    ctx = Ctx("p8")
    nc = ctx.nc
    x3n = nc.dram_tensor("x3n", [D, T], BF16, kind="ExternalInput").ap()
    wrt = nc.dram_tensor("w_rt", [1, 128, 16, 128], F32, kind="ExternalInput").ap()
    amp = nc.dram_tensor("amp", [128, 16], F32, kind="ExternalInput").ap()
    esc = nc.dram_tensor("esc", [128], F32, kind="ExternalInput").ap()
    wtil = nc.dram_tensor("wtil", [8, T], BF16, kind="ExternalOutput").ap()
    emit_p8(ctx, x3n, wrt, amp, esc, wtil)
    ctx.finish()
    return nc



def emit_p9(ctx, x3n_dr, wtil_dr, wg_dr, wu_dr, ysc_dr):
    nc, em = ctx.nc, ctx.em
    sel_np = np.zeros((8, 1024), dtype=ml_dtypes.bfloat16)
    for k in range(8):
        sel_np[k, k * 128:(k + 1) * 128] = 1
    sel, sel_b = ctx.const("sel8", sel_np)
    xb = ctx.sb("p9_xb", [128, 16 * T], BF16); xb_b = ctx.buf()
    em.dma("sq", xb[:].rearrange("p (c t) -> p c t", c=16),
           x3n_dr.rearrange("(c p) t -> p c t", p=128), writes=[xb_b])
    wtl = ctx.sb("p9_wtl", [8, T], BF16); wtl_b = ctx.buf()
    em.dma("pool", wtl[:], wtil_dr, writes=[wtl_b])
    wb_sl = ctx.sb("p9_wb", [128, T], BF16); wb_b = ctx.buf("p9_wb")
    ya = ctx.sb("p9_ya", [128, 8 * T], F32); ya_b = ctx.buf("p9_ya")
    for ic in range(8):
        em.op("pool", lambda e, ic=ic: e.memset(ya[:, ic * T:(ic + 1) * T], 0.0),
              accwrites=[ya_b])
    wgf = [ctx.sb(f"p9_wgf{i}", [128, 2048], F32) for i in range(2)]
    wgf_b = [ctx.buf() for _ in range(2)]
    wuf = [ctx.sb(f"p9_wuf{i}", [128, 2048], F32) for i in range(2)]
    wuf_b = [ctx.buf() for _ in range(2)]
    wgb = [ctx.sb(f"p9_wgb{i}", [128, 2048], BF16) for i in range(2)]
    wgb_b = [ctx.buf() for _ in range(2)]
    wub = [ctx.sb(f"p9_wub{i}", [128, 2048], BF16) for i in range(2)]
    wub_b = [ctx.buf() for _ in range(2)]
    gs = ctx.sb("p9_gs", [128, 512], F32); gs_b = ctx.buf()
    bas = ctx.sb("p9_bas", [128, 512], F32); bas_b = ctx.buf()
    prod = ctx.sb("p9_prod", [128, 512], F32); prod_b = ctx.buf()
    ysb = ctx.sb("p9_ysb", [128, T], BF16); ysb_b = ctx.buf()
    pg = [ctx.ps(f"p9_pg{i}", [128, 512]) for i in range(2)]
    pg_b = [ctx.buf() for _ in range(2)]
    pu = [ctx.ps(f"p9_pu{i}", [128, 512]) for i in range(2)]
    pu_b = [ctx.buf() for _ in range(2)]
    pbc = ctx.ps("p9_pbc", [128, 512]); pbc_b = ctx.buf()
    for k in range(8):
        for ts in range(4):
            em.op("pe", lambda e, k=k, ts=ts: e.matmul(
                pbc[:], sel[:, k * 128:(k + 1) * 128],
                wtl[:, ts * 512:(ts + 1) * 512], start=True, stop=True),
                reads=[sel_b, wtl_b], writes=[pbc_b])
            em.op("act", lambda e, ts=ts: e.activation(
                wb_sl[:, ts * 512:(ts + 1) * 512], pbc[:], AF.Copy),
                reads=[pbc_b], accwrites=[wb_b])
        for ic in range(8):
            ot = k * 8 + ic
            i = ot % 2
            em.dma("sq", wgf[i][:].rearrange("p (c f) -> p c f", c=16),
                   wg_dr[ot], writes=[wgf_b[i]])
            em.dma("sq", wuf[i][:].rearrange("p (c f) -> p c f", c=16),
                   wu_dr[ot], writes=[wuf_b[i]])
            em.op("act", lambda e, i=i: e.activation(wgb[i][:], wgf[i][:],
                                                     AF.Copy),
                  reads=[wgf_b[i]], writes=[wgb_b[i]])
            em.op("act", lambda e, i=i: e.activation(wub[i][:], wuf[i][:],
                                                     AF.Copy),
                  reads=[wuf_b[i]], writes=[wub_b[i]])
            for ts in range(4):
                j = ts % 2

                def mmg(e, i=i, ts=ts, j=j):
                    for fc in range(16):
                        e.matmul(pg[j][:], wgb[i][:, fc * 128:(fc + 1) * 128],
                                 xb[:, fc * T + ts * 512: fc * T + ts * 512 + 512],
                                 start=(fc == 0), stop=(fc == 15))
                    for fc in range(16):
                        ins = e.matmul(pu[j][:], wub[i][:, fc * 128:(fc + 1) * 128],
                                       xb[:, fc * T + ts * 512: fc * T + ts * 512 + 512],
                                       start=(fc == 0), stop=(fc == 15))
                    return ins
                em.op("pe", mmg, reads=[wgb_b[i], wub_b[i], xb_b],
                      writes=[pg_b[j], pu_b[j]])
                em.op("act", lambda e, j=j: e.activation(gs[:], pg[j][:], AF.Silu),
                      reads=[pg_b[j]], writes=[gs_b])
                em.op("dve", lambda e, j=j: e.tensor_tensor(
                    bas[:], pu[j][:], gs[:], ALU.mult),
                    reads=[pu_b[j], gs_b], writes=[bas_b])
                em.op("pool", lambda e, ts=ts: e.tensor_tensor(
                    prod[:], bas[:], wb_sl[:, ts * 512:(ts + 1) * 512], ALU.mult),
                    reads=[bas_b, wb_b], writes=[prod_b])
                em.op("dve", lambda e, ic=ic, ts=ts: e.tensor_tensor(
                    ya[:, ic * T + ts * 512: ic * T + ts * 512 + 512],
                    ya[:, ic * T + ts * 512: ic * T + ts * 512 + 512],
                    prod[:], ALU.add),
                    reads=[prod_b, ya_b], accwrites=[ya_b])
    for ic in range(8):
        em.op("dve", lambda e, ic=ic: e.tensor_copy(
            ysb[:], ya[:, ic * T:(ic + 1) * T]),
            reads=[ya_b], writes=[ysb_b])
        em.dma("sq", ysc_dr[ic], ysb[:], reads=[ysb_b])


def build_p9():
    ctx = Ctx("p9")
    nc = ctx.nc
    x3n = nc.dram_tensor("x3n", [D, T], BF16, kind="ExternalInput").ap()
    wtil = nc.dram_tensor("wtil", [8, T], BF16, kind="ExternalInput").ap()
    wg = nc.dram_tensor("w_g", [64, 128, 16, 128], F32, kind="ExternalInput").ap()
    wu = nc.dram_tensor("w_u", [64, 128, 16, 128], F32, kind="ExternalInput").ap()
    ysc = nc.dram_tensor("ysc", [8, 128, T], BF16, kind="ExternalOutput").ap()
    emit_p9(ctx, x3n, wtil, wg, wu, ysc)
    ctx.finish()
    return nc



def emit_p10(ctx, x3n_dr, wsg_dr, wsu_dr, shsc_dr):
    nc, em = ctx.nc, ctx.em
    xb = ctx.sb("pA_xb", [128, 16 * T], BF16); xb_b = ctx.buf()
    em.dma("sq", xb[:].rearrange("p (c t) -> p c t", c=16),
           x3n_dr.rearrange("(c p) t -> p c t", p=128), writes=[xb_b])
    wgf = [ctx.sb(f"pA_wgf{i}", [128, 2048], F32) for i in range(2)]
    wgf_b = [ctx.buf() for _ in range(2)]
    wuf = [ctx.sb(f"pA_wuf{i}", [128, 2048], F32) for i in range(2)]
    wuf_b = [ctx.buf() for _ in range(2)]
    wgb = [ctx.sb(f"pA_wgb{i}", [128, 2048], BF16) for i in range(2)]
    wgb_b = [ctx.buf() for _ in range(2)]
    wub = [ctx.sb(f"pA_wub{i}", [128, 2048], BF16) for i in range(2)]
    wub_b = [ctx.buf() for _ in range(2)]
    gs = ctx.sb("pA_gs", [128, 512], F32); gs_b = ctx.buf()
    shb = [ctx.sb(f"pA_shb{i}", [128, T], BF16) for i in range(2)]
    shb_b = [ctx.buf() for _ in range(2)]
    pg = [ctx.ps(f"pA_pg{i}", [128, 512]) for i in range(2)]
    pg_b = [ctx.buf() for _ in range(2)]
    pu = [ctx.ps(f"pA_pu{i}", [128, 512]) for i in range(2)]
    pu_b = [ctx.buf() for _ in range(2)]
    for ot in range(16):
        i = ot % 2
        em.dma("sq", wgf[i][:].rearrange("p (c f) -> p c f", c=16), wsg_dr[ot],
               writes=[wgf_b[i]])
        em.dma("sq", wuf[i][:].rearrange("p (c f) -> p c f", c=16), wsu_dr[ot],
               writes=[wuf_b[i]])
        em.op("dve", lambda e, i=i: e.tensor_copy(wgb[i][:], wgf[i][:]),
              reads=[wgf_b[i]], writes=[wgb_b[i]])
        em.op("dve", lambda e, i=i: e.tensor_copy(wub[i][:], wuf[i][:]),
              reads=[wuf_b[i]], writes=[wub_b[i]])
        for ts in range(4):
            j = ts % 2

            def mmg(e, i=i, ts=ts, j=j):
                for fc in range(16):
                    e.matmul(pg[j][:], wgb[i][:, fc * 128:(fc + 1) * 128],
                             xb[:, fc * T + ts * 512: fc * T + ts * 512 + 512],
                             start=(fc == 0), stop=(fc == 15))
                for fc in range(16):
                    ins = e.matmul(pu[j][:], wub[i][:, fc * 128:(fc + 1) * 128],
                                   xb[:, fc * T + ts * 512: fc * T + ts * 512 + 512],
                                   start=(fc == 0), stop=(fc == 15))
                return ins
            em.op("pe", mmg, reads=[wgb_b[i], wub_b[i], xb_b],
                  writes=[pg_b[j], pu_b[j]])
            em.op("act", lambda e, j=j: e.activation(gs[:], pg[j][:], AF.Silu),
                  reads=[pg_b[j]], writes=[gs_b])
            em.op("dve", lambda e, i=i, ts=ts, j=j: e.tensor_tensor(
                shb[i][:, ts * 512:(ts + 1) * 512], pu[j][:], gs[:], ALU.mult),
                reads=[pu_b[j], gs_b], accwrites=[shb_b[i]])
        em.dma("sq", shsc_dr[ot], shb[i][:], reads=[shb_b[i]])


def build_p10():
    ctx = Ctx("p10")
    nc = ctx.nc
    x3n = nc.dram_tensor("x3n", [D, T], BF16, kind="ExternalInput").ap()
    wsg = nc.dram_tensor("w_sg", [16, 128, 16, 128], F32, kind="ExternalInput").ap()
    wsu = nc.dram_tensor("w_su", [16, 128, 16, 128], F32, kind="ExternalInput").ap()
    shsc = nc.dram_tensor("shsc", [16, 128, T], BF16, kind="ExternalOutput").ap()
    emit_p10(ctx, x3n, wsg, wsu, shsc)
    ctx.finish()
    return nc



def emit_p11(ctx, ysc_dr, shsc_dr, wd_dr, wsd_dr, x3_dr, yT_dr):
    nc, em = ctx.nc, ctx.em
    ln = LNRes(ctx)
    wdb = ctx.sb("pB_wdb", [128, 16 * 1024], BF16); wdb_b = ctx.buf()
    wsdb = ctx.sb("pB_wsdb", [128, 16 * 2048], BF16); wsdb_b = ctx.buf()
    wf = [ctx.sb(f"pB_wf{i}", [128, 2048], F32) for i in range(2)]
    wf_b = [ctx.buf() for _ in range(2)]
    for ot in range(16):
        i = ot % 2
        em.dma("sq", wf[i][:, 0:1024].rearrange("p (c f) -> p c f", c=8),
               wd_dr[ot], writes=[wf_b[i]])
        em.op("dve", lambda e, ot=ot, i=i: e.tensor_copy(
            wdb[:, ot * 1024:(ot + 1) * 1024], wf[i][:, 0:1024]),
            reads=[wf_b[i]], accwrites=[wdb_b])
        em.dma("sq", wf[i][:].rearrange("p (c f) -> p c f", c=16),
               wsd_dr[ot], writes=[wf_b[i]])
        em.op("dve", lambda e, ot=ot, i=i: e.tensor_copy(
            wsdb[:, ot * 2048:(ot + 1) * 2048], wf[i][:]),
            reads=[wf_b[i]], accwrites=[wsdb_b])
    ysl = ctx.sb("pB_ysl", [128, 8 * 512], BF16); ysl_b = ctx.buf()
    shsl = ctx.sb("pB_shsl", [128, 16 * 512], BF16); shsl_b = ctx.buf()
    xc = [ctx.sb(f"pB_xc{i}", [128, 512], F32) for i in range(2)]
    xc_b = [ctx.buf() for _ in range(2)]
    rsl = ctx.sb("pB_rsl", [128, 16 * 512], F32); rsl_b = ctx.buf()
    pp = [ctx.ps(f"pB_ps{i}", [128, 512]) for i in range(2)]
    pp_b = [ctx.buf() for _ in range(2)]
    for ts in range(4):
        em.dma("sq", ysl[:].rearrange("p (c f) -> p c f", c=8),
               ysc_dr[:, :, ts * 512:(ts + 1) * 512].rearrange("i p f -> p i f"),
               writes=[ysl_b])
        em.dma("sq", shsl[:].rearrange("p (c f) -> p c f", c=16),
               shsc_dr[:, :, ts * 512:(ts + 1) * 512].rearrange("i p f -> p i f"),
               writes=[shsl_b])
        for d2t in range(16):
            j = d2t % 2
            em.dma("pool", xc[j][:],
                   x3_dr[d2t * 128:(d2t + 1) * 128, ts * 512:(ts + 1) * 512],
                   writes=[xc_b[j]])

            def mmg(e, d2t=d2t, j=j):
                for kc in range(8):
                    e.matmul(pp[j][:],
                             wdb[:, d2t * 1024 + kc * 128: d2t * 1024 + kc * 128 + 128],
                             ysl[:, kc * 512:(kc + 1) * 512],
                             start=(kc == 0), stop=False)
                for fc in range(16):
                    ins = e.matmul(
                        pp[j][:],
                        wsdb[:, d2t * 2048 + fc * 128: d2t * 2048 + fc * 128 + 128],
                        shsl[:, fc * 512:(fc + 1) * 512],
                        start=False, stop=(fc == 15))
                return ins
            em.op("pe", mmg, reads=[wdb_b, wsdb_b, ysl_b, shsl_b],
                  writes=[pp_b[j]])
            em.op("dve", lambda e, d2t=d2t, j=j: e.scalar_tensor_tensor(
                rsl[:, d2t * 512:(d2t + 1) * 512], xc[j][:], 1.0, pp[j][:],
                ALU.mult, ALU.add),
                reads=[xc_b[j], pp_b[j]], accwrites=[rsl_b])
        emit_ln(ctx, ln, rsl, rsl_b,
                lambda fc, ts=ts: yT_dr[fc * 128:(fc + 1) * 128,
                                        ts * 512:(ts + 1) * 512],
                None)


def build_p11():
    ctx = Ctx("p11")
    nc = ctx.nc
    ysc = nc.dram_tensor("ysc", [8, 128, T], BF16, kind="ExternalInput").ap()
    shsc = nc.dram_tensor("shsc", [16, 128, T], BF16, kind="ExternalInput").ap()
    wd = nc.dram_tensor("w_d", [16, 128, 8, 128], F32, kind="ExternalInput").ap()
    wsd = nc.dram_tensor("w_sd", [16, 128, 16, 128], F32,
                         kind="ExternalInput").ap()
    x3 = nc.dram_tensor("x3", [D, T], F32, kind="ExternalInput").ap()
    yT = nc.dram_tensor("yT", [D, T], F32, kind="ExternalOutput").ap()
    emit_p11(ctx, ysc, shsc, wd, wsd, x3, yT)
    ctx.finish()
    return nc




def build_p0():
    ctx = Ctx("p0")
    nc = ctx.nc
    xT = nc.dram_tensor("xT", [D, T], F32, kind="ExternalInput").ap()
    w_l0p = nc.dram_tensor("w_l0p", [NWT, 128, 16, 128], F32,
                           kind="ExternalInput").ap()
    proj = nc.dram_tensor("proj", [4, NWT, 128, 512], BF16,
                          kind="ExternalOutput").ap()
    emit_p0(ctx, xT, w_l0p, proj)
    ctx.finish()
    return nc



def prep_weights(ii):
    f32 = lambda a: np.asarray(a, np.float32)
    colT = lambda a, n: np.ascontiguousarray(f32(a).reshape(n, 128).T)
    w = {}
    w["w_l0p"] = host_pack_l0_proj(ii["L0_attn_Wq"], ii["L0_attn_Wk"],
                                   ii["L0_attn_Ws"], ii["L0_attn_Wv"],
                                   ii["L0_attn_Wg"])
    w["gnw"] = f32(ii["L0_attn_g_norm_weight"])
    w["w_l0o"] = pack_lhsT(f32(ii["L0_attn_Wo"]))
    w["w_l0g"] = pack_lhsT(u8_of_i32(np.asarray(ii["L0_ffn_gate_w"])).T)
    w["w_l0u"] = pack_lhsT(u8_of_i32(np.asarray(ii["L0_ffn_up_w"])).T)
    w["w_l0d"] = pack_lhsT(u8_of_i32(np.asarray(ii["L0_ffn_down_w"])).T)
    w["gsc"] = colT(ii["L0_ffn_gate_scale"], 64)
    w["usc"] = colT(ii["L0_ffn_up_scale"], 64)
    w["dsc"] = colT(ii["L0_ffn_down_scale"], 16)
    w["ns"] = colT(ii["L1_attn_norm_scale"], 16)
    w["ss"] = colT(ii["L1_attn_split_scale"], 16)
    w["w_l1q"] = pack_lhsT(f32(ii["L1_attn_Wq"]).T)
    w["w_l1k"] = pack_lhsT(f32(ii["L1_attn_Wk"]).T)
    w["w_l1v"] = pack_lhsT(f32(ii["L1_attn_Wv"]).T)
    w["w_l1o"] = pack_lhsT(f32(ii["L1_attn_Wo"]).T)
    w["w_rt"] = pack_lhsT(f32(ii["L1_ffn_w_router"]))
    w["amp"] = np.ascontiguousarray(f32(ii["L1_ffn_amplitudes"]).reshape(128, 16))
    w["esc"] = f32(ii["L1_ffn_expert_scale"])
    w["w_g"] = pack_lhsT(f32(ii["L1_ffn_w_gate"]))
    w["w_u"] = pack_lhsT(f32(ii["L1_ffn_w_up"]))
    w["w_d"] = pack_lhsT(f32(ii["L1_ffn_w_down"]))
    w["w_sg"] = pack_lhsT(f32(ii["L1_ffn_shared_w_gate"]))
    w["w_su"] = pack_lhsT(f32(ii["L1_ffn_shared_w_up"]))
    w["w_sd"] = pack_lhsT(f32(ii["L1_ffn_shared_w_down"]))
    return w


_BUILD_CACHE = {}


def _get(name, fn):
    if name not in _BUILD_CACHE:
        _BUILD_CACHE[name] = fn()
    return _BUILD_CACHE[name]



def build_full():
    ctx = Ctx("full")
    nc = ctx.nc
    EI = lambda n, s, d: nc.dram_tensor(n, s, d, kind="ExternalInput").ap()
    IN = lambda n, s, d: nc.dram_tensor(n, s, d, kind="Internal").ap()
    xT = EI("xT", [D, T], F32)
    w_l0p = EI("w_l0p", [NWT, 128, 16, 128], F32)
    gnw = EI("gnw", [128], F32)
    w_l0o = EI("w_l0o", [16, 128, 16, 128], F32)
    w_l0g = EI("w_l0g", [64, 128, 16, 128], U8)
    w_l0u = EI("w_l0u", [64, 128, 16, 128], U8)
    w_l0d = EI("w_l0d", [16, 128, 64, 128], U8)
    gsc = EI("gsc", [128, 64], F32)
    usc = EI("usc", [128, 64], F32)
    dsc = EI("dsc", [128, 16], F32)
    ns = EI("ns", [128, 16], F32)
    ss = EI("ss", [128, 16], F32)
    w_l1q = EI("w_l1q", [16, 128, 16, 128], F32)
    w_l1k = EI("w_l1k", [1, 128, 16, 128], F32)
    w_l1v = EI("w_l1v", [1, 128, 16, 128], F32)
    w_l1o = EI("w_l1o", [16, 128, 16, 128], F32)
    w_rt = EI("w_rt", [1, 128, 16, 128], F32)
    amp = EI("amp", [128, 16], F32)
    esc = EI("esc", [128], F32)
    w_g = EI("w_g", [64, 128, 16, 128], F32)
    w_u = EI("w_u", [64, 128, 16, 128], F32)
    w_d = EI("w_d", [16, 128, 8, 128], F32)
    w_sg = EI("w_sg", [16, 128, 16, 128], F32)
    w_su = EI("w_su", [16, 128, 16, 128], F32)
    w_sd = EI("w_sd", [16, 128, 16, 128], F32)
    yT = nc.dram_tensor("yT", [D, T], F32, kind="ExternalOutput").ap()

    proj = IN("s_proj", [4, NWT, 128, 512], BF16)
    og = IN("s_og", [16, 128, T], BF16)
    x1 = IN("s_x1", [D, T], F32)
    x1n = IN("s_x1n", [D, T], BF16)
    hsc = IN("s_hsc", [64, 128, T], BF16)
    x2 = IN("s_x2", [D, T], F32)
    qf = IN("s_qf", [D, T], BF16)
    kf = IN("s_kf", [128, T], BF16)
    vf = IN("s_vf", [128, T], BF16)
    oatt = IN("s_oatt", [16, 128, T], BF16)
    x3 = IN("s_x3", [D, T], F32)
    x3n = IN("s_x3n", [D, T], BF16)
    wtil = IN("s_wtil", [8, T], BF16)
    ysc = IN("s_ysc", [8, 128, T], BF16)
    shsc = IN("s_shsc", [16, 128, T], BF16)

    phases = [
        ("P0_", lambda: emit_p0(ctx, xT, w_l0p, proj)),
        ("P1_", lambda: emit_p1(ctx, proj, gnw, og)),
        ("P2_", lambda: emit_proj_res_ln(ctx, og, w_l0o, xT, x1, x1n, 1.0)),
        ("P3_", lambda: emit_p3(ctx, x1n, w_l0g, w_l0u, gsc, usc, hsc)),
        ("P4_", lambda: emit_p4(ctx, hsc, w_l0d, dsc, x1, x2, None)),
        ("P5_", lambda: emit_p5(ctx, x2, ns, ss, w_l1q, w_l1k, w_l1v,
                                qf, kf, vf)),
        ("P6_", lambda: emit_p6(ctx, qf, kf, vf,
                                oatt.rearrange("h p t -> (h p) t"))),
        ("P7_", lambda: emit_proj_res_ln(ctx, oatt, w_l1o, x2, x3, x3n, 2.0)),
        ("P8_", lambda: emit_p8(ctx, x3n, w_rt, amp, esc, wtil)),
        ("P9_", lambda: emit_p9(ctx, x3n, wtil, w_g, w_u, ysc)),
        ("PA_", lambda: emit_p10(ctx, x3n, w_sg, w_su, shsc)),
        ("PB_", lambda: emit_p11(ctx, ysc, shsc, w_d, w_sd, x3, yT)),
    ]
    for pref, fn in phases:
        ctx.push_scope(pref)
        fn()
        ctx.pop_scope()
    ctx.finish()
    return nc


def kernel(**inputs):
    import os
    if int(os.environ.get("MK_UNFUSED", "0")):
        return kernel_multilaunch(**inputs)
    w = prep_weights(inputs)
    x = np.asarray(inputs["L0_attn_hidden_states"], np.float32)
    B = x.shape[0]
    cores = list(range(B))
    nc = _get("full", build_full)
    maps = [dict(w, xT=np.ascontiguousarray(x[b].T)) for b in cores]
    res = run_bass_kernel_spmd(nc, maps, core_ids=cores)
    return np.stack([np.ascontiguousarray(res.results[b]["yT"].T)
                     for b in cores])


def kernel_multilaunch(**inputs):
    w = prep_weights(inputs)
    x = np.asarray(inputs["L0_attn_hidden_states"], np.float32)
    B = x.shape[0]
    cores = list(range(B))
    xT = [np.ascontiguousarray(x[b].T) for b in range(B)]

    def launch(name, build_fn, per_core):
        nc = _get(name, build_fn)
        res = run_bass_kernel_spmd(nc, per_core, core_ids=cores)
        return res.results

    r = launch("p0", build_p0, [dict(xT=xT[b], w_l0p=w["w_l0p"]) for b in cores])
    proj = [r[b]["proj"] for b in cores]
    r = launch("p1", build_p1, [dict(proj=proj[b], gnw=w["gnw"]) for b in cores])
    og = [r[b]["og"] for b in cores]
    r = launch("p2", build_p2, [dict(og=og[b], w_l0o=w["w_l0o"], xT=xT[b])
                                for b in cores])
    x1 = [r[b]["x1"] for b in cores]
    x1n = [r[b]["x1n"] for b in cores]
    r = launch("p3", build_p3, [dict(x1n=x1n[b], w_l0g=w["w_l0g"],
                                     w_l0u=w["w_l0u"], gsc=w["gsc"],
                                     usc=w["usc"]) for b in cores])
    hsc = [r[b]["hsc"] for b in cores]
    r = launch("p4", build_p4, [dict(hsc=hsc[b], w_l0d=w["w_l0d"],
                                     dsc=w["dsc"], x1=x1[b]) for b in cores])
    x2 = [r[b]["x2"] for b in cores]
    r = launch("p5", build_p5, [dict(x2=x2[b], ns=w["ns"], ss=w["ss"],
                                     w_l1q=w["w_l1q"], w_l1k=w["w_l1k"],
                                     w_l1v=w["w_l1v"]) for b in cores])
    qf = [r[b]["qf"] for b in cores]
    kf = [r[b]["kf"] for b in cores]
    vf = [r[b]["vf"] for b in cores]
    r = launch("p6", build_p6, [dict(qf=qf[b], kf=kf[b], vf=vf[b])
                                for b in cores])
    oatt = [r[b]["oatt"] for b in cores]
    r = launch("p7", build_p7, [dict(oatt=oatt[b].reshape(16, 128, T),
                                     w_l1o=w["w_l1o"], x2=x2[b])
                                for b in cores])
    x3 = [r[b]["x3"] for b in cores]
    x3n = [r[b]["x3n"] for b in cores]
    r = launch("p8", build_p8, [dict(x3n=x3n[b], w_rt=w["w_rt"], amp=w["amp"],
                                     esc=w["esc"]) for b in cores])
    wtil = [r[b]["wtil"] for b in cores]
    r = launch("p9", build_p9, [dict(x3n=x3n[b], wtil=wtil[b], w_g=w["w_g"],
                                     w_u=w["w_u"]) for b in cores])
    ysc = [r[b]["ysc"] for b in cores]
    r = launch("p10", build_p10, [dict(x3n=x3n[b], w_sg=w["w_sg"],
                                       w_su=w["w_su"]) for b in cores])
    shsc = [r[b]["shsc"] for b in cores]
    r = launch("p11", build_p11, [dict(ysc=ysc[b], shsc=shsc[b], w_d=w["w_d"],
                                       w_sd=w["w_sd"], x3=x3[b])
                                  for b in cores])
    out = np.stack([np.ascontiguousarray(r[b]["yT"].T) for b in cores])
    return out


# version 12 (final)
# speedup vs baseline: 1.3153x; 1.0127x over previous
import contextlib
import numpy as np
import ml_dtypes
import concourse.bass as bass
import concourse.mybir as mybir
from concourse.bass_utils import run_bass_kernel_spmd

F32 = mybir.dt.float32
BF16 = mybir.dt.bfloat16
U8 = mybir.dt.uint8
AF = mybir.ActivationFunctionType
ALU = mybir.AluOpType
AX = mybir.AxisListType

NCORES = 8
T = 2048
D = 2048


ENGINES = ("pe", "dve", "act", "pool", "sq")
NLANES = 4
_ENG_ATTR = {"pe": "tensor", "dve": "vector", "act": "scalar",
             "pool": "gpsimd", "sq": "sync"}


class Buf:
    __slots__ = ("name", "writers", "readers")

    def __init__(self, name=""):
        self.name = name
        self.writers = []
        self.readers = []


class _Op:
    __slots__ = ("fn", "deps", "inc", "observed")

    def __init__(self, fn, deps, inc):
        self.fn = fn
        self.deps = deps
        self.inc = inc
        self.observed = False


class Em:

    def __init__(self, nc):
        self.nc = nc
        self.ops = {e: [] for e in ENGINES}
        self._barrier = []
        self.phase_marks = []
        self.inst_marks = None

    def mark(self, label):
        self.phase_marks.append((label, {e: len(self.ops[e]) for e in ENGINES}))

    def barrier(self):
        bar = [(e, len(self.ops[e]) - 1) for e in ENGINES if self.ops[e]]
        for q in ("sq", "pool"):
            found = 0
            for i in range(len(self.ops[q]) - 1, -1, -1):
                if self.ops[q][i].inc == 16:
                    bar.append((q, i))
                    found += 1
                    if found >= NLANES:
                        break
        self._barrier = bar

    def op(self, eng, fn, reads=(), writes=(), accwrites=(), deps=(), dma=False):
        dep = set(deps)
        dep.update(self._barrier)
        for b in reads:
            dep.update(b.writers)
        for b in writes:
            dep.update(b.writers)
            dep.update(b.readers)
        for b in accwrites:
            dep.update(b.readers)
        tok = (eng, len(self.ops[eng]))
        dep.discard(tok)
        self.ops[eng].append(_Op(fn, sorted(dep), 16 if dma else 1))
        for b in reads:
            b.readers.append(tok)
        for b in writes:
            b.writers = [tok]
            b.readers = []
        for b in accwrites:
            b.writers.append(tok)
        return tok

    def dma(self, eng, out, in_, **bufs):
        eng = "pool" if type(out.tensor).__name__ == "DRamTensorHandle" else "sq"
        return self.op(eng, lambda e, o=out, i=in_: e.dma_start(out=o, in_=i),
                       dma=True, **bufs)

    def lower(self, es):
        sems = {e: es.enter_context(self.nc.semaphore(f"sem_{e}"))
                for e in ENGINES}
        lsem = {q: [es.enter_context(self.nc.semaphore(f"sem_{q}L{l}"))
                    for l in range(NLANES)] for q in ("sq", "pool")}
        for e in ENGINES:
            for i, op in enumerate(self.ops[e]):
                for (fe, fi) in op.deps:
                    if fe == "pe" and e == "pe":
                        continue
                    self.ops[fe][fi].observed = True
        for e in ("sq", "pool"):
            for op in self.ops[e]:
                if op.inc == 16:
                    op.observed = True
        cum = {}
        lane_final = {q: [0] * NLANES for q in ("sq", "pool")}
        lanes = {}
        for e in ENGINES:
            c = 0
            lcnt = [0] * NLANES
            ldma = 0
            lst = []
            lns = []
            for op in self.ops[e]:
                if e in ("sq", "pool") and op.inc == 16:
                    lane = ldma % NLANES
                    ldma += 1
                    lcnt[lane] += 16
                    lns.append(lane)
                    lst.append(((e, lane), lcnt[lane]))
                    lane_final[e][lane] = lcnt[lane]
                else:
                    lns.append(None)
                    if op.observed:
                        c += op.inc
                    lst.append((e, c))
            cum[e] = lst
            lanes[e] = lns
        block = es.enter_context(self.nc.Block())
        em = self

        def play(eng_name, eng_obj):
            hiwater = {}

            def wait(key, v):
                if v > hiwater.get(key, 0):
                    s = (lsem[key[0]][key[1]] if isinstance(key, tuple)
                         else sems[key])
                    eng_obj.wait_ge(s, v)
                    hiwater[key] = v

            for i, op in enumerate(em.ops[eng_name]):
                ln = lanes[eng_name][i]
                if ln is not None:
                    key, v = cum[eng_name][i]
                    if v > 16:
                        wait(key, v - 16)
                need = {}
                for (fe, fi) in op.deps:
                    if fe == "pe" and eng_name == "pe":
                        continue
                    key, v = cum[fe][fi]
                    if v > need.get(key, 0):
                        need[key] = v
                for key, v in sorted(need.items(), key=str):
                    wait(key, v)
                ins = op.fn(eng_obj)
                if ln is not None:
                    ins.then_inc(lsem[eng_name][ln], 16)
                elif op.observed:
                    ins.then_inc(sems[eng_name], op.inc)
            if eng_name in ("sq", "pool"):
                for l in range(NLANES):
                    if lane_final[eng_name][l]:
                        wait((eng_name, l), lane_final[eng_name][l])

        @block.tensor
        def _(t):
            play("pe", t)

        @block.vector
        def _(v):
            play("dve", v)

        @block.scalar
        def _(s):
            play("act", s)

        @block.gpsimd
        def _(g):
            play("pool", g)

        @block.sync
        def _(s):
            play("sq", s)



def pack_lhsT(M):
    K, N = M.shape
    A = np.asarray(M).reshape(K // 128, 128, N // 128, 128)
    return np.ascontiguousarray(A.transpose(2, 1, 0, 3))


def u8_of_i32(a):
    a = np.ascontiguousarray(a)
    assert a.dtype == np.int32
    return np.ascontiguousarray(a.view(np.uint8).reshape(*a.shape, 4)[..., 0])



class Ctx:

    def __init__(self, name):
        self.nc = bass.Bass("TRN2", target_bir_lowering=False)
        self.em = Em(self.nc)
        self.es = contextlib.ExitStack()
        self._scope = None
        self.prefix = ""
        self._consts = {}
        self._nbuf = 0

    def push_scope(self, prefix):
        assert self._scope is None
        self._scope = contextlib.ExitStack()
        self.prefix = prefix

    def pop_scope(self):
        self.em.barrier()
        self._scope.close()
        self._scope = None
        self.prefix = ""

    def sb(self, name, shape, dtype):
        es = self._scope if self._scope is not None else self.es
        t = es.enter_context(self.nc.sbuf_tensor(self.prefix + name,
                                                 list(shape), dtype))
        return t

    def ps(self, name, shape, dtype=F32):
        es = self._scope if self._scope is not None else self.es
        t = es.enter_context(self.nc.psum_tensor(self.prefix + name,
                                                 list(shape), dtype))
        return t

    def buf(self, name=""):
        self._nbuf += 1
        return Buf(name or f"b{self._nbuf}")

    def const(self, key, arr):
        if key in self._consts:
            return self._consts[key]
        arr = np.ascontiguousarray(arr)
        dt = {np.dtype(np.float32): F32,
              np.dtype(ml_dtypes.bfloat16): BF16,
              np.dtype(np.uint8): U8}[arr.dtype]
        dr = self.nc.inline_tensor(arr, name=f"c_{key}").ap()
        sb = self.es.enter_context(
            self.nc.sbuf_tensor(f"cs_{key}", list(arr.shape), dt))
        b = self.buf(f"const_{key}")
        self.em.dma("pool", sb[:], dr, accwrites=[b])
        self._consts[key] = (sb, b)
        return self._consts[key]

    def cvec(self, value, parts=128, dtype=None):
        key = ("cv", value, parts)
        if key in self._consts:
            return self._consts[key]
        sb = self.es.enter_context(
            self.nc.sbuf_tensor(f"cv_{len(self._consts)}", [parts, 1], F32))
        b = self.buf()
        self.em.op("pool", lambda e: e.memset(sb[:], value), accwrites=[b])
        self._consts[key] = (sb, b)
        return self._consts[key]

    def finish(self):
        self.em.lower(self.es)
        self.es.close()
        return self.nc


def np_triu_inc(n=128):
    return np.triu(np.ones((n, n))).astype(ml_dtypes.bfloat16)



NWT = 56
WT_Q0, WT_K0, WT_S0, WT_V0, WT_G0 = 0, 8, 16, 24, 40


def host_pack_l0_proj(wq, wk, ws, wv, wg):
    mats = [pack_lhsT(np.asarray(m, np.float32)) for m in (wq, wk, ws, wv, wg)]
    return np.concatenate(mats, axis=0)


def emit_p0(ctx, xT, w_l0p, proj):
    nc, em = ctx.nc, ctx.em
    xb = ctx.sb("p0_xb", [128, 16 * T], BF16)
    xf = [ctx.sb(f"p0_xf{i}", [128, T], F32) for i in range(2)]
    xf_b = [ctx.buf() for _ in range(2)]
    xb_b = ctx.buf("xb")
    for dc in range(16):
        i = dc % 2
        em.dma("sq", xf[i][:], xT[dc * 128:(dc + 1) * 128, :], writes=[xf_b[i]])
        em.op("dve",
              lambda e, dc=dc, i=i: e.tensor_copy(xb[:, dc * T:(dc + 1) * T],
                                                  xf[i][:]),
              reads=[xf_b[i]], accwrites=[xb_b])
    wf = [ctx.sb(f"p0_wf{i}", [128, 16 * 128], F32) for i in range(2)]
    wb = [ctx.sb(f"p0_wb{i}", [128, 16 * 128], BF16) for i in range(2)]
    wf_b = [ctx.buf() for _ in range(2)]
    wb_b = [ctx.buf() for _ in range(2)]
    pp = [ctx.ps(f"p0_ps{i}", [128, 512]) for i in range(4)]
    pp_b = [ctx.buf(f"psbank{i}") for i in range(4)]
    ob = [ctx.sb(f"p0_ob{i}", [128, 512], BF16) for i in range(4)]
    ob_b = [ctx.buf() for _ in range(4)]
    for wt in range(NWT):
        i = wt % 2
        em.dma("sq", wf[i][:].rearrange("p (c f) -> p c f", c=16),
               w_l0p[wt], writes=[wf_b[i]])
        em.op("dve", lambda e, i=i: e.tensor_copy(wb[i][:], wf[i][:]),
              reads=[wf_b[i]], writes=[wb_b[i]])
        for ts in range(4):
            j = (wt * 4 + ts) % 4
            def mmgroup(e, i=i, ts=ts, j=j):
                for dc in range(16):
                    ins = e.matmul(pp[j][:], wb[i][:, dc * 128:(dc + 1) * 128],
                                   xb[:, dc * T + ts * 512: dc * T + ts * 512 + 512],
                                   start=(dc == 0), stop=(dc == 15))
                return ins
            em.op("pe", mmgroup, reads=[wb_b[i], xb_b], writes=[pp_b[j]])
            scale = 0.125 if wt < 8 else 1.0
            em.op("act", lambda e, j=j, scale=scale: e.activation(
                ob[j][:], pp[j][:], AF.Copy, scale=scale),
                reads=[pp_b[j]], writes=[ob_b[j]])
            tg, cc = ts, None
            em.dma("sq", proj[ts, wt], ob[j][:], reads=[ob_b[j]])



def emit_p1(ctx, proj, gnw_dr, og):
    nc, em = ctx.nc, ctx.em
    H, C = 16, 128

    triu, triu_b = ctx.const("triu", np_triu_inc(128))
    i128b, i128b_b = ctx.const("i128b", np.eye(128, dtype=ml_dtypes.bfloat16))
    i128f, i128f_b = ctx.const("i128f", np.eye(128, dtype=np.float32))
    irs_np = np.zeros((128, 64), dtype=ml_dtypes.bfloat16)
    for p in range(128):
        irs_np[p, p % 64] = 1
    irs, irs_b = ctx.const("irs", irs_np)
    onesr, onesr_b = ctx.const("onesr", np.ones((1, 128), dtype=ml_dtypes.bfloat16))
    onesrf, onesrf_b = ctx.const("onesrf", np.ones((1, 128), dtype=np.float32))
    onescol, onescol_b = ctx.const("onescol", np.ones((128, 1), dtype=ml_dtypes.bfloat16))
    o128, o128_b = ctx.const("o128", np.full((128, 1), 1.0 / 128, dtype=np.float32))
    eps5, eps5_b = ctx.cvec(1e-5)

    gnw = ctx.sb("gnw_sb", [128, 1], F32)
    gnw_b = ctx.buf("gnw")
    em.dma("pool", gnw[:], gnw_dr.rearrange("(p o) -> p o", o=1), writes=[gnw_b])

    slab = [ctx.sb(f"slab{i}", [128, NWT * 512], BF16) for i in range(2)]
    slab_b = [ctx.buf(f"slab{i}") for i in range(2)]
    ogs = [ctx.sb(f"ogs{i}", [128, H * 512], BF16) for i in range(2)]
    ogs_b = [ctx.buf(f"ogs{i}") for i in range(2)]
    cUWf = ctx.sb("cUWf", [128, 8 * 192], F32)
    cUWf_b = [ctx.buf(f"cUWf{h}") for h in range(16)]
    cUWb = ctx.sb("cUWb", [128, 8 * 192], BF16)
    cUWb_b = [ctx.buf(f"cUWb{h}") for h in range(16)]
    asb = ctx.sb("asb", [1, H * 64], F32)
    asb_b = [ctx.buf(f"asb{h}") for h in range(16)]
    abf = ctx.sb("abf", [1, H * 64], BF16)
    abf_b = [ctx.buf(f"abf{h}") for h in range(16)]

    sc_ps = ctx.ps("sc_ps", [128, 128]); sc_pb = ctx.buf("sc")
    a_ps = ctx.ps("a_ps", [128, 512]); a_pb = ctx.buf("a")
    ok_ps = ctx.ps("ok_ps", [128, 64]); ok_pb = ctx.buf("ok")
    ct_ps = ctx.ps("ct_ps", [128, 512]); ct_pb = ctx.buf("ct")
    ov_ps = ctx.ps("ov_ps", [128, 128]); ov_pb = ctx.buf("ov")
    upd = [ctx.ps(f"upd{i}", [128, 192]) for i in range(2)]
    upd_b = [ctx.buf(f"upd{i}") for i in range(2)]
    trp = ctx.ps("trp", [128, 384], BF16); trp_pb = ctx.buf("trp")

    def sbuf2(name, shape, dtype):
        ts_ = [ctx.sb(f"{name}{i}", list(shape), dtype) for i in range(2)]
        bs = [ctx.buf(f"{name}{i}") for i in range(2)]
        return ts_, bs
    e_tm, e_tm_b = sbuf2("e_tm", [128, 64], BF16)
    k_tm, k_tm_b = sbuf2("k_tm", [128, 64], BF16)
    v_tm, v_tm_b = sbuf2("v_tm", [128, 128], BF16)
    skq, skq_b = sbuf2("skq", [128, 128], BF16)
    recA, recA_b = sbuf2("recA", [128, 64], F32)
    okn, okn_b = sbuf2("okn", [128, 64], F32)
    nmx, nmx_b = sbuf2("nmx", [128, 1], F32)
    pve, pve_b = sbuf2("pve", [128, 64], F32)
    ell, ell_b = sbuf2("ell", [128, 1], F32)
    rcl, rcl_b = sbuf2("rcl", [128, 1], F32)
    pab, pab_b = sbuf2("pab", [128, 64], BF16)
    pam, pam_b = sbuf2("pam", [128, 128], BF16)
    ctb, ctb_b = sbuf2("ctb", [128, 128], BF16)
    sqv, sqv_b = sbuf2("sqv", [128, 128], F32)
    ovs, ovs_b = sbuf2("ovs", [128, 128], F32)
    rsd, rsd_b = sbuf2("rsd", [128, 1], F32)
    rsT, rsT_b = sbuf2("rsT", [1, 128], F32)
    rsB, rsB_b = sbuf2("rsB", [128, 128], F32)
    gt1, gt1_b = sbuf2("gt1", [128, 128], F32)

    def sl(i, wt, part, nparts, c):
        return slab[i][part:part + nparts,
                       wt * 512 + c * 128: wt * 512 + (c + 1) * 128]

    def prep_group(tg, i):
        em.dma("sq", slab[i][:].rearrange("p (w f) -> p w f", w=NWT),
               proj[tg].rearrange("w p f -> p w f"), writes=[slab_b[i]])
        for q in range(8):
            wt = WT_S0 + q
            em.op("dve", lambda e, wt=wt, i=i: e.tensor_scalar(
                slab[i][:, wt * 512:(wt + 1) * 512],
                slab[i][:, wt * 512:(wt + 1) * 512],
                32.0, -32.0, ALU.min, ALU.max),
                reads=[slab_b[i]], accwrites=[slab_b[i]])
            em.op("act", lambda e, wt=wt, i=i: e.activation(
                slab[i][:, wt * 512:(wt + 1) * 512],
                slab[i][:, wt * 512:(wt + 1) * 512], AF.Exp),
                reads=[slab_b[i]], accwrites=[slab_b[i]])
        for q in range(16):
            wt = WT_G0 + q
            em.op("act", lambda e, wt=wt, i=i: e.activation(
                slab[i][:, wt * 512:(wt + 1) * 512],
                slab[i][:, wt * 512:(wt + 1) * 512], AF.Silu),
                reads=[slab_b[i]], accwrites=[slab_b[i]])

    prep_group(0, 0)
    for tg in range(4):
        i = tg % 2
        if tg + 1 < 4:
            prep_group(tg + 1, (tg + 1) % 2)
        oi = tg % 2
        if True:
            def head_steps(c, h, i, oi, first):
                steps = []
                _op = em.op
                def op_rec(*a, **k):
                    steps.append((a, k))
                em.op = op_rec
                par = h % 2
                qp = 64 * par
                qwt, kwt, swt = WT_Q0 + h // 2, WT_K0 + h // 2, WT_S0 + h // 2
                vwt, gwt = WT_V0 + h, WT_G0 + h
                pr = h // 2
                u_bf = cUWb[qp:qp + 64, pr * 192:pr * 192 + 64]
                w_bf = cUWb[qp:qp + 64, pr * 192 + 64:pr * 192 + 192]
                uw_f = cUWf[qp:qp + 64, pr * 192:pr * 192 + 192]
                uw_b16 = cUWb[qp:qp + 64, pr * 192:pr * 192 + 192]
                a_f32 = asb[0:1, h * 64:(h + 1) * 64]
                a_bf = abf[0:1, h * 64:(h + 1) * 64]
                trE = trp[:, 0:64]
                trK = trp[:, 64:128]
                trV = trp[:, 128:256]
                trP = trp[qp:qp + 64, 256:384]

                em.op("pe", lambda e, swt=swt, qp=qp, c=c, i=i: e.transpose(
                    trE, sl(i, swt, qp, 64, c), irs[qp:qp + 64, :]),
                    reads=[slab_b[i], irs_b], writes=[trp_pb])
                em.op("dve", lambda e, par=par: e.tensor_copy(
                    e_tm[par][:], trE),
                    reads=[trp_pb], writes=[e_tm_b[par]])
                em.op("pe", lambda e, kwt=kwt, qp=qp, c=c, i=i: e.transpose(
                    trK, sl(i, kwt, qp, 64, c), irs[qp:qp + 64, :]),
                    reads=[slab_b[i], irs_b], writes=[trp_pb])
                em.op("dve", lambda e, par=par: e.tensor_copy(
                    k_tm[par][:], trK),
                    reads=[trp_pb], writes=[k_tm_b[par]])
                em.op("pe", lambda e, vwt=vwt, c=c, i=i: e.transpose(
                    trV, sl(i, vwt, 0, 128, c), i128b[:]),
                    reads=[slab_b[i], i128b_b], writes=[trp_pb])
                em.op("dve", lambda e, par=par: e.tensor_copy(
                    v_tm[par][:], trV),
                    reads=[trp_pb], writes=[v_tm_b[par]])
                if not first:
                    em.op("dve", lambda e, uw_f=uw_f, uw_b16=uw_b16:
                          e.tensor_copy(uw_b16, uw_f),
                          reads=[cUWf_b[h]], writes=[cUWb_b[h]])
                    em.op("dve", lambda e, a_f32=a_f32, a_bf=a_bf:
                          e.tensor_copy(a_bf, a_f32),
                          reads=[asb_b[h]], writes=[abf_b[h]])
                em.op("pe", lambda e, kwt=kwt, qwt=qwt, qp=qp, c=c, i=i: e.matmul(
                    sc_ps[:], sl(i, kwt, qp, 64, c), sl(i, qwt, qp, 64, c),
                    start=True, stop=True),
                    reads=[slab_b[i]], writes=[sc_pb])
                em.op("dve", lambda e, par=par: e.tensor_tensor(
                    skq[par][:], sc_ps[:], triu[:], ALU.mult),
                    reads=[sc_pb, triu_b], writes=[skq_b[par]])
                def a_mm(e, first=first, a_bf=a_bf, par=par):
                    ins = e.matmul(a_ps[:, 0:64], triu[:], e_tm[par][:],
                                   start=True, stop=first)
                    if not first:
                        ins = e.matmul(a_ps[:, 0:64], onesr[:], a_bf,
                                       start=False, stop=True)
                    return ins
                em.op("pe", a_mm,
                      reads=[triu_b, e_tm_b[par], onesr_b, abf_b[h]],
                      writes=[a_pb])
                em.op("dve", lambda e, par=par: e.reciprocal(
                    recA[par][:], a_ps[:, 0:64]),
                    reads=[a_pb], writes=[recA_b[par]])
                def ok_mm(e, first=first, qwt=qwt, qp=qp, c=c, i=i,
                          u_bf=u_bf, par=par):
                    ins = e.matmul(ok_ps[:], skq[par][:], e_tm[par][:],
                                   start=True, stop=first)
                    if not first:
                        ins = e.matmul(ok_ps[:], sl(i, qwt, qp, 64, c), u_bf,
                                       start=False, stop=True)
                    return ins
                em.op("pe", ok_mm,
                      reads=[skq_b[par], e_tm_b[par], slab_b[i], cUWb_b[h]],
                      writes=[ok_pb])
                em.op("dve", lambda e, par=par: e.tensor_tensor(
                    okn[par][:], ok_ps[:], recA[par][:], ALU.mult),
                    reads=[ok_pb, recA_b[par]], writes=[okn_b[par]])
                em.op("dve", lambda e, par=par: e.reduce_max(
                    nmx[par][:], okn[par][:], axis=AX.X, negate=True),
                    reads=[okn_b[par]], writes=[nmx_b[par]])
                em.op("act", lambda e, par=par: e.activation(
                    pve[par][:], okn[par][:], AF.Exp, bias=nmx[par][:]),
                    reads=[okn_b[par], nmx_b[par]], writes=[pve_b[par]])
                em.op("dve", lambda e, par=par: e.reduce_sum(
                    ell[par][:], pve[par][:], axis=AX.X),
                    reads=[pve_b[par]], writes=[ell_b[par]])
                em.op("dve", lambda e, par=par: e.reciprocal(
                    rcl[par][:], ell[par][:]),
                    reads=[ell_b[par]], writes=[rcl_b[par]])
                em.op("dve", lambda e, par=par: e.scalar_tensor_tensor(
                    pab[par][:], pve[par][:], rcl[par][:], recA[par][:],
                    ALU.mult, ALU.mult),
                    reads=[pve_b[par], rcl_b[par], recA_b[par]],
                    writes=[pab_b[par]])
                em.op("pe", lambda e, qp=qp, par=par, trP=trP: e.transpose(
                    trP, pab[par][:], i128b[:], tile_position=(0, qp)),
                    reads=[pab_b[par], i128b_b], writes=[trp_pb])
                em.op("dve", lambda e, qp=qp, par=par: e.tensor_copy(
                    pam[par][qp:qp + 64, :], trp[qp:qp + 64, 256:384]),
                    reads=[trp_pb], accwrites=[pam_b[par]])
                em.op("pe", lambda e, swt=swt, qp=qp, c=c, i=i, par=par: e.matmul(
                    ct_ps[:, 0:128], sl(i, swt, qp, 64, c),
                    pam[par][qp:qp + 64, :], start=True, stop=True),
                    reads=[slab_b[i], pam_b[par]], writes=[ct_pb])
                em.op("dve", lambda e, par=par: e.tensor_tensor(
                    ctb[par][:], ct_ps[:, 0:128], triu[:], ALU.mult),
                    reads=[ct_pb, triu_b], writes=[ctb_b[par]])
                def ov_mm(e, first=first, qp=qp, w_bf=w_bf, par=par):
                    ins = e.matmul(ov_ps[:], v_tm[par][:], ctb[par][:],
                                   start=True, stop=first)
                    if not first:
                        ins = e.matmul(ov_ps[:], w_bf, pam[par][qp:qp + 64, :],
                                       start=False, stop=True)
                    return ins
                em.op("pe", ov_mm,
                      reads=[v_tm_b[par], ctb_b[par], cUWb_b[h], pam_b[par]],
                      writes=[ov_pb])
                em.op("dve", lambda e, par=par: e.tensor_copy(
                    ovs[par][:], ov_ps[:]),
                    reads=[ov_pb], writes=[ovs_b[par]])
                em.op("pool", lambda e, par=par: e.tensor_tensor(
                    sqv[par][:], ovs[par][:], ovs[par][:], ALU.mult),
                    reads=[ovs_b[par]], writes=[sqv_b[par]])
                em.op("pe", lambda e, par=par: e.matmul(
                    a_ps[:, 128:129], sqv[par][:], o128[:],
                    start=True, stop=True),
                    reads=[sqv_b[par], o128_b], writes=[a_pb])
                em.op("act", lambda e, par=par: e.activation(
                    rsd[par][:], a_ps[:, 128:129], AF.Ln, bias=eps5[:]),
                    reads=[a_pb, eps5_b], writes=[rsd_b[par]])
                em.op("act", lambda e, par=par: e.activation(
                    rsd[par][:], rsd[par][:], AF.Exp, scale=-0.5),
                    reads=[rsd_b[par]], accwrites=[rsd_b[par]])
                em.op("pe", lambda e, par=par: e.transpose(
                    ct_ps[0:1, 384:512], rsd[par][:], i128f[:]),
                    reads=[rsd_b[par], i128f_b], writes=[ct_pb])
                em.op("dve", lambda e, par=par: e.tensor_copy(
                    rsT[par][:], ct_ps[0:1, 384:512]),
                    reads=[ct_pb], writes=[rsT_b[par]])
                em.op("pe", lambda e, par=par: e.matmul(
                    ct_ps[:, 256:384], onesrf[:], rsT[par][:],
                    start=True, stop=True),
                    reads=[onesrf_b, rsT_b[par]], writes=[ct_pb])
                em.op("dve", lambda e, par=par: e.tensor_copy(
                    rsB[par][:], ct_ps[:, 256:384]),
                    reads=[ct_pb], writes=[rsB_b[par]])
                em.op("pool", lambda e, par=par: e.tensor_tensor(
                    gt1[par][:], ovs[par][:], rsB[par][:], ALU.mult),
                    reads=[ovs_b[par], rsB_b[par]], writes=[gt1_b[par]])
                em.op("dve", lambda e, h=h, c=c, gwt=gwt, i=i, oi=oi, par=par:
                      e.scalar_tensor_tensor(
                          ogs[oi][:, h * 512 + c * 128: h * 512 + (c + 1) * 128],
                          gt1[par][:], gnw[:], sl(i, gwt, 0, 128, c),
                          ALU.mult, ALU.mult),
                      reads=[gt1_b[par], gnw_b, slab_b[i]],
                      accwrites=[ogs_b[oi]])
                uj = h % 2

                def upd_mm(e, qp=qp, uj=uj, par=par):
                    e.matmul(upd[uj][qp:qp + 64, 0:64], k_tm[par][:],
                             e_tm[par][:], start=True, stop=True,
                             tile_position=(0, qp))
                    return e.matmul(upd[uj][qp:qp + 64, 64:192], e_tm[par][:],
                                    v_tm[par][:], start=True, stop=True,
                                    tile_position=(0, qp))
                em.op("pe", upd_mm,
                      reads=[k_tm_b[par], e_tm_b[par], v_tm_b[par]],
                      writes=[upd_b[uj]])
                if first:
                    em.op("dve", lambda e, uw_f=uw_f, qp=qp, uj=uj: e.tensor_copy(
                        uw_f, upd[uj][qp:qp + 64, :]),
                        reads=[upd_b[uj]], writes=[cUWf_b[h]])
                else:
                    em.op("dve", lambda e, uw_f=uw_f, qp=qp, uj=uj: e.tensor_tensor(
                        uw_f, uw_f, upd[uj][qp:qp + 64, :], ALU.add),
                        reads=[upd_b[uj], cUWf_b[h]], accwrites=[cUWf_b[h]])
                em.op("pe", lambda e, par=par: e.matmul(
                    a_ps[0:1, 192:256], onescol[:], e_tm[par][:],
                    start=True, stop=True),
                    reads=[onescol_b, e_tm_b[par]], writes=[a_pb])
                if first:
                    em.op("dve", lambda e, a_f32=a_f32: e.tensor_copy(
                        a_f32, a_ps[0:1, 192:256]),
                        reads=[a_pb], writes=[asb_b[h]])
                else:
                    em.op("dve", lambda e, a_f32=a_f32: e.tensor_tensor(
                        a_f32, a_f32, a_ps[0:1, 192:256], ALU.add),
                        reads=[a_pb, asb_b[h]], accwrites=[asb_b[h]])
                em.op = _op
                return steps

            for c in range(4):
                first = (tg == 0 and c == 0)
                GR = 1000
                for hp in range(0, H, 2):
                    sa = head_steps(c, hp, i, oi, first)
                    sb2 = head_steps(c, hp + 1, i, oi, first)
                    na, nb = len(sa), len(sb2)
                    ia = ib = 0
                    while ia < na or ib < nb:
                        for _ in range(GR):
                            if ia < na:
                                a, k = sa[ia]; em.op(*a, **k); ia += 1
                        for _ in range(GR):
                            if ib < nb:
                                a, k = sb2[ib]; em.op(*a, **k); ib += 1
        for hh in range(H):
            em.dma("sq", og[hh, :, tg * 512:(tg + 1) * 512],
                   ogs[oi][:, hh * 512:(hh + 1) * 512], reads=[ogs_b[oi]])


def build_p1():
    ctx = Ctx("p1")
    nc = ctx.nc
    proj = nc.dram_tensor("proj", [4, NWT, 128, 512], BF16,
                          kind="ExternalInput").ap()
    gnw = nc.dram_tensor("gnw", [128], F32, kind="ExternalInput").ap()
    og = nc.dram_tensor("og", [16, 128, T], BF16, kind="ExternalOutput").ap()
    emit_p1(ctx, proj, gnw, og)
    ctx.finish()
    return nc



class LNRes:
    def __init__(self, ctx):
        self.onescol, self.onescol_b = ctx.const(
            "onescolf", np.ones((128, 1), dtype=np.float32))
        self.onesrow, self.onesrow_b = ctx.const(
            "onesrowf", np.ones((1, 128), dtype=np.float32))
        self.eps1, self.eps1_b = ctx.cvec(1e-5, parts=1)
        self.st1 = ctx.ps("ln_st1", [1, 512]); self.st1_b = ctx.buf("ln_st1")
        self.st2 = ctx.ps("ln_st2", [1, 512]); self.st2_b = ctx.buf("ln_st2")
        self.bca = ctx.ps("ln_bca", [128, 512]); self.bca_b = ctx.buf("ln_bca")
        self.bcb = ctx.ps("ln_bcb", [128, 512]); self.bcb_b = ctx.buf("ln_bcb")
        self.sq = [ctx.sb(f"ln_sq{i}", [128, 512], F32) for i in range(2)]
        self.sq_b = [ctx.buf() for _ in range(2)]
        self.rows = ctx.sb("ln_rows", [1, 512 * 5], F32)
        self.rows_b = ctx.buf("ln_rows")
        self.absb = ctx.sb("ln_absb", [128, 512], F32); self.absb_b = ctx.buf()
        self.t1 = ctx.sb("ln_t1", [128, 512], F32); self.t1_b = ctx.buf()
        self.xo = [ctx.sb(f"ln_xo{i}", [128, 512], F32) for i in range(2)]
        self.xo_b = [ctx.buf() for _ in range(2)]
        self.xn = [ctx.sb(f"ln_xn{i}", [128, 512], BF16) for i in range(2)]
        self.xn_b = [ctx.buf() for _ in range(2)]


def emit_ln(ctx, ln, rsl, rsl_b, xo_dst, xn_dst):
    em = ctx.em
    R = lambda fc: rsl[:, fc * 512:(fc + 1) * 512]
    mrow = ln.rows[:, 0:512]
    qrow = ln.rows[:, 512:1024]
    var = ln.rows[:, 1024:1536]
    arow = ln.rows[:, 1536:2048]
    brow = ln.rows[:, 2048:2560]
    for fc in range(16):
        i = fc % 2
        em.op("act", lambda e, fc=fc, i=i: e.activation(
            ln.sq[i][:], R(fc), AF.Square), reads=[rsl_b], writes=[ln.sq_b[i]])
        em.op("pe", lambda e, fc=fc: e.matmul(
            ln.st1[:], ln.onescol[:], R(fc), start=(fc == 0), stop=(fc == 15)),
            reads=[ln.onescol_b, rsl_b],
            writes=[ln.st1_b] if fc == 0 else (), accwrites=() if fc == 0 else [ln.st1_b])
        em.op("pe", lambda e, i=i, fc=fc: e.matmul(
            ln.st2[:], ln.onescol[:], ln.sq[i][:], start=(fc == 0), stop=(fc == 15)),
            reads=[ln.onescol_b, ln.sq_b[i]],
            writes=[ln.st2_b] if fc == 0 else (), accwrites=() if fc == 0 else [ln.st2_b])
    em.op("act", lambda e: e.activation(mrow, ln.st1[:], AF.Copy, scale=1.0 / D),
          reads=[ln.st1_b], accwrites=[ln.rows_b])
    em.op("act", lambda e: e.activation(qrow, ln.st2[:], AF.Copy, scale=1.0 / D),
          reads=[ln.st2_b], accwrites=[ln.rows_b])
    em.op("dve", lambda e: e.tensor_tensor(var, mrow, mrow, ALU.mult),
          reads=[ln.rows_b], accwrites=[ln.rows_b])
    em.op("dve", lambda e: e.tensor_tensor(var, qrow, var, ALU.subtract),
          reads=[ln.rows_b], accwrites=[ln.rows_b])
    em.op("act", lambda e: e.activation(var, var, AF.Sqrt, bias=ln.eps1[:]),
          reads=[ln.rows_b, ln.eps1_b], accwrites=[ln.rows_b])
    em.op("dve", lambda e: e.reciprocal(arow, var),
          reads=[ln.rows_b], accwrites=[ln.rows_b])
    em.op("dve", lambda e: e.scalar_tensor_tensor(
        brow, mrow, -1.0, arow, ALU.mult, ALU.mult),
        reads=[ln.rows_b], accwrites=[ln.rows_b])
    em.op("pe", lambda e: e.matmul(ln.bca[:], ln.onesrow[:], arow,
                                   start=True, stop=True),
          reads=[ln.onesrow_b, ln.rows_b], writes=[ln.bca_b])
    em.op("pe", lambda e: e.matmul(ln.bcb[:], ln.onesrow[:], brow,
                                   start=True, stop=True),
          reads=[ln.onesrow_b, ln.rows_b], writes=[ln.bcb_b])
    em.op("dve", lambda e: e.tensor_copy(ln.absb[:], ln.bca[:]),
          reads=[ln.bca_b], writes=[ln.absb_b])
    for fc in range(16):
        i = fc % 2
        em.op("dve", lambda e, fc=fc: e.tensor_tensor(
            ln.t1[:], R(fc), ln.absb[:], ALU.mult),
            reads=[rsl_b, ln.absb_b], writes=[ln.t1_b])
        em.op("dve", lambda e, i=i: e.tensor_tensor(
            ln.xo[i][:], ln.t1[:], ln.bcb[:], ALU.add),
            reads=[ln.t1_b, ln.bcb_b], writes=[ln.xo_b[i]])
        em.dma("sq", xo_dst(fc), ln.xo[i][:], reads=[ln.xo_b[i]])
        if xn_dst is not None:
            em.op("act", lambda e, i=i: e.activation(ln.xn[i][:], ln.xo[i][:],
                                                     AF.Copy),
                  reads=[ln.xo_b[i]], writes=[ln.xn_b[i]])
            em.dma("pool", xn_dst(fc), ln.xn[i][:], reads=[ln.xn_b[i]])



def emit_proj_res_ln(ctx, src_dr, w_dr, xres_dr, xo_dr, xn_dr, res_scale=1.0):
    nc, em = ctx.nc, ctx.em
    ln = LNRes(ctx)
    wob = ctx.sb("prl_wob", [128, 16 * 2048], BF16)
    wob_b = ctx.buf("wob")
    wof = [ctx.sb(f"prl_wof{i}", [128, 2048], F32) for i in range(2)]
    wof_b = [ctx.buf() for _ in range(2)]
    for ot in range(16):
        i = ot % 2
        em.dma("sq", wof[i][:].rearrange("p (c f) -> p c f", c=16), w_dr[ot],
               writes=[wof_b[i]])
        em.op("dve", lambda e, ot=ot, i=i: e.tensor_copy(
            wob[:, ot * 2048:(ot + 1) * 2048], wof[i][:]),
            reads=[wof_b[i]], accwrites=[wob_b])
    srcs = [ctx.sb(f"prl_src{i}", [128, 16 * 512], BF16) for i in range(2)]
    srcs_b = [ctx.buf() for _ in range(2)]
    xc = [ctx.sb(f"prl_xc{i}", [128, 512], F32) for i in range(2)]
    xc_b = [ctx.buf() for _ in range(2)]
    rsl = ctx.sb("prl_rsl", [128, 16 * 512], F32)
    rsl_b = ctx.buf("prl_rsl")
    pp = [ctx.ps(f"prl_ps{i}", [128, 512]) for i in range(2)]
    pp_b = [ctx.buf() for _ in range(2)]
    for ts in range(4):
        i2 = ts % 2
        em.dma("sq", srcs[i2][:].rearrange("p (c f) -> p c f", c=16),
               src_dr[:, :, ts * 512:(ts + 1) * 512].rearrange("h p f -> p h f"),
               writes=[srcs_b[i2]])
        for d2t in range(16):
            j = d2t % 2
            em.dma("pool", xc[j][:],
                   xres_dr[d2t * 128:(d2t + 1) * 128, ts * 512:(ts + 1) * 512],
                   writes=[xc_b[j]])

            def mmg(e, d2t=d2t, i2=i2, j=j):
                for fc in range(16):
                    ins = e.matmul(pp[j][:],
                                   wob[:, d2t * 2048 + fc * 128: d2t * 2048 + fc * 128 + 128],
                                   srcs[i2][:, fc * 512:(fc + 1) * 512],
                                   start=(fc == 0), stop=(fc == 15))
                return ins
            em.op("pe", mmg, reads=[wob_b, srcs_b[i2]], writes=[pp_b[j]])
            em.op("dve", lambda e, d2t=d2t, j=j, rs=res_scale: e.scalar_tensor_tensor(
                rsl[:, d2t * 512:(d2t + 1) * 512], xc[j][:], rs, pp[j][:],
                ALU.mult, ALU.add),
                reads=[xc_b[j], pp_b[j]], accwrites=[rsl_b])
        emit_ln(ctx, ln, rsl, rsl_b,
                lambda fc, ts=ts: xo_dr[fc * 128:(fc + 1) * 128,
                                        ts * 512:(ts + 1) * 512],
                None if xn_dr is None else
                (lambda fc, ts=ts: xn_dr[fc * 128:(fc + 1) * 128,
                                         ts * 512:(ts + 1) * 512]))


def build_p2():
    ctx = Ctx("p2")
    nc = ctx.nc
    og = nc.dram_tensor("og", [16, 128, T], BF16, kind="ExternalInput").ap()
    w_l0o = nc.dram_tensor("w_l0o", [16, 128, 16, 128], F32,
                           kind="ExternalInput").ap()
    xT = nc.dram_tensor("xT", [D, T], F32, kind="ExternalInput").ap()
    x1 = nc.dram_tensor("x1", [D, T], F32, kind="ExternalOutput").ap()
    x1n = nc.dram_tensor("x1n", [D, T], BF16, kind="ExternalOutput").ap()
    emit_proj_res_ln(ctx, og, w_l0o, xT, x1, x1n, res_scale=1.0)
    ctx.finish()
    return nc



def emit_p3(ctx, x1n_dr, wg_dr, wu_dr, gsc_dr, usc_dr, hsc_dr):
    nc, em = ctx.nc, ctx.em
    xb = ctx.sb("p3_xb", [128, 16 * T], BF16)
    xb_b = ctx.buf("p3_xb")
    em.dma("sq", xb[:].rearrange("p (c t) -> p c t", c=16),
           x1n_dr.rearrange("(c p) t -> p c t", p=128), writes=[xb_b])
    gscs = ctx.sb("p3_gsc", [128, 64], F32); gscs_b = ctx.buf()
    uscs = ctx.sb("p3_usc", [128, 64], F32); uscs_b = ctx.buf()
    em.dma("pool", gscs[:], gsc_dr, writes=[gscs_b])
    em.dma("pool", uscs[:], usc_dr, writes=[uscs_b])
    wg8 = [ctx.sb(f"p3_wg8{i}", [128, 2048], U8) for i in range(2)]
    wg8_b = [ctx.buf() for _ in range(2)]
    wu8 = [ctx.sb(f"p3_wu8{i}", [128, 2048], U8) for i in range(2)]
    wu8_b = [ctx.buf() for _ in range(2)]
    wgb = [ctx.sb(f"p3_wgb{i}", [128, 2048], BF16) for i in range(2)]
    wgb_b = [ctx.buf() for _ in range(2)]
    wub = [ctx.sb(f"p3_wub{i}", [128, 2048], BF16) for i in range(2)]
    wub_b = [ctx.buf() for _ in range(2)]
    gs = ctx.sb("p3_gs", [128, 512], F32); gs_b = ctx.buf()
    hst = [ctx.sb(f"p3_hst{i}", [128, 2048], BF16) for i in range(2)]
    hst_b = [ctx.buf() for _ in range(2)]
    pg = [ctx.ps(f"p3_pg{i}", [128, 512]) for i in range(2)]
    pg_b = [ctx.buf() for _ in range(2)]
    pu = [ctx.ps(f"p3_pu{i}", [128, 512]) for i in range(2)]
    pu_b = [ctx.buf() for _ in range(2)]
    for it in range(64):
        i = it % 2
        em.dma("sq", wg8[i][:].rearrange("p (c f) -> p c f", c=16), wg_dr[it],
               writes=[wg8_b[i]])
        em.dma("sq", wu8[i][:].rearrange("p (c f) -> p c f", c=16), wu_dr[it],
               writes=[wu8_b[i]])
        em.op("act", lambda e, i=i: e.activation(wgb[i][:], wg8[i][:], AF.Copy,
                                                 bias=-128.0),
              reads=[wg8_b[i]], writes=[wgb_b[i]])
        em.op("act", lambda e, i=i: e.activation(wub[i][:], wu8[i][:], AF.Copy,
                                                 bias=-128.0),
              reads=[wu8_b[i]], writes=[wub_b[i]])
        for ts in range(4):
            j = ts % 2

            def mmg(e, i=i, ts=ts, j=j, w=None):
                for fc in range(16):
                    ins = e.matmul(pg[j][:], wgb[i][:, fc * 128:(fc + 1) * 128],
                                   xb[:, fc * T + ts * 512: fc * T + ts * 512 + 512],
                                   start=(fc == 0), stop=(fc == 15))
                for fc in range(16):
                    ins = e.matmul(pu[j][:], wub[i][:, fc * 128:(fc + 1) * 128],
                                   xb[:, fc * T + ts * 512: fc * T + ts * 512 + 512],
                                   start=(fc == 0), stop=(fc == 15))
                return ins
            em.op("pe", mmg, reads=[wgb_b[i], wub_b[i], xb_b],
                  writes=[pg_b[j], pu_b[j]])
            em.op("act", lambda e, it=it, j=j: e.activation(
                gs[:], pg[j][:], AF.Silu, scale=gscs[:, it:it + 1]),
                reads=[pg_b[j], gscs_b], writes=[gs_b])
            em.op("dve", lambda e, it=it, i=i, ts=ts, j=j: e.scalar_tensor_tensor(
                hst[i][:, ts * 512:(ts + 1) * 512], pu[j][:], uscs[:, it:it + 1],
                gs[:], ALU.mult, ALU.mult),
                reads=[pu_b[j], uscs_b, gs_b], accwrites=[hst_b[i]])
        em.dma("sq", hsc_dr[it], hst[i][:], reads=[hst_b[i]], writes=())


def build_p3():
    ctx = Ctx("p3")
    nc = ctx.nc
    x1n = nc.dram_tensor("x1n", [D, T], BF16, kind="ExternalInput").ap()
    wg = nc.dram_tensor("w_l0g", [64, 128, 16, 128], U8, kind="ExternalInput").ap()
    wu = nc.dram_tensor("w_l0u", [64, 128, 16, 128], U8, kind="ExternalInput").ap()
    gsc = nc.dram_tensor("gsc", [128, 64], F32, kind="ExternalInput").ap()
    usc = nc.dram_tensor("usc", [128, 64], F32, kind="ExternalInput").ap()
    hsc = nc.dram_tensor("hsc", [64, 128, T], BF16, kind="ExternalOutput").ap()
    emit_p3(ctx, x1n, wg, wu, gsc, usc, hsc)
    ctx.finish()
    return nc



def emit_p4(ctx, hsc_dr, wd_dr, dsc_dr, x1_dr, x2_dr, x2n_dr):
    nc, em = ctx.nc, ctx.em
    ln = LNRes(ctx)
    dscs = ctx.sb("p4_dsc", [128, 16], F32); dscs_b = ctx.buf()
    em.dma("pool", dscs[:], dsc_dr, writes=[dscs_b])
    hsl = ctx.sb("p4_hsl", [128, 64 * 512], BF16)
    hsl_b = ctx.buf("p4_hsl")
    wd8 = [ctx.sb(f"p4_wd8{i}", [128, 8192], U8) for i in range(2)]
    wd8_b = [ctx.buf() for _ in range(2)]
    wdb = [ctx.sb(f"p4_wdb{i}", [128, 8192], BF16) for i in range(2)]
    wdb_b = [ctx.buf() for _ in range(2)]
    xc = [ctx.sb(f"p4_xc{i}", [128, 512], F32) for i in range(2)]
    xc_b = [ctx.buf() for _ in range(2)]
    rsl = ctx.sb("p4_rsl", [128, 16 * 512], F32)
    rsl_b = ctx.buf("p4_rsl")
    pp = [ctx.ps(f"p4_ps{i}", [128, 512]) for i in range(2)]
    pp_b = [ctx.buf() for _ in range(2)]
    for tc in range(4):
        em.dma("sq", hsl[:].rearrange("p (c f) -> p c f", c=64),
               hsc_dr[:, :, tc * 512:(tc + 1) * 512].rearrange("i p f -> p i f"),
               writes=[hsl_b])
        for d2t in range(16):
            i = d2t % 2
            j = d2t % 2
            em.dma("sq", wd8[i][:].rearrange("p (c f) -> p c f", c=64),
                   wd_dr[d2t], writes=[wd8_b[i]])
            em.op("act", lambda e, i=i: e.activation(wdb[i][:], wd8[i][:],
                                                     AF.Copy, bias=-128.0),
                  reads=[wd8_b[i]], writes=[wdb_b[i]])
            em.dma("pool", xc[j][:],
                   x1_dr[d2t * 128:(d2t + 1) * 128, tc * 512:(tc + 1) * 512],
                   writes=[xc_b[j]])

            def mmg(e, i=i, j=j):
                for kc in range(64):
                    ins = e.matmul(pp[j][:], wdb[i][:, kc * 128:(kc + 1) * 128],
                                   hsl[:, kc * 512:(kc + 1) * 512],
                                   start=(kc == 0), stop=(kc == 63))
                return ins
            em.op("pe", mmg, reads=[wdb_b[i], hsl_b], writes=[pp_b[j]])
            em.op("dve", lambda e, d2t=d2t, j=j: e.scalar_tensor_tensor(
                rsl[:, d2t * 512:(d2t + 1) * 512], pp[j][:], dscs[:, d2t:d2t + 1],
                xc[j][:], ALU.mult, ALU.add),
                reads=[pp_b[j], dscs_b, xc_b[j]], accwrites=[rsl_b])
        emit_ln(ctx, ln, rsl, rsl_b,
                lambda fc, tc=tc: x2_dr[fc * 128:(fc + 1) * 128,
                                        tc * 512:(tc + 1) * 512],
                None if x2n_dr is None else
                (lambda fc, tc=tc: x2n_dr[fc * 128:(fc + 1) * 128,
                                          tc * 512:(tc + 1) * 512]))


def build_p4():
    ctx = Ctx("p4")
    nc = ctx.nc
    hsc = nc.dram_tensor("hsc", [64, 128, T], BF16, kind="ExternalInput").ap()
    wd = nc.dram_tensor("w_l0d", [16, 128, 64, 128], U8, kind="ExternalInput").ap()
    dsc = nc.dram_tensor("dsc", [128, 16], F32, kind="ExternalInput").ap()
    x1 = nc.dram_tensor("x1", [D, T], F32, kind="ExternalInput").ap()
    x2 = nc.dram_tensor("x2", [D, T], F32, kind="ExternalOutput").ap()
    x2n = nc.dram_tensor("x2n", [D, T], BF16, kind="ExternalOutput").ap()
    emit_p4(ctx, hsc, wd, dsc, x1, x2, x2n)
    ctx.finish()
    return nc



def emit_p5(ctx, x2_dr, ns_dr, ss_dr, wq_dr, wk_dr, wv_dr, qf_dr, kf_dr, vf_dr):
    nc, em = ctx.nc, ctx.em
    onescol, onescol_b = ctx.const("onescolf", np.ones((128, 1), dtype=np.float32))
    onesrow, onesrow_b = ctx.const("onesrowf", np.ones((1, 128), dtype=np.float32))
    eps1, eps1_b = ctx.cvec(1e-8, parts=1)
    nss = ctx.sb("p5_nss", [128, 16], F32); nss_b = ctx.buf()
    for name, dr in (("ns", ns_dr), ("ss", ss_dr)):
        t = ctx.sb(f"p5_{name}", [128, 16], F32)
        b = ctx.buf()
        em.dma("pool", t[:], dr, writes=[b])
        if name == "ns":
            tns, tns_b = t, b
        else:
            tss, tss_b = t, b
    em.op("dve", lambda e: e.tensor_tensor(nss[:], tns[:], tss[:], ALU.mult),
          reads=[tns_b, tss_b], writes=[nss_b])

    xsl = ctx.sb("p5_xsl", [128, 16 * 512], F32); xsl_b = ctx.buf()
    sq = [ctx.sb(f"p5_sq{i}", [128, 512], F32) for i in range(2)]
    sq_b = [ctx.buf() for _ in range(2)]
    rows = ctx.sb("p5_rows", [1, 512], F32); rows_b = ctx.buf()
    cbs = ctx.sb("p5_cbs", [128, 512], F32); cbs_b = ctx.buf()
    hsl = ctx.sb("p5_hsl", [128, 16 * T], BF16); hsl_b = ctx.buf("p5_hsl")
    st = ctx.ps("p5_st", [1, 512]); st_b = ctx.buf()
    bc = ctx.ps("p5_bc", [128, 512]); bc_b = ctx.buf()
    for ts in range(4):
        for fc in range(16):
            i = fc % 2
            em.dma("sq", xsl[:, fc * 512:(fc + 1) * 512],
                   x2_dr[fc * 128:(fc + 1) * 128, ts * 512:(ts + 1) * 512],
                   accwrites=[xsl_b])
            em.op("act", lambda e, fc=fc, i=i: e.activation(
                sq[i][:], xsl[:, fc * 512:(fc + 1) * 512], AF.Square),
                reads=[xsl_b], writes=[sq_b[i]])
            em.op("pe", lambda e, fc=fc, i=i: e.matmul(
                st[:], onescol[:], sq[i][:], start=(fc == 0), stop=(fc == 15)),
                reads=[onescol_b, sq_b[i]],
                writes=[st_b] if fc == 0 else (), accwrites=() if fc == 0 else [st_b])
        em.op("act", lambda e: e.activation(rows[:], st[:], AF.Sqrt,
                                            scale=1.0 / D),
              reads=[st_b], writes=[rows_b])
        em.op("dve", lambda e: e.tensor_scalar(rows[:], rows[:], eps1[:], None,
                                               ALU.add),
              reads=[rows_b, eps1_b], accwrites=[rows_b])
        em.op("dve", lambda e: e.reciprocal(rows[:], rows[:]),
              reads=[rows_b], accwrites=[rows_b])
        em.op("pe", lambda e: e.matmul(bc[:], onesrow[:], rows[:],
                                       start=True, stop=True),
              reads=[onesrow_b, rows_b], writes=[bc_b])
        em.op("dve", lambda e: e.tensor_copy(cbs[:], bc[:]),
              reads=[bc_b], writes=[cbs_b])
        for fc in range(16):
            em.op("dve", lambda e, fc=fc, ts=ts: e.scalar_tensor_tensor(
                hsl[:, fc * T + ts * 512: fc * T + ts * 512 + 512],
                xsl[:, fc * 512:(fc + 1) * 512], nss[:, fc:fc + 1], cbs[:],
                ALU.mult, ALU.mult),
                reads=[xsl_b, nss_b, cbs_b], accwrites=[hsl_b])
    wf = [ctx.sb(f"p5_wf{i}", [128, 2048], F32) for i in range(2)]
    wf_b = [ctx.buf() for _ in range(2)]
    wb = [ctx.sb(f"p5_wb{i}", [128, 2048], BF16) for i in range(2)]
    wb_b = [ctx.buf() for _ in range(2)]
    ob = [ctx.sb(f"p5_ob{i}", [128, 512], BF16) for i in range(4)]
    ob_b = [ctx.buf() for _ in range(4)]
    pp = [ctx.ps(f"p5_pp{i}", [128, 512]) for i in range(2)]
    pp_b = [ctx.buf() for _ in range(2)]
    jobs = [(wq_dr, ot, qf_dr, ot, 0.125) for ot in range(16)]
    jobs += [(wk_dr, 0, kf_dr, 0, 1.0), (wv_dr, 0, vf_dr, 0, 1.0)]
    for n, (wdr, ot, odr, oot, qsc) in enumerate(jobs):
        i = n % 2
        em.dma("sq", wf[i][:].rearrange("p (c f) -> p c f", c=16), wdr[ot],
               writes=[wf_b[i]])
        em.op("dve", lambda e, i=i: e.tensor_copy(wb[i][:], wf[i][:]),
              reads=[wf_b[i]], writes=[wb_b[i]])
        for ts in range(4):
            j = (n * 4 + ts) % 2
            jo = (n * 4 + ts) % 4

            def mmg(e, i=i, ts=ts, j=j):
                for fc in range(16):
                    ins = e.matmul(pp[j][:], wb[i][:, fc * 128:(fc + 1) * 128],
                                   hsl[:, fc * T + ts * 512: fc * T + ts * 512 + 512],
                                   start=(fc == 0), stop=(fc == 15))
                return ins
            em.op("pe", mmg, reads=[wb_b[i], hsl_b], writes=[pp_b[j]])
            em.op("act", lambda e, j=j, jo=jo, qsc=qsc: e.activation(
                ob[jo][:], pp[j][:], AF.Copy, scale=qsc),
                reads=[pp_b[j]], writes=[ob_b[jo]])
            em.dma("sq", odr[oot * 128:(oot + 1) * 128, ts * 512:(ts + 1) * 512],
                   ob[jo][:], reads=[ob_b[jo]])


def build_p5():
    ctx = Ctx("p5")
    nc = ctx.nc
    x2 = nc.dram_tensor("x2", [D, T], F32, kind="ExternalInput").ap()
    ns = nc.dram_tensor("ns", [128, 16], F32, kind="ExternalInput").ap()
    ss = nc.dram_tensor("ss", [128, 16], F32, kind="ExternalInput").ap()
    wq = nc.dram_tensor("w_l1q", [16, 128, 16, 128], F32, kind="ExternalInput").ap()
    wk = nc.dram_tensor("w_l1k", [1, 128, 16, 128], F32, kind="ExternalInput").ap()
    wv = nc.dram_tensor("w_l1v", [1, 128, 16, 128], F32, kind="ExternalInput").ap()
    qf = nc.dram_tensor("qf", [D, T], BF16, kind="ExternalOutput").ap()
    kf = nc.dram_tensor("kf", [128, T], BF16, kind="ExternalOutput").ap()
    vf = nc.dram_tensor("vf", [128, T], BF16, kind="ExternalOutput").ap()
    emit_p5(ctx, x2, ns, ss, wq, wk, wv, qf, kf, vf)
    ctx.finish()
    return nc



P6_DEBUG = False


def emit_p6(ctx, qf_dr, kf_dr, vf_dr, oatt_dr):
    nc, em = ctx.nc, ctx.em
    i128f, i128f_b = ctx.const("i128f", np.eye(128, dtype=np.float32))
    irs_np = np.zeros((128, 64), dtype=ml_dtypes.bfloat16)
    for p in range(128):
        irs_np[p, p % 64] = 1
    irs, irs_b = ctx.const("irs", irs_np)
    onesall, onesall_b = ctx.const("onesall",
                                   np.ones((128, 128), dtype=ml_dtypes.bfloat16))
    ones64f, ones64f_b = ctx.const("ones64f", np.ones((1, 64), dtype=np.float32))

    qsb = ctx.sb("qsb", [128, 16 * T], BF16); qsb_b = ctx.buf("qsb")
    ksb = ctx.sb("ksb", [128, T], BF16); ksb_b = ctx.buf()
    vsb = ctx.sb("vsb", [128, T], BF16); vsb_b = ctx.buf()
    vtm = ctx.sb("vtm", [128, 32 * 65], BF16); vtm_b = ctx.buf()
    pta = ctx.sb("pta", [128, 16 * 1024], BF16); pta_b = ctx.buf("pta")
    ptb2 = ctx.sb("ptb2", [128, 16 * 1024], BF16); ptb2_b = ctx.buf("ptb2")
    mall = [ctx.sb(f"mall{i}", [128, 32], F32) for i in range(3)]
    mall_b = [ctx.buf() for _ in range(3)]
    mbf = ctx.sb("mbf", [128, 32], BF16); mbf_b = ctx.buf()
    mrow = ctx.sb("mrow", [16, 256], BF16); mrow_b = ctx.buf()
    ksbA = ctx.sb("ksbA", [128, T], BF16); ksbA_b = ctx.buf()
    ksbB = ctx.sb("ksbB", [128, T], BF16); ksbB_b = ctx.buf()
    qaugA = [ctx.sb(f"qaugA{i}", [128, T], BF16) for i in range(3)]
    qaugA_b = [ctx.buf() for _ in range(3)]
    qaugB = [ctx.sb(f"qaugB{i}", [128, T], BF16) for i in range(3)]
    qaugB_b = [ctx.buf() for _ in range(3)]
    mcs = [ctx.sb(f"mcs{i}", [128, 8], F32) for i in range(3)]
    mcs_b = [ctx.buf() for _ in range(3)]
    rcl = ctx.sb("rcl", [1, 1024], F32); rcl_b = ctx.buf()
    bcs = ctx.sb("bcs", [64, 1024], F32); bcs_b = ctx.buf()
    obf = [ctx.sb(f"obf{i}", [64, 512], BF16) for i in range(4)]
    obf_b = [ctx.buf() for _ in range(4)]
    mdr = nc.dram_tensor("p6_mscr", [3, 2, T], BF16, kind="Internal").ap()
    mdr_b = [ctx.buf(f"mdr{i}") for i in range(3)]
    if P6_DEBUG:
        dbg_mall = nc.dram_tensor("dbg_mall", [128, 32], F32,
                                  kind="ExternalOutput").ap()
        dbg_mneg = nc.dram_tensor("dbg_mneg", [2, T], BF16,
                                  kind="ExternalOutput").ap()
        dbg_pta = nc.dram_tensor("dbg_pta", [128, 1024], BF16,
                                 kind="ExternalOutput").ap()
        dbg_vtm = nc.dram_tensor("dbg_vtm", [128, 32 * 65], BF16,
                                 kind="ExternalOutput").ap()

    G = [ctx.ps(f"G{i}", [128, 512]) for i in range(8)]
    G_b = [ctx.buf(f"G{i}") for i in range(8)]


    for g in range(2):
        em.dma("sq", qsb[g * 64:(g + 1) * 64, :].rearrange("p (j t) -> p j t", j=16),
               qf_dr.rearrange("(g j p) t -> g p j t", g=2, p=64)[g],
               accwrites=[qsb_b])
    em.dma("sq", ksb[:], kf_dr, writes=[ksb_b])
    em.dma("sq", vsb[:], vf_dr, writes=[vsb_b])
    em.op("pool", lambda e: e.memset(ksbA[:], 0.0), writes=[ksbA_b])
    em.op("pool", lambda e: e.memset(ksbB[:], 0.0), writes=[ksbB_b])
    em.dma("sq", ksbA[0:64, :], kf_dr[0:64, :],
           reads=[ksbA_b], accwrites=[ksbA_b])
    em.dma("sq", ksbB[64:128, :], kf_dr[64:128, :],
           reads=[ksbB_b], accwrites=[ksbB_b])
    em.op("pool", lambda e: e.memset(ksbA[64:65, :], 1.0),
          reads=[ksbA_b], accwrites=[ksbA_b])
    em.op("pool", lambda e: e.memset(ksbB[0:1, :], 1.0),
          reads=[ksbB_b], accwrites=[ksbB_b])
    for i in range(3):
        em.op("pool", lambda e, i=i: e.memset(qaugA[i][64:128, :], 0.0),
              accwrites=[qaugA_b[i]])
        em.op("pool", lambda e, i=i: e.memset(qaugB[i][0:64, :], 0.0),
              accwrites=[qaugB_b[i]])
    em.op("pool", lambda e: e.memset(
        vtm[:].rearrange("p (k c) -> p k c", c=65)[:, :, 64:65], 1.0),
        accwrites=[vtm_b])
    g0bf = G[0][:].bitcast(BF16)
    for g in range(2):
        gp = 64 * g
        for kt in range(16):
            em.op("pe", lambda e, gp=gp, kt=kt: e.transpose(
                g0bf[:, 0:64], vsb[gp:gp + 64, kt * 128:(kt + 1) * 128],
                irs[gp:gp + 64, :]),
                reads=[vsb_b, irs_b], writes=[G_b[0]])
            em.op("dve", lambda e, g=g, kt=kt: e.tensor_copy(
                vtm[:, (g * 16 + kt) * 65:(g * 16 + kt) * 65 + 64],
                g0bf[:, 0:64]),
                reads=[G_b[0]], accwrites=[vtm_b])

    N_STEPS = 16

    def emit_qcopy(j):
        pj = j % 3
        em.op("dve", lambda e, j=j, pj=pj: e.tensor_copy(
            qaugA[pj][0:64, :], qsb[0:64, j * T:(j + 1) * T]),
            reads=[qsb_b], accwrites=[qaugA_b[pj]])
        em.op("dve", lambda e, j=j, pj=pj: e.tensor_copy(
            qaugB[pj][64:128, :], qsb[64:128, j * T:(j + 1) * T]),
            reads=[qsb_b], accwrites=[qaugB_b[pj]])

    def emit_stats_step(j, step):
        pj = j % 3
        qt = step
        qq = 0
        c = 0
        qof = j * T
        ga, gb = 6, 7

        def smm(e, qt=qt, qq=qq, ga=ga, gb=gb, pj=pj):
            e.matmul(G[ga][:], qaugA[pj][:, qt * 128:(qt + 1) * 128],
                     ksbA[:, qq * 512:(qq + 1) * 512],
                     start=True, stop=True)
            return e.matmul(G[gb][:], qaugB[pj][:, qt * 128:(qt + 1) * 128],
                            ksbB[:, qq * 512:(qq + 1) * 512],
                            start=True, stop=True)
        em.op("pe", smm, reads=[qaugA_b[pj], qaugB_b[pj], ksbA_b, ksbB_b],
              writes=[G_b[ga], G_b[gb]])
        gav = G[ga][:].rearrange("p (f two) -> p f two", two=2)[:, :, 0]
        gbv = G[gb][:].rearrange("p (f two) -> p f two", two=2)[:, :, 0]
        em.op("dve", lambda e, qq=qq, gav=gav, ga=ga, pj=pj: e.reduce_max(
            mcs[pj][:, qq:qq + 1], gav, axis=AX.X),
            reads=[G_b[ga]], accwrites=[mcs_b[pj]])
        em.op("dve", lambda e, qq=qq, gbv=gbv, gb=gb, pj=pj: e.reduce_max(
            mcs[pj][:, 4 + qq:5 + qq], gbv, axis=AX.X),
            reads=[G_b[gb]], accwrites=[mcs_b[pj]])
        def fin3(e, qt=qt, pj=pj, ga=ga, gb=gb):
            gav2 = G[ga][:].rearrange("p (f two) -> p f two", two=2)[:, :, 0]
            return None
        em.op("dve", lambda e, qt=qt, pj=pj: e.tensor_copy(
            mall[pj][:, qt:qt + 1], mcs[pj][:, 0:1]),
            reads=[mcs_b[pj]], accwrites=[mall_b[pj]])
        em.op("dve", lambda e, qt=qt, pj=pj: e.tensor_copy(
            mall[pj][:, 16 + qt:17 + qt], mcs[pj][:, 4:5]),
            reads=[mcs_b[pj]], accwrites=[mall_b[pj]])

    def emit_mrow(j):
        pj = j % 3
        em.op("dve", lambda e, pj=pj: e.tensor_copy(mbf[:], mall[pj][:]),
              reads=[mall_b[pj]], writes=[mbf_b])
        for s, base in ((0, 0), (1, 64)):
            em.op("pe", lambda e, s=s, pj=pj: e.transpose(
                G[2][0:16, 0:128], mall[pj][:, s * 16:(s + 1) * 16], i128f[:]),
                reads=[mall_b[pj], i128f_b], writes=[G_b[2]])
            em.op("act", lambda e, s=s: e.activation(
                mrow[:, s * 128:(s + 1) * 128], G[2][0:16, 0:128], AF.Copy,
                scale=-1.0),
                reads=[G_b[2]], accwrites=[mrow_b])
            em.dma("pool",
                   mdr[pj, s:s + 1, :].rearrange("o (j f) -> (o j) f", j=16),
                   mrow[:, s * 128:(s + 1) * 128], reads=[mrow_b],
                   accwrites=[mdr_b[pj]])
            dst = qaugA[pj][64:65, :] if s == 0 else qaugB[pj][0:1, :]
            dstb = qaugA_b[pj] if s == 0 else qaugB_b[pj]
            em.dma("sq", dst, mdr[pj, s:s + 1, :],
                   reads=[mdr_b[pj]], accwrites=[dstb])

    def emit_work_steps(j):
        pj = j % 3
        qof = j * T
        hA, hB = j, 16 + j
        for qh in range(2):
            for kt in range(16):
                for qs in range(2):
                    c = (kt * 2 + qs) % 3
                    ga = (0, 1, 4)[c]
                    gb2 = (2, 3, 5)[c]
                    def stmm(e, kt=kt, qh=qh, qs=qs, pj=pj, ga=ga, gb2=gb2):
                        qsl = slice(qh * 1024 + qs * 512,
                                    qh * 1024 + qs * 512 + 512)
                        e.matmul(G[ga][:], ksbA[:, kt * 128:(kt + 1) * 128],
                                 qaugA[pj][:, qsl], start=True, stop=True)
                        return e.matmul(G[gb2][:],
                                        ksbB[:, kt * 128:(kt + 1) * 128],
                                        qaugB[pj][:, qsl],
                                        start=True, stop=True)
                    def stw(stmm=stmm, pj=pj, ga=ga, gb2=gb2):
                        em.op("pe", stmm,
                              reads=[ksbA_b, ksbB_b, qaugA_b[pj], qaugB_b[pj]],
                              writes=[G_b[ga], G_b[gb2]])
                    yield stw
                    def expw(kt=kt, qs=qs, ga=ga, gb2=gb2):
                        off = kt * 1024 + qs * 512
                        em.op("act", lambda e, off=off, ga=ga: e.activation(
                            pta[:, off:off + 512], G[ga][:], AF.Exp),
                            reads=[G_b[ga]], accwrites=[pta_b])
                        em.op("act", lambda e, off=off, gb2=gb2: e.activation(
                            ptb2[:, off:off + 512], G[gb2][:], AF.Exp),
                            reads=[G_b[gb2]], accwrites=[ptb2_b])
                    yield expw
            for t2 in range(2):
                ts = qh * 2 + t2
                for s in range(2):
                    pt_sl, pt_bb = (pta, pta_b) if s == 0 else (ptb2, ptb2_b)
                    h = hA if s == 0 else hB

                    def avw(s=s, t2=t2, pt_sl=pt_sl, pt_bb=pt_bb, h=h, ts=ts):
                        def avmm(e, g=s, t2=t2, pt_sl=pt_sl):
                            for kt in range(16):
                                ins = e.matmul(
                                    G[0][0:65, :],
                                    vtm[:, (g * 16 + kt) * 65:
                                        (g * 16 + kt) * 65 + 65],
                                    pt_sl[:, kt * 1024 + t2 * 512:
                                          kt * 1024 + t2 * 512 + 512],
                                    start=(kt == 0), stop=(kt == 15))
                            return ins
                        em.op("pe", avmm, reads=[vtm_b, pt_bb],
                              writes=[G_b[0]])
                        em.op("dve", lambda e, s=s: e.reciprocal(
                            rcl[:, s * 512:(s + 1) * 512], G[0][64:65, :]),
                            reads=[G_b[0]], accwrites=[rcl_b])
                        em.op("pe", lambda e, s=s: e.matmul(
                            G[1][0:64, :], ones64f[:],
                            rcl[:, s * 512:(s + 1) * 512],
                            start=True, stop=True),
                            reads=[ones64f_b, rcl_b], writes=[G_b[1]])
                        em.op("dve", lambda e, s=s: e.tensor_copy(
                            bcs[:, s * 512:(s + 1) * 512], G[1][0:64, :]),
                            reads=[G_b[1]], accwrites=[bcs_b])
                        ii = s * 2 + t2
                        em.op("dve", lambda e, s=s, ii=ii: e.tensor_tensor(
                            obf[ii][:], G[0][0:64, :],
                            bcs[:, s * 512:(s + 1) * 512], ALU.mult),
                            reads=[G_b[0], bcs_b], writes=[obf_b[ii]])
                        em.dma("sq", oatt_dr[h * 64:(h + 1) * 64,
                                             ts * 512:(ts + 1) * 512],
                               obf[ii][:], reads=[obf_b[ii]])
                    yield avw

    emit_qcopy(0)
    for step in range(N_STEPS):
        emit_stats_step(0, step)
    emit_mrow(0)
    emit_qcopy(1)
    if P6_DEBUG:
        em.dma("sq", dbg_vtm, vtm[:], reads=[vtm_b])
        em.dma("sq", dbg_mall, mall[0][:], reads=[mall_b[0]])
        em.dma("sq", dbg_mneg[0:1, :], mneg2[0][0:1, :], reads=[mneg2_b[0]])
        em.dma("sq", dbg_mneg[1:2, :], mneg2[0][64:65, :], reads=[mneg2_b[0]])
    for j in range(16):
        nxt = iter(range(N_STEPS)) if j + 1 < 16 else iter(())
        for s in range(10):
            s2 = next(nxt, None)
            if s2 is not None:
                emit_stats_step(j + 1, s2)
                if s2 == N_STEPS - 1:
                    emit_mrow(j + 1)
                    if j + 2 < 16:
                        emit_qcopy(j + 2)
        wl = list(emit_work_steps(j))
        for i, wfn in enumerate(wl):
            wfn()
            if P6_DEBUG and j == 0 and i == 64:
                em.dma("sq", dbg_pta, pta[:, 0:1024], reads=[pta_b])
            if i % 2 == 1:
                s = next(nxt, None)
                if s is not None:
                    emit_stats_step(j + 1, s)
                    if s == N_STEPS - 1:
                        emit_mrow(j + 1)
                        if j + 2 < 16:
                            emit_qcopy(j + 2)
        if j + 1 < 16:
            for s in nxt:
                emit_stats_step(j + 1, s)
                if s == N_STEPS - 1:
                    emit_mrow(j + 1)
                    if j + 2 < 16:
                        emit_qcopy(j + 2)


def build_p6():
    ctx = Ctx("p6")
    nc = ctx.nc
    qf = nc.dram_tensor("qf", [D, T], BF16, kind="ExternalInput").ap()
    kf = nc.dram_tensor("kf", [128, T], BF16, kind="ExternalInput").ap()
    vf = nc.dram_tensor("vf", [128, T], BF16, kind="ExternalInput").ap()
    oatt = nc.dram_tensor("oatt", [D, T], BF16, kind="ExternalOutput").ap()
    emit_p6(ctx, qf, kf, vf, oatt)
    ctx.finish()
    return nc


def build_p7():
    ctx = Ctx("p7")
    nc = ctx.nc
    oatt = nc.dram_tensor("oatt", [16, 128, T], BF16, kind="ExternalInput").ap()
    w_l1o = nc.dram_tensor("w_l1o", [16, 128, 16, 128], F32,
                           kind="ExternalInput").ap()
    x2 = nc.dram_tensor("x2", [D, T], F32, kind="ExternalInput").ap()
    x3 = nc.dram_tensor("x3", [D, T], F32, kind="ExternalOutput").ap()
    x3n = nc.dram_tensor("x3n", [D, T], BF16, kind="ExternalOutput").ap()
    emit_proj_res_ln(ctx, oatt, w_l1o, x2, x3, x3n, res_scale=2.0)
    ctx.finish()
    return nc



def emit_p8(ctx, x3n_dr, wrt_dr, amp_dr, esc_dr, wtil_dr):
    nc, em = ctx.nc, ctx.em
    i128f, i128f_b = ctx.const("i128f", np.eye(128, dtype=np.float32))
    i128b, i128b_b = ctx.const("i128b", np.eye(128, dtype=ml_dtypes.bfloat16))
    xb = ctx.sb("p8_xb", [128, 16 * T], BF16); xb_b = ctx.buf()
    em.dma("sq", xb[:].rearrange("p (c t) -> p c t", c=16),
           x3n_dr.rearrange("(c p) t -> p c t", p=128), writes=[xb_b])
    wrf = ctx.sb("p8_wrf", [128, 2048], F32); wrf_b = ctx.buf()
    wrb = ctx.sb("p8_wrb", [128, 2048], BF16); wrb_b = ctx.buf()
    em.dma("sq", wrf[:].rearrange("p (c f) -> p c f", c=16), wrt_dr[0],
           writes=[wrf_b])
    em.op("dve", lambda e: e.tensor_copy(wrb[:], wrf[:]),
          reads=[wrf_b], writes=[wrb_b])
    lgs = ctx.sb("p8_lgs", [128, T], F32); lgs_b = ctx.buf()
    ltm = ctx.sb("p8_ltm", [128, T], F32); ltm_b = ctx.buf()
    afm = ctx.sb("p8_afm", [128, T], BF16); afm_b = ctx.buf()
    nm = ctx.sb("p8_nm", [128, 1], F32); nm_b = ctx.buf()
    pex = ctx.sb("p8_pex", [128, 128], F32); pex_b = ctx.buf()
    pc = [ctx.sb(f"p8_pc{i}", [128, 128], F32) for i in range(2)]
    pc_b = [ctx.buf() for _ in range(2)]
    zz = ctx.sb("p8_zz", [128, 4], F32); zz_b = ctx.buf()
    alp = ctx.sb("p8_alp", [128, 128], F32); alp_b = ctx.buf()
    atm = ctx.sb("p8_atm", [128, 128], BF16); atm_b = ctx.buf()
    amp_sb = ctx.sb("p8_amp", [128, 16], F32); amp_b = ctx.buf()
    esc_sb = ctx.sb("p8_esc", [128, 1], F32); esc_b = ctx.buf()
    sqa = ctx.sb("p8_sqa", [128, 16], F32); sqa_b = ctx.buf()
    pk8 = ctx.sb("p8_pk8", [128, 8], F32); pk8_b = ctx.buf()
    mixb = ctx.sb("p8_mixb", [128, 8], BF16); mixb_b = ctx.buf()
    wts = ctx.sb("p8_wts", [8, T], BF16); wts_b = ctx.buf()
    pp = ctx.ps("p8_pp", [128, 512]); pp_b = ctx.buf()
    ptt = ctx.ps("p8_ptt", [128, 128]); ptt_b = ctx.buf()
    ptb = ctx.ps("p8_ptb", [128, 128], BF16); ptb_b = ctx.buf()
    pw = ctx.ps("p8_pw", [8, 512]); pw_b = ctx.buf()
    em.dma("pool", amp_sb[:], amp_dr, writes=[amp_b])
    em.dma("pool", esc_sb[:], esc_dr.rearrange("(p o) -> p o", o=1), writes=[esc_b])
    for ts in range(4):
        def rmm(e, ts=ts):
            for fc in range(16):
                ins = e.matmul(pp[:], wrb[:, fc * 128:(fc + 1) * 128],
                               xb[:, fc * T + ts * 512: fc * T + ts * 512 + 512],
                               start=(fc == 0), stop=(fc == 15))
            return ins
        em.op("pe", rmm, reads=[wrb_b, xb_b], writes=[pp_b])
        em.op("dve", lambda e, ts=ts: e.tensor_copy(
            lgs[:, ts * 512:(ts + 1) * 512], pp[:]),
            reads=[pp_b], accwrites=[lgs_b])
    for tt in range(16):
        em.op("pe", lambda e, tt=tt: e.transpose(
            ptt[:], lgs[:, tt * 128:(tt + 1) * 128], i128f[:]),
            reads=[lgs_b, i128f_b], writes=[ptt_b])
        em.op("dve", lambda e, tt=tt: e.tensor_copy(
            ltm[:, tt * 128:(tt + 1) * 128], ptt[:]),
            reads=[ptt_b], accwrites=[ltm_b])
        L = ltm[:, tt * 128:(tt + 1) * 128]
        em.op("dve", lambda e, L=L: e.reduce_max(nm[:], L, axis=AX.X, negate=True),
              reads=[ltm_b], writes=[nm_b])
        em.op("act", lambda e, L=L: e.activation(pex[:], L, AF.Exp, bias=nm[:]),
              reads=[ltm_b, nm_b], writes=[pex_b])
        em.op("dve", lambda e: e.reduce_sum(zz[:, 0:1], pex[:], axis=AX.X),
              reads=[pex_b], writes=[zz_b])
        em.op("dve", lambda e: e.reduce_max(zz[:, 1:2], pex[:], axis=AX.X),
              reads=[pex_b], accwrites=[zz_b])
        src, src_b = pex, pex_b
        for j in range(3):
            dst, dst_b = pc[j % 2], pc_b[j % 2]
            em.op("dve", lambda e, src=src, dst=dst: e.scalar_tensor_tensor(
                dst[:], src[:], zz[:, 1:2], src[:], ALU.is_lt, ALU.mult),
                reads=[src_b, zz_b], writes=[dst_b])
            em.op("dve", lambda e, dst=dst: e.reduce_max(zz[:, 1:2], dst[:],
                                                         axis=AX.X),
                  reads=[dst_b], accwrites=[zz_b])
            src, src_b = dst, dst_b
        em.op("dve", lambda e: e.scalar_tensor_tensor(
            alp[:], pex[:], zz[:, 1:2], pex[:], ALU.is_ge, ALU.mult),
            reads=[pex_b, zz_b], writes=[alp_b])
        em.op("dve", lambda e: e.reduce_sum(zz[:, 2:3], alp[:], axis=AX.X),
              reads=[alp_b], accwrites=[zz_b])
        em.op("dve", lambda e: e.scalar_tensor_tensor(
            zz[:, 3:4], zz[:, 0:1], 1e-8, zz[:, 2:3], ALU.mult, ALU.add),
            reads=[zz_b], accwrites=[zz_b])
        em.op("dve", lambda e: e.reciprocal(zz[:, 3:4], zz[:, 3:4]),
              reads=[zz_b], accwrites=[zz_b])
        em.op("dve", lambda e: e.tensor_scalar(atm[:], alp[:], zz[:, 3:4], None,
                                               ALU.mult),
              reads=[alp_b, zz_b], writes=[atm_b])
        em.op("pe", lambda e: e.transpose(ptb[:], atm[:], i128b[:]),
              reads=[atm_b, i128b_b], writes=[ptb_b])
        em.op("dve", lambda e, tt=tt: e.tensor_copy(
            afm[:, tt * 128:(tt + 1) * 128], ptb[:]),
            reads=[ptb_b], accwrites=[afm_b])
    em.op("act", lambda e: e.activation(sqa[:], amp_sb[:], AF.Square),
          reads=[amp_b], writes=[sqa_b])
    sqv = sqa[:].rearrange("p (k j) -> p k j", j=2)
    em.op("dve", lambda e: e.tensor_tensor(pk8[:], sqv[:, :, 0], sqv[:, :, 1],
                                           ALU.add),
          reads=[sqa_b], writes=[pk8_b])
    em.op("dve", lambda e: e.reduce_sum(zz[:, 0:1], pk8[:], axis=AX.X),
          reads=[pk8_b], writes=[zz_b])
    em.op("dve", lambda e: e.tensor_scalar(zz[:, 0:1], zz[:, 0:1], 1e-8, None,
                                           ALU.add),
          reads=[zz_b], accwrites=[zz_b])
    em.op("dve", lambda e: e.reciprocal(zz[:, 0:1], zz[:, 0:1]),
          reads=[zz_b], accwrites=[zz_b])
    em.op("dve", lambda e: e.tensor_scalar(mixb[:], pk8[:], zz[:, 0:1],
                                           esc_sb[:], ALU.mult, ALU.mult),
          reads=[pk8_b, zz_b, esc_b], writes=[mixb_b])
    for ts in range(4):
        em.op("pe", lambda e, ts=ts: e.matmul(
            pw[:], mixb[:], afm[:, ts * 512:(ts + 1) * 512],
            start=True, stop=True),
            reads=[mixb_b, afm_b], writes=[pw_b])
        em.op("act", lambda e, ts=ts: e.activation(
            wts[:, ts * 512:(ts + 1) * 512], pw[:], AF.Copy),
            reads=[pw_b], accwrites=[wts_b])
    em.dma("sq", wtil_dr, wts[:], reads=[wts_b])


def build_p8():
    ctx = Ctx("p8")
    nc = ctx.nc
    x3n = nc.dram_tensor("x3n", [D, T], BF16, kind="ExternalInput").ap()
    wrt = nc.dram_tensor("w_rt", [1, 128, 16, 128], F32, kind="ExternalInput").ap()
    amp = nc.dram_tensor("amp", [128, 16], F32, kind="ExternalInput").ap()
    esc = nc.dram_tensor("esc", [128], F32, kind="ExternalInput").ap()
    wtil = nc.dram_tensor("wtil", [8, T], BF16, kind="ExternalOutput").ap()
    emit_p8(ctx, x3n, wrt, amp, esc, wtil)
    ctx.finish()
    return nc



def emit_p9(ctx, x3n_dr, wtil_dr, wg_dr, wu_dr, ysc_dr):
    nc, em = ctx.nc, ctx.em
    sel_np = np.zeros((8, 1024), dtype=ml_dtypes.bfloat16)
    for k in range(8):
        sel_np[k, k * 128:(k + 1) * 128] = 1
    sel, sel_b = ctx.const("sel8", sel_np)
    xb = ctx.sb("p9_xb", [128, 16 * T], BF16); xb_b = ctx.buf()
    em.dma("sq", xb[:].rearrange("p (c t) -> p c t", c=16),
           x3n_dr.rearrange("(c p) t -> p c t", p=128), writes=[xb_b])
    wtl = ctx.sb("p9_wtl", [8, T], BF16); wtl_b = ctx.buf()
    em.dma("pool", wtl[:], wtil_dr, writes=[wtl_b])
    wb_sl = ctx.sb("p9_wb", [128, T], BF16); wb_b = ctx.buf("p9_wb")
    ya = ctx.sb("p9_ya", [128, 8 * T], F32); ya_b = ctx.buf("p9_ya")
    for ic in range(8):
        em.op("pool", lambda e, ic=ic: e.memset(ya[:, ic * T:(ic + 1) * T], 0.0),
              accwrites=[ya_b])
    wgf = [ctx.sb(f"p9_wgf{i}", [128, 2048], F32) for i in range(2)]
    wgf_b = [ctx.buf() for _ in range(2)]
    wuf = [ctx.sb(f"p9_wuf{i}", [128, 2048], F32) for i in range(2)]
    wuf_b = [ctx.buf() for _ in range(2)]
    wgb = [ctx.sb(f"p9_wgb{i}", [128, 2048], BF16) for i in range(2)]
    wgb_b = [ctx.buf() for _ in range(2)]
    wub = [ctx.sb(f"p9_wub{i}", [128, 2048], BF16) for i in range(2)]
    wub_b = [ctx.buf() for _ in range(2)]
    gs = ctx.sb("p9_gs", [128, 512], F32); gs_b = ctx.buf()
    bas = ctx.sb("p9_bas", [128, 512], F32); bas_b = ctx.buf()
    prod = ctx.sb("p9_prod", [128, 512], F32); prod_b = ctx.buf()
    ysb = ctx.sb("p9_ysb", [128, T], BF16); ysb_b = ctx.buf()
    pg = [ctx.ps(f"p9_pg{i}", [128, 512]) for i in range(2)]
    pg_b = [ctx.buf() for _ in range(2)]
    pu = [ctx.ps(f"p9_pu{i}", [128, 512]) for i in range(2)]
    pu_b = [ctx.buf() for _ in range(2)]
    pbc = ctx.ps("p9_pbc", [128, 512]); pbc_b = ctx.buf()
    for k in range(8):
        for ts in range(4):
            em.op("pe", lambda e, k=k, ts=ts: e.matmul(
                pbc[:], sel[:, k * 128:(k + 1) * 128],
                wtl[:, ts * 512:(ts + 1) * 512], start=True, stop=True),
                reads=[sel_b, wtl_b], writes=[pbc_b])
            em.op("act", lambda e, ts=ts: e.activation(
                wb_sl[:, ts * 512:(ts + 1) * 512], pbc[:], AF.Copy),
                reads=[pbc_b], accwrites=[wb_b])
        for ic in range(8):
            ot = k * 8 + ic
            i = ot % 2
            em.dma("sq", wgf[i][:].rearrange("p (c f) -> p c f", c=16),
                   wg_dr[ot], writes=[wgf_b[i]])
            em.dma("sq", wuf[i][:].rearrange("p (c f) -> p c f", c=16),
                   wu_dr[ot], writes=[wuf_b[i]])
            em.op("act", lambda e, i=i: e.activation(wgb[i][:], wgf[i][:],
                                                     AF.Copy),
                  reads=[wgf_b[i]], writes=[wgb_b[i]])
            em.op("act", lambda e, i=i: e.activation(wub[i][:], wuf[i][:],
                                                     AF.Copy),
                  reads=[wuf_b[i]], writes=[wub_b[i]])
            for ts in range(4):
                j = ts % 2

                def mmg(e, i=i, ts=ts, j=j):
                    for fc in range(16):
                        e.matmul(pg[j][:], wgb[i][:, fc * 128:(fc + 1) * 128],
                                 xb[:, fc * T + ts * 512: fc * T + ts * 512 + 512],
                                 start=(fc == 0), stop=(fc == 15))
                    for fc in range(16):
                        ins = e.matmul(pu[j][:], wub[i][:, fc * 128:(fc + 1) * 128],
                                       xb[:, fc * T + ts * 512: fc * T + ts * 512 + 512],
                                       start=(fc == 0), stop=(fc == 15))
                    return ins
                em.op("pe", mmg, reads=[wgb_b[i], wub_b[i], xb_b],
                      writes=[pg_b[j], pu_b[j]])
                em.op("act", lambda e, j=j: e.activation(gs[:], pg[j][:], AF.Silu),
                      reads=[pg_b[j]], writes=[gs_b])
                em.op("dve", lambda e, j=j: e.tensor_tensor(
                    bas[:], pu[j][:], gs[:], ALU.mult),
                    reads=[pu_b[j], gs_b], writes=[bas_b])
                em.op("pool", lambda e, ts=ts: e.tensor_tensor(
                    prod[:], bas[:], wb_sl[:, ts * 512:(ts + 1) * 512], ALU.mult),
                    reads=[bas_b, wb_b], writes=[prod_b])
                em.op("dve", lambda e, ic=ic, ts=ts: e.tensor_tensor(
                    ya[:, ic * T + ts * 512: ic * T + ts * 512 + 512],
                    ya[:, ic * T + ts * 512: ic * T + ts * 512 + 512],
                    prod[:], ALU.add),
                    reads=[prod_b, ya_b], accwrites=[ya_b])
    for ic in range(8):
        em.op("dve", lambda e, ic=ic: e.tensor_copy(
            ysb[:], ya[:, ic * T:(ic + 1) * T]),
            reads=[ya_b], writes=[ysb_b])
        em.dma("sq", ysc_dr[ic], ysb[:], reads=[ysb_b])


def build_p9():
    ctx = Ctx("p9")
    nc = ctx.nc
    x3n = nc.dram_tensor("x3n", [D, T], BF16, kind="ExternalInput").ap()
    wtil = nc.dram_tensor("wtil", [8, T], BF16, kind="ExternalInput").ap()
    wg = nc.dram_tensor("w_g", [64, 128, 16, 128], F32, kind="ExternalInput").ap()
    wu = nc.dram_tensor("w_u", [64, 128, 16, 128], F32, kind="ExternalInput").ap()
    ysc = nc.dram_tensor("ysc", [8, 128, T], BF16, kind="ExternalOutput").ap()
    emit_p9(ctx, x3n, wtil, wg, wu, ysc)
    ctx.finish()
    return nc



def emit_p10(ctx, x3n_dr, wsg_dr, wsu_dr, shsc_dr):
    nc, em = ctx.nc, ctx.em
    xb = ctx.sb("pA_xb", [128, 16 * T], BF16); xb_b = ctx.buf()
    em.dma("sq", xb[:].rearrange("p (c t) -> p c t", c=16),
           x3n_dr.rearrange("(c p) t -> p c t", p=128), writes=[xb_b])
    wgf = [ctx.sb(f"pA_wgf{i}", [128, 2048], F32) for i in range(2)]
    wgf_b = [ctx.buf() for _ in range(2)]
    wuf = [ctx.sb(f"pA_wuf{i}", [128, 2048], F32) for i in range(2)]
    wuf_b = [ctx.buf() for _ in range(2)]
    wgb = [ctx.sb(f"pA_wgb{i}", [128, 2048], BF16) for i in range(2)]
    wgb_b = [ctx.buf() for _ in range(2)]
    wub = [ctx.sb(f"pA_wub{i}", [128, 2048], BF16) for i in range(2)]
    wub_b = [ctx.buf() for _ in range(2)]
    gs = ctx.sb("pA_gs", [128, 512], F32); gs_b = ctx.buf()
    shb = [ctx.sb(f"pA_shb{i}", [128, T], BF16) for i in range(2)]
    shb_b = [ctx.buf() for _ in range(2)]
    pg = [ctx.ps(f"pA_pg{i}", [128, 512]) for i in range(2)]
    pg_b = [ctx.buf() for _ in range(2)]
    pu = [ctx.ps(f"pA_pu{i}", [128, 512]) for i in range(2)]
    pu_b = [ctx.buf() for _ in range(2)]
    for ot in range(16):
        i = ot % 2
        em.dma("sq", wgf[i][:].rearrange("p (c f) -> p c f", c=16), wsg_dr[ot],
               writes=[wgf_b[i]])
        em.dma("sq", wuf[i][:].rearrange("p (c f) -> p c f", c=16), wsu_dr[ot],
               writes=[wuf_b[i]])
        em.op("dve", lambda e, i=i: e.tensor_copy(wgb[i][:], wgf[i][:]),
              reads=[wgf_b[i]], writes=[wgb_b[i]])
        em.op("dve", lambda e, i=i: e.tensor_copy(wub[i][:], wuf[i][:]),
              reads=[wuf_b[i]], writes=[wub_b[i]])
        for ts in range(4):
            j = ts % 2

            def mmg(e, i=i, ts=ts, j=j):
                for fc in range(16):
                    e.matmul(pg[j][:], wgb[i][:, fc * 128:(fc + 1) * 128],
                             xb[:, fc * T + ts * 512: fc * T + ts * 512 + 512],
                             start=(fc == 0), stop=(fc == 15))
                for fc in range(16):
                    ins = e.matmul(pu[j][:], wub[i][:, fc * 128:(fc + 1) * 128],
                                   xb[:, fc * T + ts * 512: fc * T + ts * 512 + 512],
                                   start=(fc == 0), stop=(fc == 15))
                return ins
            em.op("pe", mmg, reads=[wgb_b[i], wub_b[i], xb_b],
                  writes=[pg_b[j], pu_b[j]])
            em.op("act", lambda e, j=j: e.activation(gs[:], pg[j][:], AF.Silu),
                  reads=[pg_b[j]], writes=[gs_b])
            em.op("dve", lambda e, i=i, ts=ts, j=j: e.tensor_tensor(
                shb[i][:, ts * 512:(ts + 1) * 512], pu[j][:], gs[:], ALU.mult),
                reads=[pu_b[j], gs_b], accwrites=[shb_b[i]])
        em.dma("sq", shsc_dr[ot], shb[i][:], reads=[shb_b[i]])


def build_p10():
    ctx = Ctx("p10")
    nc = ctx.nc
    x3n = nc.dram_tensor("x3n", [D, T], BF16, kind="ExternalInput").ap()
    wsg = nc.dram_tensor("w_sg", [16, 128, 16, 128], F32, kind="ExternalInput").ap()
    wsu = nc.dram_tensor("w_su", [16, 128, 16, 128], F32, kind="ExternalInput").ap()
    shsc = nc.dram_tensor("shsc", [16, 128, T], BF16, kind="ExternalOutput").ap()
    emit_p10(ctx, x3n, wsg, wsu, shsc)
    ctx.finish()
    return nc



def emit_p11(ctx, ysc_dr, shsc_dr, wd_dr, wsd_dr, x3_dr, yT_dr):
    nc, em = ctx.nc, ctx.em
    ln = LNRes(ctx)
    wdb = ctx.sb("pB_wdb", [128, 16 * 1024], BF16); wdb_b = ctx.buf()
    wsdb = ctx.sb("pB_wsdb", [128, 16 * 2048], BF16); wsdb_b = ctx.buf()
    wf = [ctx.sb(f"pB_wf{i}", [128, 2048], F32) for i in range(2)]
    wf_b = [ctx.buf() for _ in range(2)]
    for ot in range(16):
        i = ot % 2
        em.dma("sq", wf[i][:, 0:1024].rearrange("p (c f) -> p c f", c=8),
               wd_dr[ot], writes=[wf_b[i]])
        em.op("dve", lambda e, ot=ot, i=i: e.tensor_copy(
            wdb[:, ot * 1024:(ot + 1) * 1024], wf[i][:, 0:1024]),
            reads=[wf_b[i]], accwrites=[wdb_b])
        em.dma("sq", wf[i][:].rearrange("p (c f) -> p c f", c=16),
               wsd_dr[ot], writes=[wf_b[i]])
        em.op("dve", lambda e, ot=ot, i=i: e.tensor_copy(
            wsdb[:, ot * 2048:(ot + 1) * 2048], wf[i][:]),
            reads=[wf_b[i]], accwrites=[wsdb_b])
    ysl = ctx.sb("pB_ysl", [128, 8 * 512], BF16); ysl_b = ctx.buf()
    shsl = ctx.sb("pB_shsl", [128, 16 * 512], BF16); shsl_b = ctx.buf()
    xc = [ctx.sb(f"pB_xc{i}", [128, 512], F32) for i in range(2)]
    xc_b = [ctx.buf() for _ in range(2)]
    rsl = ctx.sb("pB_rsl", [128, 16 * 512], F32); rsl_b = ctx.buf()
    pp = [ctx.ps(f"pB_ps{i}", [128, 512]) for i in range(2)]
    pp_b = [ctx.buf() for _ in range(2)]
    for ts in range(4):
        em.dma("sq", ysl[:].rearrange("p (c f) -> p c f", c=8),
               ysc_dr[:, :, ts * 512:(ts + 1) * 512].rearrange("i p f -> p i f"),
               writes=[ysl_b])
        em.dma("sq", shsl[:].rearrange("p (c f) -> p c f", c=16),
               shsc_dr[:, :, ts * 512:(ts + 1) * 512].rearrange("i p f -> p i f"),
               writes=[shsl_b])
        for d2t in range(16):
            j = d2t % 2
            em.dma("pool", xc[j][:],
                   x3_dr[d2t * 128:(d2t + 1) * 128, ts * 512:(ts + 1) * 512],
                   writes=[xc_b[j]])

            def mmg(e, d2t=d2t, j=j):
                for kc in range(8):
                    e.matmul(pp[j][:],
                             wdb[:, d2t * 1024 + kc * 128: d2t * 1024 + kc * 128 + 128],
                             ysl[:, kc * 512:(kc + 1) * 512],
                             start=(kc == 0), stop=False)
                for fc in range(16):
                    ins = e.matmul(
                        pp[j][:],
                        wsdb[:, d2t * 2048 + fc * 128: d2t * 2048 + fc * 128 + 128],
                        shsl[:, fc * 512:(fc + 1) * 512],
                        start=False, stop=(fc == 15))
                return ins
            em.op("pe", mmg, reads=[wdb_b, wsdb_b, ysl_b, shsl_b],
                  writes=[pp_b[j]])
            em.op("dve", lambda e, d2t=d2t, j=j: e.scalar_tensor_tensor(
                rsl[:, d2t * 512:(d2t + 1) * 512], xc[j][:], 1.0, pp[j][:],
                ALU.mult, ALU.add),
                reads=[xc_b[j], pp_b[j]], accwrites=[rsl_b])
        emit_ln(ctx, ln, rsl, rsl_b,
                lambda fc, ts=ts: yT_dr[fc * 128:(fc + 1) * 128,
                                        ts * 512:(ts + 1) * 512],
                None)


def build_p11():
    ctx = Ctx("p11")
    nc = ctx.nc
    ysc = nc.dram_tensor("ysc", [8, 128, T], BF16, kind="ExternalInput").ap()
    shsc = nc.dram_tensor("shsc", [16, 128, T], BF16, kind="ExternalInput").ap()
    wd = nc.dram_tensor("w_d", [16, 128, 8, 128], F32, kind="ExternalInput").ap()
    wsd = nc.dram_tensor("w_sd", [16, 128, 16, 128], F32,
                         kind="ExternalInput").ap()
    x3 = nc.dram_tensor("x3", [D, T], F32, kind="ExternalInput").ap()
    yT = nc.dram_tensor("yT", [D, T], F32, kind="ExternalOutput").ap()
    emit_p11(ctx, ysc, shsc, wd, wsd, x3, yT)
    ctx.finish()
    return nc




def build_p0():
    ctx = Ctx("p0")
    nc = ctx.nc
    xT = nc.dram_tensor("xT", [D, T], F32, kind="ExternalInput").ap()
    w_l0p = nc.dram_tensor("w_l0p", [NWT, 128, 16, 128], F32,
                           kind="ExternalInput").ap()
    proj = nc.dram_tensor("proj", [4, NWT, 128, 512], BF16,
                          kind="ExternalOutput").ap()
    emit_p0(ctx, xT, w_l0p, proj)
    ctx.finish()
    return nc



def prep_weights(ii):
    f32 = lambda a: np.asarray(a, np.float32)
    colT = lambda a, n: np.ascontiguousarray(f32(a).reshape(n, 128).T)
    w = {}
    w["w_l0p"] = host_pack_l0_proj(ii["L0_attn_Wq"], ii["L0_attn_Wk"],
                                   ii["L0_attn_Ws"], ii["L0_attn_Wv"],
                                   ii["L0_attn_Wg"])
    w["gnw"] = f32(ii["L0_attn_g_norm_weight"])
    w["w_l0o"] = pack_lhsT(f32(ii["L0_attn_Wo"]))
    w["w_l0g"] = pack_lhsT(u8_of_i32(np.asarray(ii["L0_ffn_gate_w"])).T)
    w["w_l0u"] = pack_lhsT(u8_of_i32(np.asarray(ii["L0_ffn_up_w"])).T)
    w["w_l0d"] = pack_lhsT(u8_of_i32(np.asarray(ii["L0_ffn_down_w"])).T)
    w["gsc"] = colT(ii["L0_ffn_gate_scale"], 64)
    w["usc"] = colT(ii["L0_ffn_up_scale"], 64)
    w["dsc"] = colT(ii["L0_ffn_down_scale"], 16)
    w["ns"] = colT(ii["L1_attn_norm_scale"], 16)
    w["ss"] = colT(ii["L1_attn_split_scale"], 16)
    w["w_l1q"] = pack_lhsT(f32(ii["L1_attn_Wq"]).T)
    w["w_l1k"] = pack_lhsT(f32(ii["L1_attn_Wk"]).T)
    w["w_l1v"] = pack_lhsT(f32(ii["L1_attn_Wv"]).T)
    w["w_l1o"] = pack_lhsT(f32(ii["L1_attn_Wo"]).T)
    w["w_rt"] = pack_lhsT(f32(ii["L1_ffn_w_router"]))
    w["amp"] = np.ascontiguousarray(f32(ii["L1_ffn_amplitudes"]).reshape(128, 16))
    w["esc"] = f32(ii["L1_ffn_expert_scale"])
    w["w_g"] = pack_lhsT(f32(ii["L1_ffn_w_gate"]))
    w["w_u"] = pack_lhsT(f32(ii["L1_ffn_w_up"]))
    w["w_d"] = pack_lhsT(f32(ii["L1_ffn_w_down"]))
    w["w_sg"] = pack_lhsT(f32(ii["L1_ffn_shared_w_gate"]))
    w["w_su"] = pack_lhsT(f32(ii["L1_ffn_shared_w_up"]))
    w["w_sd"] = pack_lhsT(f32(ii["L1_ffn_shared_w_down"]))
    return w


_BUILD_CACHE = {}


def _get(name, fn):
    if name not in _BUILD_CACHE:
        _BUILD_CACHE[name] = fn()
    return _BUILD_CACHE[name]



def build_full():
    ctx = Ctx("full")
    nc = ctx.nc
    EI = lambda n, s, d: nc.dram_tensor(n, s, d, kind="ExternalInput").ap()
    IN = lambda n, s, d: nc.dram_tensor(n, s, d, kind="Internal").ap()
    xT = EI("xT", [D, T], F32)
    w_l0p = EI("w_l0p", [NWT, 128, 16, 128], F32)
    gnw = EI("gnw", [128], F32)
    w_l0o = EI("w_l0o", [16, 128, 16, 128], F32)
    w_l0g = EI("w_l0g", [64, 128, 16, 128], U8)
    w_l0u = EI("w_l0u", [64, 128, 16, 128], U8)
    w_l0d = EI("w_l0d", [16, 128, 64, 128], U8)
    gsc = EI("gsc", [128, 64], F32)
    usc = EI("usc", [128, 64], F32)
    dsc = EI("dsc", [128, 16], F32)
    ns = EI("ns", [128, 16], F32)
    ss = EI("ss", [128, 16], F32)
    w_l1q = EI("w_l1q", [16, 128, 16, 128], F32)
    w_l1k = EI("w_l1k", [1, 128, 16, 128], F32)
    w_l1v = EI("w_l1v", [1, 128, 16, 128], F32)
    w_l1o = EI("w_l1o", [16, 128, 16, 128], F32)
    w_rt = EI("w_rt", [1, 128, 16, 128], F32)
    amp = EI("amp", [128, 16], F32)
    esc = EI("esc", [128], F32)
    w_g = EI("w_g", [64, 128, 16, 128], F32)
    w_u = EI("w_u", [64, 128, 16, 128], F32)
    w_d = EI("w_d", [16, 128, 8, 128], F32)
    w_sg = EI("w_sg", [16, 128, 16, 128], F32)
    w_su = EI("w_su", [16, 128, 16, 128], F32)
    w_sd = EI("w_sd", [16, 128, 16, 128], F32)
    yT = nc.dram_tensor("yT", [D, T], F32, kind="ExternalOutput").ap()

    proj = IN("s_proj", [4, NWT, 128, 512], BF16)
    og = IN("s_og", [16, 128, T], BF16)
    x1 = IN("s_x1", [D, T], F32)
    x1n = IN("s_x1n", [D, T], BF16)
    hsc = IN("s_hsc", [64, 128, T], BF16)
    x2 = IN("s_x2", [D, T], F32)
    qf = IN("s_qf", [D, T], BF16)
    kf = IN("s_kf", [128, T], BF16)
    vf = IN("s_vf", [128, T], BF16)
    oatt = IN("s_oatt", [16, 128, T], BF16)
    x3 = IN("s_x3", [D, T], F32)
    x3n = IN("s_x3n", [D, T], BF16)
    wtil = IN("s_wtil", [8, T], BF16)
    ysc = IN("s_ysc", [8, 128, T], BF16)
    shsc = IN("s_shsc", [16, 128, T], BF16)

    phases = [
        ("P0_", lambda: emit_p0(ctx, xT, w_l0p, proj)),
        ("P1_", lambda: emit_p1(ctx, proj, gnw, og)),
        ("P2_", lambda: emit_proj_res_ln(ctx, og, w_l0o, xT, x1, x1n, 1.0)),
        ("P3_", lambda: emit_p3(ctx, x1n, w_l0g, w_l0u, gsc, usc, hsc)),
        ("P4_", lambda: emit_p4(ctx, hsc, w_l0d, dsc, x1, x2, None)),
        ("P5_", lambda: emit_p5(ctx, x2, ns, ss, w_l1q, w_l1k, w_l1v,
                                qf, kf, vf)),
        ("P6_", lambda: emit_p6(ctx, qf, kf, vf,
                                oatt.rearrange("h p t -> (h p) t"))),
        ("P7_", lambda: emit_proj_res_ln(ctx, oatt, w_l1o, x2, x3, x3n, 2.0)),
        ("P8_", lambda: emit_p8(ctx, x3n, w_rt, amp, esc, wtil)),
        ("P9_", lambda: emit_p9(ctx, x3n, wtil, w_g, w_u, ysc)),
        ("PA_", lambda: emit_p10(ctx, x3n, w_sg, w_su, shsc)),
        ("PB_", lambda: emit_p11(ctx, ysc, shsc, w_d, w_sd, x3, yT)),
    ]
    for pref, fn in phases:
        ctx.push_scope(pref)
        fn()
        ctx.pop_scope()
    ctx.finish()
    return nc


def kernel(**inputs):
    import os
    if int(os.environ.get("MK_UNFUSED", "0")):
        return kernel_multilaunch(**inputs)
    w = prep_weights(inputs)
    x = np.asarray(inputs["L0_attn_hidden_states"], np.float32)
    B = x.shape[0]
    cores = list(range(B))
    nc = _get("full", build_full)
    maps = [dict(w, xT=np.ascontiguousarray(x[b].T)) for b in cores]
    res = run_bass_kernel_spmd(nc, maps, core_ids=cores)
    return np.stack([np.ascontiguousarray(res.results[b]["yT"].T)
                     for b in cores])


def kernel_multilaunch(**inputs):
    w = prep_weights(inputs)
    x = np.asarray(inputs["L0_attn_hidden_states"], np.float32)
    B = x.shape[0]
    cores = list(range(B))
    xT = [np.ascontiguousarray(x[b].T) for b in range(B)]

    def launch(name, build_fn, per_core):
        nc = _get(name, build_fn)
        res = run_bass_kernel_spmd(nc, per_core, core_ids=cores)
        return res.results

    r = launch("p0", build_p0, [dict(xT=xT[b], w_l0p=w["w_l0p"]) for b in cores])
    proj = [r[b]["proj"] for b in cores]
    r = launch("p1", build_p1, [dict(proj=proj[b], gnw=w["gnw"]) for b in cores])
    og = [r[b]["og"] for b in cores]
    r = launch("p2", build_p2, [dict(og=og[b], w_l0o=w["w_l0o"], xT=xT[b])
                                for b in cores])
    x1 = [r[b]["x1"] for b in cores]
    x1n = [r[b]["x1n"] for b in cores]
    r = launch("p3", build_p3, [dict(x1n=x1n[b], w_l0g=w["w_l0g"],
                                     w_l0u=w["w_l0u"], gsc=w["gsc"],
                                     usc=w["usc"]) for b in cores])
    hsc = [r[b]["hsc"] for b in cores]
    r = launch("p4", build_p4, [dict(hsc=hsc[b], w_l0d=w["w_l0d"],
                                     dsc=w["dsc"], x1=x1[b]) for b in cores])
    x2 = [r[b]["x2"] for b in cores]
    r = launch("p5", build_p5, [dict(x2=x2[b], ns=w["ns"], ss=w["ss"],
                                     w_l1q=w["w_l1q"], w_l1k=w["w_l1k"],
                                     w_l1v=w["w_l1v"]) for b in cores])
    qf = [r[b]["qf"] for b in cores]
    kf = [r[b]["kf"] for b in cores]
    vf = [r[b]["vf"] for b in cores]
    r = launch("p6", build_p6, [dict(qf=qf[b], kf=kf[b], vf=vf[b])
                                for b in cores])
    oatt = [r[b]["oatt"] for b in cores]
    r = launch("p7", build_p7, [dict(oatt=oatt[b].reshape(16, 128, T),
                                     w_l1o=w["w_l1o"], x2=x2[b])
                                for b in cores])
    x3 = [r[b]["x3"] for b in cores]
    x3n = [r[b]["x3n"] for b in cores]
    r = launch("p8", build_p8, [dict(x3n=x3n[b], w_rt=w["w_rt"], amp=w["amp"],
                                     esc=w["esc"]) for b in cores])
    wtil = [r[b]["wtil"] for b in cores]
    r = launch("p9", build_p9, [dict(x3n=x3n[b], wtil=wtil[b], w_g=w["w_g"],
                                     w_u=w["w_u"]) for b in cores])
    ysc = [r[b]["ysc"] for b in cores]
    r = launch("p10", build_p10, [dict(x3n=x3n[b], w_sg=w["w_sg"],
                                       w_su=w["w_su"]) for b in cores])
    shsc = [r[b]["shsc"] for b in cores]
    r = launch("p11", build_p11, [dict(ysc=ysc[b], shsc=shsc[b], w_d=w["w_d"],
                                       w_sd=w["w_sd"], x3=x3[b])
                                  for b in cores])
    out = np.stack([np.ascontiguousarray(r[b]["yT"].T) for b in cores])
    return out
